# version 13
# speedup vs baseline: 1.4696x; 1.0004x over previous
import contextlib
import numpy as np
import ml_dtypes
import concourse.bass as bass
import concourse.mybir as mybir
from concourse.bass_utils import run_bass_kernel_spmd

F32 = mybir.dt.float32
BF16 = mybir.dt.bfloat16
U32 = mybir.dt.uint32
AF = mybir.ActivationFunctionType
ALU = mybir.AluOpType
AX = mybir.AxisListType

NCORES = 8
D = 1024
NF = 8
TOWN = 2048
TB = 1024
SEQ = 8192
NB = 2

BF = ml_dtypes.bfloat16



class Ctx:

    def __init__(self, nc, block, stack):
        self.nc = nc
        self.block = block
        self.stack = stack
        self.S = {}
        self.cnt = {}
        self.T = {}
        self.A = {}
        for s in ("gd", "wd", "sd", "mm", "ac", "dv", "pl", "cc"):
            self.S[s] = stack.enter_context(nc.semaphore(s))
            self.cnt[s] = 0

    def inc(self, instr, s, n=1):
        instr.then_inc(self.S[s], n)
        self.cnt[s] += n
        return self.cnt[s]

    def sb(self, stack, name, shape, dtype):
        t = stack.enter_context(self.nc.sbuf_tensor(name, shape, dtype))
        return t

    def psum_guard(self, te):
        te.wait_ge(self.S["dv"], self.cnt["dv"])
        te.wait_ge(self.S["ac"], self.cnt["ac"])

    def fence(self, eng, sems=("mm", "dv", "ac", "pl", "wd", "gd", "sd")):
        for s in sems:
            if self.cnt[s]:
                eng.wait_ge(self.S[s], self.cnt[s])

    def ps(self, stack, name, shape, dtype=F32):
        return stack.enter_context(self.nc.psum_tensor(name, shape, dtype))



def _p_load(c):
    nc, block = c.nc, c.block
    x_in = c.A["x_in"]
    x_c = c.T["x_c"]
    ident = c.T["ident"]
    st = contextlib.ExitStack()
    xa = c.sb(st, "xa_stage", [128, 16 * 1024], F32)
    ps_tr = [c.ps(st, f"ps_tr{i}", [128, 512], F32) for i in range(2)]

    def _(g):
        c.fence(g)
        c.inc(g.dma_start(out=xa[:, :].rearrange("p (tt f) -> p tt f", tt=16),
                          in_=x_in[:, :].rearrange("(tt p) f -> p tt f", p=128)), "gd", 16)
    block.gpsimd(_)
    ld_done = c.cnt["gd"]

    evac_cnt = [0]
    groups = [(tt, fg) for tt in range(16) for fg in range(2)]
    for gi, (tt, fg) in enumerate(groups):
        def _(te, tt=tt, fg=fg, gi=gi):
            if gi == 0:
                te.wait_ge(c.S["gd"], ld_done)
                te.wait_ge(c.S["wd"], c.cnt["wd"])
            if gi >= 2:
                te.wait_ge(c.S["dv"], evac_done[gi - 2])
            for j in range(4):
                fc = fg * 4 + j
                ins = te.transpose(ps_tr[gi % 2][:, j * 128:(j + 1) * 128],
                                   xa[:, tt * 1024 + fc * 128: tt * 1024 + (fc + 1) * 128],
                                   ident[:, :])
                if j == 3:
                    c.inc(ins, "mm", 1)
        block.tensor(_)
        mm_here = c.cnt["mm"]

        def _(ve, tt=tt, fg=fg, gi=gi, mm_here=mm_here):
            ve.wait_ge(c.S["mm"], mm_here)
            dst = x_c[:, :].rearrange("p (f t) -> p f t", f=8)[
                :, fg * 4:(fg + 1) * 4, tt * 128:(tt + 1) * 128]
            c.inc(ve.tensor_copy(dst, ps_tr[gi % 2][:, :].rearrange("p (f t) -> p f t", f=4)),
                  "dv", 1)
        block.vector(_)
        if gi == 0:
            evac_done = {}
        evac_done[gi] = c.cnt["dv"]
    st.close()


def _p_store(c, dst_name="y_out"):
    nc, block = c.nc, c.block
    y = c.A[dst_name]
    x_c = c.T["x_c"]
    ident = c.T["ident"]
    st = contextlib.ExitStack()
    ya = c.sb(st, "ya_stage", [128, 1024], F32)
    ps_tr = [c.ps(st, f"ps_trS{i}", [128, 512], F32) for i in range(2)]
    store_done = {}
    for tt in range(16):
        for fg in range(2):
            gi = tt * 2 + fg

            def _(te, tt=tt, fg=fg, gi=gi):
                if gi == 0:
                    c.fence(te)
                if gi >= 2:
                    te.wait_ge(c.S["dv"], evac_done[gi - 2])
                if tt >= 1:
                    te.wait_ge(c.S["wd"], store_done.get(tt - 1, 0))
                for j in range(4):
                    fc = fg * 4 + j
                    ins = te.transpose(ps_tr[gi % 2][:, j * 128:(j + 1) * 128],
                                       x_c[:, fc * 2048 + tt * 128: fc * 2048 + (tt + 1) * 128],
                                       ident[:, :])
                    if j == 3:
                        c.inc(ins, "mm", 1)
            block.tensor(_)
            mm_here = c.cnt["mm"]

            def _(ve, fg=fg, gi=gi, mm_here=mm_here):
                if gi == 0:
                    c.fence(ve)
                ve.wait_ge(c.S["mm"], mm_here)
                c.inc(ve.tensor_copy(ya[:, fg * 512:(fg + 1) * 512], ps_tr[gi % 2][:, :]),
                      "dv", 1)
            block.vector(_)
            if gi == 0:
                evac_done = {}
            evac_done[gi] = c.cnt["dv"]

        dv_here = c.cnt["dv"]

        def _(sy, tt=tt, dv_here=dv_here):
            sy.wait_ge(c.S["dv"], dv_here)
            c.inc(sy.dma_start(out=y[tt * 128:(tt + 1) * 128, :], in_=ya[:, :]), "wd", 16)
        block.sync(_)
        store_done[tt] = c.cnt["wd"]

    def _(sy):
        sy.wait_ge(c.S["wd"], c.cnt["wd"])
    block.sync(_)
    st.close()



def _row_broadcast(c, st, rows, n_tok=TOWN, tag=""):
    out = {}
    ones_row = c.T["ones_row"]
    ps = {}
    for ap, name in rows:
        ps[name] = c.ps(st, f"bc_{name}{tag}", [128, n_tok], F32)

    def _(te):
        te.wait_ge(c.S["dv"], c.cnt["dv"])
        te.wait_ge(c.S["ac"], c.cnt["ac"])
        n = len(rows) * (n_tok // 512)
        i = 0
        for ap, name in rows:
            for q in range(n_tok // 512):
                ins = te.matmul(ps[name][:, q * 512:(q + 1) * 512], ones_row[:, :],
                                ap[0:1, q * 512:(q + 1) * 512], start=True, stop=True)
                i += 1
                if i == n:
                    c.inc(ins, "mm", 1)
    c.block.tensor(_)
    return ps, c.cnt["mm"]


def p_layernorm(c, out_name, eps=1e-5, fold_eps2=None, affine=None,
                add_to_stream=False, rms=False, src_bf=None, dbg_rows_ap=None,
                tag=""):
    nc, block = c.nc, c.block
    x_c = c.T["x_c"]
    ones_col = c.T["ones_col_bf"]
    out_bf = c.T[out_name]
    st = contextlib.ExitStack()
    xbf = [c.sb(st, f"ln_xbf{i}{tag}", [128, 2048], BF16) for i in range(1)] * 2
    sqbf = [c.sb(st, f"ln_sqbf{i}{tag}", [128, 2048], BF16) for i in range(1)] * 2
    rowbuf = c.sb(st, f"ln_rows{tag}", [1, 3 * 2048], F32)
    mu_row = rowbuf[0:1, 0:2048]
    tA_row = rowbuf[0:1, 2048:4096]
    tB_row = rowbuf[0:1, 4096:6144]
    t1_row = tA_row
    t2_row = tB_row
    c_row = tA_row

    stats_st = contextlib.ExitStack()
    sum_ps = None if rms else c.ps(stats_st, f"ln_sum{tag}", [1, 2048], F32)
    sq_ps = c.ps(stats_st, f"ln_sq{tag}", [1, 2048], F32)

    prep_done = {}
    mm_chunk = {}
    for fc in range(NF):
        def _(ve, fc=fc):
            if fc >= 1:
                ve.wait_ge(c.S["mm"], mm_chunk[fc - 1])
            if rms:
                src = src_bf[:, fc * 2048:(fc + 1) * 2048]
                ins = ve.tensor_tensor(sqbf[fc % 2][:, :], src, src, op=ALU.mult)
            else:
                ve.tensor_copy(xbf[fc % 2][:, :], x_c[:, fc * 2048:(fc + 1) * 2048])
                ins = ve.tensor_tensor(sqbf[fc % 2][:, :], xbf[fc % 2][:, :], xbf[fc % 2][:, :],
                                       op=ALU.mult)
            c.inc(ins, "dv", 1)
        block.vector(_)
        prep_done[fc] = c.cnt["dv"]

        def _(te, fc=fc):
            te.wait_ge(c.S["dv"], prep_done[fc])
            for q in range(4):
                if not rms:
                    te.matmul(sum_ps[0:1, q * 512:(q + 1) * 512], ones_col[:, :],
                              xbf[fc % 2][:, q * 512:(q + 1) * 512],
                              start=(fc == 0), stop=(fc == NF - 1))
                ins = te.matmul(sq_ps[0:1, q * 512:(q + 1) * 512], ones_col[:, :],
                                sqbf[fc % 2][:, q * 512:(q + 1) * 512],
                                start=(fc == 0), stop=(fc == NF - 1))
                if q == 3:
                    c.inc(ins, "mm", 1)
        block.tensor(_)
        mm_chunk[fc] = c.cnt["mm"]

    stats_mm = c.cnt["mm"]

    def _(ve):
        ve.wait_ge(c.S["mm"], stats_mm)
        if rms:
            ins = ve.tensor_scalar(t1_row, sq_ps[0:1, :], 1.0 / D, eps, op0=ALU.mult, op1=ALU.add)
        else:
            ve.tensor_scalar(mu_row, sum_ps[0:1, :], 1.0 / D, None, op0=ALU.mult)
            ve.tensor_scalar(t1_row, sq_ps[0:1, :], 1.0 / D, None, op0=ALU.mult)
            ve.tensor_tensor(t2_row, mu_row, mu_row, op=ALU.mult)
            ve.tensor_tensor(t1_row, t1_row, t2_row, op=ALU.subtract)
            if fold_eps2 is not None:
                e2 = fold_eps2
                ins = ve.tensor_scalar(t1_row, t1_row, 1.0 + e2, eps * e2, op0=ALU.mult, op1=ALU.add)
            else:
                ins = ve.tensor_scalar(t1_row, t1_row, eps, None, op0=ALU.add)
        c.inc(ins, "dv", 1)
    block.vector(_)
    row_dv = c.cnt["dv"]

    def _(se):
        se.wait_ge(c.S["dv"], row_dv)
        c.inc(se.activation(t2_row, t1_row, AF.Sqrt), "ac", 1)
    block.scalar(_)
    row_ac = c.cnt["ac"]

    def _(ve):
        ve.wait_ge(c.S["ac"], row_ac)
        c.inc(ve.reciprocal(c_row, t2_row), "dv", 1)
    block.vector(_)

    stats_st.close()

    rows = [(c_row, "c")] if rms else [(mu_row, "mu"), (c_row, "c")]
    bc, bc_mm = _row_broadcast(c, st, rows, tag=tag)

    dtmp = c.sb(st, f"ln_dtmp{tag}", [128, 2048], F32)
    etmp = sqbf[0]
    for fc in range(NF):
        def _(ve, fc=fc):
            if fc == 0:
                ve.wait_ge(c.S["mm"], bc_mm)
            dst = out_bf[:, fc * 2048:(fc + 1) * 2048]
            if rms:
                src = src_bf[:, fc * 2048:(fc + 1) * 2048]
                ve.tensor_tensor(dtmp[:, :], bc["c"][:, :], src, op=ALU.mult)
                g_col = affine[0]
                ins = ve.tensor_scalar(dst, dtmp[:, :], g_col[:, fc:fc + 1], None, op0=ALU.mult)
            else:
                src = x_c[:, fc * 2048:(fc + 1) * 2048]
                ve.tensor_tensor(dtmp[:, :], src, bc["mu"][:, :], op=ALU.subtract)
                if affine is not None:
                    ve.tensor_tensor(etmp[:, :], dtmp[:, :], bc["c"][:, :], op=ALU.mult)
                    g_col, b_col = affine
                    ins = ve.tensor_scalar(dst, etmp[:, :], g_col[:, fc:fc + 1],
                                           b_col[:, fc:fc + 1], op0=ALU.mult, op1=ALU.add)
                else:
                    ins = ve.tensor_tensor(dst, dtmp[:, :], bc["c"][:, :], op=ALU.mult)
            if add_to_stream:
                ins = ve.tensor_tensor(src, src, dst, op=ALU.add)
            c.inc(ins, "dv", 1)
        block.vector(_)
    final_dv = c.cnt["dv"]

    if dbg_rows_ap is not None:
        def _(sy):
            sy.wait_ge(c.S["dv"], final_dv)
            c.inc(sy.dma_start(out=dbg_rows_ap[:, :], in_=rowbuf[:, :]), "wd", 16)
            sy.wait_ge(c.S["wd"], c.cnt["wd"])
        block.sync(_)

    def _(te):
        te.wait_ge(c.S["dv"], final_dv)
    block.tensor(_)
    st.close()



def p2_qkv_l0(c):
    nc, block = c.nc, c.block
    A = c.A
    xn = c.T["xn_bf"]
    ones_row = c.T["ones_row"]
    ones_col = c.T["ones_col_bf"]
    st = contextlib.ExitStack()

    k_stage = [c.sb(st, f"k_stage{i}", [128, 2048], BF16) for i in range(2)]
    sq = [c.sb(st, f"p2_sq{i}", [128, 2048], BF16) for i in range(2)]
    nrows = c.sb(st, "p2_nrows", [1, 512], F32)
    qrow = nrows[0:1, 0:32]
    krow = nrows[0:1, 32:64]
    kall = nrows[0:1, 64:320]
    trow = nrows[0:1, 320:352]
    srow = nrows[0:1, 352:384]
    brow = nrows[0:1, 384:416]

    wst = contextlib.ExitStack()
    w_sb = c.sb(wst, "w_sb", [128, 8 * 1024], BF16)

    def _(g):
        c.fence(g)
        c.inc(g.dma_start(out=w_sb[:, :].rearrange("p (fc n) -> p fc n", fc=8),
                          in_=A["wk"][:, :].rearrange("(fc p) n -> p fc n", p=128)),
              "gd", 16)
    block.gpsimd(_)
    w_loaded = c.cnt["gd"]

    ps_st = contextlib.ExitStack()
    qk_ps = c.ps(ps_st, "qk_ps", [128, 2048], F32)
    nr_ps = c.ps(ps_st, "nr_ps", [1, 2048], F32)

    evac_done = {}
    red_done = {}
    pay_done = {}
    for jt in range(24):
        is_q = jt >= 8
        p = jt - 8 if is_q else jt

        if jt == 8:
            def _(g):
                g.wait_ge(c.S["dv"], red_done[(7, 1)][0])
                c.inc(g.dma_start(out=A["pay0m"][:, :], in_=krow), "gd", 16)
                g.wait_ge(c.S["gd"], c.cnt["gd"])
                g.wait_ge(c.S["wd"], c.cnt["wd"])
                c.inc(g.collective_compute(
                    "AllGather", ALU.bypass, replica_groups=[list(range(8))],
                    ins=[A["pay0m"][:, :].opt()], outs=[A["gath0m"][:, :].opt()]),
                    "cc", 1)
                c.p2_ccm = c.cnt["cc"]
                c.inc(g.collective_compute(
                    "AllGather", ALU.bypass, replica_groups=[list(range(8))],
                    ins=[A["pay0k0"][:, :].opt()], outs=[A["gath0k0"][:, :].opt()]),
                    "cc", 1)
            block.gpsimd(_)
            def _(g):
                g.wait_ge(c.S["mm"], c.cnt["mm"])
                c.inc(g.dma_start(out=w_sb[:, :].rearrange("p (fc n) -> p fc n", fc=8),
                                  in_=A["wq"][:, :].rearrange("(fc p) n -> p fc n", p=128)),
                      "gd", 16)
            block.gpsimd(_)
            wq_loaded = c.cnt["gd"]

        def _(te, jt=jt, p=p, is_q=is_q):
            if jt == 0:
                c.fence(te)
                te.wait_ge(c.S["gd"], w_loaded)
            if jt == 8:
                te.wait_ge(c.S["gd"], wq_loaded)
            if jt >= 1:
                te.wait_ge(c.S["ac"], evac_done[jt - 1])
            for tq in range(4):
                if is_q:
                    for hb in range(2):
                        for fc in range(NF):
                            ins = te.matmul(
                                qk_ps[hb * 64:hb * 64 + 64, tq * 512:(tq + 1) * 512],
                                w_sb[:, fc * 1024 + p * 64: fc * 1024 + (p + 1) * 64],
                                xn[:, fc * 2048 + tq * 512: fc * 2048 + (tq + 1) * 512],
                                start=(fc == 0), stop=(fc == NF - 1))
                else:
                    for fc in range(NF):
                        ins = te.matmul(
                            qk_ps[:, tq * 512:(tq + 1) * 512],
                            w_sb[:, fc * 1024 + p * 128: fc * 1024 + (p + 1) * 128],
                            xn[:, fc * 2048 + tq * 512: fc * 2048 + (tq + 1) * 512],
                            start=(fc == 0), stop=(fc == NF - 1))
            c.inc(ins, "mm", 1)
        block.tensor(_)
        mm_proj = c.cnt["mm"]

        def _(se, jt=jt, p=p, is_q=is_q, mm_proj=mm_proj):
            se.wait_ge(c.S["mm"], mm_proj)
            if is_q:
                dst = c.T["Q_sb"][:, p * 2048:(p + 1) * 2048]
            else:
                if jt >= 2:
                    se.wait_ge(c.S["wd"], pay_done[jt - 2])
                dst = k_stage[jt % 2][:, :]
            c.inc(se.copy(dst, qk_ps[:, :]), "ac", 1)
        block.scalar(_)
        evac_done[jt] = c.cnt["ac"]

        if not is_q:
            def _(sy, jt=jt, p=p):
                sy.wait_ge(c.S["ac"], evac_done[jt])
                c.inc(sy.dma_start(
                    out=A[f"pay0k{p // 4}"][(p % 4) * 128:(p % 4 + 1) * 128, :],
                    in_=k_stage[jt % 2][:, :]), "wd", 16)
            block.sync(_)
            pay_done[jt] = c.cnt["wd"]

        def _(ve, jt=jt, p=p, is_q=is_q):
            ve.wait_ge(c.S["ac"], evac_done[jt])
            if jt >= 2:
                ve.wait_ge(c.S["mm"], red_done[(jt - 2, 1)][1])
            src = (c.T["Q_sb"][:, p * 2048:(p + 1) * 2048] if is_q
                   else k_stage[jt % 2][:, :])
            c.inc(ve.tensor_tensor(sq[jt % 2][:, :], src, src, op=ALU.mult), "dv", 1)
        block.vector(_)
        sq_dv = c.cnt["dv"]

        hh_list = [0] if is_q else [0, 1]
        for hh in hh_list:
            def _(te, jt=jt, hh=hh, is_q=is_q, sq_dv=sq_dv):
                te.wait_ge(c.S["dv"], sq_dv)
                if (jt, hh) != (0, 0):
                    prev = (jt, hh - 1) if hh == 1 else (jt - 1, 1)
                    te.wait_ge(c.S["dv"], red_done[prev][0])
                if is_q:
                    base, kk = 0, 128
                else:
                    base, kk = hh * 64, 64
                for tq in range(4):
                    ins = te.matmul(nr_ps[0:1, tq * 512:(tq + 1) * 512],
                                    ones_col[base:base + kk, 0:1],
                                    sq[jt % 2][base:base + kk, tq * 512:(tq + 1) * 512],
                                    start=True, stop=True)
                c.inc(ins, "mm", 1)
            block.tensor(_)
            nm_mm = c.cnt["mm"]

            def _(ve, jt=jt, p=p, hh=hh, is_q=is_q, nm_mm=nm_mm):
                ve.wait_ge(c.S["mm"], nm_mm)
                row = qrow if is_q else krow
                h = p if is_q else 2 * p + hh
                for b in range(2):
                    ins = ve.reduce_max(row[0:1, b * 16 + h: b * 16 + h + 1],
                                        nr_ps[0:1, b * 1024:(b + 1) * 1024],
                                        axis=AX.X)
                c.inc(ins, "dv", 1)
            block.vector(_)
            red_done[(jt, hh)] = (c.cnt["dv"], c.cnt["mm"])
            if is_q:
                red_done[(jt, 1)] = red_done[(jt, 0)]

    wst.close()
    ps_st.close()

    wv_st = contextlib.ExitStack()
    wv_sb = c.sb(wv_st, "wv_sb", [128, 8 * 1024], BF16)
    v_stage = [c.sb(wv_st, f"v_stage{i}", [128, 1024], BF16) for i in range(2)]
    v_ps = c.ps(wv_st, "v_ps", [128, 1024], F32)

    def _(g):
        c.fence(g)
        c.inc(g.dma_start(out=wv_sb[:, :].rearrange("p (fc n) -> p fc n", fc=8),
                          in_=A["wv"][:, :].rearrange("(fc p) n -> p fc n", p=128)),
              "gd", 16)
    block.gpsimd(_)
    wv_loaded = c.cnt["gd"]

    vevac = {}
    vpay = {}
    for vj, (hg, ts_i) in enumerate([(hg, t) for hg in range(2) for t in range(16)]):
        def _(te, vj=vj, hg=hg, ts_i=ts_i):
            if vj == 0:
                c.fence(te)
                te.wait_ge(c.S["gd"], wv_loaded)
            if vj >= 2:
                te.wait_ge(c.S["ac"], vevac[vj - 2])
            for fc in range(NF):
                ins = te.matmul(v_ps[:, (vj % 2) * 512:(vj % 2 + 1) * 512],
                                xn[:, fc * 2048 + ts_i * 128: fc * 2048 + (ts_i + 1) * 128],
                                wv_sb[:, fc * 1024 + hg * 512: fc * 1024 + (hg + 1) * 512],
                                start=(fc == 0), stop=(fc == NF - 1))
            c.inc(ins, "mm", 1)
        block.tensor(_)
        vmm = c.cnt["mm"]

        def _(se, vj=vj, vmm=vmm):
            se.wait_ge(c.S["mm"], vmm)
            if vj >= 2:
                se.wait_ge(c.S["wd"], vpay[vj - 2])
            c.inc(se.copy(v_stage[vj % 2][:, 0:512], v_ps[:, (vj % 2) * 512:(vj % 2 + 1) * 512]),
                  "ac", 1)
        block.scalar(_)
        vevac[vj] = c.cnt["ac"]

        def _(sy, vj=vj, hg=hg, ts_i=ts_i):
            sy.wait_ge(c.S["ac"], vevac[vj])
            dst = A[f"pay0v{hg}"][ts_i * 128:(ts_i + 1) * 128, :]
            c.inc(sy.dma_start(out=dst, in_=v_stage[vj % 2][:, 0:512]), "wd", 16)
        block.sync(_)
        vpay[vj] = c.cnt["wd"]

        if ts_i == 15:
            def _(g, hg=hg):
                g.wait_ge(c.S["wd"], c.cnt["wd"])
                c.inc(g.collective_compute(
                    "AllGather", ALU.bypass, replica_groups=[list(range(8))],
                    ins=[A[f"pay0v{hg}"][:, :].opt()],
                    outs=[A[f"gath0v{hg}"][:, :].opt()]), "cc", 1)
                if hg == 0:
                    c.inc(g.collective_compute(
                        "AllGather", ALU.bypass, replica_groups=[list(range(8))],
                        ins=[A["pay0k1"][:, :].opt()],
                        outs=[A["gath0k1"][:, :].opt()]), "cc", 1)
            block.gpsimd(_)
            if hg == 0:
                c.p2_cc_g0 = c.cnt["cc"] - 1
            else:
                c.p2_cc_g1 = c.cnt["cc"]

    wv_st.close()

    def _(g):
        g.wait_ge(c.S["cc"], c.p2_ccm)
        src = A["gath0m"][:, :].rearrange("(o r) n -> o r n", o=1)
        c.inc(g.dma_start(out=kall.rearrange("o (r n) -> o r n", r=8), in_=src), "gd", 16)
    block.gpsimd(_)
    kall_ld = c.cnt["gd"]

    def _(ve):
        ve.wait_ge(c.S["gd"], kall_ld)
        ve.tensor_tensor(kall[0:1, 0:128], kall[0:1, 0:128], kall[0:1, 128:256], op=ALU.max)
        ve.tensor_tensor(kall[0:1, 0:64], kall[0:1, 0:64], kall[0:1, 64:128], op=ALU.max)
        ve.tensor_tensor(kall[0:1, 0:32], kall[0:1, 0:32], kall[0:1, 32:64], op=ALU.max)
        c.inc(ve.tensor_tensor(trow, qrow, kall[0:1, 0:32], op=ALU.mult), "dv", 1)
    block.vector(_)
    t_dv = c.cnt["dv"]

    def _(se):
        se.wait_ge(c.S["dv"], t_dv)
        c.inc(se.activation(srow, trow, AF.Sqrt), "ac", 1)
    block.scalar(_)
    s_ac = c.cnt["ac"]

    def _(ve):
        ve.wait_ge(c.S["ac"], s_ac)
        c.inc(ve.tensor_scalar(brow, srow, -1.0 / (32.0 * float(np.sqrt(2.0))), None, op0=ALU.mult), "dv", 1)
    block.vector(_)
    b_dv = c.cnt["dv"]

    bb_ps = c.ps(st, "bb_ps", [128, 32], F32)

    def _(te):
        te.wait_ge(c.S["dv"], b_dv)
        c.psum_guard(te)
        c.inc(te.matmul(bb_ps[:, :], ones_row[:, :], brow, start=True, stop=True), "mm", 1)
    block.tensor(_)
    bb_mm = c.cnt["mm"]

    def _(ve):
        ve.wait_ge(c.S["mm"], bb_mm)
        c.inc(ve.tensor_copy(c.T["bias_cols"][:, :], bb_ps[:, :]), "dv", 1)
    block.vector(_)

    def _(te):
        te.wait_ge(c.S["dv"], c.cnt["dv"])
    block.tensor(_)
    st.close()


def p3_flash_l0(c):
    nc, block = c.nc, c.block
    A = c.A
    Q_sb = c.T["Q_sb"]
    bias_cols = c.T["bias_cols"]
    st = contextlib.ExitStack()
    Kbuf = [c.sb(st, f"l0_K{i}", [128, 8192], BF16) for i in range(2)]
    Vbuf = [c.sb(st, f"l0_V{i}", [128, 65 * 64], BF16) for i in range(2)]
    NBUF = 3
    probs = [c.sb(st, f"l0_pr{i}", [128, 1024], BF16) for i in range(NBUF)]
    oacc_sb = c.sb(st, "l0_oacc", [128, 1024], F32)
    otmp = c.sb(st, "l0_otmp", [64, 1024], BF16)
    rc = c.sb(st, "l0_rc", [1, 1024], F32)
    S_ps = [c.ps(st, f"l0_S{i}", [128, 1024], F32) for i in range(NBUF)]
    oa_ps = c.ps(st, "l0_oa", [65, 1024], F32)

    def _(g):
        c.fence(g)
        c.inc(g.memset(Vbuf[0][:, :], 1.0), "pl", 1)
        c.inc(g.memset(Vbuf[1][:, :], 1.0), "pl", 1)
    block.gpsimd(_)
    ones_set = c.cnt["pl"]

    KB = 64
    it_list = [(b, h) for h in range(16) for b in range(2)]
    ld_done = {}
    it_done = {}

    for it, (b, h) in enumerate(it_list):
        def _(sy, it=it, b=b, h=h):
            if it == 0:
                c.fence(sy)
                sy.wait_ge(c.S["pl"], ones_set)
            if it >= 2:
                sy.wait_ge(c.S["mm"], it_done[it - 2])
            buf = it % 2
            sy.wait_ge(c.S["cc"], c.p2_cc_g0 if h < 8 else c.p2_cc_g1)
            gk = A[f"gath0k{h // 8}"]
            gv = A[f"gath0v{h // 8}"]
            hl = h % 8
            for r in range(8):
                for hb in range(2):
                    c.inc(sy.dma_start(
                        out=Kbuf[buf][hb * 64: hb * 64 + 64, r * 1024:(r + 1) * 1024],
                        in_=gk[r * 512 + 64 * hl: r * 512 + 64 * hl + 64,
                               b * 1024:(b + 1) * 1024]), "wd", 16)
            for r in range(8):
                src = gv[r * 2048 + b * 1024: r * 2048 + (b + 1) * 1024,
                         64 * hl: 64 * hl + 64]
                src = src.rearrange("(blk p) n -> p blk n", p=128)
                dst = Vbuf[buf][:, :].rearrange("p (blk n) -> p blk n",
                                                n=65)[:, r * 8:(r + 1) * 8, 0:64]
                c.inc(sy.dma_start(out=dst, in_=src), "wd", 16)
        block.sync(_)
        ld_done[it] = c.cnt["wd"]

        buf = it % 2
        qcol0 = h * 2048 + b * 1024
        prow0 = 64 * (h % 2)
        mm_base = c.cnt["mm"]
        ac_base = c.cnt["ac"]
        qk_sched = {kb: mm_base + (kb + 1 if kb < 2 else 2 * kb - 1) for kb in range(KB)}
        exp_sched = {kb: ac_base + kb + 1 for kb in range(KB)}

        def _(te, it=it, buf=buf, qcol0=qcol0):
            for kb in range(KB):
                if kb == 0:
                    te.wait_ge(c.S["wd"], ld_done[it])
                    if it >= 1:
                        te.wait_ge(c.S["ac"], exp_prev_last[0])
                        te.wait_ge(c.S["dv"], epi_prev[0])
                if kb >= NBUF:
                    te.wait_ge(c.S["ac"], exp_sched[kb - NBUF])
                for half in range(2):
                    ins = te.matmul(
                        S_ps[kb % NBUF][:, half * 512:(half + 1) * 512],
                        Kbuf[buf][:, kb * 128:(kb + 1) * 128],
                        Q_sb[:, qcol0 + half * 512: qcol0 + (half + 1) * 512],
                        start=True, stop=True)
                c.inc(ins, "mm", 1)
                if kb >= 2:
                    pkb = kb - 2
                    te.wait_ge(c.S["ac"], exp_sched[pkb])
                    for half in range(2):
                        ins = te.matmul(
                            oa_ps[0:65, half * 512:(half + 1) * 512],
                            Vbuf[buf][:, 65 * pkb: 65 * pkb + 65],
                            probs[pkb % NBUF][:, half * 512:(half + 1) * 512],
                            start=(pkb == 0), stop=False)
                    c.inc(ins, "mm", 1)
            for pkb in (KB - 2, KB - 1):
                te.wait_ge(c.S["ac"], exp_sched[pkb])
                for half in range(2):
                    ins = te.matmul(oa_ps[0:65, half * 512:(half + 1) * 512],
                                    Vbuf[buf][:, 65 * pkb: 65 * pkb + 65],
                                    probs[pkb % NBUF][:, half * 512:(half + 1) * 512],
                                    start=False, stop=(pkb == KB - 1))
                c.inc(ins, "mm", 1)
        block.tensor(_)
        assert c.cnt["mm"] == mm_base + 2 * KB

        def _(se, b=b, h=h):
            for kb in range(KB):
                se.wait_ge(c.S["mm"], qk_sched[kb])
                c.inc(se.activation(probs[kb % NBUF][:, :], S_ps[kb % NBUF][:, :], AF.Exp,
                                    bias=bias_cols[:, b * 16 + h: b * 16 + h + 1],
                                    scale=1.0 / 64.0), "ac", 1)
        block.scalar(_)
        assert c.cnt["ac"] == ac_base + KB
        it_done[it] = c.cnt["mm"]
        exp_prev_last = (ac_base + KB,)

        def _(ve, it=it):
            ve.wait_ge(c.S["mm"], it_done[it])
            c.inc(ve.tensor_copy(oacc_sb[0:65, :], oa_ps[0:65, :]), "dv", 1)
            c.inc(ve.reciprocal(rc[0:1, :], oacc_sb[64:65, :]), "dv", 1)
        block.vector(_)
        rc_dv = c.cnt["dv"]

        def _(te, it=it):
            te.wait_ge(c.S["dv"], rc_dv)
            for half in range(2):
                ins = te.matmul(S_ps[0][0:64, half * 512:(half + 1) * 512],
                                c.T["ones_row"][0:1, 0:64],
                                rc[0:1, half * 512:(half + 1) * 512],
                                start=True, stop=True)
            c.inc(ins, "mm", 1)
        block.tensor(_)
        rb_mm = c.cnt["mm"]

        def _(ve, it=it, rb_mm=rb_mm):
            ve.wait_ge(c.S["mm"], rb_mm)
            c.inc(ve.tensor_tensor(otmp[:, :], S_ps[0][0:64, :], oacc_sb[0:64, :],
                                   op=ALU.mult), "dv", 1)
        block.vector(_)
        mul_dv = c.cnt["dv"]

        xsl = c.T["x_c"][64 * (h % 2): 64 * (h % 2) + 64,
                         (h // 2) * 2048 + b * 1024: (h // 2) * 2048 + (b + 1) * 1024]
        if h % 2 == 0:
            def _(ve, xsl=xsl, mul_dv=mul_dv):
                c.inc(ve.tensor_tensor(xsl, xsl, otmp[:, :], op=ALU.add), "dv", 1)
            block.vector(_)
        else:
            def _(te, mul_dv=mul_dv):
                te.wait_ge(c.S["dv"], mul_dv)
                for half in range(2):
                    ins = te.matmul(S_ps[1][64:128, half * 512:(half + 1) * 512],
                                    c.T["ident_bf"][0:64, 0:64],
                                    otmp[:, half * 512:(half + 1) * 512],
                                    start=True, stop=True)
                c.inc(ins, "mm", 1)
            block.tensor(_)
            mv_mm = c.cnt["mm"]

            def _(ve, xsl=xsl, mv_mm=mv_mm):
                ve.wait_ge(c.S["mm"], mv_mm)
                c.inc(ve.tensor_tensor(xsl, xsl, S_ps[1][64:128, :], op=ALU.add), "dv", 1)
            block.vector(_)
        epi_prev = (c.cnt["dv"],)

    def _(te):
        te.wait_ge(c.S["dv"], c.cnt["dv"])
    block.tensor(_)
    st.close()



def p4_ffn_l0(c):
    nc, block = c.nc, c.block
    A = c.A
    st = contextlib.ExitStack()
    h1 = c.sb(st, "h1_bf", [128, 16384], BF16)
    c.T["h1_bf"] = h1
    p_layernorm(c, "h1_bf", eps=1e-5, tag="_p4")

    ones_row = c.T["ones_row"]

    cst = contextlib.ExitStack()
    gw_sb = c.sb(st, "gw_sb", [128, 64], BF16)
    bvec = c.sb(st, "bvec", [128, 36], F32)
    b2_sb = c.sb(st, "b2_sb", [8, 1024], BF16)
    bs2row = c.sb(st, "bs2row", [1, 1024], BF16)
    sel8 = c.sb(st, "sel8_sb", [8, 1024], BF16)
    ones_bfr = c.sb(st, "ones_bfr", [1, 512], BF16)
    rs_col = c.sb(st, "rs_col", [128, 1], F32)
    routeT = c.sb(st, "routeT", [8, 2048], BF16)
    gfb_sb = c.sb(st, "gfb_sb", [128, 2048], BF16)
    s_sb = c.sb(st, "s_sb", [128, 4 * 2048], BF16)
    ge_tmp = [c.sb(st, f"ge_tmp{i}", [128, 1024], BF16) for i in range(2)]

    b_stage = c.sb(cst, "b_stage", [8, 1024], F32)

    def _(g):
        c.fence(g)
        c.inc(g.dma_start(out=gw_sb[:, :].rearrange("p (fc e) -> p fc e", fc=8),
                          in_=A["gate_W"][:, :].rearrange("(fc p) e -> p fc e", p=128)),
              "gd", 16)
        c.inc(g.dma_start(out=b2_sb[:, :], in_=A["b2"][:, :]), "gd", 16)
        c.inc(g.dma_start(out=bs2row[:, :], in_=A["bs2"][:, :]), "gd", 16)
        c.inc(g.dma_start(out=sel8[:, :], in_=A["sel8"][:, :]), "gd", 16)
        c.inc(g.memset(ones_bfr[:, :], 1.0), "pl", 1)
        c.inc(g.dma_start(out=b_stage[0:8, 0:512], in_=A["b1"][:, :]), "gd", 16)
        c.inc(g.dma_start(out=b_stage[0:1, 512:1024], in_=A["bs1"][:, :]), "gd", 16)
    block.gpsimd(_)
    consts_gd = c.cnt["gd"]

    bps = c.ps(cst, "b_ps", [128, 40], F32)
    rs_row = c.sb(cst, "rs_row", [1, 1], F32)

    def _(g):
        c.inc(g.dma_start(out=rs_row[:, :], in_=A["rs"][:, :]), "gd", 16)
    block.gpsimd(_)
    rs_gd = c.cnt["gd"]

    def _(te):
        c.fence(te)
        te.wait_ge(c.S["gd"], consts_gd)
        te.wait_ge(c.S["gd"], rs_gd)
        for hc in range(4):
            te.transpose(bps[:, hc * 8:(hc + 1) * 8],
                         b_stage[0:8, hc * 128:(hc + 1) * 128], c.T["ident"][0:8, 0:8])
        for hc in range(4):
            te.transpose(bps[:, 32 + hc: 33 + hc],
                         b_stage[0:1, 512 + hc * 128: 512 + (hc + 1) * 128],
                         c.T["ident"][0:1, 0:1])
        c.inc(te.matmul(bps[:, 36:37], ones_row[:, :], rs_row[:, :],
                        start=True, stop=True), "mm", 1)
    block.tensor(_)
    bt_mm = c.cnt["mm"]

    def _(ve):
        ve.wait_ge(c.S["mm"], bt_mm)
        ve.tensor_copy(bvec[:, :], bps[:, 0:36])
        c.inc(ve.tensor_copy(rs_col[:, :], bps[:, 36:37]), "dv", 1)
    block.vector(_)
    bt_dv = c.cnt["dv"]

    def _(te):
        te.wait_ge(c.S["dv"], bt_dv)
    block.tensor(_)
    cst.close()

    gate_st = contextlib.ExitStack()
    lg_ps = c.ps(gate_st, "lg_ps", [128, 8], F32)
    rt_ps = c.ps(gate_st, "rt_ps", [8, 128], F32)
    gsc = c.sb(gate_st, "gate_sc", [128, 64], F32)
    sg = gsc[:, 0:8]
    s2 = gsc[:, 8:16]
    eq1 = gsc[:, 16:24]
    s3 = gsc[:, 24:32]
    eq2 = gsc[:, 32:40]
    route_t = gsc[:, 40:48]
    colv = gsc[:, 48:56]
    m1 = colv[:, 0:1]
    m2 = colv[:, 1:2]
    wsum = colv[:, 2:3]
    rw = colv[:, 3:4]
    w1c = colv[:, 4:5]
    w2c = colv[:, 5:6]

    lg_done = {}
    sg_done = {}
    gm_done = {}
    tr_done = {}
    rt_done = {}
    for tt in range(16):
        def _(te, tt=tt):
            if tt == 0:
                c.psum_guard(te)
            else:
                te.wait_ge(c.S["ac"], sg_done[tt - 1])
                te.wait_ge(c.S["ac"], rt_done.get(tt - 1, 0))
            for fc in range(NF):
                ins = te.matmul(lg_ps[:, :],
                                h1[:, fc * 2048 + tt * 128: fc * 2048 + (tt + 1) * 128],
                                gw_sb[:, fc * 8:(fc + 1) * 8],
                                start=(fc == 0), stop=(fc == NF - 1))
            c.inc(ins, "mm", 1)
        block.tensor(_)
        lg_done[tt] = c.cnt["mm"]

        def _(se, tt=tt):
            se.wait_ge(c.S["mm"], lg_done[tt])
            if tt >= 1:
                se.wait_ge(c.S["dv"], gm_done[tt - 1])
            c.inc(se.activation(sg, lg_ps[:, :], AF.Sigmoid), "ac", 1)
        block.scalar(_)
        sg_done[tt] = c.cnt["ac"]

        def _(ve, tt=tt):
            ve.wait_ge(c.S["ac"], sg_done[tt])
            if tt >= 1:
                ve.wait_ge(c.S["mm"], tr_done[tt - 1])
            ve.tensor_scalar(s2, sg, rs_col[:, 0:1], None, op0=ALU.mult)
            ve.drain()
            ve.reduce_max(m1, s2, axis=AX.X)
            ve.drain()
            ve.tensor_scalar(eq1, s2, m1, None, op0=ALU.is_equal)
            ve.drain()
            ve.tensor_scalar(s3, eq1, -1e30, None, op0=ALU.mult)
            ve.drain()
            ve.tensor_tensor(s3, s3, s2, op=ALU.add)
            ve.drain()
            ve.reduce_max(m2, s3, axis=AX.X)
            ve.drain()
            ve.tensor_scalar(eq2, s3, m2, None, op0=ALU.is_equal)
            ve.tensor_tensor(wsum, m1, m2, op=ALU.add)
            ve.drain()
            ve.reciprocal(rw, wsum)
            ve.drain()
            ve.tensor_tensor(w1c, m1, rw, op=ALU.mult)
            ve.tensor_tensor(w2c, m2, rw, op=ALU.mult)
            ve.drain()
            ve.tensor_scalar(route_t, eq1, w1c, None, op0=ALU.mult)
            ve.drain()
            c.inc(ve.scalar_tensor_tensor(route_t, eq2, w2c, route_t,
                                          op0=ALU.mult, op1=ALU.add), "dv", 1)
        block.vector(_)
        gm_done[tt] = c.cnt["dv"]

        def _(te, tt=tt):
            te.wait_ge(c.S["dv"], gm_done[tt])
            c.inc(te.transpose(rt_ps[:, :], route_t, c.T["ident"][:, :]), "mm", 1)
        block.tensor(_)
        tr_done[tt] = c.cnt["mm"]

        def _(se, tt=tt):
            se.wait_ge(c.S["mm"], tr_done[tt])
            c.inc(se.copy(routeT[0:8, tt * 128:(tt + 1) * 128], rt_ps[0:8, :]), "ac", 1)
        block.scalar(_)
        rt_done[tt] = c.cnt["ac"]

    if "dbg_gsc" in A:
        def _(g):
            g.wait_ge(c.S["ac"], rt_done[15])
            c.inc(g.dma_start(out=A["dbg_gsc"][:, :], in_=gsc[:, :]), "gd", 16)
            g.wait_ge(c.S["gd"], c.cnt["gd"])
        block.gpsimd(_)

    def _(te):
        te.wait_ge(c.S["ac"], rt_done[15])
        if "dbg_gsc" in A:
            te.wait_ge(c.S["gd"], c.cnt["gd"])
    block.tensor(_)
    gate_st.close()

    wbuf_st = contextlib.ExitStack()
    w1_sb = [c.sb(wbuf_st, f"w1_sb{i}", [128, 8 * 512], BF16) for i in range(2)]
    w2_sb = [c.sb(wbuf_st, f"w2_sb{i}", [128, 4 * 1024], BF16) for i in range(2)]
    h_ps = [c.ps(wbuf_st, f"h_ps{i}", [128, 1024], F32) for i in range(2)]
    y_ps = [c.ps(wbuf_st, f"y_ps{i}", [128, 1024], F32) for i in range(2)]

    wld = {}
    mlp1_done = {}
    mlp2_done = {}
    gelu_done = {}
    smul_done = {}
    yadd_done = {}
    gfb_done = {}

    for e in range(9):
        shared = (e == 8)

        def _(g, e=e, shared=shared):
            if e == 0:
                c.fence(g)
            if e >= 2:
                g.wait_ge(c.S["mm"], mlp1_done[e - 2])
                g.wait_ge(c.S["mm"], mlp2_done[e - 2])
            if shared:
                s1, s2_ = A["Ws1"][:, :], A["Ws2"][:, :]
            else:
                s1 = A["W1"][e, :, :]
                s2_ = A["W2"][e, :, :]
            c.inc(g.dma_start(out=w1_sb[e % 2][:, :].rearrange("p (fc h) -> p fc h", fc=8),
                              in_=s1.rearrange("(fc p) h -> p fc h", p=128)), "gd", 16)
            c.inc(g.dma_start(out=w2_sb[e % 2][:, :].rearrange("p (hc n) -> p hc n", hc=4),
                              in_=s2_.rearrange("(hc p) n -> p hc n", p=128)), "gd", 16)
        block.gpsimd(_)
        wld[e] = c.cnt["gd"]

        if not shared:
            def _(te, e=e):
                te.wait_ge(c.S["ac"], c.cnt["ac"])
                if e >= 1:
                    te.wait_ge(c.S["dv"], yadd_done[e - 1])
                for yb in range(2):
                    for q2 in range(2):
                        ins = te.matmul(
                            y_ps[yb][:, q2 * 512:(q2 + 1) * 512],
                            sel8[:, e * 128:(e + 1) * 128],
                            routeT[0:8, yb * 1024 + q2 * 512: yb * 1024 + (q2 + 1) * 512],
                            start=True, stop=True)
                c.inc(ins, "mm", 1)
            block.tensor(_)
            gfb_mm = c.cnt["mm"]

            def _(ve, e=e, gfb_mm=gfb_mm):
                ve.wait_ge(c.S["mm"], gfb_mm)
                ve.tensor_copy(gfb_sb[:, 0:1024], y_ps[0][:, :])
                c.inc(ve.tensor_copy(gfb_sb[:, 1024:2048], y_ps[1][:, :]), "dv", 1)
            block.vector(_)
            gfb_done[e] = c.cnt["dv"]

            if e == 0 and "dbg_gfb" in A:
                def _(g):
                    g.wait_ge(c.S["dv"], gfb_done[0])
                    c.inc(g.dma_start(out=A["dbg_gfb"][:, :], in_=gfb_sb[:, :]), "gd", 16)
                    c.inc(g.dma_start(out=A["dbg_route"][:, :], in_=routeT[:, :]), "gd", 16)
                block.gpsimd(_)

        ac_base4 = c.cnt["ac"]
        gelu_sched = {j: ac_base4 + j + 1 for j in range(8)}
        mlp1_mm = {}
        jlist = [(hc, th) for hc in range(4) for th in range(2)]
        for j, (hc, th) in enumerate(jlist):
            def _(te, e=e, j=j, hc=hc, th=th):
                if j == 0:
                    te.wait_ge(c.S["gd"], wld[e])
                    if not shared:
                        te.wait_ge(c.S["dv"], gfb_done[e])
                if j >= 2:
                    te.wait_ge(c.S["ac"], gelu_sched[j - 2])
                for q2 in range(2):
                    for fc in range(NF):
                        ins = te.matmul(
                            h_ps[j % 2][:, q2 * 512:(q2 + 1) * 512],
                            w1_sb[e % 2][:, fc * 512 + hc * 128: fc * 512 + (hc + 1) * 128],
                            h1[:, fc * 2048 + th * 1024 + q2 * 512:
                               fc * 2048 + th * 1024 + (q2 + 1) * 512],
                            start=(fc == 0), stop=(fc == NF - 1))
                c.inc(ins, "mm", 1)
            block.tensor(_)
            mlp1_done[e] = c.cnt["mm"]
            mlp1_mm[j] = c.cnt["mm"]

            if not shared:
                def _(ve, e=e, j=j, hc=hc, th=th):
                    ve.wait_ge(c.S["ac"], gelu_sched[j])
                    dst = s_sb[:, hc * 2048 + th * 1024: hc * 2048 + (th + 1) * 1024]
                    c.inc(ve.tensor_tensor(
                        dst, ge_tmp[j % 2][:, :],
                        gfb_sb[:, th * 1024:(th + 1) * 1024], op=ALU.mult), "dv", 1)
                block.vector(_)
                smul_done[(e, j)] = c.cnt["dv"]

        def _(se, e=e, shared=shared):
            for j, (hc, th) in enumerate(jlist):
                se.wait_ge(c.S["mm"], mlp1_mm[j])
                bias = (bvec[:, (32 + hc):(33 + hc)] if shared
                        else bvec[:, hc * 8 + e: hc * 8 + e + 1])
                if shared:
                    dst = s_sb[:, hc * 2048 + th * 1024: hc * 2048 + (th + 1) * 1024]
                    c.inc(se.activation(dst, h_ps[j % 2][:, :], AF.Gelu, bias=bias),
                          "ac", 1)
                else:
                    if j >= 2:
                        se.wait_ge(c.S["dv"], smul_done[(e, j - 2)])
                    c.inc(se.activation(ge_tmp[j % 2][:, :], h_ps[j % 2][:, :], AF.Gelu,
                                        bias=bias), "ac", 1)
        block.scalar(_)
        assert c.cnt["ac"] == ac_base4 + 8

        s_ready = c.cnt["dv"] if not shared else None
        s_ready_ac = c.cnt["ac"]

        for j, (fc, th) in enumerate([(fc, th) for fc in range(NF) for th in range(2)]):
            def _(te, e=e, j=j, fc=fc, th=th, shared=shared,
                  s_ready=s_ready, s_ready_ac=s_ready_ac):
                if j == 0:
                    if shared:
                        te.wait_ge(c.S["ac"], s_ready_ac)
                    else:
                        te.wait_ge(c.S["dv"], s_ready)
                if j >= 2:
                    te.wait_ge(c.S["dv"], yadd_done[(e, j - 2)])
                for q2 in range(2):
                    col0 = th * 1024 + q2 * 512
                    for hc in range(4):
                        ins = te.matmul(
                            y_ps[j % 2][:, q2 * 512:(q2 + 1) * 512],
                            w2_sb[e % 2][:, hc * 1024 + fc * 128: hc * 1024 + (fc + 1) * 128],
                            s_sb[:, hc * 2048 + col0: hc * 2048 + col0 + 512],
                            start=(hc == 0), stop=(hc == 3 and not shared))
                    if shared:
                        te.matmul(y_ps[j % 2][:, q2 * 512:(q2 + 1) * 512],
                                  b2_sb[0:8, fc * 128:(fc + 1) * 128],
                                  routeT[0:8, col0: col0 + 512],
                                  start=False, stop=False)
                        ins = te.matmul(y_ps[j % 2][:, q2 * 512:(q2 + 1) * 512],
                                        bs2row[0:1, fc * 128:(fc + 1) * 128],
                                        ones_bfr[0:1, 0:512],
                                        start=False, stop=True)
                c.inc(ins, "mm", 1)
            block.tensor(_)
            mlp2_done[e] = c.cnt["mm"]
            y_mm = c.cnt["mm"]

            def _(ve, e=e, j=j, fc=fc, th=th, y_mm=y_mm):
                ve.wait_ge(c.S["mm"], y_mm)
                xsl = c.T["x_c"][:, fc * 2048 + th * 1024: fc * 2048 + (th + 1) * 1024]
                c.inc(ve.tensor_tensor(xsl, xsl, y_ps[j % 2][:, :], op=ALU.add), "dv", 1)
            block.vector(_)
            yadd_done[(e, j)] = c.cnt["dv"]
        yadd_done[e] = c.cnt["dv"]

    def _(te):
        te.wait_ge(c.S["dv"], c.cnt["dv"])
    block.tensor(_)
    wbuf_st.close()
    st.close()



def p5_proj_l1(c, b, st):
    nc, block = c.nc, c.block
    A = c.A
    Q1, K1, V1 = c.T["Q1_sb"], c.T["K1_sb"], c.T["V1_sb"]
    wq1, wk1, wv1 = c.T["wq1_sb"], c.T["wk1_sb"], c.T["wv1_sb"]
    cosT, sinT, pmat = c.T["cosT"], c.T["sinT"], c.T["pmat"]
    ones_col = c.T["ones_col_bf"]
    ones_row = c.T["ones_row"]

    pst = contextlib.ExitStack()
    h2r = c.sb(pst, f"h2r_{b}", [128, 8 * 1024], BF16)
    proj_ps_st = contextlib.ExitStack()
    q_ps = c.ps(proj_ps_st, f"l1q_ps_{b}", [128, 1024], F32)
    k_ps = c.ps(proj_ps_st, f"l1k_ps_{b}", [128, 1024], F32)
    v_ps = c.ps(proj_ps_st, f"l1v_ps_{b}", [128, 1024], F32)

    ld_done = {}
    ev_done = {}
    for r in range(8):
        def _(g, r=r):
            if r == 0:
                c.fence(g)
                g.wait_ge(c.S["cc"], c.cnt["cc"])
            else:
                g.wait_ge(c.S["mm"], ld_done[r - 1][1])
            src = A["gath1"][r * 1024:(r + 1) * 1024, b * 1024:(b + 1) * 1024]
            c.inc(g.dma_start(out=h2r[:, :].rearrange("p (fc n) -> p fc n", fc=8),
                              in_=src.rearrange("(fc p) n -> p fc n", p=128)), "gd", 16)
        block.gpsimd(_)
        gd_here = c.cnt["gd"]

        def _(te, r=r, gd_here=gd_here):
            te.wait_ge(c.S["gd"], gd_here)
            if r == 0:
                c.psum_guard(te)
            else:
                te.wait_ge(c.S["ac"], ev_done[r - 1])
            for th in range(2):
                for fc in range(NF):
                    te.matmul(q_ps[:, th * 512:(th + 1) * 512],
                              wq1[:, fc * 128:(fc + 1) * 128],
                              h2r[:, fc * 1024 + th * 512: fc * 1024 + (th + 1) * 512],
                              start=(fc == 0), stop=(fc == NF - 1))
            for th in range(2):
                for fc in range(NF):
                    te.matmul(k_ps[:, th * 512:(th + 1) * 512],
                              wk1[:, fc * 128:(fc + 1) * 128],
                              h2r[:, fc * 1024 + th * 512: fc * 1024 + (th + 1) * 512],
                              start=(fc == 0), stop=(fc == NF - 1))
            for s in range(8):
                for fc in range(NF):
                    ins = te.matmul(v_ps[:, s * 128:(s + 1) * 128],
                                    h2r[:, fc * 1024 + s * 128: fc * 1024 + (s + 1) * 128],
                                    wv1[:, fc * 128:(fc + 1) * 128],
                                    start=(fc == 0), stop=(fc == NF - 1))
            c.inc(ins, "mm", 1)
        block.tensor(_)
        ld_done[r] = (gd_here, c.cnt["mm"])
        mm_here = c.cnt["mm"]

        def _(se, r=r, mm_here=mm_here):
            se.wait_ge(c.S["mm"], mm_here)
            se.copy(Q1[:, r * 1024:(r + 1) * 1024], q_ps[:, :])
            se.copy(K1[:, r * 1024:(r + 1) * 1024], k_ps[:, :])
            c.inc(se.copy(V1[:, r * 1024:(r + 1) * 1024], v_ps[:, :]), "ac", 1)
        block.scalar(_)
        ev_done[r] = c.cnt["ac"]

    proj_ps_st.close()

    rope_ps_st = contextlib.ExitStack()
    rot_ps = [c.ps(rope_ps_st, f"rot_ps{i}_{b}", [128, 512], F32) for i in range(2)]
    rsc = c.sb(pst, f"rope_sc_{b}", [128, 1024], F32)
    rope_mm = {}
    rope_dv = {}
    for ti, (tgt, tile) in enumerate([(t, i) for t in (Q1, K1) for i in range(16)]):
        def _(te, ti=ti, tgt=tgt, tile=tile):
            if ti == 0:
                te.wait_ge(c.S["ac"], ev_done[7])
                c.psum_guard(te)
            if ti >= 2:
                te.wait_ge(c.S["dv"], rope_dv[ti - 2])
            c.inc(te.matmul(rot_ps[ti % 2][:, :], pmat[:, :],
                            tgt[:, tile * 512:(tile + 1) * 512],
                            start=True, stop=True), "mm", 1)
        block.tensor(_)
        rope_mm[ti] = c.cnt["mm"]

        def _(ve, ti=ti, tgt=tgt, tile=tile):
            ve.wait_ge(c.S["mm"], rope_mm[ti])
            sl = slice(tile * 512, (tile + 1) * 512)
            ve.tensor_tensor(rsc[:, 0:512], rot_ps[ti % 2][:, :], sinT[:, sl], op=ALU.mult)
            ve.tensor_tensor(rsc[:, 512:1024], tgt[:, sl], cosT[:, sl], op=ALU.mult)
            ve.drain()
            c.inc(ve.tensor_tensor(tgt[:, sl], rsc[:, 0:512], rsc[:, 512:1024],
                                   op=ALU.add), "dv", 1)
        block.vector(_)
        rope_dv[ti] = c.cnt["dv"]

    rope_ps_st.close()

    bnd_ps_st = contextlib.ExitStack()
    nb_ps = c.ps(bnd_ps_st, f"nb_ps_{b}", [1, 2048], F32)
    bb_ps = c.ps(bnd_ps_st, f"l1bb_ps_{b}", [128, 2], F32)
    sqc = c.sb(pst, f"l1_sqc_{b}", [128, 2048], BF16)
    nrow = c.sb(pst, f"l1_nrow_{b}", [1, 16], F32)
    for si, src in enumerate((Q1, K1)):
        for tc in range(4):
            def _(ve, si=si, tc=tc, src=src):
                if (si, tc) == (0, 0):
                    ve.wait_ge(c.S["dv"], rope_dv[31])
                else:
                    ve.wait_ge(c.S["mm"], c.cnt["mm"])
                c.inc(ve.tensor_tensor(sqc[:, :], src[:, tc * 2048:(tc + 1) * 2048],
                                       src[:, tc * 2048:(tc + 1) * 2048], op=ALU.mult),
                      "dv", 1)
            block.vector(_)
            sq_dv = c.cnt["dv"]

            def _(te, si=si, tc=tc, sq_dv=sq_dv):
                te.wait_ge(c.S["dv"], sq_dv)
                if (si, tc) == (0, 0):
                    c.psum_guard(te)
                for q in range(4):
                    ins = te.matmul(nb_ps[0:1, q * 512:(q + 1) * 512], ones_col[:, 0:1],
                                    sqc[:, q * 512:(q + 1) * 512], start=True, stop=True)
                c.inc(ins, "mm", 1)
            block.tensor(_)
            nb_mm = c.cnt["mm"]

            def _(ve, si=si, tc=tc, nb_mm=nb_mm):
                ve.wait_ge(c.S["mm"], nb_mm)
                c.inc(ve.reduce_max(nrow[0:1, tc:tc + 1], nb_ps[0:1, :], axis=AX.X),
                      "dv", 1)
            block.vector(_)

        def _(ve, si=si):
            ve.drain()
            ve.tensor_tensor(nrow[0:1, 0:2], nrow[0:1, 0:2], nrow[0:1, 2:4], op=ALU.max)
            ve.drain()
            c.inc(ve.tensor_tensor(nrow[0:1, 4 + si:5 + si], nrow[0:1, 0:1],
                                   nrow[0:1, 1:2], op=ALU.max), "dv", 1)
        block.vector(_)

    def _(ve):
        ve.drain()
        c.inc(ve.tensor_tensor(nrow[0:1, 6:7], nrow[0:1, 4:5], nrow[0:1, 5:6],
                               op=ALU.mult), "dv", 1)
    block.vector(_)
    t_dv = c.cnt["dv"]

    def _(se):
        se.wait_ge(c.S["dv"], t_dv)
        c.inc(se.activation(nrow[0:1, 7:8], nrow[0:1, 6:7], AF.Sqrt), "ac", 1)
    block.scalar(_)
    s_ac = c.cnt["ac"]

    def _(ve):
        ve.wait_ge(c.S["ac"], s_ac)
        c.inc(ve.tensor_scalar(nrow[0:1, 8:9], nrow[0:1, 7:8],
                               -1.0 / float(np.sqrt(128.0)), None, op0=ALU.mult), "dv", 1)
    block.vector(_)
    br_dv = c.cnt["dv"]

    def _(te):
        te.wait_ge(c.S["dv"], br_dv)
        c.inc(te.matmul(bb_ps[:, b:b + 1], ones_row[:, :], nrow[0:1, 8:9],
                        start=True, stop=True), "mm", 1)
    block.tensor(_)
    bb_mm = c.cnt["mm"]

    def _(ve):
        ve.wait_ge(c.S["mm"], bb_mm)
        c.inc(ve.tensor_copy(c.T["bias1"][:, b:b + 1], bb_ps[:, b:b + 1]), "dv", 1)
    block.vector(_)

    def _(te):
        te.wait_ge(c.S["dv"], c.cnt["dv"])
    block.tensor(_)
    bnd_ps_st.close()
    pst.close()


def p6_flash_l1(c, b, st):
    nc, block = c.nc, c.block
    A = c.A
    Q1, K1, V1 = c.T["Q1_sb"], c.T["K1_sb"], c.T["V1_sb"]
    masks = c.T["masks_sb"]
    ones_col = c.T["ones_col_bf"]

    fst = contextlib.ExitStack()
    probs = [c.sb(fst, f"l1_pr{i}_{b}", [128, 1024], BF16) for i in range(2)]
    oacc = c.sb(fst, f"l1_oacc_{b}", [128, 1024], F32)
    obuf = c.sb(fst, f"l1_obuf_{b}", [128, 1024], BF16)
    rc = c.sb(fst, f"l1_rc_{b}", [1, 1024], F32)
    S_ps = [c.ps(fst, f"l1_S{i}_{b}", [128, 1024], F32) for i in range(2)]
    oa_ps = c.ps(fst, f"l1_oa_{b}", [128, 1024], F32)
    sum_ps = c.ps(fst, f"l1_sum_{b}", [1, 1024], F32)

    opay_done = 0
    for qh in range(8):
        nkb = 8 * qh + 8
        mm_base = c.cnt["mm"]
        ac_base = c.cnt["ac"]
        dv_base = c.cnt["dv"]
        qk_sched = {kb: mm_base + (1 if kb == 0 else 2 * kb) for kb in range(nkb)}
        exp_sched = {kb: ac_base + kb + 1 for kb in range(nkb)}
        mask_sched = {kb: dv_base + (kb - 8 * qh) + 1 for kb in range(8 * qh, nkb)}

        def _(te, qh=qh, nkb=nkb):
            for kb in range(nkb):
                if qh == 0 and kb == 0:
                    c.fence(te)
                if kb >= 2:
                    te.wait_ge(c.S["ac"], exp_sched[kb - 2])
                elif kb < 2 and qh > 0:
                    te.wait_ge(c.S["dv"], epi_dv)
                for half in range(2):
                    ins = te.matmul(
                        S_ps[kb % 2][:, half * 512:(half + 1) * 512],
                        K1[:, kb * 128:(kb + 1) * 128],
                        Q1[:, qh * 1024 + half * 512: qh * 1024 + (half + 1) * 512],
                        start=True, stop=True)
                c.inc(ins, "mm", 1)
                if kb >= 1:
                    te.wait_ge(c.S["ac"], exp_sched[kb - 1])
                    pkb = kb - 1
                    for half in range(2):
                        te.matmul(oa_ps[:, half * 512:(half + 1) * 512],
                                  V1[:, pkb * 128:(pkb + 1) * 128],
                                  probs[pkb % 2][:, half * 512:(half + 1) * 512],
                                  start=(pkb == 0), stop=False)
                        ins = te.matmul(sum_ps[0:1, half * 512:(half + 1) * 512],
                                        ones_col[:, 0:1],
                                        probs[pkb % 2][:, half * 512:(half + 1) * 512],
                                        start=(pkb == 0), stop=False)
                    c.inc(ins, "mm", 1)
            te.wait_ge(c.S["ac"], exp_sched[nkb - 1])
            pkb = nkb - 1
            for half in range(2):
                te.matmul(oa_ps[:, half * 512:(half + 1) * 512],
                          V1[:, pkb * 128:(pkb + 1) * 128],
                          probs[pkb % 2][:, half * 512:(half + 1) * 512],
                          start=(pkb == 0), stop=True)
                ins = te.matmul(sum_ps[0:1, half * 512:(half + 1) * 512],
                                ones_col[:, 0:1],
                                probs[pkb % 2][:, half * 512:(half + 1) * 512],
                                start=(pkb == 0), stop=True)
            c.inc(ins, "mm", 1)
        block.tensor(_)
        assert c.cnt["mm"] == mm_base + 2 * nkb

        def _(ve, qh=qh, nkb=nkb):
            for kb in range(8 * qh, nkb):
                ve.wait_ge(c.S["mm"], qk_sched[kb])
                j = kb - 8 * qh
                c.inc(ve.tensor_tensor(S_ps[kb % 2][:, :], S_ps[kb % 2][:, :],
                                       masks[:, j * 1024:(j + 1) * 1024],
                                       op=ALU.add), "dv", 1)
        block.vector(_)
        assert c.cnt["dv"] == dv_base + 8

        def _(se, qh=qh, nkb=nkb, b=b):
            for kb in range(nkb):
                diag = kb >= 8 * qh
                if diag:
                    se.wait_ge(c.S["dv"], mask_sched[kb])
                else:
                    se.wait_ge(c.S["mm"], qk_sched[kb])
                c.inc(se.activation(probs[kb % 2][:, :], S_ps[kb % 2][:, :], AF.Exp,
                                    bias=c.T["bias1"][:, b:b + 1],
                                    scale=float(1.0 / np.sqrt(128.0))), "ac", 1)
        block.scalar(_)
        assert c.cnt["ac"] == ac_base + nkb
        pv_mm = c.cnt["mm"]

        def _(ve, pv_mm=pv_mm):
            ve.wait_ge(c.S["mm"], pv_mm)
            ve.tensor_copy(oacc[:, :], oa_ps[:, :])
            c.inc(ve.reciprocal(rc[0:1, :], sum_ps[0:1, :]), "dv", 1)
        block.vector(_)
        rc_dv = c.cnt["dv"]

        def _(te, rc_dv=rc_dv):
            te.wait_ge(c.S["dv"], rc_dv)
            for half in range(2):
                ins = te.matmul(S_ps[0][:, half * 512:(half + 1) * 512],
                                c.T["ones_row"][:, :],
                                rc[0:1, half * 512:(half + 1) * 512],
                                start=True, stop=True)
            c.inc(ins, "mm", 1)
        block.tensor(_)
        rb_mm = c.cnt["mm"]

        def _(ve, rb_mm=rb_mm, opay_done=opay_done):
            ve.wait_ge(c.S["mm"], rb_mm)
            if opay_done:
                ve.wait_ge(c.S["wd"], opay_done)
            c.inc(ve.tensor_tensor(obuf[:, :], S_ps[0][:, :], oacc[:, :],
                                   op=ALU.mult), "dv", 1)
        block.vector(_)
        ob_dv = c.cnt["dv"]
        epi_dv = c.cnt["dv"]

        def _(sy, qh=qh, b=b, ob_dv=ob_dv):
            sy.wait_ge(c.S["dv"], ob_dv)
            c.inc(sy.dma_start(out=A[f"pay2b{b}"][:, qh * 1024:(qh + 1) * 1024],
                               in_=obuf[:, :]), "wd", 16)
        block.sync(_)
        opay_done = c.cnt["wd"]

    def _(te):
        te.wait_ge(c.S["wd"], c.cnt["wd"])
        te.wait_ge(c.S["dv"], c.cnt["dv"])
    block.tensor(_)
    fst.close()


def p7_oproj_b(c, b, o_own, wo_sb, idx_sb):
    nc, block = c.nc, c.block
    A = c.A
    ps_st = contextlib.ExitStack()
    y_ps = [c.ps(ps_st, f"l1y_ps{i}_{b}", [128, 1024], F32) for i in range(2)]

    def _(g, b=b):
        g.wait_ge(c.S["cc"], c.p6_ccb[b])
        g.wait_ge(c.S["mm"], c.cnt["mm"])
        gview = A[f"gath2b{b}"][:, :].rearrange("r (cb n) -> (r cb) n", cb=8)
        for r in range(8):
            c.inc(g.indirect_dma_start(
                out=o_own[:, r * 1024:(r + 1) * 1024],
                out_offset=None,
                in_=gview,
                in_offset=bass.IndirectOffsetOnAxis(
                    ap=idx_sb[:, r: r + 1], axis=0)), "gd", 16)
    block.gpsimd(_)
    ext_gd = c.cnt["gd"]

    yadds = {}
    for j, fc in enumerate(range(NF)):
        def _(te, j=j, b=b, fc=fc):
            if j == 0:
                te.wait_ge(c.S["gd"], ext_gd)
                c.psum_guard(te)
            if j >= 2:
                te.wait_ge(c.S["dv"], yadds[j - 2])
            for q2 in range(2):
                for r in range(8):
                    ins = te.matmul(
                        y_ps[j % 2][:, q2 * 512:(q2 + 1) * 512],
                        wo_sb[:, r * 1024 + fc * 128: r * 1024 + (fc + 1) * 128],
                        o_own[:, r * 1024 + q2 * 512: r * 1024 + (q2 + 1) * 512],
                        start=(r == 0), stop=(r == 7))
            c.inc(ins, "mm", 1)
        block.tensor(_)
        y_mm = c.cnt["mm"]

        def _(ve, j=j, b=b, fc=fc, y_mm=y_mm):
            ve.wait_ge(c.S["mm"], y_mm)
            xsl = c.T["x_c"][:, fc * 2048 + b * 1024: fc * 2048 + (b + 1) * 1024]
            c.inc(ve.tensor_tensor(xsl, xsl, y_ps[j % 2][:, :], op=ALU.add), "dv", 1)
        block.vector(_)
        yadds[j] = c.cnt["dv"]

    def _(te):
        te.wait_ge(c.S["dv"], c.cnt["dv"])
    block.tensor(_)
    ps_st.close()


def p5to7_l1_attn(c):
    nc, block = c.nc, c.block
    A = c.A
    st = contextlib.ExitStack()

    h2_st = contextlib.ExitStack()
    c.T["h2_bf"] = c.sb(h2_st, "h2_bf", [128, 16384], BF16)
    p_layernorm(c, "h2_bf", eps=1e-5, tag="_p5")
    hdv = c.cnt["dv"]

    def _(sy):
        sy.wait_ge(c.S["dv"], hdv)
        for fc in range(NF):
            c.inc(sy.dma_start(out=A["pay1"][fc * 128:(fc + 1) * 128, :],
                               in_=c.T["h2_bf"][:, fc * 2048:(fc + 1) * 2048]), "wd", 16)
    block.sync(_)
    pay_wd = c.cnt["wd"]

    def _(g):
        g.wait_ge(c.S["wd"], pay_wd)
        c.inc(g.collective_compute(
            "AllGather", ALU.bypass, replica_groups=[list(range(8))],
            ins=[A["pay1"][:, :].opt()], outs=[A["gath1"][:, :].opt()]), "cc", 1)
    block.gpsimd(_)
    h2_st.close()

    attn_st = contextlib.ExitStack()
    c.T["Q1_sb"] = c.sb(attn_st, "Q1_sb", [128, 8192], BF16)
    c.T["K1_sb"] = c.sb(attn_st, "K1_sb", [128, 8192], BF16)
    c.T["V1_sb"] = c.sb(attn_st, "V1_sb", [128, 8192], BF16)
    c.T["wq1_sb"] = c.sb(attn_st, "wq1_sb", [128, 8 * 128], BF16)
    c.T["wk1_sb"] = c.sb(attn_st, "wk1_sb", [128, 8 * 128], BF16)
    c.T["wv1_sb"] = c.sb(attn_st, "wv1_sb", [128, 8 * 128], BF16)
    c.T["masks_sb"] = c.sb(attn_st, "masks_sb", [128, 8 * 1024], BF16)
    c.T["bias1"] = c.sb(attn_st, "bias1", [128, 4], F32)
    rope_st = contextlib.ExitStack()
    c.T["cosT"] = c.sb(rope_st, "cosT", [128, 8192], BF16)
    c.T["sinT"] = c.sb(rope_st, "sinT", [128, 8192], BF16)
    c.T["pmat"] = c.sb(rope_st, "pmat", [128, 128], BF16)

    def _(g):
        c.fence(g)
        for name, dst in (("wq1o", "wq1_sb"), ("wk1o", "wk1_sb"), ("wv1o", "wv1_sb")):
            c.inc(g.dma_start(out=c.T[dst][:, :].rearrange("p (fc n) -> p fc n", fc=8),
                              in_=A[name][:, :].rearrange("(fc p) n -> p fc n", p=128)),
                  "gd", 16)
        c.inc(g.dma_start(out=c.T["cosT"][:, :], in_=A["cos_in"][:, :]), "gd", 16)
        c.inc(g.dma_start(out=c.T["sinT"][:, :], in_=A["sin_in"][:, :]), "gd", 16)
        c.inc(g.dma_start(out=c.T["pmat"][:, :], in_=A["pmat_in"][:, :]), "gd", 16)
        c.inc(g.dma_start(out=c.T["masks_sb"][:, :], in_=A["masks_in"][:, :]), "gd", 16)
    block.gpsimd(_)

    c.p6_ccb = {}
    p5_proj_l1(c, 0, st)
    p6_flash_l1(c, 0, st)

    def _(g):
        g.wait_ge(c.S["wd"], c.cnt["wd"])
        c.inc(g.collective_compute(
            "AllGather", ALU.bypass, replica_groups=[list(range(8))],
            ins=[A["pay2b0"][:, :].opt()], outs=[A["gath2b0"][:, :].opt()]), "cc", 1)
    block.gpsimd(_)
    c.p6_ccb[0] = c.cnt["cc"]

    p5_proj_l1(c, 1, st)
    rope_st.close()

    o_own = c.sb(attn_st, "o_own", [128, 8 * 1024], BF16)
    wo_sb = c.sb(attn_st, "wo_sb", [128, 8 * 1024], BF16)
    idx_sb = c.sb(attn_st, "idx_sb", [128, 16], U32)

    def _(g):
        c.fence(g)
        c.inc(g.dma_start(out=wo_sb[:, :].rearrange("p (ic n) -> p ic n", ic=8),
                          in_=A["wo1"][:, :].rearrange("(ic p) n -> p ic n", p=128)),
              "gd", 16)
        c.inc(g.dma_start(out=idx_sb[:, :], in_=A["idx_o"][:, :]), "gd", 16)
    block.gpsimd(_)

    p6_flash_l1(c, 1, st)

    def _(g):
        g.wait_ge(c.S["wd"], c.cnt["wd"])
        c.inc(g.collective_compute(
            "AllGather", ALU.bypass, replica_groups=[list(range(8))],
            ins=[A["pay2b1"][:, :].opt()], outs=[A["gath2b1"][:, :].opt()]), "cc", 1)
    block.gpsimd(_)
    c.p6_ccb[1] = c.cnt["cc"]

    p7_oproj_b(c, 0, o_own, wo_sb, idx_sb)
    p7_oproj_b(c, 1, o_own, wo_sb, idx_sb)
    attn_st.close()
    st.close()



def _p8_weight_cast(c):
    A, block = c.A, c.block

    def _(g):
        for e in range(16):
            c.inc(g.dma_start(out=A["m1e_bf"][e, :, :], in_=A["m1w"][e, 0:2048:2, :]),
                  "gd", 16)
            c.inc(g.dma_start(out=A["m2_bf"][e, :, :], in_=A["m2w"][e, :, :]), "gd", 16)
    block.gpsimd(_)
    c.p8_cast_gd = c.cnt["gd"]


def p8_ffn_l1(c):
    nc, block = c.nc, c.block
    A = c.A
    st = contextlib.ExitStack()

    cast_gd = getattr(c, "p8_cast_gd", None)
    if cast_gd is None:
        _p8_weight_cast(c)
        cast_gd = c.p8_cast_gd

    t_bf = c.sb(st, "t_bf", [128, 16384], BF16)
    c.T["t_bf"] = t_bf
    h3_st = contextlib.ExitStack()
    c.T["h3_bf"] = c.sb(h3_st, "h3_bf", [128, 16384], BF16)
    scol_st = contextlib.ExitStack()
    sc_col = c.sb(scol_st, "l1sc_col", [128, NF], F32)
    sc_stage = c.sb(scol_st, "l1sc_stage", [8, 128], F32)
    scps_st = contextlib.ExitStack()
    sc_ps = c.ps(scps_st, "l1sc_ps", [128, 8], F32)

    def _(g):
        c.fence(g)
        c.inc(g.dma_start(out=sc_stage[:, :], in_=A["l1scale"][:, :]), "gd", 16)
    block.gpsimd(_)
    sc_gd = c.cnt["gd"]

    def _(te):
        te.wait_ge(c.S["gd"], sc_gd)
        c.psum_guard(te)
        c.inc(te.transpose(sc_ps[:, :], sc_stage[:, :], c.T["ident"][0:8, 0:8]), "mm", 1)
    block.tensor(_)
    sc_mm = c.cnt["mm"]

    def _(ve):
        ve.wait_ge(c.S["mm"], sc_mm)
        c.inc(ve.tensor_copy(sc_col[:, :], sc_ps[:, :]), "dv", 1)
    block.vector(_)

    def _(te):
        te.wait_ge(c.S["dv"], c.cnt["dv"])
    block.tensor(_)
    scps_st.close()

    p_layernorm(c, "h3_bf", eps=1e-5, add_to_stream=True, tag="_p8a")
    p_layernorm(c, "t_bf", eps=1e-5, rms=True, src_bf=c.T["h3_bf"],
                affine=(sc_col,), tag="_p8b")
    scol_st.close()
    h3_st.close()

    cst = contextlib.ExitStack()
    gwT = c.sb(st, "gwT", [128, 8 * 16], BF16)
    gb_bf = c.sb(st, "gb_bf", [1, 16], BF16)
    onesr_bf = c.sb(st, "onesr_bf", [1, 128], BF16)
    b1e_t = c.sb(st, "b1e_t", [128, 8 * 16], F32)
    m2b_sb = c.sb(st, "m2b_sb", [16, 1024], BF16)
    routeT1 = c.sb(st, "routeT1", [16, 2048], BF16)
    sel16_sb = c.sb(st, "sel16_sb", [16, 16 * 128], BF16)
    gfb8 = c.sb(st, "gfb8_sb", [128, 2048], BF16)

    gw_stage = c.sb(cst, "gw_stage", [16, 2048], F32)
    b_stage = c.sb(cst, "l1b_stage", [16, 2048], F32)
    bev = c.sb(cst, "l1_bev", [16, 1024], F32)
    gt_ps = c.ps(cst, "l1gt_ps", [128, 16], F32)

    def _(g):
        c.fence(g)
        c.inc(g.dma_start(out=gw_stage[0:16, 0:1024], in_=A["gw1"][:, :]), "gd", 16)
        c.inc(g.dma_start(out=gb_bf[:, :], in_=A["gb1"][:, :]), "gd", 16)
        c.inc(g.dma_start(out=b_stage[:, :], in_=A["m1b"][:, :]), "gd", 16)
        c.inc(g.dma_start(out=m2b_sb[:, :], in_=A["m2b"][:, :]), "gd", 16)
        c.inc(g.dma_start(out=sel16_sb[:, :], in_=A["sel16"][:, :]), "gd", 16)
        c.inc(g.memset(onesr_bf[:, :], 1.0), "pl", 1)
    block.gpsimd(_)
    cst_gd = c.cnt["gd"]
    cst_pl = c.cnt["pl"]

    def _(ve):
        ve.wait_ge(c.S["gd"], cst_gd)
        c.inc(ve.tensor_copy(bev[:, :],
                             b_stage[:, :].rearrange("p (i two) -> p i two", two=2)[:, :, 0]),
              "dv", 1)
    block.vector(_)
    bev_dv = c.cnt["dv"]

    jobs = [("gw", fc) for fc in range(8)] + [("b1", ic) for ic in range(8)]
    tcopy_dv = {}
    for ji, (kind, idx) in enumerate(jobs):
        def _(te, ji=ji, kind=kind, idx=idx):
            if ji == 0:
                te.wait_ge(c.S["dv"], bev_dv)
                c.psum_guard(te)
            else:
                te.wait_ge(c.S["dv"], tcopy_dv[ji - 1])
            src = (gw_stage[0:16, idx * 128:(idx + 1) * 128] if kind == "gw"
                   else bev[0:16, idx * 128:(idx + 1) * 128])
            c.inc(te.transpose(gt_ps[:, :], src, c.T["ident"][0:16, 0:16]), "mm", 1)
        block.tensor(_)
        mm_h = c.cnt["mm"]

        def _(ve, ji=ji, kind=kind, idx=idx, mm_h=mm_h):
            ve.wait_ge(c.S["mm"], mm_h)
            dst = (gwT[:, idx * 16:(idx + 1) * 16] if kind == "gw"
                   else b1e_t[:, idx * 16:(idx + 1) * 16])
            c.inc(ve.tensor_copy(dst, gt_ps[:, :]), "dv", 1)
        block.vector(_)
        tcopy_dv[ji] = c.cnt["dv"]

    gate_st = contextlib.ExitStack()
    lg_ps = c.ps(gate_st, "lg1_ps", [128, 16], F32)
    rt_ps = c.ps(gate_st, "rt1_ps", [16, 128], F32)
    gsc = c.sb(gate_st, "l1gate_sc", [128, 160], F32)
    cur = gsc[:, 0:16]
    eqs = [gsc[:, 16 + i * 16: 32 + i * 16] for i in range(4)]
    route_t = gsc[:, 80:96]
    tmp16 = gsc[:, 96:112]
    mcol = [gsc[:, 112 + i: 113 + i] for i in range(4)]
    ecol = [gsc[:, 116 + i: 117 + i] for i in range(4)]
    nm1 = gsc[:, 120:121]
    ssum = gsc[:, 121:122]
    rw = gsc[:, 122:123]
    wcol = [gsc[:, 123 + i: 124 + i] for i in range(4)]

    lg_done = {}
    gm_done = {}
    tr_done = {}
    rt_done = {}
    for tt in range(16):
        def _(te, tt=tt):
            if tt == 0:
                c.psum_guard(te)
                te.wait_ge(c.S["pl"], cst_pl)
            else:
                te.wait_ge(c.S["dv"], gm_done[tt - 1])
                te.wait_ge(c.S["ac"], rt_done.get(tt - 1, 0))
            for fc in range(NF):
                te.matmul(lg_ps[:, :],
                          t_bf[:, fc * 2048 + tt * 128: fc * 2048 + (tt + 1) * 128],
                          gwT[:, fc * 16:(fc + 1) * 16],
                          start=(fc == 0), stop=False)
            c.inc(te.matmul(lg_ps[:, :], onesr_bf[:, :], gb_bf[:, :],
                            start=False, stop=True), "mm", 1)
        block.tensor(_)
        lg_done[tt] = c.cnt["mm"]

        def _(ve, tt=tt):
            ve.wait_ge(c.S["mm"], lg_done[tt])
            ve.tensor_copy(cur[:, :], lg_ps[:, :])
            ve.drain()
            for i in range(4):
                ve.reduce_max(mcol[i], cur[:, :], axis=AX.X)
                ve.drain()
                ve.tensor_scalar(eqs[i], cur[:, :], mcol[i], None, op0=ALU.is_equal)
                ve.drain()
                if i < 3:
                    ve.tensor_scalar(tmp16, eqs[i], -1e30, None, op0=ALU.mult)
                    ve.drain()
                    ve.tensor_tensor(cur[:, :], cur[:, :], tmp16, op=ALU.add)
                    ve.drain()
            c.inc(ve.tensor_scalar(nm1, mcol[0], -1.0, None, op0=ALU.mult), "dv", 1)
        block.vector(_)
        topk_dv = c.cnt["dv"]

        def _(se, tt=tt, topk_dv=topk_dv):
            se.wait_ge(c.S["dv"], topk_dv)
            for i in range(4):
                ins = se.activation(ecol[i], mcol[i], AF.Exp, bias=nm1)
            c.inc(ins, "ac", 1)
        block.scalar(_)
        exp_ac = c.cnt["ac"]

        def _(ve, tt=tt, exp_ac=exp_ac):
            ve.wait_ge(c.S["ac"], exp_ac)
            ve.tensor_tensor(ssum, ecol[0], ecol[1], op=ALU.add)
            ve.drain()
            ve.tensor_tensor(ssum, ssum, ecol[2], op=ALU.add)
            ve.drain()
            ve.tensor_tensor(ssum, ssum, ecol[3], op=ALU.add)
            ve.drain()
            ve.reciprocal(rw, ssum)
            ve.drain()
            for i in range(4):
                ve.tensor_tensor(wcol[i], ecol[i], rw, op=ALU.mult)
            ve.drain()
            ve.tensor_scalar(route_t, eqs[0], wcol[0], None, op0=ALU.mult)
            ve.drain()
            for i in range(1, 4):
                ve.scalar_tensor_tensor(route_t, eqs[i], wcol[i], route_t,
                                        op0=ALU.mult, op1=ALU.add)
                ve.drain()
            c.inc(ve.tensor_copy(tmp16, route_t), "dv", 1)
        block.vector(_)
        gm_done[tt] = c.cnt["dv"]

        def _(te, tt=tt):
            te.wait_ge(c.S["dv"], gm_done[tt])
            c.inc(te.transpose(rt_ps[:, :], route_t, c.T["ident"][:, :]), "mm", 1)
        block.tensor(_)
        tr_done[tt] = c.cnt["mm"]

        def _(se, tt=tt):
            se.wait_ge(c.S["mm"], tr_done[tt])
            c.inc(se.copy(routeT1[0:16, tt * 128:(tt + 1) * 128], rt_ps[0:16, :]), "ac", 1)
        block.scalar(_)
        rt_done[tt] = c.cnt["ac"]

    def _(te):
        te.wait_ge(c.S["ac"], rt_done[15])
    block.tensor(_)
    gate_st.close()
    cst.close()

    wbuf_st = contextlib.ExitStack()
    w1T = [c.sb(wbuf_st, f"w1T{i}", [128, 8 * 1024], BF16) for i in range(2)]
    w2T = c.sb(wbuf_st, "w2T", [128, 8 * 1024], BF16)
    s1_sb = c.sb(wbuf_st, "s1_sb", [128, 8 * 2048], BF16)
    lim_sc = [c.sb(wbuf_st, f"lim_sc{i}", [128, 1024], BF16) for i in range(2)]
    sig_sc = [c.sb(wbuf_st, f"sig_sc{i}", [128, 1024], BF16) for i in range(2)]
    h_ps = [c.ps(wbuf_st, f"h1_ps{i}", [128, 1024], F32) for i in range(2)]
    y_ps = [c.ps(wbuf_st, f"y1_ps{i}", [128, 1024], F32) for i in range(2)]

    w1ld = {}
    w2ld = {}
    mlp1_done = {}
    mlp2_done = {}
    gelu_done = {}
    smul_done = {}
    yadd_done = {}
    gfb_done = {}

    for e in range(16):
        def _(sy, e=e):
            if e == 0:
                c.fence(sy)
                sy.wait_ge(c.S["gd"], cast_gd)
            if e >= 2:
                sy.wait_ge(c.S["mm"], mlp1_done[e - 2])
            for fc in range(NF):
                c.inc(sy.dma_start(out=w1T[e % 2][:, fc * 1024:(fc + 1) * 1024],
                                   in_=A["m1e_bf"][e, :, fc * 128:(fc + 1) * 128],
                                   transpose=True), "wd", 16)
        block.sync(_)
        w1ld[e] = c.cnt["wd"]

        def _(te, e=e):
            if e == 0:
                te.wait_ge(c.S["ac"], rt_done[15])
                c.psum_guard(te)
            else:
                te.wait_ge(c.S["dv"], yadd_done[e - 1])
            for yb in range(2):
                for q2 in range(2):
                    ins = te.matmul(
                        y_ps[yb][:, q2 * 512:(q2 + 1) * 512],
                        sel16_sb[:, e * 128:(e + 1) * 128],
                        routeT1[0:16, yb * 1024 + q2 * 512: yb * 1024 + (q2 + 1) * 512],
                        start=True, stop=True)
            c.inc(ins, "mm", 1)
        block.tensor(_)
        gfb_mm = c.cnt["mm"]

        def _(ve, e=e, gfb_mm=gfb_mm):
            ve.wait_ge(c.S["mm"], gfb_mm)
            ve.tensor_copy(gfb8[:, 0:1024], y_ps[0][:, :])
            c.inc(ve.tensor_copy(gfb8[:, 1024:2048], y_ps[1][:, :]), "dv", 1)
        block.vector(_)
        gfb_done[e] = c.cnt["dv"]

        ac_base8 = c.cnt["ac"]
        sig_sched = {j: ac_base8 + j + 1 for j in range(16)}
        lim_dvs = {}
        h_mms = {}
        jlist8 = [(ic, th) for ic in range(8) for th in range(2)]
        for j, (ic, th) in enumerate(jlist8):
            def _(te, e=e, j=j, ic=ic, th=th):
                if j == 0:
                    te.wait_ge(c.S["wd"], w1ld[e])
                    te.wait_ge(c.S["dv"], gfb_done[e])
                if j >= 2:
                    te.wait_ge(c.S["dv"], smul_done[(e, j - 2)])
                for q2 in range(2):
                    for fc in range(NF):
                        ins = te.matmul(
                            h_ps[j % 2][:, q2 * 512:(q2 + 1) * 512],
                            w1T[e % 2][:, fc * 1024 + ic * 128: fc * 1024 + (ic + 1) * 128],
                            t_bf[:, fc * 2048 + th * 1024 + q2 * 512:
                                 fc * 2048 + th * 1024 + (q2 + 1) * 512],
                            start=(fc == 0), stop=(fc == NF - 1))
                c.inc(ins, "mm", 1)
            block.tensor(_)
            mlp1_done[e] = c.cnt["mm"]
            h_mms[j] = c.cnt["mm"]

            def _(ve, e=e, j=j, ic=ic, th=th):
                ve.wait_ge(c.S["mm"], h_mms[j])
                if j >= 2:
                    ve.wait_ge(c.S["ac"], sig_sched[j - 2])
                c.inc(ve.tensor_scalar(lim_sc[j % 2][:, :], h_ps[j % 2][:, :],
                                       b1e_t[:, ic * 16 + e: ic * 16 + e + 1], 7.0,
                                       op0=ALU.add, op1=ALU.min), "dv", 1)
                ve.wait_ge(c.S["ac"], sig_sched[j])
                dst = s1_sb[:, ic * 2048 + th * 1024: ic * 2048 + (th + 1) * 1024]
                ve.tensor_tensor(dst, lim_sc[j % 2][:, :],
                                 gfb8[:, th * 1024:(th + 1) * 1024], op=ALU.mult)
                ve.drain()
                c.inc(ve.tensor_tensor(dst, dst, sig_sc[j % 2][:, :], op=ALU.mult),
                      "dv", 1)
            block.vector(_)
            lim_dvs[j] = c.cnt["dv"] - 1
            smul_done[(e, j)] = c.cnt["dv"]

        def _(se, e=e):
            for j in range(16):
                se.wait_ge(c.S["dv"], lim_dvs[j])
                if j >= 2:
                    se.wait_ge(c.S["dv"], smul_done[(e, j - 2)])
                c.inc(se.activation(sig_sc[j % 2][:, :], lim_sc[j % 2][:, :],
                                    AF.Sigmoid, scale=1.702), "ac", 1)
        block.scalar(_)
        assert c.cnt["ac"] == ac_base8 + 16

        s_ready = c.cnt["dv"]

        def _(sy, e=e):
            if e >= 1:
                sy.wait_ge(c.S["mm"], mlp2_done[e - 1])
            for ic in range(NF):
                c.inc(sy.dma_start(out=w2T[:, ic * 1024:(ic + 1) * 1024],
                                   in_=A["m2_bf"][e, :, ic * 128:(ic + 1) * 128],
                                   transpose=True), "wd", 16)
        block.sync(_)
        w2ld[e] = c.cnt["wd"]

        for j, (fc, th) in enumerate([(fc, th) for fc in range(NF) for th in range(2)]):
            def _(te, e=e, j=j, fc=fc, th=th, s_ready=s_ready):
                if j == 0:
                    te.wait_ge(c.S["dv"], s_ready)
                    te.wait_ge(c.S["wd"], w2ld[e])
                if j >= 2:
                    te.wait_ge(c.S["dv"], yadd_done[(e, j - 2)])
                for q2 in range(2):
                    col0 = th * 1024 + q2 * 512
                    for ic in range(8):
                        ins = te.matmul(
                            y_ps[j % 2][:, q2 * 512:(q2 + 1) * 512],
                            w2T[:, ic * 1024 + fc * 128: ic * 1024 + (fc + 1) * 128],
                            s1_sb[:, ic * 2048 + col0: ic * 2048 + col0 + 512],
                            start=(ic == 0), stop=(ic == 7 and e != 0))
                    if e == 0:
                        ins = te.matmul(y_ps[j % 2][:, q2 * 512:(q2 + 1) * 512],
                                        m2b_sb[0:16, fc * 128:(fc + 1) * 128],
                                        routeT1[0:16, col0: col0 + 512],
                                        start=False, stop=True)
                c.inc(ins, "mm", 1)
            block.tensor(_)
            mlp2_done[e] = c.cnt["mm"]
            y_mm = c.cnt["mm"]

            def _(ve, e=e, j=j, fc=fc, th=th, y_mm=y_mm):
                ve.wait_ge(c.S["mm"], y_mm)
                xsl = c.T["x_c"][:, fc * 2048 + th * 1024: fc * 2048 + (th + 1) * 1024]
                c.inc(ve.tensor_tensor(xsl, xsl, y_ps[j % 2][:, :], op=ALU.add), "dv", 1)
            block.vector(_)
            yadd_done[(e, j)] = c.cnt["dv"]
        yadd_done[e] = c.cnt["dv"]

    def _(te):
        te.wait_ge(c.S["dv"], c.cnt["dv"])
    block.tensor(_)
    wbuf_st.close()
    st.close()



def build_launch(phases, dbg=()):
    nc = bass.Bass("TRN2", target_bir_lowering=False)
    root = contextlib.ExitStack()
    A = {}

    A["x_in"] = nc.dram_tensor("x_in", [TOWN, D], F32, kind="ExternalInput").ap()
    A["ident_in"] = nc.dram_tensor("ident_in", [128, 128], F32, kind="ExternalInput").ap()
    A["y_out"] = nc.dram_tensor("y_out", [TOWN, D], F32, kind="ExternalOutput").ap()
    if "p1" in phases:
        A["ln_g"] = nc.dram_tensor("ln_g", [8, 128], F32, kind="ExternalInput").ap()
        A["ln_b"] = nc.dram_tensor("ln_b", [8, 128], F32, kind="ExternalInput").ap()
    if "p2" in phases:
        A["wq"] = nc.dram_tensor("wq", [D, D], F32, kind="ExternalInput").ap()
        A["wk"] = nc.dram_tensor("wk", [D, D], F32, kind="ExternalInput").ap()
        A["wv"] = nc.dram_tensor("wv", [D, D], F32, kind="ExternalInput").ap()
        A["pay0m"] = nc.dram_tensor("pay0m", [1, 32], BF16, kind="Internal").ap()
        A["gath0m"] = nc.dram_tensor("gath0m", [8, 32], BF16, kind="Internal",
                                     addr_space="Shared").ap()
        for g in range(2):
            A[f"pay0k{g}"] = nc.dram_tensor(f"pay0k{g}", [512, 2048], BF16,
                                            kind="Internal").ap()
            A[f"gath0k{g}"] = nc.dram_tensor(f"gath0k{g}", [8 * 512, 2048], BF16,
                                             kind="Internal", addr_space="Shared").ap()
            A[f"pay0v{g}"] = nc.dram_tensor(f"pay0v{g}", [2048, 512], BF16,
                                            kind="Internal").ap()
            A[f"gath0v{g}"] = nc.dram_tensor(f"gath0v{g}", [8 * 2048, 512], BF16,
                                             kind="Internal", addr_space="Shared").ap()
    if "p4" in phases:
        A["gate_W"] = nc.dram_tensor("gate_W", [D, 8], F32, kind="ExternalInput").ap()
        A["W1"] = nc.dram_tensor("W1", [8, D, 512], F32, kind="ExternalInput").ap()
        A["b1"] = nc.dram_tensor("b1", [8, 512], F32, kind="ExternalInput").ap()
        A["W2"] = nc.dram_tensor("W2", [8, 512, D], F32, kind="ExternalInput").ap()
        A["b2"] = nc.dram_tensor("b2", [8, D], F32, kind="ExternalInput").ap()
        A["Ws1"] = nc.dram_tensor("Ws1", [D, 512], F32, kind="ExternalInput").ap()
        A["bs1"] = nc.dram_tensor("bs1", [1, 512], F32, kind="ExternalInput").ap()
        A["Ws2"] = nc.dram_tensor("Ws2", [512, D], F32, kind="ExternalInput").ap()
        A["bs2"] = nc.dram_tensor("bs2", [1, D], F32, kind="ExternalInput").ap()
        A["rs"] = nc.dram_tensor("rs", [1, 1], F32, kind="ExternalInput").ap()
        A["sel8"] = nc.dram_tensor("sel8", [8, 1024], BF16, kind="ExternalInput").ap()
    if "p5" in phases:
        A["wq1o"] = nc.dram_tensor("wq1o", [D, 128], F32, kind="ExternalInput").ap()
        A["wk1o"] = nc.dram_tensor("wk1o", [D, 128], F32, kind="ExternalInput").ap()
        A["wv1o"] = nc.dram_tensor("wv1o", [D, 128], F32, kind="ExternalInput").ap()
        A["wo1"] = nc.dram_tensor("wo1", [D, D], F32, kind="ExternalInput").ap()
        A["cos_in"] = nc.dram_tensor("cos_in", [128, 8192], BF16, kind="ExternalInput").ap()
        A["sin_in"] = nc.dram_tensor("sin_in", [128, 8192], BF16, kind="ExternalInput").ap()
        A["pmat_in"] = nc.dram_tensor("pmat_in", [128, 128], BF16, kind="ExternalInput").ap()
        A["masks_in"] = nc.dram_tensor("masks_in", [128, 8192], BF16, kind="ExternalInput").ap()
        A["idx_o"] = nc.dram_tensor("idx_o", [128, 16], U32, kind="ExternalInput").ap()
        A["pay1"] = nc.dram_tensor("pay1", [1024, 2048], BF16, kind="Internal").ap()
        A["gath1"] = nc.dram_tensor("gath1", [8192, 2048], BF16, kind="Internal",
                                    addr_space="Shared").ap()
        for bb in range(2):
            A[f"pay2b{bb}"] = nc.dram_tensor(f"pay2b{bb}", [128, 8192], BF16,
                                             kind="Internal").ap()
            A[f"gath2b{bb}"] = nc.dram_tensor(f"gath2b{bb}", [1024, 8192], BF16,
                                              kind="Internal", addr_space="Shared").ap()
    if "p8" in phases:
        A["l1scale"] = nc.dram_tensor("l1scale", [8, 128], F32, kind="ExternalInput").ap()
        A["gw1"] = nc.dram_tensor("gw1", [16, D], F32, kind="ExternalInput").ap()
        A["gb1"] = nc.dram_tensor("gb1", [1, 16], F32, kind="ExternalInput").ap()
        A["m1w"] = nc.dram_tensor("m1w", [16, 2048, D], F32, kind="ExternalInput").ap()
        A["m1b"] = nc.dram_tensor("m1b", [16, 2048], F32, kind="ExternalInput").ap()
        A["m2w"] = nc.dram_tensor("m2w", [16, D, D], F32, kind="ExternalInput").ap()
        A["m2b"] = nc.dram_tensor("m2b", [16, D], F32, kind="ExternalInput").ap()
        A["sel16"] = nc.dram_tensor("sel16", [16, 2048], BF16, kind="ExternalInput").ap()
        A["m1e_bf"] = nc.dram_tensor("m1e_bf", [16, 1024, D], BF16, kind="Internal").ap()
        A["m2_bf"] = nc.dram_tensor("m2_bf", [16, D, D], BF16, kind="Internal").ap()
    for d in dbg:
        if d == "xn":
            A["dbg_xn"] = nc.dram_tensor("dbg_xn", [128, 16384], F32, kind="ExternalOutput").ap()
        if d == "gate":
            A["dbg_route"] = nc.dram_tensor("dbg_route", [8, 2048], F32, kind="ExternalOutput").ap()
            A["dbg_gfb"] = nc.dram_tensor("dbg_gfb", [128, 2048], F32, kind="ExternalOutput").ap()
            A["dbg_gsc"] = nc.dram_tensor("dbg_gsc", [128, 64], F32, kind="ExternalOutput").ap()
        if d == "rows":
            A["dbg_rows"] = nc.dram_tensor("dbg_rows", [1, 8192], F32, kind="ExternalOutput").ap()
            A["dbg_gb"] = nc.dram_tensor("dbg_gb", [128, 48], F32, kind="ExternalOutput").ap()

    with root:
        block = root.enter_context(nc.Block())
        c = Ctx(nc, block, root)
        c.A = A

        c.T["x_c"] = c.sb(root, "x_c", [128, 16384], F32)
        c.T["ident"] = c.sb(root, "ident", [128, 128], F32)
        c.T["ident_bf"] = c.sb(root, "ident_bf", [128, 128], BF16)
        c.T["ones_row"] = c.sb(root, "ones_row", [1, 128], F32)
        c.T["ones_col_bf"] = c.sb(root, "ones_col_bf", [128, 1], BF16)

        def _(sy):
            c.inc(sy.dma_start(out=c.T["ident"][:, :], in_=A["ident_in"][:, :]), "wd", 16)
        block.sync(_)
        id_wd = c.cnt["wd"]

        def _(g):
            g.memset(c.T["ones_row"][:, :], 1.0)
            c.inc(g.memset(c.T["ones_col_bf"][:, :], 1.0), "pl", 1)
        block.gpsimd(_)

        def _(ve):
            ve.wait_ge(c.S["wd"], id_wd)
            c.inc(ve.tensor_copy(c.T["ident_bf"][:, :], c.T["ident"][:, :]), "dv", 1)
        block.vector(_)

        def _(te):
            te.wait_ge(c.S["pl"], c.cnt["pl"])
        block.tensor(_)

        if "p1" in phases:
            gb_st = contextlib.ExitStack()
            g_col = c.sb(root, "g_col", [128, NF], F32)
            b_col = c.sb(root, "b_col", [128, NF], F32)
            gb_sb = c.sb(gb_st, "gb_stage", [8, 256], F32)
            gb_ps = c.ps(gb_st, "gb_ps", [128, 16], F32)

            def _(sy):
                c.fence(sy)
                c.inc(sy.dma_start(out=gb_sb[0:8, 0:128], in_=A["ln_g"][:, :]), "wd", 16)
                c.inc(sy.dma_start(out=gb_sb[0:8, 128:256], in_=A["ln_b"][:, :]), "wd", 16)
            block.sync(_)
            wd_here = c.cnt["wd"]

            def _(te):
                te.wait_ge(c.S["wd"], wd_here)
                c.psum_guard(te)
                te.transpose(gb_ps[:, 0:8], gb_sb[0:8, 0:128], c.T["ident"][0:8, 0:8])
                c.inc(te.transpose(gb_ps[:, 8:16], gb_sb[0:8, 128:256],
                                   c.T["ident"][0:8, 0:8]), "mm", 1)
            block.tensor(_)
            mm_here = c.cnt["mm"]

            def _(ve):
                ve.wait_ge(c.S["mm"], mm_here)
                ve.tensor_copy(g_col[:, :], gb_ps[:, 0:8])
                c.inc(ve.tensor_copy(b_col[:, :], gb_ps[:, 8:16]), "dv", 1)
            block.vector(_)
            if "dbg_gb" in A:
                def _(sy):
                    sy.wait_ge(c.S["dv"], c.cnt["dv"])
                    c.inc(sy.dma_start(out=A["dbg_gb"][0:8, 16:48],
                                       in_=gb_sb[0:8, 0:32]), "wd", 16)
                block.sync(_)
            gb_st.close()

        _p_load(c)

        if "p1" in phases:
            if "dbg_gb" in A:
                def _(sy):
                    sy.wait_ge(c.S["dv"], c.cnt["dv"])
                    c.inc(sy.dma_start(out=A["dbg_gb"][:, 0:8], in_=g_col[:, :]), "wd", 16)
                    c.inc(sy.dma_start(out=A["dbg_gb"][:, 8:16], in_=b_col[:, :]), "wd", 16)
                block.sync(_)

            if "p2" in phases:
                p23_st = contextlib.ExitStack()
                c.T["Q_sb"] = c.sb(p23_st, "Q_sb", [128, 16 * 2048], BF16)
                c.T["bias_cols"] = c.sb(p23_st, "bias_cols", [128, 32], F32)
            xn_st = contextlib.ExitStack()
            c.T["xn_bf"] = c.sb(xn_st, "xn_bf", [128, 16384], BF16)
            p_layernorm(c, "xn_bf", eps=1e-5, fold_eps2=1e-6,
                        affine=(g_col, b_col), add_to_stream=True,
                        dbg_rows_ap=A.get("dbg_rows"), tag="_p1")

            if "dbg_xn" in A:
                st2 = contextlib.ExitStack()
                xf = c.sb(st2, "dbg_xf", [128, 2048], F32)
                for fc in range(NF):
                    def _(ve, fc=fc):
                        ve.wait_ge(c.S["wd"], c.cnt["wd"])
                        c.inc(ve.tensor_copy(xf[:, :],
                                             c.T["xn_bf"][:, fc * 2048:(fc + 1) * 2048]),
                              "dv", 1)
                    block.vector(_)
                    dv_here = c.cnt["dv"]

                    def _(sy, fc=fc, dv_here=dv_here):
                        sy.wait_ge(c.S["dv"], dv_here)
                        c.inc(sy.dma_start(out=A["dbg_xn"][:, fc * 2048:(fc + 1) * 2048],
                                           in_=xf[:, :]), "wd", 16)
                    block.sync(_)
                st2.close()

        if "p2" in phases:
            p2_qkv_l0(c)
            xn_st.close()
            if "p8" in phases:
                _p8_weight_cast(c)
            if "p3" in phases:
                p3_flash_l0(c)
            p23_st.close()

        if "p4" in phases:
            p4_ffn_l0(c)

        if "p5" in phases:
            p5to7_l1_attn(c)

        if "p8" in phases:
            p8_ffn_l1(c)

        _p_store(c)
    return nc



def _consts():
    return {"ident_in": np.eye(128, dtype=np.float32)}


def _dup_wq(wq):
    return np.ascontiguousarray(np.concatenate(
        [np.concatenate([wq[:, h * 64:(h + 1) * 64]] * 2, axis=1) for h in range(16)],
        axis=1))


def _sel8():
    s = np.zeros((8, 1024), dtype=BF)
    for e in range(8):
        s[e, e * 128:(e + 1) * 128] = 1
    return s


def _rope_tables():
    t = np.arange(8192, dtype=np.float32)
    invf = (1.0 / 10000.0 ** (np.arange(0, 128, 2, dtype=np.float32) / 128.0)).astype(np.float32)
    freqs = np.outer(t, invf).astype(np.float32)
    emb = np.concatenate([freqs, freqs], axis=-1)
    return (np.ascontiguousarray(np.cos(emb).T.astype(BF)),
            np.ascontiguousarray(np.sin(emb).T.astype(BF)))


def _pmat():
    p = np.zeros((128, 128), dtype=BF)
    for d in range(64):
        p[d + 64, d] = -1.0
        p[d, d + 64] = 1.0
    return p


def _masks():
    m = np.zeros((128, 8192), dtype=np.float32)
    for j in range(8):
        for kk in range(128):
            qq = np.arange(1024)
            m[kk, j * 1024:(j + 1) * 1024] = np.where(j * 128 + kk > qq, -1e30, 0.0)
    return m.astype(BF)


def _idx_o(cid):
    idx = np.zeros((128, 16), dtype=np.uint32)
    for r in range(8):
        for p in range(128):
            idx[p, r] = (r * 128 + p) * 8 + cid
    return idx


def _sel16():
    s = np.zeros((16, 2048), dtype=BF)
    for e in range(16):
        s[e, e * 128:(e + 1) * 128] = 8.0
    return s


def _launch_d_maps(inputs, x3s):
    consts = _consts()
    f32 = lambda k: np.ascontiguousarray(np.asarray(inputs[k], np.float32))
    maps = []
    for cid in range(NCORES):
        m = {"x_in": x3s[cid],
             "l1scale": f32("L1_ffn_scale").reshape(8, 128),
             "gw1": f32("L1_ffn_gate_w"), "gb1": f32("L1_ffn_gate_b").reshape(1, 16),
             "m1w": f32("L1_ffn_mlp1_w"), "m1b": f32("L1_ffn_mlp1_b"),
             "m2w": f32("L1_ffn_mlp2_w"), "m2b": f32("L1_ffn_mlp2_b"),
             "sel16": _sel16()}
        m.update(consts)
        maps.append(m)
    return maps


def _launch_c_maps(inputs, x2s):
    consts = _consts()
    cos_t, sin_t = _rope_tables()
    f32 = lambda k: np.ascontiguousarray(np.asarray(inputs[k], np.float32))
    Wq = f32("L1_attn_Wq")
    Wk = f32("L1_attn_Wk")
    Wv = f32("L1_attn_Wv")
    maps = []
    for cid in range(NCORES):
        m = {"x_in": x2s[cid],
             "wq1o": np.ascontiguousarray(Wq[:, 128 * cid:128 * (cid + 1)]),
             "wk1o": np.ascontiguousarray(Wk[:, 128 * (cid // 2):128 * (cid // 2 + 1)]),
             "wv1o": np.ascontiguousarray(Wv[:, 128 * (cid // 2):128 * (cid // 2 + 1)]),
             "wo1": f32("L1_attn_Wo"),
             "cos_in": cos_t, "sin_in": sin_t, "pmat_in": _pmat(),
             "masks_in": _masks(), "idx_o": _idx_o(cid)}
        m.update(consts)
        maps.append(m)
    return maps


def _launch_b_maps(inputs, x1s):
    consts = _consts()
    f32 = lambda k: np.ascontiguousarray(np.asarray(inputs[k], np.float32))
    maps = []
    for cid in range(NCORES):
        m = {"x_in": x1s[cid],
             "gate_W": f32("L0_ffn_gate_W"),
             "W1": f32("L0_ffn_W1"), "b1": f32("L0_ffn_b1"),
             "W2": f32("L0_ffn_W2"), "b2": f32("L0_ffn_b2"),
             "Ws1": f32("L0_ffn_Ws1"), "bs1": f32("L0_ffn_bs1").reshape(1, 512),
             "Ws2": f32("L0_ffn_Ws2"), "bs2": f32("L0_ffn_bs2").reshape(1, 1024),
             "rs": f32("L0_ffn_route_scale").reshape(1, 1),
             "sel8": _sel8()}
        m.update(consts)
        maps.append(m)
    return maps


def _shard_x(x):
    out = []
    for cid in range(NCORES):
        out.append(np.ascontiguousarray(
            np.concatenate([x[0, 1024 * cid:1024 * (cid + 1)],
                            x[1, 1024 * cid:1024 * (cid + 1)]], axis=0)))
    return out


def _unshard_y(shards):
    y = np.empty((2, 8192, 1024), dtype=np.float32)
    for cid in range(NCORES):
        y[0, 1024 * cid:1024 * (cid + 1)] = shards[cid][0:1024]
        y[1, 1024 * cid:1024 * (cid + 1)] = shards[cid][1024:2048]
    return y


def _launch_a_maps(inputs, xs):
    consts = _consts()
    f32 = lambda k: np.ascontiguousarray(np.asarray(inputs[k], np.float32))
    maps = []
    for cid in range(NCORES):
        m = {"x_in": xs[cid],
             "ln_g": f32("L0_attn_ln_scale").reshape(8, 128),
             "ln_b": f32("L0_attn_ln_bias").reshape(8, 128),
             "wq": f32("L0_attn_wq"), "wk": f32("L0_attn_wk"), "wv": f32("L0_attn_wv")}
        m.update(consts)
        maps.append(m)
    return maps


FUSED = True


def kernel(**inputs):
    x = np.asarray(inputs["L0_attn_x"], dtype=np.float32)
    xs = _shard_x(x)
    if FUSED:
        nc = build_launch({"p1", "p2", "p3", "p4", "p5", "p8"})
        maps = []
        for cid in range(NCORES):
            m = {}
            m.update(_launch_a_maps(inputs, xs)[cid])
            m.update(_launch_b_maps(inputs, xs)[cid])
            m.update(_launch_c_maps(inputs, xs)[cid])
            m.update(_launch_d_maps(inputs, xs)[cid])
            m["x_in"] = xs[cid]
            maps.append(m)
        res = run_bass_kernel_spmd(nc, maps, core_ids=list(range(NCORES)))
        return _unshard_y([r["y_out"] for r in res.results])
    nc_a = build_launch({"p1", "p2", "p3"})
    res = run_bass_kernel_spmd(nc_a, _launch_a_maps(inputs, xs), core_ids=list(range(NCORES)))
    x1s = [np.ascontiguousarray(r["y_out"]) for r in res.results]
    nc_b = build_launch({"p4"})
    res = run_bass_kernel_spmd(nc_b, _launch_b_maps(inputs, x1s), core_ids=list(range(NCORES)))
    x2s = [np.ascontiguousarray(r["y_out"]) for r in res.results]
    nc_c = build_launch({"p5"})
    res = run_bass_kernel_spmd(nc_c, _launch_c_maps(inputs, x2s), core_ids=list(range(NCORES)))
    x3s = [np.ascontiguousarray(r["y_out"]) for r in res.results]
    nc_d = build_launch({"p8"})
    res = run_bass_kernel_spmd(nc_d, _launch_d_maps(inputs, x3s), core_ids=list(range(NCORES)))
    return _unshard_y([r["y_out"] for r in res.results])


# version 14
# speedup vs baseline: 1.4808x; 1.0076x over previous
import contextlib
import numpy as np
import ml_dtypes
import concourse.bass as bass
import concourse.mybir as mybir
from concourse.bass_utils import run_bass_kernel_spmd

F32 = mybir.dt.float32
BF16 = mybir.dt.bfloat16
U32 = mybir.dt.uint32
AF = mybir.ActivationFunctionType
ALU = mybir.AluOpType
AX = mybir.AxisListType

NCORES = 8
D = 1024
NF = 8
TOWN = 2048
TB = 1024
SEQ = 8192
NB = 2

BF = ml_dtypes.bfloat16



class Ctx:

    def __init__(self, nc, block, stack):
        self.nc = nc
        self.block = block
        self.stack = stack
        self.S = {}
        self.cnt = {}
        self.T = {}
        self.A = {}
        for s in ("gd", "wd", "sd", "mm", "ac", "dv", "pl", "cc"):
            self.S[s] = stack.enter_context(nc.semaphore(s))
            self.cnt[s] = 0

    def inc(self, instr, s, n=1):
        instr.then_inc(self.S[s], n)
        self.cnt[s] += n
        return self.cnt[s]

    def sb(self, stack, name, shape, dtype):
        t = stack.enter_context(self.nc.sbuf_tensor(name, shape, dtype))
        return t

    def psum_guard(self, te):
        te.wait_ge(self.S["dv"], self.cnt["dv"])
        te.wait_ge(self.S["ac"], self.cnt["ac"])

    def fence(self, eng, sems=("mm", "dv", "ac", "pl", "wd", "gd", "sd")):
        for s in sems:
            if self.cnt[s]:
                eng.wait_ge(self.S[s], self.cnt[s])

    def ps(self, stack, name, shape, dtype=F32):
        return stack.enter_context(self.nc.psum_tensor(name, shape, dtype))



def _p_load(c):
    nc, block = c.nc, c.block
    x_in = c.A["x_in"]
    x_c = c.T["x_c"]
    ident = c.T["ident"]
    st = contextlib.ExitStack()
    xa = c.sb(st, "xa_stage", [128, 16 * 1024], F32)
    ps_tr = [c.ps(st, f"ps_tr{i}", [128, 512], F32) for i in range(2)]

    def _(g):
        c.fence(g)
        c.inc(g.dma_start(out=xa[:, :].rearrange("p (tt f) -> p tt f", tt=16),
                          in_=x_in[:, :].rearrange("(tt p) f -> p tt f", p=128)), "gd", 16)
    block.gpsimd(_)
    ld_done = c.cnt["gd"]

    evac_cnt = [0]
    groups = [(tt, fg) for tt in range(16) for fg in range(2)]
    for gi, (tt, fg) in enumerate(groups):
        def _(te, tt=tt, fg=fg, gi=gi):
            if gi == 0:
                te.wait_ge(c.S["gd"], ld_done)
                te.wait_ge(c.S["wd"], c.cnt["wd"])
            if gi >= 2:
                te.wait_ge(c.S["dv"], evac_done[gi - 2])
            for j in range(4):
                fc = fg * 4 + j
                ins = te.transpose(ps_tr[gi % 2][:, j * 128:(j + 1) * 128],
                                   xa[:, tt * 1024 + fc * 128: tt * 1024 + (fc + 1) * 128],
                                   ident[:, :])
                if j == 3:
                    c.inc(ins, "mm", 1)
        block.tensor(_)
        mm_here = c.cnt["mm"]

        def _(ve, tt=tt, fg=fg, gi=gi, mm_here=mm_here):
            ve.wait_ge(c.S["mm"], mm_here)
            dst = x_c[:, :].rearrange("p (f t) -> p f t", f=8)[
                :, fg * 4:(fg + 1) * 4, tt * 128:(tt + 1) * 128]
            c.inc(ve.tensor_copy(dst, ps_tr[gi % 2][:, :].rearrange("p (f t) -> p f t", f=4)),
                  "dv", 1)
        block.vector(_)
        if gi == 0:
            evac_done = {}
        evac_done[gi] = c.cnt["dv"]
    st.close()


def _p_store(c, dst_name="y_out"):
    nc, block = c.nc, c.block
    y = c.A[dst_name]
    x_c = c.T["x_c"]
    ident = c.T["ident"]
    st = contextlib.ExitStack()
    ya = c.sb(st, "ya_stage", [128, 1024], F32)
    ps_tr = [c.ps(st, f"ps_trS{i}", [128, 512], F32) for i in range(2)]
    store_done = {}
    for tt in range(16):
        for fg in range(2):
            gi = tt * 2 + fg

            def _(te, tt=tt, fg=fg, gi=gi):
                if gi == 0:
                    c.fence(te)
                if gi >= 2:
                    te.wait_ge(c.S["dv"], evac_done[gi - 2])
                if tt >= 1:
                    te.wait_ge(c.S["wd"], store_done.get(tt - 1, 0))
                for j in range(4):
                    fc = fg * 4 + j
                    ins = te.transpose(ps_tr[gi % 2][:, j * 128:(j + 1) * 128],
                                       x_c[:, fc * 2048 + tt * 128: fc * 2048 + (tt + 1) * 128],
                                       ident[:, :])
                    if j == 3:
                        c.inc(ins, "mm", 1)
            block.tensor(_)
            mm_here = c.cnt["mm"]

            def _(ve, fg=fg, gi=gi, mm_here=mm_here):
                if gi == 0:
                    c.fence(ve)
                ve.wait_ge(c.S["mm"], mm_here)
                c.inc(ve.tensor_copy(ya[:, fg * 512:(fg + 1) * 512], ps_tr[gi % 2][:, :]),
                      "dv", 1)
            block.vector(_)
            if gi == 0:
                evac_done = {}
            evac_done[gi] = c.cnt["dv"]

        dv_here = c.cnt["dv"]

        def _(sy, tt=tt, dv_here=dv_here):
            sy.wait_ge(c.S["dv"], dv_here)
            c.inc(sy.dma_start(out=y[tt * 128:(tt + 1) * 128, :], in_=ya[:, :]), "wd", 16)
        block.sync(_)
        store_done[tt] = c.cnt["wd"]

    def _(sy):
        sy.wait_ge(c.S["wd"], c.cnt["wd"])
    block.sync(_)
    st.close()



def _row_broadcast(c, st, rows, n_tok=TOWN, tag=""):
    out = {}
    ones_row = c.T["ones_row"]
    ps = {}
    for ap, name in rows:
        ps[name] = c.ps(st, f"bc_{name}{tag}", [128, n_tok], F32)

    def _(te):
        te.wait_ge(c.S["dv"], c.cnt["dv"])
        te.wait_ge(c.S["ac"], c.cnt["ac"])
        n = len(rows) * (n_tok // 512)
        i = 0
        for ap, name in rows:
            for q in range(n_tok // 512):
                ins = te.matmul(ps[name][:, q * 512:(q + 1) * 512], ones_row[:, :],
                                ap[0:1, q * 512:(q + 1) * 512], start=True, stop=True)
                i += 1
                if i == n:
                    c.inc(ins, "mm", 1)
    c.block.tensor(_)
    return ps, c.cnt["mm"]


def p_layernorm(c, out_name, eps=1e-5, fold_eps2=None, affine=None,
                add_to_stream=False, rms=False, src_bf=None, dbg_rows_ap=None,
                tag=""):
    nc, block = c.nc, c.block
    x_c = c.T["x_c"]
    ones_col = c.T["ones_col_bf"]
    out_bf = c.T[out_name]
    st = contextlib.ExitStack()
    xbf = [c.sb(st, f"ln_xbf{i}{tag}", [128, 2048], BF16) for i in range(1)] * 2
    sqbf = [c.sb(st, f"ln_sqbf{i}{tag}", [128, 2048], BF16) for i in range(1)] * 2
    rowbuf = c.sb(st, f"ln_rows{tag}", [1, 3 * 2048], F32)
    mu_row = rowbuf[0:1, 0:2048]
    tA_row = rowbuf[0:1, 2048:4096]
    tB_row = rowbuf[0:1, 4096:6144]
    t1_row = tA_row
    t2_row = tB_row
    c_row = tA_row

    stats_st = contextlib.ExitStack()
    sum_ps = None if rms else c.ps(stats_st, f"ln_sum{tag}", [1, 2048], F32)
    sq_ps = c.ps(stats_st, f"ln_sq{tag}", [1, 2048], F32)

    prep_done = {}
    mm_chunk = {}
    for fc in range(NF):
        def _(ve, fc=fc):
            if fc >= 1:
                ve.wait_ge(c.S["mm"], mm_chunk[fc - 1])
            if rms:
                src = src_bf[:, fc * 2048:(fc + 1) * 2048]
                ins = ve.tensor_tensor(sqbf[fc % 2][:, :], src, src, op=ALU.mult)
            else:
                ve.tensor_copy(xbf[fc % 2][:, :], x_c[:, fc * 2048:(fc + 1) * 2048])
                ins = ve.tensor_tensor(sqbf[fc % 2][:, :], xbf[fc % 2][:, :], xbf[fc % 2][:, :],
                                       op=ALU.mult)
            c.inc(ins, "dv", 1)
        block.vector(_)
        prep_done[fc] = c.cnt["dv"]

        def _(te, fc=fc):
            te.wait_ge(c.S["dv"], prep_done[fc])
            for q in range(4):
                if not rms:
                    te.matmul(sum_ps[0:1, q * 512:(q + 1) * 512], ones_col[:, :],
                              xbf[fc % 2][:, q * 512:(q + 1) * 512],
                              start=(fc == 0), stop=(fc == NF - 1))
                ins = te.matmul(sq_ps[0:1, q * 512:(q + 1) * 512], ones_col[:, :],
                                sqbf[fc % 2][:, q * 512:(q + 1) * 512],
                                start=(fc == 0), stop=(fc == NF - 1))
                if q == 3:
                    c.inc(ins, "mm", 1)
        block.tensor(_)
        mm_chunk[fc] = c.cnt["mm"]

    stats_mm = c.cnt["mm"]

    def _(ve):
        ve.wait_ge(c.S["mm"], stats_mm)
        if rms:
            ins = ve.tensor_scalar(t1_row, sq_ps[0:1, :], 1.0 / D, eps, op0=ALU.mult, op1=ALU.add)
        else:
            ve.tensor_scalar(mu_row, sum_ps[0:1, :], 1.0 / D, None, op0=ALU.mult)
            ve.tensor_scalar(t1_row, sq_ps[0:1, :], 1.0 / D, None, op0=ALU.mult)
            ve.tensor_tensor(t2_row, mu_row, mu_row, op=ALU.mult)
            ve.tensor_tensor(t1_row, t1_row, t2_row, op=ALU.subtract)
            if fold_eps2 is not None:
                e2 = fold_eps2
                ins = ve.tensor_scalar(t1_row, t1_row, 1.0 + e2, eps * e2, op0=ALU.mult, op1=ALU.add)
            else:
                ins = ve.tensor_scalar(t1_row, t1_row, eps, None, op0=ALU.add)
        c.inc(ins, "dv", 1)
    block.vector(_)
    row_dv = c.cnt["dv"]

    def _(se):
        se.wait_ge(c.S["dv"], row_dv)
        c.inc(se.activation(t2_row, t1_row, AF.Sqrt), "ac", 1)
    block.scalar(_)
    row_ac = c.cnt["ac"]

    def _(ve):
        ve.wait_ge(c.S["ac"], row_ac)
        c.inc(ve.reciprocal(c_row, t2_row), "dv", 1)
    block.vector(_)

    stats_st.close()

    rows = [(c_row, "c")] if rms else [(mu_row, "mu"), (c_row, "c")]
    bc, bc_mm = _row_broadcast(c, st, rows, tag=tag)

    dtmp = c.sb(st, f"ln_dtmp{tag}", [128, 2048], F32)
    etmp = sqbf[0]
    for fc in range(NF):
        def _(ve, fc=fc):
            if fc == 0:
                ve.wait_ge(c.S["mm"], bc_mm)
            dst = out_bf[:, fc * 2048:(fc + 1) * 2048]
            if rms:
                src = src_bf[:, fc * 2048:(fc + 1) * 2048]
                ve.tensor_tensor(dtmp[:, :], bc["c"][:, :], src, op=ALU.mult)
                g_col = affine[0]
                ins = ve.tensor_scalar(dst, dtmp[:, :], g_col[:, fc:fc + 1], None, op0=ALU.mult)
            else:
                src = x_c[:, fc * 2048:(fc + 1) * 2048]
                ve.tensor_tensor(dtmp[:, :], src, bc["mu"][:, :], op=ALU.subtract)
                if affine is not None:
                    ve.tensor_tensor(etmp[:, :], dtmp[:, :], bc["c"][:, :], op=ALU.mult)
                    g_col, b_col = affine
                    ins = ve.tensor_scalar(dst, etmp[:, :], g_col[:, fc:fc + 1],
                                           b_col[:, fc:fc + 1], op0=ALU.mult, op1=ALU.add)
                else:
                    ins = ve.tensor_tensor(dst, dtmp[:, :], bc["c"][:, :], op=ALU.mult)
            if add_to_stream:
                ins = ve.tensor_tensor(src, src, dst, op=ALU.add)
            c.inc(ins, "dv", 1)
        block.vector(_)
    final_dv = c.cnt["dv"]

    if dbg_rows_ap is not None:
        def _(sy):
            sy.wait_ge(c.S["dv"], final_dv)
            c.inc(sy.dma_start(out=dbg_rows_ap[:, :], in_=rowbuf[:, :]), "wd", 16)
            sy.wait_ge(c.S["wd"], c.cnt["wd"])
        block.sync(_)

    def _(te):
        te.wait_ge(c.S["dv"], final_dv)
    block.tensor(_)
    st.close()



def p2_qkv_l0(c):
    nc, block = c.nc, c.block
    A = c.A
    xn = c.T["xn_bf"]
    ones_row = c.T["ones_row"]
    ones_col = c.T["ones_col_bf"]
    st = contextlib.ExitStack()

    k_stage = [c.sb(st, f"k_stage{i}", [128, 2048], BF16) for i in range(2)]
    sq = [c.sb(st, f"p2_sq{i}", [128, 2048], BF16) for i in range(2)]
    nrows = c.sb(st, "p2_nrows", [1, 512], F32)
    qrow = nrows[0:1, 0:32]
    krow = nrows[0:1, 32:64]
    kall = nrows[0:1, 64:320]
    trow = nrows[0:1, 320:352]
    srow = nrows[0:1, 352:384]
    brow = nrows[0:1, 384:416]

    wst = contextlib.ExitStack()
    w_sb = c.sb(wst, "w_sb", [128, 8 * 1024], BF16)

    def _(g):
        c.fence(g)
        c.inc(g.dma_start(out=w_sb[:, :].rearrange("p (fc n) -> p fc n", fc=8),
                          in_=A["wk"][:, :].rearrange("(fc p) n -> p fc n", p=128)),
              "gd", 16)
    block.gpsimd(_)
    w_loaded = c.cnt["gd"]

    ps_st = contextlib.ExitStack()
    qk_ps = c.ps(ps_st, "qk_ps", [128, 2048], F32)
    nr_ps = c.ps(ps_st, "nr_ps", [1, 2048], F32)

    evac_done = {}
    red_done = {}
    pay_done = {}
    for jt in range(24):
        is_q = jt >= 8
        p = jt - 8 if is_q else jt

        if jt == 8:
            def _(g):
                g.wait_ge(c.S["dv"], red_done[(7, 1)][0])
                c.inc(g.dma_start(out=A["pay0m"][:, :], in_=krow), "gd", 16)
                g.wait_ge(c.S["gd"], c.cnt["gd"])
                g.wait_ge(c.S["wd"], c.cnt["wd"])
                c.inc(g.collective_compute(
                    "AllGather", ALU.bypass, replica_groups=[list(range(8))],
                    ins=[A["pay0m"][:, :].opt()], outs=[A["gath0m"][:, :].opt()]),
                    "cc", 1)
                c.p2_ccm = c.cnt["cc"]
                c.inc(g.collective_compute(
                    "AllGather", ALU.bypass, replica_groups=[list(range(8))],
                    ins=[A["pay0k0"][:, :].opt()], outs=[A["gath0k0"][:, :].opt()]),
                    "cc", 1)
            block.gpsimd(_)
            def _(g):
                g.wait_ge(c.S["mm"], c.cnt["mm"])
                c.inc(g.dma_start(out=w_sb[:, :].rearrange("p (fc n) -> p fc n", fc=8),
                                  in_=A["wq"][:, :].rearrange("(fc p) n -> p fc n", p=128)),
                      "gd", 16)
            block.gpsimd(_)
            wq_loaded = c.cnt["gd"]

        def _(te, jt=jt, p=p, is_q=is_q):
            if jt == 0:
                c.fence(te)
                te.wait_ge(c.S["gd"], w_loaded)
            if jt == 8:
                te.wait_ge(c.S["gd"], wq_loaded)
            if jt >= 1:
                te.wait_ge(c.S["ac"], evac_done[jt - 1])
            for tq in range(4):
                if is_q:
                    for hb in range(2):
                        for fc in range(NF):
                            ins = te.matmul(
                                qk_ps[hb * 64:hb * 64 + 64, tq * 512:(tq + 1) * 512],
                                w_sb[:, fc * 1024 + p * 64: fc * 1024 + (p + 1) * 64],
                                xn[:, fc * 2048 + tq * 512: fc * 2048 + (tq + 1) * 512],
                                start=(fc == 0), stop=(fc == NF - 1))
                else:
                    for fc in range(NF):
                        ins = te.matmul(
                            qk_ps[:, tq * 512:(tq + 1) * 512],
                            w_sb[:, fc * 1024 + p * 128: fc * 1024 + (p + 1) * 128],
                            xn[:, fc * 2048 + tq * 512: fc * 2048 + (tq + 1) * 512],
                            start=(fc == 0), stop=(fc == NF - 1))
            c.inc(ins, "mm", 1)
        block.tensor(_)
        mm_proj = c.cnt["mm"]

        def _(se, jt=jt, p=p, is_q=is_q, mm_proj=mm_proj):
            se.wait_ge(c.S["mm"], mm_proj)
            if is_q:
                dst = c.T["Q_sb"][:, p * 2048:(p + 1) * 2048]
            else:
                if jt >= 2:
                    se.wait_ge(c.S["wd"], pay_done[jt - 2])
                dst = k_stage[jt % 2][:, :]
            c.inc(se.copy(dst, qk_ps[:, :]), "ac", 1)
        block.scalar(_)
        evac_done[jt] = c.cnt["ac"]

        if not is_q:
            def _(sy, jt=jt, p=p):
                sy.wait_ge(c.S["ac"], evac_done[jt])
                c.inc(sy.dma_start(
                    out=A[f"pay0k{p // 4}"][(p % 4) * 128:(p % 4 + 1) * 128, :],
                    in_=k_stage[jt % 2][:, :]), "wd", 16)
            block.sync(_)
            pay_done[jt] = c.cnt["wd"]

        def _(ve, jt=jt, p=p, is_q=is_q):
            ve.wait_ge(c.S["ac"], evac_done[jt])
            if jt >= 2:
                ve.wait_ge(c.S["mm"], red_done[(jt - 2, 1)][1])
            src = (c.T["Q_sb"][:, p * 2048:(p + 1) * 2048] if is_q
                   else k_stage[jt % 2][:, :])
            c.inc(ve.tensor_tensor(sq[jt % 2][:, :], src, src, op=ALU.mult), "dv", 1)
        block.vector(_)
        sq_dv = c.cnt["dv"]

        hh_list = [0] if is_q else [0, 1]
        for hh in hh_list:
            def _(te, jt=jt, hh=hh, is_q=is_q, sq_dv=sq_dv):
                te.wait_ge(c.S["dv"], sq_dv)
                if (jt, hh) != (0, 0):
                    prev = (jt, hh - 1) if hh == 1 else (jt - 1, 1)
                    te.wait_ge(c.S["dv"], red_done[prev][0])
                if is_q:
                    base, kk = 0, 128
                else:
                    base, kk = hh * 64, 64
                for tq in range(4):
                    ins = te.matmul(nr_ps[0:1, tq * 512:(tq + 1) * 512],
                                    ones_col[base:base + kk, 0:1],
                                    sq[jt % 2][base:base + kk, tq * 512:(tq + 1) * 512],
                                    start=True, stop=True)
                c.inc(ins, "mm", 1)
            block.tensor(_)
            nm_mm = c.cnt["mm"]

            def _(ve, jt=jt, p=p, hh=hh, is_q=is_q, nm_mm=nm_mm):
                ve.wait_ge(c.S["mm"], nm_mm)
                row = qrow if is_q else krow
                h = p if is_q else 2 * p + hh
                for b in range(2):
                    ins = ve.reduce_max(row[0:1, b * 16 + h: b * 16 + h + 1],
                                        nr_ps[0:1, b * 1024:(b + 1) * 1024],
                                        axis=AX.X)
                c.inc(ins, "dv", 1)
            block.vector(_)
            red_done[(jt, hh)] = (c.cnt["dv"], c.cnt["mm"])
            if is_q:
                red_done[(jt, 1)] = red_done[(jt, 0)]

    wst.close()
    ps_st.close()

    wv_st = contextlib.ExitStack()
    wv_sb = c.sb(wv_st, "wv_sb", [128, 8 * 1024], BF16)
    v_stage = [c.sb(wv_st, f"v_stage{i}", [128, 1024], BF16) for i in range(2)]
    v_ps = c.ps(wv_st, "v_ps", [128, 1024], F32)

    def _(g):
        c.fence(g)
        c.inc(g.dma_start(out=wv_sb[:, :].rearrange("p (fc n) -> p fc n", fc=8),
                          in_=A["wv"][:, :].rearrange("(fc p) n -> p fc n", p=128)),
              "gd", 16)
    block.gpsimd(_)
    wv_loaded = c.cnt["gd"]

    vevac = {}
    vpay = {}
    for vj, (hg, ts_i) in enumerate([(hg, t) for hg in range(2) for t in range(16)]):
        def _(te, vj=vj, hg=hg, ts_i=ts_i):
            if vj == 0:
                c.fence(te)
                te.wait_ge(c.S["gd"], wv_loaded)
            if vj >= 2:
                te.wait_ge(c.S["ac"], vevac[vj - 2])
            for fc in range(NF):
                ins = te.matmul(v_ps[:, (vj % 2) * 512:(vj % 2 + 1) * 512],
                                xn[:, fc * 2048 + ts_i * 128: fc * 2048 + (ts_i + 1) * 128],
                                wv_sb[:, fc * 1024 + hg * 512: fc * 1024 + (hg + 1) * 512],
                                start=(fc == 0), stop=(fc == NF - 1))
            c.inc(ins, "mm", 1)
        block.tensor(_)
        vmm = c.cnt["mm"]

        def _(se, vj=vj, vmm=vmm):
            se.wait_ge(c.S["mm"], vmm)
            if vj >= 2:
                se.wait_ge(c.S["wd"], vpay[vj - 2])
            c.inc(se.copy(v_stage[vj % 2][:, 0:512], v_ps[:, (vj % 2) * 512:(vj % 2 + 1) * 512]),
                  "ac", 1)
        block.scalar(_)
        vevac[vj] = c.cnt["ac"]

        def _(sy, vj=vj, hg=hg, ts_i=ts_i):
            sy.wait_ge(c.S["ac"], vevac[vj])
            dst = A[f"pay0v{hg}"][ts_i * 128:(ts_i + 1) * 128, :]
            c.inc(sy.dma_start(out=dst, in_=v_stage[vj % 2][:, 0:512]), "wd", 16)
        block.sync(_)
        vpay[vj] = c.cnt["wd"]

        if ts_i == 15:
            def _(g, hg=hg):
                g.wait_ge(c.S["wd"], c.cnt["wd"])
                c.inc(g.collective_compute(
                    "AllGather", ALU.bypass, replica_groups=[list(range(8))],
                    ins=[A[f"pay0v{hg}"][:, :].opt()],
                    outs=[A[f"gath0v{hg}"][:, :].opt()]), "cc", 1)
                if hg == 0:
                    c.inc(g.collective_compute(
                        "AllGather", ALU.bypass, replica_groups=[list(range(8))],
                        ins=[A["pay0k1"][:, :].opt()],
                        outs=[A["gath0k1"][:, :].opt()]), "cc", 1)
            block.gpsimd(_)
            if hg == 0:
                c.p2_cc_g0 = c.cnt["cc"] - 1
            else:
                c.p2_cc_g1 = c.cnt["cc"]

    wv_st.close()

    def _(g):
        g.wait_ge(c.S["cc"], c.p2_ccm)
        src = A["gath0m"][:, :].rearrange("(o r) n -> o r n", o=1)
        c.inc(g.dma_start(out=kall.rearrange("o (r n) -> o r n", r=8), in_=src), "gd", 16)
    block.gpsimd(_)
    kall_ld = c.cnt["gd"]

    def _(ve):
        ve.wait_ge(c.S["gd"], kall_ld)
        ve.tensor_tensor(kall[0:1, 0:128], kall[0:1, 0:128], kall[0:1, 128:256], op=ALU.max)
        ve.tensor_tensor(kall[0:1, 0:64], kall[0:1, 0:64], kall[0:1, 64:128], op=ALU.max)
        ve.tensor_tensor(kall[0:1, 0:32], kall[0:1, 0:32], kall[0:1, 32:64], op=ALU.max)
        c.inc(ve.tensor_tensor(trow, qrow, kall[0:1, 0:32], op=ALU.mult), "dv", 1)
    block.vector(_)
    t_dv = c.cnt["dv"]

    def _(se):
        se.wait_ge(c.S["dv"], t_dv)
        c.inc(se.activation(srow, trow, AF.Sqrt), "ac", 1)
    block.scalar(_)
    s_ac = c.cnt["ac"]

    def _(ve):
        ve.wait_ge(c.S["ac"], s_ac)
        c.inc(ve.tensor_scalar(brow, srow, -1.0 / (32.0 * float(np.sqrt(2.0))), None, op0=ALU.mult), "dv", 1)
    block.vector(_)
    b_dv = c.cnt["dv"]

    bb_ps = c.ps(st, "bb_ps", [128, 32], F32)

    def _(te):
        te.wait_ge(c.S["dv"], b_dv)
        c.psum_guard(te)
        c.inc(te.matmul(bb_ps[:, :], ones_row[:, :], brow, start=True, stop=True), "mm", 1)
    block.tensor(_)
    bb_mm = c.cnt["mm"]

    def _(ve):
        ve.wait_ge(c.S["mm"], bb_mm)
        c.inc(ve.tensor_copy(c.T["bias_cols"][:, :], bb_ps[:, :]), "dv", 1)
    block.vector(_)

    def _(te):
        te.wait_ge(c.S["dv"], c.cnt["dv"])
    block.tensor(_)
    st.close()


def p3_flash_l0(c):
    nc, block = c.nc, c.block
    A = c.A
    Q_sb = c.T["Q_sb"]
    bias_cols = c.T["bias_cols"]
    st = contextlib.ExitStack()
    Kbuf = [c.sb(st, f"l0_K{i}", [128, 8192], BF16) for i in range(2)]
    Vbuf = [c.sb(st, f"l0_V{i}", [128, 65 * 64], BF16) for i in range(2)]
    NBUF = 3
    probs = [c.sb(st, f"l0_pr{i}", [128, 1024], BF16) for i in range(NBUF)]
    oacc_sb = c.sb(st, "l0_oacc", [128, 1024], F32)
    otmp = c.sb(st, "l0_otmp", [64, 1024], BF16)
    rc = c.sb(st, "l0_rc", [1, 1024], F32)
    S_ps = [c.ps(st, f"l0_S{i}", [128, 1024], F32) for i in range(NBUF)]
    oa_ps = c.ps(st, "l0_oa", [65, 1024], F32)

    def _(g):
        c.fence(g)
        c.inc(g.memset(Vbuf[0][:, :], 1.0), "pl", 1)
        c.inc(g.memset(Vbuf[1][:, :], 1.0), "pl", 1)
    block.gpsimd(_)
    ones_set = c.cnt["pl"]

    KB = 64
    it_list = [(b, h) for h in range(16) for b in range(2)]
    ld_done = {}
    it_done = {}

    for it, (b, h) in enumerate(it_list):
        def _(sy, it=it, b=b, h=h):
            if it == 0:
                c.fence(sy)
                sy.wait_ge(c.S["pl"], ones_set)
            if it >= 2:
                sy.wait_ge(c.S["mm"], it_done[it - 2])
            buf = it % 2
            sy.wait_ge(c.S["cc"], c.p2_cc_g0 if h < 8 else c.p2_cc_g1)
            gk = A[f"gath0k{h // 8}"]
            gv = A[f"gath0v{h // 8}"]
            hl = h % 8
            for r in range(8):
                for hb in range(2):
                    c.inc(sy.dma_start(
                        out=Kbuf[buf][hb * 64: hb * 64 + 64, r * 1024:(r + 1) * 1024],
                        in_=gk[r * 512 + 64 * hl: r * 512 + 64 * hl + 64,
                               b * 1024:(b + 1) * 1024]), "wd", 16)
            for r in range(8):
                src = gv[r * 2048 + b * 1024: r * 2048 + (b + 1) * 1024,
                         64 * hl: 64 * hl + 64]
                src = src.rearrange("(blk p) n -> p blk n", p=128)
                dst = Vbuf[buf][:, :].rearrange("p (blk n) -> p blk n",
                                                n=65)[:, r * 8:(r + 1) * 8, 0:64]
                c.inc(sy.dma_start(out=dst, in_=src), "wd", 16)
        block.sync(_)
        ld_done[it] = c.cnt["wd"]

        buf = it % 2
        qcol0 = h * 2048 + b * 1024
        prow0 = 64 * (h % 2)
        mm_base = c.cnt["mm"]
        ac_base = c.cnt["ac"]
        qk_sched = {kb: mm_base + (kb + 1 if kb < 2 else 2 * kb - 1) for kb in range(KB)}
        exp_sched = {kb: ac_base + kb + 1 for kb in range(KB)}

        def _(te, it=it, buf=buf, qcol0=qcol0):
            for kb in range(KB):
                if kb == 0:
                    te.wait_ge(c.S["wd"], ld_done[it])
                    if it >= 1:
                        te.wait_ge(c.S["ac"], exp_prev_last[0])
                        te.wait_ge(c.S["dv"], epi_prev[0])
                if kb >= NBUF:
                    te.wait_ge(c.S["ac"], exp_sched[kb - NBUF])
                for half in range(2):
                    ins = te.matmul(
                        S_ps[kb % NBUF][:, half * 512:(half + 1) * 512],
                        Kbuf[buf][:, kb * 128:(kb + 1) * 128],
                        Q_sb[:, qcol0 + half * 512: qcol0 + (half + 1) * 512],
                        start=True, stop=True)
                c.inc(ins, "mm", 1)
                if kb >= 2:
                    pkb = kb - 2
                    te.wait_ge(c.S["ac"], exp_sched[pkb])
                    for half in range(2):
                        ins = te.matmul(
                            oa_ps[0:65, half * 512:(half + 1) * 512],
                            Vbuf[buf][:, 65 * pkb: 65 * pkb + 65],
                            probs[pkb % NBUF][:, half * 512:(half + 1) * 512],
                            start=(pkb == 0), stop=False)
                    c.inc(ins, "mm", 1)
            for pkb in (KB - 2, KB - 1):
                te.wait_ge(c.S["ac"], exp_sched[pkb])
                for half in range(2):
                    ins = te.matmul(oa_ps[0:65, half * 512:(half + 1) * 512],
                                    Vbuf[buf][:, 65 * pkb: 65 * pkb + 65],
                                    probs[pkb % NBUF][:, half * 512:(half + 1) * 512],
                                    start=False, stop=(pkb == KB - 1))
                c.inc(ins, "mm", 1)
        block.tensor(_)
        assert c.cnt["mm"] == mm_base + 2 * KB

        def _(se, b=b, h=h):
            for kb in range(KB):
                se.wait_ge(c.S["mm"], qk_sched[kb])
                c.inc(se.activation(probs[kb % NBUF][:, :], S_ps[kb % NBUF][:, :], AF.Exp,
                                    bias=bias_cols[:, b * 16 + h: b * 16 + h + 1],
                                    scale=1.0 / 64.0), "ac", 1)
        block.scalar(_)
        assert c.cnt["ac"] == ac_base + KB
        it_done[it] = c.cnt["mm"]
        exp_prev_last = (ac_base + KB,)

        def _(ve, it=it):
            ve.wait_ge(c.S["mm"], it_done[it])
            c.inc(ve.tensor_copy(oacc_sb[0:65, :], oa_ps[0:65, :]), "dv", 1)
            c.inc(ve.reciprocal(rc[0:1, :], oacc_sb[64:65, :]), "dv", 1)
        block.vector(_)
        rc_dv = c.cnt["dv"]

        def _(te, it=it):
            te.wait_ge(c.S["dv"], rc_dv)
            for half in range(2):
                ins = te.matmul(S_ps[0][0:64, half * 512:(half + 1) * 512],
                                c.T["ones_row"][0:1, 0:64],
                                rc[0:1, half * 512:(half + 1) * 512],
                                start=True, stop=True)
            c.inc(ins, "mm", 1)
        block.tensor(_)
        rb_mm = c.cnt["mm"]

        def _(ve, it=it, rb_mm=rb_mm):
            ve.wait_ge(c.S["mm"], rb_mm)
            c.inc(ve.tensor_tensor(otmp[:, :], S_ps[0][0:64, :], oacc_sb[0:64, :],
                                   op=ALU.mult), "dv", 1)
        block.vector(_)
        mul_dv = c.cnt["dv"]

        xsl = c.T["x_c"][64 * (h % 2): 64 * (h % 2) + 64,
                         (h // 2) * 2048 + b * 1024: (h // 2) * 2048 + (b + 1) * 1024]
        if h % 2 == 0:
            def _(ve, xsl=xsl, mul_dv=mul_dv):
                c.inc(ve.tensor_tensor(xsl, xsl, otmp[:, :], op=ALU.add), "dv", 1)
            block.vector(_)
        else:
            def _(te, mul_dv=mul_dv):
                te.wait_ge(c.S["dv"], mul_dv)
                for half in range(2):
                    ins = te.matmul(S_ps[1][64:128, half * 512:(half + 1) * 512],
                                    c.T["ident_bf"][0:64, 0:64],
                                    otmp[:, half * 512:(half + 1) * 512],
                                    start=True, stop=True)
                c.inc(ins, "mm", 1)
            block.tensor(_)
            mv_mm = c.cnt["mm"]

            def _(ve, xsl=xsl, mv_mm=mv_mm):
                ve.wait_ge(c.S["mm"], mv_mm)
                c.inc(ve.tensor_tensor(xsl, xsl, S_ps[1][64:128, :], op=ALU.add), "dv", 1)
            block.vector(_)
        epi_prev = (c.cnt["dv"],)

    def _(te):
        te.wait_ge(c.S["dv"], c.cnt["dv"])
    block.tensor(_)
    st.close()



def p4_ffn_l0(c):
    nc, block = c.nc, c.block
    A = c.A
    st = contextlib.ExitStack()
    h1 = c.sb(st, "h1_bf", [128, 16384], BF16)
    c.T["h1_bf"] = h1
    p_layernorm(c, "h1_bf", eps=1e-5, tag="_p4")

    ones_row = c.T["ones_row"]

    cst = contextlib.ExitStack()
    gw_sb = c.sb(st, "gw_sb", [128, 64], BF16)
    bvec = c.sb(st, "bvec", [128, 36], F32)
    b2_sb = c.sb(st, "b2_sb", [8, 1024], BF16)
    bs2row = c.sb(st, "bs2row", [1, 1024], BF16)
    sel8 = c.sb(st, "sel8_sb", [8, 1024], BF16)
    ones_bfr = c.sb(st, "ones_bfr", [1, 512], BF16)
    rs_col = c.sb(st, "rs_col", [128, 1], F32)
    routeT = c.sb(st, "routeT", [8, 2048], BF16)
    gfb_sb = c.sb(st, "gfb_sb", [128, 2048], BF16)
    s_sb = c.sb(st, "s_sb", [128, 4 * 2048], BF16)
    ge_tmp = [c.sb(st, f"ge_tmp{i}", [128, 1024], BF16) for i in range(2)]

    b_stage = c.sb(cst, "b_stage", [8, 1024], F32)

    def _(g):
        c.fence(g)
        c.inc(g.dma_start(out=gw_sb[:, :].rearrange("p (fc e) -> p fc e", fc=8),
                          in_=A["gate_W"][:, :].rearrange("(fc p) e -> p fc e", p=128)),
              "gd", 16)
        c.inc(g.dma_start(out=b2_sb[:, :], in_=A["b2"][:, :]), "gd", 16)
        c.inc(g.dma_start(out=bs2row[:, :], in_=A["bs2"][:, :]), "gd", 16)
        c.inc(g.dma_start(out=sel8[:, :], in_=A["sel8"][:, :]), "gd", 16)
        c.inc(g.memset(ones_bfr[:, :], 1.0), "pl", 1)
        c.inc(g.dma_start(out=b_stage[0:8, 0:512], in_=A["b1"][:, :]), "gd", 16)
        c.inc(g.dma_start(out=b_stage[0:1, 512:1024], in_=A["bs1"][:, :]), "gd", 16)
    block.gpsimd(_)
    consts_gd = c.cnt["gd"]

    bps = c.ps(cst, "b_ps", [128, 40], F32)
    rs_row = c.sb(cst, "rs_row", [1, 1], F32)

    def _(g):
        c.inc(g.dma_start(out=rs_row[:, :], in_=A["rs"][:, :]), "gd", 16)
    block.gpsimd(_)
    rs_gd = c.cnt["gd"]

    def _(te):
        c.fence(te)
        te.wait_ge(c.S["gd"], consts_gd)
        te.wait_ge(c.S["gd"], rs_gd)
        for hc in range(4):
            te.transpose(bps[:, hc * 8:(hc + 1) * 8],
                         b_stage[0:8, hc * 128:(hc + 1) * 128], c.T["ident"][0:8, 0:8])
        for hc in range(4):
            te.transpose(bps[:, 32 + hc: 33 + hc],
                         b_stage[0:1, 512 + hc * 128: 512 + (hc + 1) * 128],
                         c.T["ident"][0:1, 0:1])
        c.inc(te.matmul(bps[:, 36:37], ones_row[:, :], rs_row[:, :],
                        start=True, stop=True), "mm", 1)
    block.tensor(_)
    bt_mm = c.cnt["mm"]

    def _(ve):
        ve.wait_ge(c.S["mm"], bt_mm)
        ve.tensor_copy(bvec[:, :], bps[:, 0:36])
        c.inc(ve.tensor_copy(rs_col[:, :], bps[:, 36:37]), "dv", 1)
    block.vector(_)
    bt_dv = c.cnt["dv"]

    def _(te):
        te.wait_ge(c.S["dv"], bt_dv)
    block.tensor(_)
    cst.close()

    gate_st = contextlib.ExitStack()
    lg_ps = c.ps(gate_st, "lg_ps", [128, 8], F32)
    rt_ps = c.ps(gate_st, "rt_ps", [8, 128], F32)
    gsc = c.sb(gate_st, "gate_sc", [128, 64], F32)
    sg = gsc[:, 0:8]
    s2 = gsc[:, 8:16]
    eq1 = gsc[:, 16:24]
    s3 = gsc[:, 24:32]
    eq2 = gsc[:, 32:40]
    route_t = gsc[:, 40:48]
    colv = gsc[:, 48:56]
    m1 = colv[:, 0:1]
    m2 = colv[:, 1:2]
    wsum = colv[:, 2:3]
    rw = colv[:, 3:4]
    w1c = colv[:, 4:5]
    w2c = colv[:, 5:6]

    lg_done = {}
    sg_done = {}
    gm_done = {}
    tr_done = {}
    rt_done = {}
    for tt in range(16):
        def _(te, tt=tt):
            if tt == 0:
                c.psum_guard(te)
            else:
                te.wait_ge(c.S["ac"], sg_done[tt - 1])
                te.wait_ge(c.S["ac"], rt_done.get(tt - 1, 0))
            for fc in range(NF):
                ins = te.matmul(lg_ps[:, :],
                                h1[:, fc * 2048 + tt * 128: fc * 2048 + (tt + 1) * 128],
                                gw_sb[:, fc * 8:(fc + 1) * 8],
                                start=(fc == 0), stop=(fc == NF - 1))
            c.inc(ins, "mm", 1)
        block.tensor(_)
        lg_done[tt] = c.cnt["mm"]

        def _(se, tt=tt):
            se.wait_ge(c.S["mm"], lg_done[tt])
            if tt >= 1:
                se.wait_ge(c.S["dv"], gm_done[tt - 1])
            c.inc(se.activation(sg, lg_ps[:, :], AF.Sigmoid), "ac", 1)
        block.scalar(_)
        sg_done[tt] = c.cnt["ac"]

        def _(ve, tt=tt):
            ve.wait_ge(c.S["ac"], sg_done[tt])
            if tt >= 1:
                ve.wait_ge(c.S["mm"], tr_done[tt - 1])
            ve.tensor_scalar(s2, sg, rs_col[:, 0:1], None, op0=ALU.mult)
            ve.drain()
            ve.reduce_max(m1, s2, axis=AX.X)
            ve.drain()
            ve.tensor_scalar(eq1, s2, m1, None, op0=ALU.is_equal)
            ve.drain()
            ve.tensor_scalar(s3, eq1, -1e30, None, op0=ALU.mult)
            ve.drain()
            ve.tensor_tensor(s3, s3, s2, op=ALU.add)
            ve.drain()
            ve.reduce_max(m2, s3, axis=AX.X)
            ve.drain()
            ve.tensor_scalar(eq2, s3, m2, None, op0=ALU.is_equal)
            ve.tensor_tensor(wsum, m1, m2, op=ALU.add)
            ve.drain()
            ve.reciprocal(rw, wsum)
            ve.drain()
            ve.tensor_tensor(w1c, m1, rw, op=ALU.mult)
            ve.tensor_tensor(w2c, m2, rw, op=ALU.mult)
            ve.drain()
            ve.tensor_scalar(route_t, eq1, w1c, None, op0=ALU.mult)
            ve.drain()
            c.inc(ve.scalar_tensor_tensor(route_t, eq2, w2c, route_t,
                                          op0=ALU.mult, op1=ALU.add), "dv", 1)
        block.vector(_)
        gm_done[tt] = c.cnt["dv"]

        def _(te, tt=tt):
            te.wait_ge(c.S["dv"], gm_done[tt])
            c.inc(te.transpose(rt_ps[:, :], route_t, c.T["ident"][:, :]), "mm", 1)
        block.tensor(_)
        tr_done[tt] = c.cnt["mm"]

        def _(se, tt=tt):
            se.wait_ge(c.S["mm"], tr_done[tt])
            c.inc(se.copy(routeT[0:8, tt * 128:(tt + 1) * 128], rt_ps[0:8, :]), "ac", 1)
        block.scalar(_)
        rt_done[tt] = c.cnt["ac"]

    if "dbg_gsc" in A:
        def _(g):
            g.wait_ge(c.S["ac"], rt_done[15])
            c.inc(g.dma_start(out=A["dbg_gsc"][:, :], in_=gsc[:, :]), "gd", 16)
            g.wait_ge(c.S["gd"], c.cnt["gd"])
        block.gpsimd(_)

    def _(te):
        te.wait_ge(c.S["ac"], rt_done[15])
        if "dbg_gsc" in A:
            te.wait_ge(c.S["gd"], c.cnt["gd"])
    block.tensor(_)
    gate_st.close()

    wbuf_st = contextlib.ExitStack()
    w1_sb = [c.sb(wbuf_st, f"w1_sb{i}", [128, 8 * 512], BF16) for i in range(2)]
    w2_sb = [c.sb(wbuf_st, f"w2_sb{i}", [128, 4 * 1024], BF16) for i in range(2)]
    h_ps = [c.ps(wbuf_st, f"h_ps{i}", [128, 1024], F32) for i in range(2)]
    y_ps = [c.ps(wbuf_st, f"y_ps{i}", [128, 1024], F32) for i in range(2)]

    wld = {}
    mlp1_done = {}
    mlp2_done = {}
    gelu_done = {}
    smul_done = {}
    yadd_done = {}
    gfb_done = {}

    for e in range(9):
        shared = (e == 8)

        def _(g, e=e, shared=shared):
            if e == 0:
                c.fence(g)
            if e >= 2:
                g.wait_ge(c.S["mm"], mlp1_done[e - 2])
                g.wait_ge(c.S["mm"], mlp2_done[e - 2])
            if shared:
                s1, s2_ = A["Ws1"][:, :], A["Ws2"][:, :]
            else:
                s1 = A["W1"][e, :, :]
                s2_ = A["W2"][e, :, :]
            c.inc(g.dma_start(out=w1_sb[e % 2][:, :].rearrange("p (fc h) -> p fc h", fc=8),
                              in_=s1.rearrange("(fc p) h -> p fc h", p=128)), "gd", 16)
            c.inc(g.dma_start(out=w2_sb[e % 2][:, :].rearrange("p (hc n) -> p hc n", hc=4),
                              in_=s2_.rearrange("(hc p) n -> p hc n", p=128)), "gd", 16)
        block.gpsimd(_)
        wld[e] = c.cnt["gd"]

        if not shared:
            def _(te, e=e):
                te.wait_ge(c.S["ac"], c.cnt["ac"])
                if e >= 1:
                    te.wait_ge(c.S["dv"], yadd_done[e - 1])
                for yb in range(2):
                    for q2 in range(2):
                        ins = te.matmul(
                            y_ps[yb][:, q2 * 512:(q2 + 1) * 512],
                            sel8[:, e * 128:(e + 1) * 128],
                            routeT[0:8, yb * 1024 + q2 * 512: yb * 1024 + (q2 + 1) * 512],
                            start=True, stop=True)
                c.inc(ins, "mm", 1)
            block.tensor(_)
            gfb_mm = c.cnt["mm"]

            def _(ve, e=e, gfb_mm=gfb_mm):
                ve.wait_ge(c.S["mm"], gfb_mm)
                ve.tensor_copy(gfb_sb[:, 0:1024], y_ps[0][:, :])
                c.inc(ve.tensor_copy(gfb_sb[:, 1024:2048], y_ps[1][:, :]), "dv", 1)
            block.vector(_)
            gfb_done[e] = c.cnt["dv"]

            if e == 0 and "dbg_gfb" in A:
                def _(g):
                    g.wait_ge(c.S["dv"], gfb_done[0])
                    c.inc(g.dma_start(out=A["dbg_gfb"][:, :], in_=gfb_sb[:, :]), "gd", 16)
                    c.inc(g.dma_start(out=A["dbg_route"][:, :], in_=routeT[:, :]), "gd", 16)
                block.gpsimd(_)

        ac_base4 = c.cnt["ac"]
        gelu_sched = {j: ac_base4 + j + 1 for j in range(8)}
        mlp1_mm = {}
        jlist = [(hc, th) for hc in range(4) for th in range(2)]
        for j, (hc, th) in enumerate(jlist):
            def _(te, e=e, j=j, hc=hc, th=th):
                if j == 0:
                    te.wait_ge(c.S["gd"], wld[e])
                    if not shared:
                        te.wait_ge(c.S["dv"], gfb_done[e])
                if j >= 2:
                    te.wait_ge(c.S["ac"], gelu_sched[j - 2])
                for q2 in range(2):
                    for fc in range(NF):
                        ins = te.matmul(
                            h_ps[j % 2][:, q2 * 512:(q2 + 1) * 512],
                            w1_sb[e % 2][:, fc * 512 + hc * 128: fc * 512 + (hc + 1) * 128],
                            h1[:, fc * 2048 + th * 1024 + q2 * 512:
                               fc * 2048 + th * 1024 + (q2 + 1) * 512],
                            start=(fc == 0), stop=(fc == NF - 1))
                c.inc(ins, "mm", 1)
            block.tensor(_)
            mlp1_done[e] = c.cnt["mm"]
            mlp1_mm[j] = c.cnt["mm"]

            if not shared:
                def _(ve, e=e, j=j, hc=hc, th=th):
                    ve.wait_ge(c.S["ac"], gelu_sched[j])
                    dst = s_sb[:, hc * 2048 + th * 1024: hc * 2048 + (th + 1) * 1024]
                    c.inc(ve.tensor_tensor(
                        dst, ge_tmp[j % 2][:, :],
                        gfb_sb[:, th * 1024:(th + 1) * 1024], op=ALU.mult), "dv", 1)
                block.vector(_)
                smul_done[(e, j)] = c.cnt["dv"]

        def _(se, e=e, shared=shared):
            for j, (hc, th) in enumerate(jlist):
                se.wait_ge(c.S["mm"], mlp1_mm[j])
                bias = (bvec[:, (32 + hc):(33 + hc)] if shared
                        else bvec[:, hc * 8 + e: hc * 8 + e + 1])
                if shared:
                    dst = s_sb[:, hc * 2048 + th * 1024: hc * 2048 + (th + 1) * 1024]
                    c.inc(se.activation(dst, h_ps[j % 2][:, :], AF.Gelu, bias=bias),
                          "ac", 1)
                else:
                    if j >= 2:
                        se.wait_ge(c.S["dv"], smul_done[(e, j - 2)])
                    c.inc(se.activation(ge_tmp[j % 2][:, :], h_ps[j % 2][:, :], AF.Gelu,
                                        bias=bias), "ac", 1)
        block.scalar(_)
        assert c.cnt["ac"] == ac_base4 + 8

        s_ready = c.cnt["dv"] if not shared else None
        s_ready_ac = c.cnt["ac"]

        for j, (fc, th) in enumerate([(fc, th) for fc in range(NF) for th in range(2)]):
            def _(te, e=e, j=j, fc=fc, th=th, shared=shared,
                  s_ready=s_ready, s_ready_ac=s_ready_ac):
                if j == 0:
                    if shared:
                        te.wait_ge(c.S["ac"], s_ready_ac)
                    else:
                        te.wait_ge(c.S["dv"], s_ready)
                if j >= 2:
                    te.wait_ge(c.S["dv"], yadd_done[(e, j - 2)])
                for q2 in range(2):
                    col0 = th * 1024 + q2 * 512
                    for hc in range(4):
                        ins = te.matmul(
                            y_ps[j % 2][:, q2 * 512:(q2 + 1) * 512],
                            w2_sb[e % 2][:, hc * 1024 + fc * 128: hc * 1024 + (fc + 1) * 128],
                            s_sb[:, hc * 2048 + col0: hc * 2048 + col0 + 512],
                            start=(hc == 0), stop=(hc == 3 and not shared))
                    if shared:
                        te.matmul(y_ps[j % 2][:, q2 * 512:(q2 + 1) * 512],
                                  b2_sb[0:8, fc * 128:(fc + 1) * 128],
                                  routeT[0:8, col0: col0 + 512],
                                  start=False, stop=False)
                        ins = te.matmul(y_ps[j % 2][:, q2 * 512:(q2 + 1) * 512],
                                        bs2row[0:1, fc * 128:(fc + 1) * 128],
                                        ones_bfr[0:1, 0:512],
                                        start=False, stop=True)
                c.inc(ins, "mm", 1)
            block.tensor(_)
            mlp2_done[e] = c.cnt["mm"]
            y_mm = c.cnt["mm"]

            def _(ve, e=e, j=j, fc=fc, th=th, y_mm=y_mm):
                ve.wait_ge(c.S["mm"], y_mm)
                xsl = c.T["x_c"][:, fc * 2048 + th * 1024: fc * 2048 + (th + 1) * 1024]
                c.inc(ve.tensor_tensor(xsl, xsl, y_ps[j % 2][:, :], op=ALU.add), "dv", 1)
            block.vector(_)
            yadd_done[(e, j)] = c.cnt["dv"]
        yadd_done[e] = c.cnt["dv"]

    def _(te):
        te.wait_ge(c.S["dv"], c.cnt["dv"])
    block.tensor(_)
    wbuf_st.close()
    st.close()



def p5_proj_l1(c, b, st):
    nc, block = c.nc, c.block
    A = c.A
    Q1, K1, V1 = c.T["Q1_sb"], c.T["K1_sb"], c.T["V1_sb"]
    wq1, wk1, wv1 = c.T["wq1_sb"], c.T["wk1_sb"], c.T["wv1_sb"]
    cosT, sinT, pmat = c.T["cosT"], c.T["sinT"], c.T["pmat"]
    ones_col = c.T["ones_col_bf"]
    ones_row = c.T["ones_row"]

    pst = contextlib.ExitStack()
    h2r = c.sb(pst, f"h2r_{b}", [128, 8 * 1024], BF16)
    proj_ps_st = contextlib.ExitStack()
    q_ps = c.ps(proj_ps_st, f"l1q_ps_{b}", [128, 1024], F32)
    k_ps = c.ps(proj_ps_st, f"l1k_ps_{b}", [128, 1024], F32)
    v_ps = c.ps(proj_ps_st, f"l1v_ps_{b}", [128, 1024], F32)

    ld_done = {}
    ev_done = {}
    for r in range(8):
        def _(g, r=r):
            if r == 0:
                c.fence(g)
                g.wait_ge(c.S["cc"], c.p5_cc)
            else:
                g.wait_ge(c.S["mm"], ld_done[r - 1][1])
            src = A["gath1"][r * 1024:(r + 1) * 1024, b * 1024:(b + 1) * 1024]
            c.inc(g.dma_start(out=h2r[:, :].rearrange("p (fc n) -> p fc n", fc=8),
                              in_=src.rearrange("(fc p) n -> p fc n", p=128)), "gd", 16)
        block.gpsimd(_)
        gd_here = c.cnt["gd"]

        def _(te, r=r, gd_here=gd_here):
            te.wait_ge(c.S["gd"], gd_here)
            if r == 0:
                c.psum_guard(te)
            else:
                te.wait_ge(c.S["ac"], ev_done[r - 1])
            for th in range(2):
                for fc in range(NF):
                    te.matmul(q_ps[:, th * 512:(th + 1) * 512],
                              wq1[:, fc * 128:(fc + 1) * 128],
                              h2r[:, fc * 1024 + th * 512: fc * 1024 + (th + 1) * 512],
                              start=(fc == 0), stop=(fc == NF - 1))
            for th in range(2):
                for fc in range(NF):
                    te.matmul(k_ps[:, th * 512:(th + 1) * 512],
                              wk1[:, fc * 128:(fc + 1) * 128],
                              h2r[:, fc * 1024 + th * 512: fc * 1024 + (th + 1) * 512],
                              start=(fc == 0), stop=(fc == NF - 1))
            for s in range(8):
                for fc in range(NF):
                    ins = te.matmul(v_ps[:, s * 128:(s + 1) * 128],
                                    h2r[:, fc * 1024 + s * 128: fc * 1024 + (s + 1) * 128],
                                    wv1[:, fc * 128:(fc + 1) * 128],
                                    start=(fc == 0), stop=(fc == NF - 1))
            c.inc(ins, "mm", 1)
        block.tensor(_)
        ld_done[r] = (gd_here, c.cnt["mm"])
        mm_here = c.cnt["mm"]

        def _(se, r=r, mm_here=mm_here):
            se.wait_ge(c.S["mm"], mm_here)
            se.copy(Q1[:, r * 1024:(r + 1) * 1024], q_ps[:, :])
            se.copy(K1[:, r * 1024:(r + 1) * 1024], k_ps[:, :])
            c.inc(se.copy(V1[:, r * 1024:(r + 1) * 1024], v_ps[:, :]), "ac", 1)
        block.scalar(_)
        ev_done[r] = c.cnt["ac"]

    proj_ps_st.close()

    rope_ps_st = contextlib.ExitStack()
    rot_ps = [c.ps(rope_ps_st, f"rot_ps{i}_{b}", [128, 512], F32) for i in range(2)]
    rsc = c.sb(pst, f"rope_sc_{b}", [128, 1024], F32)
    rope_mm = {}
    rope_dv = {}
    for ti, (tgt, tile) in enumerate([(t, i) for t in (Q1, K1) for i in range(16)]):
        def _(te, ti=ti, tgt=tgt, tile=tile):
            if ti == 0:
                te.wait_ge(c.S["ac"], ev_done[7])
                c.psum_guard(te)
            if ti >= 2:
                te.wait_ge(c.S["dv"], rope_dv[ti - 2])
            c.inc(te.matmul(rot_ps[ti % 2][:, :], pmat[:, :],
                            tgt[:, tile * 512:(tile + 1) * 512],
                            start=True, stop=True), "mm", 1)
        block.tensor(_)
        rope_mm[ti] = c.cnt["mm"]

        def _(ve, ti=ti, tgt=tgt, tile=tile):
            ve.wait_ge(c.S["mm"], rope_mm[ti])
            sl = slice(tile * 512, (tile + 1) * 512)
            ve.tensor_tensor(rsc[:, 0:512], rot_ps[ti % 2][:, :], sinT[:, sl], op=ALU.mult)
            ve.tensor_tensor(rsc[:, 512:1024], tgt[:, sl], cosT[:, sl], op=ALU.mult)
            ve.drain()
            c.inc(ve.tensor_tensor(tgt[:, sl], rsc[:, 0:512], rsc[:, 512:1024],
                                   op=ALU.add), "dv", 1)
        block.vector(_)
        rope_dv[ti] = c.cnt["dv"]

    rope_ps_st.close()

    bnd_ps_st = contextlib.ExitStack()
    nb_ps = c.ps(bnd_ps_st, f"nb_ps_{b}", [1, 2048], F32)
    bb_ps = c.ps(bnd_ps_st, f"l1bb_ps_{b}", [128, 2], F32)
    sqc = c.sb(pst, f"l1_sqc_{b}", [128, 2048], BF16)
    nrow = c.sb(pst, f"l1_nrow_{b}", [1, 16], F32)
    for si, src in enumerate((Q1, K1)):
        for tc in range(4):
            def _(ve, si=si, tc=tc, src=src):
                if (si, tc) == (0, 0):
                    ve.wait_ge(c.S["dv"], rope_dv[31])
                else:
                    ve.wait_ge(c.S["mm"], c.cnt["mm"])
                c.inc(ve.tensor_tensor(sqc[:, :], src[:, tc * 2048:(tc + 1) * 2048],
                                       src[:, tc * 2048:(tc + 1) * 2048], op=ALU.mult),
                      "dv", 1)
            block.vector(_)
            sq_dv = c.cnt["dv"]

            def _(te, si=si, tc=tc, sq_dv=sq_dv):
                te.wait_ge(c.S["dv"], sq_dv)
                if (si, tc) == (0, 0):
                    c.psum_guard(te)
                for q in range(4):
                    ins = te.matmul(nb_ps[0:1, q * 512:(q + 1) * 512], ones_col[:, 0:1],
                                    sqc[:, q * 512:(q + 1) * 512], start=True, stop=True)
                c.inc(ins, "mm", 1)
            block.tensor(_)
            nb_mm = c.cnt["mm"]

            def _(ve, si=si, tc=tc, nb_mm=nb_mm):
                ve.wait_ge(c.S["mm"], nb_mm)
                c.inc(ve.reduce_max(nrow[0:1, tc:tc + 1], nb_ps[0:1, :], axis=AX.X),
                      "dv", 1)
            block.vector(_)

        def _(ve, si=si):
            ve.drain()
            ve.tensor_tensor(nrow[0:1, 0:2], nrow[0:1, 0:2], nrow[0:1, 2:4], op=ALU.max)
            ve.drain()
            c.inc(ve.tensor_tensor(nrow[0:1, 4 + si:5 + si], nrow[0:1, 0:1],
                                   nrow[0:1, 1:2], op=ALU.max), "dv", 1)
        block.vector(_)

    def _(ve):
        ve.drain()
        c.inc(ve.tensor_tensor(nrow[0:1, 6:7], nrow[0:1, 4:5], nrow[0:1, 5:6],
                               op=ALU.mult), "dv", 1)
    block.vector(_)
    t_dv = c.cnt["dv"]

    def _(se):
        se.wait_ge(c.S["dv"], t_dv)
        c.inc(se.activation(nrow[0:1, 7:8], nrow[0:1, 6:7], AF.Sqrt), "ac", 1)
    block.scalar(_)
    s_ac = c.cnt["ac"]

    def _(ve):
        ve.wait_ge(c.S["ac"], s_ac)
        c.inc(ve.tensor_scalar(nrow[0:1, 8:9], nrow[0:1, 7:8],
                               -1.0 / float(np.sqrt(128.0)), None, op0=ALU.mult), "dv", 1)
    block.vector(_)
    br_dv = c.cnt["dv"]

    def _(te):
        te.wait_ge(c.S["dv"], br_dv)
        c.inc(te.matmul(bb_ps[:, b:b + 1], ones_row[:, :], nrow[0:1, 8:9],
                        start=True, stop=True), "mm", 1)
    block.tensor(_)
    bb_mm = c.cnt["mm"]

    def _(ve):
        ve.wait_ge(c.S["mm"], bb_mm)
        c.inc(ve.tensor_copy(c.T["bias1"][:, b:b + 1], bb_ps[:, b:b + 1]), "dv", 1)
    block.vector(_)

    def _(te):
        te.wait_ge(c.S["dv"], c.cnt["dv"])
    block.tensor(_)
    bnd_ps_st.close()
    pst.close()


def p6_flash_l1(c, b, st):
    nc, block = c.nc, c.block
    A = c.A
    Q1, K1, V1 = c.T["Q1_sb"], c.T["K1_sb"], c.T["V1_sb"]
    masks = c.T["masks_sb"]
    ones_col = c.T["ones_col_bf"]

    fst = contextlib.ExitStack()
    probs = [c.sb(fst, f"l1_pr{i}_{b}", [128, 1024], BF16) for i in range(2)]
    oacc = c.sb(fst, f"l1_oacc_{b}", [128, 1024], F32)
    obuf = c.sb(fst, f"l1_obuf_{b}", [128, 1024], BF16)
    rc = c.sb(fst, f"l1_rc_{b}", [1, 1024], F32)
    S_ps = [c.ps(fst, f"l1_S{i}_{b}", [128, 1024], F32) for i in range(2)]
    oa_ps = c.ps(fst, f"l1_oa_{b}", [128, 1024], F32)
    sum_ps = c.ps(fst, f"l1_sum_{b}", [1, 1024], F32)

    opay_done = 0
    for qh in range(8):
        nkb = 8 * qh + 8
        mm_base = c.cnt["mm"]
        ac_base = c.cnt["ac"]
        dv_base = c.cnt["dv"]
        qk_sched = {kb: mm_base + (1 if kb == 0 else 2 * kb) for kb in range(nkb)}
        exp_sched = {kb: ac_base + kb + 1 for kb in range(nkb)}
        mask_sched = {kb: dv_base + (kb - 8 * qh) + 1 for kb in range(8 * qh, nkb)}

        def _(te, qh=qh, nkb=nkb):
            for kb in range(nkb):
                if qh == 0 and kb == 0:
                    c.fence(te)
                if kb >= 2:
                    te.wait_ge(c.S["ac"], exp_sched[kb - 2])
                elif kb < 2 and qh > 0:
                    te.wait_ge(c.S["dv"], epi_dv)
                for half in range(2):
                    ins = te.matmul(
                        S_ps[kb % 2][:, half * 512:(half + 1) * 512],
                        K1[:, kb * 128:(kb + 1) * 128],
                        Q1[:, qh * 1024 + half * 512: qh * 1024 + (half + 1) * 512],
                        start=True, stop=True)
                c.inc(ins, "mm", 1)
                if kb >= 1:
                    te.wait_ge(c.S["ac"], exp_sched[kb - 1])
                    pkb = kb - 1
                    for half in range(2):
                        te.matmul(oa_ps[:, half * 512:(half + 1) * 512],
                                  V1[:, pkb * 128:(pkb + 1) * 128],
                                  probs[pkb % 2][:, half * 512:(half + 1) * 512],
                                  start=(pkb == 0), stop=False)
                        ins = te.matmul(sum_ps[0:1, half * 512:(half + 1) * 512],
                                        ones_col[:, 0:1],
                                        probs[pkb % 2][:, half * 512:(half + 1) * 512],
                                        start=(pkb == 0), stop=False)
                    c.inc(ins, "mm", 1)
            te.wait_ge(c.S["ac"], exp_sched[nkb - 1])
            pkb = nkb - 1
            for half in range(2):
                te.matmul(oa_ps[:, half * 512:(half + 1) * 512],
                          V1[:, pkb * 128:(pkb + 1) * 128],
                          probs[pkb % 2][:, half * 512:(half + 1) * 512],
                          start=(pkb == 0), stop=True)
                ins = te.matmul(sum_ps[0:1, half * 512:(half + 1) * 512],
                                ones_col[:, 0:1],
                                probs[pkb % 2][:, half * 512:(half + 1) * 512],
                                start=(pkb == 0), stop=True)
            c.inc(ins, "mm", 1)
        block.tensor(_)
        assert c.cnt["mm"] == mm_base + 2 * nkb

        def _(ve, qh=qh, nkb=nkb):
            for kb in range(8 * qh, nkb):
                ve.wait_ge(c.S["mm"], qk_sched[kb])
                j = kb - 8 * qh
                c.inc(ve.tensor_tensor(S_ps[kb % 2][:, :], S_ps[kb % 2][:, :],
                                       masks[:, j * 1024:(j + 1) * 1024],
                                       op=ALU.add), "dv", 1)
        block.vector(_)
        assert c.cnt["dv"] == dv_base + 8

        def _(se, qh=qh, nkb=nkb, b=b):
            for kb in range(nkb):
                diag = kb >= 8 * qh
                if diag:
                    se.wait_ge(c.S["dv"], mask_sched[kb])
                else:
                    se.wait_ge(c.S["mm"], qk_sched[kb])
                c.inc(se.activation(probs[kb % 2][:, :], S_ps[kb % 2][:, :], AF.Exp,
                                    bias=c.T["bias1"][:, b:b + 1],
                                    scale=float(1.0 / np.sqrt(128.0))), "ac", 1)
        block.scalar(_)
        assert c.cnt["ac"] == ac_base + nkb
        pv_mm = c.cnt["mm"]

        def _(ve, pv_mm=pv_mm):
            ve.wait_ge(c.S["mm"], pv_mm)
            ve.tensor_copy(oacc[:, :], oa_ps[:, :])
            c.inc(ve.reciprocal(rc[0:1, :], sum_ps[0:1, :]), "dv", 1)
        block.vector(_)
        rc_dv = c.cnt["dv"]

        def _(te, rc_dv=rc_dv):
            te.wait_ge(c.S["dv"], rc_dv)
            for half in range(2):
                ins = te.matmul(S_ps[0][:, half * 512:(half + 1) * 512],
                                c.T["ones_row"][:, :],
                                rc[0:1, half * 512:(half + 1) * 512],
                                start=True, stop=True)
            c.inc(ins, "mm", 1)
        block.tensor(_)
        rb_mm = c.cnt["mm"]

        def _(ve, rb_mm=rb_mm, opay_done=opay_done):
            ve.wait_ge(c.S["mm"], rb_mm)
            if opay_done:
                ve.wait_ge(c.S["wd"], opay_done)
            c.inc(ve.tensor_tensor(obuf[:, :], S_ps[0][:, :], oacc[:, :],
                                   op=ALU.mult), "dv", 1)
        block.vector(_)
        ob_dv = c.cnt["dv"]
        epi_dv = c.cnt["dv"]

        def _(sy, qh=qh, b=b, ob_dv=ob_dv):
            sy.wait_ge(c.S["dv"], ob_dv)
            c.inc(sy.dma_start(out=A[f"pay2b{b}"][:, qh * 1024:(qh + 1) * 1024],
                               in_=obuf[:, :]), "wd", 16)
        block.sync(_)
        opay_done = c.cnt["wd"]

    def _(te):
        te.wait_ge(c.S["wd"], c.cnt["wd"])
        te.wait_ge(c.S["dv"], c.cnt["dv"])
    block.tensor(_)
    fst.close()


def p7_oproj_b(c, b, o_own, wo_sb, idx_sb):
    nc, block = c.nc, c.block
    A = c.A
    ps_st = contextlib.ExitStack()
    y_ps = [c.ps(ps_st, f"l1y_ps{i}_{b}", [128, 1024], F32) for i in range(2)]

    def _(g, b=b):
        g.wait_ge(c.S["cc"], c.p6_ccb[b])
        g.wait_ge(c.S["mm"], c.cnt["mm"])
        gview = A[f"gath2b{b}"][:, :].rearrange("r (cb n) -> (r cb) n", cb=8)
        for r in range(8):
            c.inc(g.indirect_dma_start(
                out=o_own[:, r * 1024:(r + 1) * 1024],
                out_offset=None,
                in_=gview,
                in_offset=bass.IndirectOffsetOnAxis(
                    ap=idx_sb[:, r: r + 1], axis=0)), "gd", 16)
    block.gpsimd(_)
    ext_gd = c.cnt["gd"]

    yadds = {}
    for j, fc in enumerate(range(NF)):
        def _(te, j=j, b=b, fc=fc):
            if j == 0:
                te.wait_ge(c.S["gd"], ext_gd)
                c.psum_guard(te)
            if j >= 2:
                te.wait_ge(c.S["dv"], yadds[j - 2])
            for q2 in range(2):
                for r in range(8):
                    ins = te.matmul(
                        y_ps[j % 2][:, q2 * 512:(q2 + 1) * 512],
                        wo_sb[:, r * 1024 + fc * 128: r * 1024 + (fc + 1) * 128],
                        o_own[:, r * 1024 + q2 * 512: r * 1024 + (q2 + 1) * 512],
                        start=(r == 0), stop=(r == 7))
            c.inc(ins, "mm", 1)
        block.tensor(_)
        y_mm = c.cnt["mm"]

        def _(ve, j=j, b=b, fc=fc, y_mm=y_mm):
            ve.wait_ge(c.S["mm"], y_mm)
            xsl = c.T["x_c"][:, fc * 2048 + b * 1024: fc * 2048 + (b + 1) * 1024]
            c.inc(ve.tensor_tensor(xsl, xsl, y_ps[j % 2][:, :], op=ALU.add), "dv", 1)
        block.vector(_)
        yadds[j] = c.cnt["dv"]

    def _(te):
        te.wait_ge(c.S["dv"], c.cnt["dv"])
    block.tensor(_)
    ps_st.close()


def p5to7_l1_attn(c):
    nc, block = c.nc, c.block
    A = c.A
    st = contextlib.ExitStack()

    h2_st = contextlib.ExitStack()
    c.T["h2_bf"] = c.sb(h2_st, "h2_bf", [128, 16384], BF16)
    p_layernorm(c, "h2_bf", eps=1e-5, tag="_p5")
    hdv = c.cnt["dv"]

    def _(sy):
        sy.wait_ge(c.S["dv"], hdv)
        for fc in range(NF):
            c.inc(sy.dma_start(out=A["pay1"][fc * 128:(fc + 1) * 128, :],
                               in_=c.T["h2_bf"][:, fc * 2048:(fc + 1) * 2048]), "wd", 16)
    block.sync(_)
    pay_wd = c.cnt["wd"]

    def _(g):
        g.wait_ge(c.S["wd"], pay_wd)
        c.inc(g.collective_compute(
            "AllGather", ALU.bypass, replica_groups=[list(range(8))],
            ins=[A["pay1"][:, :].opt()], outs=[A["gath1"][:, :].opt()]), "cc", 1)
    block.gpsimd(_)
    c.p5_cc = c.cnt["cc"]
    h2_st.close()

    attn_st = contextlib.ExitStack()
    c.T["Q1_sb"] = c.sb(attn_st, "Q1_sb", [128, 8192], BF16)
    c.T["K1_sb"] = c.sb(attn_st, "K1_sb", [128, 8192], BF16)
    c.T["V1_sb"] = c.sb(attn_st, "V1_sb", [128, 8192], BF16)
    c.T["wq1_sb"] = c.sb(attn_st, "wq1_sb", [128, 8 * 128], BF16)
    c.T["wk1_sb"] = c.sb(attn_st, "wk1_sb", [128, 8 * 128], BF16)
    c.T["wv1_sb"] = c.sb(attn_st, "wv1_sb", [128, 8 * 128], BF16)
    c.T["masks_sb"] = c.sb(attn_st, "masks_sb", [128, 8 * 1024], BF16)
    c.T["bias1"] = c.sb(attn_st, "bias1", [128, 4], F32)
    rope_st = contextlib.ExitStack()
    c.T["cosT"] = c.sb(rope_st, "cosT", [128, 8192], BF16)
    c.T["sinT"] = c.sb(rope_st, "sinT", [128, 8192], BF16)
    c.T["pmat"] = c.sb(rope_st, "pmat", [128, 128], BF16)

    def _(g):
        c.fence(g)
        for name, dst in (("wq1o", "wq1_sb"), ("wk1o", "wk1_sb"), ("wv1o", "wv1_sb")):
            c.inc(g.dma_start(out=c.T[dst][:, :].rearrange("p (fc n) -> p fc n", fc=8),
                              in_=A[name][:, :].rearrange("(fc p) n -> p fc n", p=128)),
                  "gd", 16)
        c.inc(g.dma_start(out=c.T["cosT"][:, :], in_=A["cos_in"][:, :]), "gd", 16)
        c.inc(g.dma_start(out=c.T["sinT"][:, :], in_=A["sin_in"][:, :]), "gd", 16)
        c.inc(g.dma_start(out=c.T["pmat"][:, :], in_=A["pmat_in"][:, :]), "gd", 16)
        c.inc(g.dma_start(out=c.T["masks_sb"][:, :], in_=A["masks_in"][:, :]), "gd", 16)
    block.gpsimd(_)

    c.p6_ccb = {}
    p5_proj_l1(c, 0, st)
    p6_flash_l1(c, 0, st)

    def _(g):
        g.wait_ge(c.S["wd"], c.cnt["wd"])
        c.inc(g.collective_compute(
            "AllGather", ALU.bypass, replica_groups=[list(range(8))],
            ins=[A["pay2b0"][:, :].opt()], outs=[A["gath2b0"][:, :].opt()]), "cc", 1)
    block.gpsimd(_)
    c.p6_ccb[0] = c.cnt["cc"]

    p5_proj_l1(c, 1, st)
    rope_st.close()

    o_own = c.sb(attn_st, "o_own", [128, 8 * 1024], BF16)
    wo_sb = c.sb(attn_st, "wo_sb", [128, 8 * 1024], BF16)
    idx_sb = c.sb(attn_st, "idx_sb", [128, 16], U32)

    def _(g):
        c.fence(g)
        c.inc(g.dma_start(out=wo_sb[:, :].rearrange("p (ic n) -> p ic n", ic=8),
                          in_=A["wo1"][:, :].rearrange("(ic p) n -> p ic n", p=128)),
              "gd", 16)
        c.inc(g.dma_start(out=idx_sb[:, :], in_=A["idx_o"][:, :]), "gd", 16)
    block.gpsimd(_)

    p6_flash_l1(c, 1, st)

    def _(g):
        g.wait_ge(c.S["wd"], c.cnt["wd"])
        c.inc(g.collective_compute(
            "AllGather", ALU.bypass, replica_groups=[list(range(8))],
            ins=[A["pay2b1"][:, :].opt()], outs=[A["gath2b1"][:, :].opt()]), "cc", 1)
    block.gpsimd(_)
    c.p6_ccb[1] = c.cnt["cc"]

    p7_oproj_b(c, 0, o_own, wo_sb, idx_sb)
    p7_oproj_b(c, 1, o_own, wo_sb, idx_sb)
    attn_st.close()
    st.close()



def _p8_weight_cast(c):
    A, block = c.A, c.block

    def _(g):
        for e in range(16):
            c.inc(g.dma_start(out=A["m1e_bf"][e, :, :], in_=A["m1w"][e, 0:2048:2, :]),
                  "gd", 16)
            c.inc(g.dma_start(out=A["m2_bf"][e, :, :], in_=A["m2w"][e, :, :]), "gd", 16)
    block.gpsimd(_)
    c.p8_cast_gd = c.cnt["gd"]


def p8_ffn_l1(c):
    nc, block = c.nc, c.block
    A = c.A
    st = contextlib.ExitStack()

    cast_gd = getattr(c, "p8_cast_gd", None)
    if cast_gd is None:
        _p8_weight_cast(c)
        cast_gd = c.p8_cast_gd

    t_bf = c.sb(st, "t_bf", [128, 16384], BF16)
    c.T["t_bf"] = t_bf
    h3_st = contextlib.ExitStack()
    c.T["h3_bf"] = c.sb(h3_st, "h3_bf", [128, 16384], BF16)
    scol_st = contextlib.ExitStack()
    sc_col = c.sb(scol_st, "l1sc_col", [128, NF], F32)
    sc_stage = c.sb(scol_st, "l1sc_stage", [8, 128], F32)
    scps_st = contextlib.ExitStack()
    sc_ps = c.ps(scps_st, "l1sc_ps", [128, 8], F32)

    def _(g):
        c.fence(g)
        c.inc(g.dma_start(out=sc_stage[:, :], in_=A["l1scale"][:, :]), "gd", 16)
    block.gpsimd(_)
    sc_gd = c.cnt["gd"]

    def _(te):
        te.wait_ge(c.S["gd"], sc_gd)
        c.psum_guard(te)
        c.inc(te.transpose(sc_ps[:, :], sc_stage[:, :], c.T["ident"][0:8, 0:8]), "mm", 1)
    block.tensor(_)
    sc_mm = c.cnt["mm"]

    def _(ve):
        ve.wait_ge(c.S["mm"], sc_mm)
        c.inc(ve.tensor_copy(sc_col[:, :], sc_ps[:, :]), "dv", 1)
    block.vector(_)

    def _(te):
        te.wait_ge(c.S["dv"], c.cnt["dv"])
    block.tensor(_)
    scps_st.close()

    p_layernorm(c, "h3_bf", eps=1e-5, add_to_stream=True, tag="_p8a")
    p_layernorm(c, "t_bf", eps=1e-5, rms=True, src_bf=c.T["h3_bf"],
                affine=(sc_col,), tag="_p8b")
    scol_st.close()
    h3_st.close()

    cst = contextlib.ExitStack()
    gwT = c.sb(st, "gwT", [128, 8 * 16], BF16)
    gb_bf = c.sb(st, "gb_bf", [1, 16], BF16)
    onesr_bf = c.sb(st, "onesr_bf", [1, 128], BF16)
    b1e_t = c.sb(st, "b1e_t", [128, 8 * 16], F32)
    m2b_sb = c.sb(st, "m2b_sb", [16, 1024], BF16)
    routeT1 = c.sb(st, "routeT1", [16, 2048], BF16)
    sel16_sb = c.sb(st, "sel16_sb", [16, 16 * 128], BF16)
    gfb8 = c.sb(st, "gfb8_sb", [128, 2048], BF16)

    gw_stage = c.sb(cst, "gw_stage", [16, 2048], F32)
    b_stage = c.sb(cst, "l1b_stage", [16, 2048], F32)
    bev = c.sb(cst, "l1_bev", [16, 1024], F32)
    gt_ps = c.ps(cst, "l1gt_ps", [128, 16], F32)

    def _(g):
        c.fence(g)
        c.inc(g.dma_start(out=gw_stage[0:16, 0:1024], in_=A["gw1"][:, :]), "gd", 16)
        c.inc(g.dma_start(out=gb_bf[:, :], in_=A["gb1"][:, :]), "gd", 16)
        c.inc(g.dma_start(out=b_stage[:, :], in_=A["m1b"][:, :]), "gd", 16)
        c.inc(g.dma_start(out=m2b_sb[:, :], in_=A["m2b"][:, :]), "gd", 16)
        c.inc(g.dma_start(out=sel16_sb[:, :], in_=A["sel16"][:, :]), "gd", 16)
        c.inc(g.memset(onesr_bf[:, :], 1.0), "pl", 1)
    block.gpsimd(_)
    cst_gd = c.cnt["gd"]
    cst_pl = c.cnt["pl"]

    def _(ve):
        ve.wait_ge(c.S["gd"], cst_gd)
        c.inc(ve.tensor_copy(bev[:, :],
                             b_stage[:, :].rearrange("p (i two) -> p i two", two=2)[:, :, 0]),
              "dv", 1)
    block.vector(_)
    bev_dv = c.cnt["dv"]

    jobs = [("gw", fc) for fc in range(8)] + [("b1", ic) for ic in range(8)]
    tcopy_dv = {}
    for ji, (kind, idx) in enumerate(jobs):
        def _(te, ji=ji, kind=kind, idx=idx):
            if ji == 0:
                te.wait_ge(c.S["dv"], bev_dv)
                c.psum_guard(te)
            else:
                te.wait_ge(c.S["dv"], tcopy_dv[ji - 1])
            src = (gw_stage[0:16, idx * 128:(idx + 1) * 128] if kind == "gw"
                   else bev[0:16, idx * 128:(idx + 1) * 128])
            c.inc(te.transpose(gt_ps[:, :], src, c.T["ident"][0:16, 0:16]), "mm", 1)
        block.tensor(_)
        mm_h = c.cnt["mm"]

        def _(ve, ji=ji, kind=kind, idx=idx, mm_h=mm_h):
            ve.wait_ge(c.S["mm"], mm_h)
            dst = (gwT[:, idx * 16:(idx + 1) * 16] if kind == "gw"
                   else b1e_t[:, idx * 16:(idx + 1) * 16])
            c.inc(ve.tensor_copy(dst, gt_ps[:, :]), "dv", 1)
        block.vector(_)
        tcopy_dv[ji] = c.cnt["dv"]

    gate_st = contextlib.ExitStack()
    lg_ps = c.ps(gate_st, "lg1_ps", [128, 16], F32)
    rt_ps = c.ps(gate_st, "rt1_ps", [16, 128], F32)
    gsc = c.sb(gate_st, "l1gate_sc", [128, 160], F32)
    cur = gsc[:, 0:16]
    eqs = [gsc[:, 16 + i * 16: 32 + i * 16] for i in range(4)]
    route_t = gsc[:, 80:96]
    tmp16 = gsc[:, 96:112]
    mcol = [gsc[:, 112 + i: 113 + i] for i in range(4)]
    ecol = [gsc[:, 116 + i: 117 + i] for i in range(4)]
    nm1 = gsc[:, 120:121]
    ssum = gsc[:, 121:122]
    rw = gsc[:, 122:123]
    wcol = [gsc[:, 123 + i: 124 + i] for i in range(4)]

    lg_done = {}
    gm_done = {}
    tr_done = {}
    rt_done = {}
    for tt in range(16):
        def _(te, tt=tt):
            if tt == 0:
                c.psum_guard(te)
                te.wait_ge(c.S["pl"], cst_pl)
            else:
                te.wait_ge(c.S["dv"], gm_done[tt - 1])
                te.wait_ge(c.S["ac"], rt_done.get(tt - 1, 0))
            for fc in range(NF):
                te.matmul(lg_ps[:, :],
                          t_bf[:, fc * 2048 + tt * 128: fc * 2048 + (tt + 1) * 128],
                          gwT[:, fc * 16:(fc + 1) * 16],
                          start=(fc == 0), stop=False)
            c.inc(te.matmul(lg_ps[:, :], onesr_bf[:, :], gb_bf[:, :],
                            start=False, stop=True), "mm", 1)
        block.tensor(_)
        lg_done[tt] = c.cnt["mm"]

        def _(ve, tt=tt):
            ve.wait_ge(c.S["mm"], lg_done[tt])
            ve.tensor_copy(cur[:, :], lg_ps[:, :])
            ve.drain()
            for i in range(4):
                ve.reduce_max(mcol[i], cur[:, :], axis=AX.X)
                ve.drain()
                ve.tensor_scalar(eqs[i], cur[:, :], mcol[i], None, op0=ALU.is_equal)
                ve.drain()
                if i < 3:
                    ve.tensor_scalar(tmp16, eqs[i], -1e30, None, op0=ALU.mult)
                    ve.drain()
                    ve.tensor_tensor(cur[:, :], cur[:, :], tmp16, op=ALU.add)
                    ve.drain()
            c.inc(ve.tensor_scalar(nm1, mcol[0], -1.0, None, op0=ALU.mult), "dv", 1)
        block.vector(_)
        topk_dv = c.cnt["dv"]

        def _(se, tt=tt, topk_dv=topk_dv):
            se.wait_ge(c.S["dv"], topk_dv)
            for i in range(4):
                ins = se.activation(ecol[i], mcol[i], AF.Exp, bias=nm1)
            c.inc(ins, "ac", 1)
        block.scalar(_)
        exp_ac = c.cnt["ac"]

        def _(ve, tt=tt, exp_ac=exp_ac):
            ve.wait_ge(c.S["ac"], exp_ac)
            ve.tensor_tensor(ssum, ecol[0], ecol[1], op=ALU.add)
            ve.drain()
            ve.tensor_tensor(ssum, ssum, ecol[2], op=ALU.add)
            ve.drain()
            ve.tensor_tensor(ssum, ssum, ecol[3], op=ALU.add)
            ve.drain()
            ve.reciprocal(rw, ssum)
            ve.drain()
            for i in range(4):
                ve.tensor_tensor(wcol[i], ecol[i], rw, op=ALU.mult)
            ve.drain()
            ve.tensor_scalar(route_t, eqs[0], wcol[0], None, op0=ALU.mult)
            ve.drain()
            for i in range(1, 4):
                ve.scalar_tensor_tensor(route_t, eqs[i], wcol[i], route_t,
                                        op0=ALU.mult, op1=ALU.add)
                ve.drain()
            c.inc(ve.tensor_copy(tmp16, route_t), "dv", 1)
        block.vector(_)
        gm_done[tt] = c.cnt["dv"]

        def _(te, tt=tt):
            te.wait_ge(c.S["dv"], gm_done[tt])
            c.inc(te.transpose(rt_ps[:, :], route_t, c.T["ident"][:, :]), "mm", 1)
        block.tensor(_)
        tr_done[tt] = c.cnt["mm"]

        def _(se, tt=tt):
            se.wait_ge(c.S["mm"], tr_done[tt])
            c.inc(se.copy(routeT1[0:16, tt * 128:(tt + 1) * 128], rt_ps[0:16, :]), "ac", 1)
        block.scalar(_)
        rt_done[tt] = c.cnt["ac"]

    def _(te):
        te.wait_ge(c.S["ac"], rt_done[15])
    block.tensor(_)
    gate_st.close()
    cst.close()

    wbuf_st = contextlib.ExitStack()
    w1T = [c.sb(wbuf_st, f"w1T{i}", [128, 8 * 1024], BF16) for i in range(2)]
    w2T = c.sb(wbuf_st, "w2T", [128, 8 * 1024], BF16)
    s1_sb = c.sb(wbuf_st, "s1_sb", [128, 8 * 2048], BF16)
    lim_sc = [c.sb(wbuf_st, f"lim_sc{i}", [128, 1024], BF16) for i in range(2)]
    sig_sc = [c.sb(wbuf_st, f"sig_sc{i}", [128, 1024], BF16) for i in range(2)]
    h_ps = [c.ps(wbuf_st, f"h1_ps{i}", [128, 1024], F32) for i in range(2)]
    y_ps = [c.ps(wbuf_st, f"y1_ps{i}", [128, 1024], F32) for i in range(2)]

    w1ld = {}
    w2ld = {}
    mlp1_done = {}
    mlp2_done = {}
    gelu_done = {}
    smul_done = {}
    yadd_done = {}
    gfb_done = {}

    for e in range(16):
        def _(sy, e=e):
            if e == 0:
                c.fence(sy)
                sy.wait_ge(c.S["gd"], cast_gd)
            if e >= 2:
                sy.wait_ge(c.S["mm"], mlp1_done[e - 2])
            for fc in range(NF):
                c.inc(sy.dma_start(out=w1T[e % 2][:, fc * 1024:(fc + 1) * 1024],
                                   in_=A["m1e_bf"][e, :, fc * 128:(fc + 1) * 128],
                                   transpose=True), "wd", 16)
        block.sync(_)
        w1ld[e] = c.cnt["wd"]

        def _(te, e=e):
            if e == 0:
                te.wait_ge(c.S["ac"], rt_done[15])
                c.psum_guard(te)
            else:
                te.wait_ge(c.S["dv"], yadd_done[e - 1])
            for yb in range(2):
                for q2 in range(2):
                    ins = te.matmul(
                        y_ps[yb][:, q2 * 512:(q2 + 1) * 512],
                        sel16_sb[:, e * 128:(e + 1) * 128],
                        routeT1[0:16, yb * 1024 + q2 * 512: yb * 1024 + (q2 + 1) * 512],
                        start=True, stop=True)
            c.inc(ins, "mm", 1)
        block.tensor(_)
        gfb_mm = c.cnt["mm"]

        def _(ve, e=e, gfb_mm=gfb_mm):
            ve.wait_ge(c.S["mm"], gfb_mm)
            ve.tensor_copy(gfb8[:, 0:1024], y_ps[0][:, :])
            c.inc(ve.tensor_copy(gfb8[:, 1024:2048], y_ps[1][:, :]), "dv", 1)
        block.vector(_)
        gfb_done[e] = c.cnt["dv"]

        ac_base8 = c.cnt["ac"]
        sig_sched = {j: ac_base8 + j + 1 for j in range(16)}
        lim_dvs = {}
        h_mms = {}
        jlist8 = [(ic, th) for ic in range(8) for th in range(2)]
        for j, (ic, th) in enumerate(jlist8):
            def _(te, e=e, j=j, ic=ic, th=th):
                if j == 0:
                    te.wait_ge(c.S["wd"], w1ld[e])
                    te.wait_ge(c.S["dv"], gfb_done[e])
                if j >= 2:
                    te.wait_ge(c.S["dv"], smul_done[(e, j - 2)])
                for q2 in range(2):
                    for fc in range(NF):
                        ins = te.matmul(
                            h_ps[j % 2][:, q2 * 512:(q2 + 1) * 512],
                            w1T[e % 2][:, fc * 1024 + ic * 128: fc * 1024 + (ic + 1) * 128],
                            t_bf[:, fc * 2048 + th * 1024 + q2 * 512:
                                 fc * 2048 + th * 1024 + (q2 + 1) * 512],
                            start=(fc == 0), stop=(fc == NF - 1))
                c.inc(ins, "mm", 1)
            block.tensor(_)
            mlp1_done[e] = c.cnt["mm"]
            h_mms[j] = c.cnt["mm"]

            def _(ve, e=e, j=j, ic=ic, th=th):
                ve.wait_ge(c.S["mm"], h_mms[j])
                if j >= 2:
                    ve.wait_ge(c.S["ac"], sig_sched[j - 2])
                c.inc(ve.tensor_scalar(lim_sc[j % 2][:, :], h_ps[j % 2][:, :],
                                       b1e_t[:, ic * 16 + e: ic * 16 + e + 1], 7.0,
                                       op0=ALU.add, op1=ALU.min), "dv", 1)
                ve.wait_ge(c.S["ac"], sig_sched[j])
                dst = s1_sb[:, ic * 2048 + th * 1024: ic * 2048 + (th + 1) * 1024]
                ve.tensor_tensor(dst, lim_sc[j % 2][:, :],
                                 gfb8[:, th * 1024:(th + 1) * 1024], op=ALU.mult)
                ve.drain()
                c.inc(ve.tensor_tensor(dst, dst, sig_sc[j % 2][:, :], op=ALU.mult),
                      "dv", 1)
            block.vector(_)
            lim_dvs[j] = c.cnt["dv"] - 1
            smul_done[(e, j)] = c.cnt["dv"]

        def _(se, e=e):
            for j in range(16):
                se.wait_ge(c.S["dv"], lim_dvs[j])
                if j >= 2:
                    se.wait_ge(c.S["dv"], smul_done[(e, j - 2)])
                c.inc(se.activation(sig_sc[j % 2][:, :], lim_sc[j % 2][:, :],
                                    AF.Sigmoid, scale=1.702), "ac", 1)
        block.scalar(_)
        assert c.cnt["ac"] == ac_base8 + 16

        s_ready = c.cnt["dv"]

        def _(sy, e=e):
            if e >= 1:
                sy.wait_ge(c.S["mm"], mlp2_done[e - 1])
            for ic in range(NF):
                c.inc(sy.dma_start(out=w2T[:, ic * 1024:(ic + 1) * 1024],
                                   in_=A["m2_bf"][e, :, ic * 128:(ic + 1) * 128],
                                   transpose=True), "wd", 16)
        block.sync(_)
        w2ld[e] = c.cnt["wd"]

        for j, (fc, th) in enumerate([(fc, th) for fc in range(NF) for th in range(2)]):
            def _(te, e=e, j=j, fc=fc, th=th, s_ready=s_ready):
                if j == 0:
                    te.wait_ge(c.S["dv"], s_ready)
                    te.wait_ge(c.S["wd"], w2ld[e])
                if j >= 2:
                    te.wait_ge(c.S["dv"], yadd_done[(e, j - 2)])
                for q2 in range(2):
                    col0 = th * 1024 + q2 * 512
                    for ic in range(8):
                        ins = te.matmul(
                            y_ps[j % 2][:, q2 * 512:(q2 + 1) * 512],
                            w2T[:, ic * 1024 + fc * 128: ic * 1024 + (fc + 1) * 128],
                            s1_sb[:, ic * 2048 + col0: ic * 2048 + col0 + 512],
                            start=(ic == 0), stop=(ic == 7 and e != 0))
                    if e == 0:
                        ins = te.matmul(y_ps[j % 2][:, q2 * 512:(q2 + 1) * 512],
                                        m2b_sb[0:16, fc * 128:(fc + 1) * 128],
                                        routeT1[0:16, col0: col0 + 512],
                                        start=False, stop=True)
                c.inc(ins, "mm", 1)
            block.tensor(_)
            mlp2_done[e] = c.cnt["mm"]
            y_mm = c.cnt["mm"]

            def _(ve, e=e, j=j, fc=fc, th=th, y_mm=y_mm):
                ve.wait_ge(c.S["mm"], y_mm)
                xsl = c.T["x_c"][:, fc * 2048 + th * 1024: fc * 2048 + (th + 1) * 1024]
                c.inc(ve.tensor_tensor(xsl, xsl, y_ps[j % 2][:, :], op=ALU.add), "dv", 1)
            block.vector(_)
            yadd_done[(e, j)] = c.cnt["dv"]
        yadd_done[e] = c.cnt["dv"]

    def _(te):
        te.wait_ge(c.S["dv"], c.cnt["dv"])
    block.tensor(_)
    wbuf_st.close()
    st.close()



def build_launch(phases, dbg=()):
    nc = bass.Bass("TRN2", target_bir_lowering=False)
    root = contextlib.ExitStack()
    A = {}

    A["x_in"] = nc.dram_tensor("x_in", [TOWN, D], F32, kind="ExternalInput").ap()
    A["ident_in"] = nc.dram_tensor("ident_in", [128, 128], F32, kind="ExternalInput").ap()
    A["y_out"] = nc.dram_tensor("y_out", [TOWN, D], F32, kind="ExternalOutput").ap()
    if "p1" in phases:
        A["ln_g"] = nc.dram_tensor("ln_g", [8, 128], F32, kind="ExternalInput").ap()
        A["ln_b"] = nc.dram_tensor("ln_b", [8, 128], F32, kind="ExternalInput").ap()
    if "p2" in phases:
        A["wq"] = nc.dram_tensor("wq", [D, D], F32, kind="ExternalInput").ap()
        A["wk"] = nc.dram_tensor("wk", [D, D], F32, kind="ExternalInput").ap()
        A["wv"] = nc.dram_tensor("wv", [D, D], F32, kind="ExternalInput").ap()
        A["pay0m"] = nc.dram_tensor("pay0m", [1, 32], BF16, kind="Internal").ap()
        A["gath0m"] = nc.dram_tensor("gath0m", [8, 32], BF16, kind="Internal",
                                     addr_space="Shared").ap()
        for g in range(2):
            A[f"pay0k{g}"] = nc.dram_tensor(f"pay0k{g}", [512, 2048], BF16,
                                            kind="Internal").ap()
            A[f"gath0k{g}"] = nc.dram_tensor(f"gath0k{g}", [8 * 512, 2048], BF16,
                                             kind="Internal", addr_space="Shared").ap()
            A[f"pay0v{g}"] = nc.dram_tensor(f"pay0v{g}", [2048, 512], BF16,
                                            kind="Internal").ap()
            A[f"gath0v{g}"] = nc.dram_tensor(f"gath0v{g}", [8 * 2048, 512], BF16,
                                             kind="Internal", addr_space="Shared").ap()
    if "p4" in phases:
        A["gate_W"] = nc.dram_tensor("gate_W", [D, 8], F32, kind="ExternalInput").ap()
        A["W1"] = nc.dram_tensor("W1", [8, D, 512], F32, kind="ExternalInput").ap()
        A["b1"] = nc.dram_tensor("b1", [8, 512], F32, kind="ExternalInput").ap()
        A["W2"] = nc.dram_tensor("W2", [8, 512, D], F32, kind="ExternalInput").ap()
        A["b2"] = nc.dram_tensor("b2", [8, D], F32, kind="ExternalInput").ap()
        A["Ws1"] = nc.dram_tensor("Ws1", [D, 512], F32, kind="ExternalInput").ap()
        A["bs1"] = nc.dram_tensor("bs1", [1, 512], F32, kind="ExternalInput").ap()
        A["Ws2"] = nc.dram_tensor("Ws2", [512, D], F32, kind="ExternalInput").ap()
        A["bs2"] = nc.dram_tensor("bs2", [1, D], F32, kind="ExternalInput").ap()
        A["rs"] = nc.dram_tensor("rs", [1, 1], F32, kind="ExternalInput").ap()
        A["sel8"] = nc.dram_tensor("sel8", [8, 1024], BF16, kind="ExternalInput").ap()
    if "p5" in phases:
        A["wq1o"] = nc.dram_tensor("wq1o", [D, 128], F32, kind="ExternalInput").ap()
        A["wk1o"] = nc.dram_tensor("wk1o", [D, 128], F32, kind="ExternalInput").ap()
        A["wv1o"] = nc.dram_tensor("wv1o", [D, 128], F32, kind="ExternalInput").ap()
        A["wo1"] = nc.dram_tensor("wo1", [D, D], F32, kind="ExternalInput").ap()
        A["cos_in"] = nc.dram_tensor("cos_in", [128, 8192], BF16, kind="ExternalInput").ap()
        A["sin_in"] = nc.dram_tensor("sin_in", [128, 8192], BF16, kind="ExternalInput").ap()
        A["pmat_in"] = nc.dram_tensor("pmat_in", [128, 128], BF16, kind="ExternalInput").ap()
        A["masks_in"] = nc.dram_tensor("masks_in", [128, 8192], BF16, kind="ExternalInput").ap()
        A["idx_o"] = nc.dram_tensor("idx_o", [128, 16], U32, kind="ExternalInput").ap()
        A["pay1"] = nc.dram_tensor("pay1", [1024, 2048], BF16, kind="Internal").ap()
        A["gath1"] = nc.dram_tensor("gath1", [8192, 2048], BF16, kind="Internal",
                                    addr_space="Shared").ap()
        for bb in range(2):
            A[f"pay2b{bb}"] = nc.dram_tensor(f"pay2b{bb}", [128, 8192], BF16,
                                             kind="Internal").ap()
            A[f"gath2b{bb}"] = nc.dram_tensor(f"gath2b{bb}", [1024, 8192], BF16,
                                              kind="Internal", addr_space="Shared").ap()
    if "p8" in phases:
        A["l1scale"] = nc.dram_tensor("l1scale", [8, 128], F32, kind="ExternalInput").ap()
        A["gw1"] = nc.dram_tensor("gw1", [16, D], F32, kind="ExternalInput").ap()
        A["gb1"] = nc.dram_tensor("gb1", [1, 16], F32, kind="ExternalInput").ap()
        A["m1w"] = nc.dram_tensor("m1w", [16, 2048, D], F32, kind="ExternalInput").ap()
        A["m1b"] = nc.dram_tensor("m1b", [16, 2048], F32, kind="ExternalInput").ap()
        A["m2w"] = nc.dram_tensor("m2w", [16, D, D], F32, kind="ExternalInput").ap()
        A["m2b"] = nc.dram_tensor("m2b", [16, D], F32, kind="ExternalInput").ap()
        A["sel16"] = nc.dram_tensor("sel16", [16, 2048], BF16, kind="ExternalInput").ap()
        A["m1e_bf"] = nc.dram_tensor("m1e_bf", [16, 1024, D], BF16, kind="Internal").ap()
        A["m2_bf"] = nc.dram_tensor("m2_bf", [16, D, D], BF16, kind="Internal").ap()
    for d in dbg:
        if d == "xn":
            A["dbg_xn"] = nc.dram_tensor("dbg_xn", [128, 16384], F32, kind="ExternalOutput").ap()
        if d == "gate":
            A["dbg_route"] = nc.dram_tensor("dbg_route", [8, 2048], F32, kind="ExternalOutput").ap()
            A["dbg_gfb"] = nc.dram_tensor("dbg_gfb", [128, 2048], F32, kind="ExternalOutput").ap()
            A["dbg_gsc"] = nc.dram_tensor("dbg_gsc", [128, 64], F32, kind="ExternalOutput").ap()
        if d == "rows":
            A["dbg_rows"] = nc.dram_tensor("dbg_rows", [1, 8192], F32, kind="ExternalOutput").ap()
            A["dbg_gb"] = nc.dram_tensor("dbg_gb", [128, 48], F32, kind="ExternalOutput").ap()

    with root:
        block = root.enter_context(nc.Block())
        c = Ctx(nc, block, root)
        c.A = A

        c.T["x_c"] = c.sb(root, "x_c", [128, 16384], F32)
        c.T["ident"] = c.sb(root, "ident", [128, 128], F32)
        c.T["ident_bf"] = c.sb(root, "ident_bf", [128, 128], BF16)
        c.T["ones_row"] = c.sb(root, "ones_row", [1, 128], F32)
        c.T["ones_col_bf"] = c.sb(root, "ones_col_bf", [128, 1], BF16)

        def _(sy):
            c.inc(sy.dma_start(out=c.T["ident"][:, :], in_=A["ident_in"][:, :]), "wd", 16)
        block.sync(_)
        id_wd = c.cnt["wd"]

        def _(g):
            g.memset(c.T["ones_row"][:, :], 1.0)
            c.inc(g.memset(c.T["ones_col_bf"][:, :], 1.0), "pl", 1)
        block.gpsimd(_)

        def _(ve):
            ve.wait_ge(c.S["wd"], id_wd)
            c.inc(ve.tensor_copy(c.T["ident_bf"][:, :], c.T["ident"][:, :]), "dv", 1)
        block.vector(_)

        def _(te):
            te.wait_ge(c.S["pl"], c.cnt["pl"])
        block.tensor(_)

        if "p1" in phases:
            gb_st = contextlib.ExitStack()
            g_col = c.sb(root, "g_col", [128, NF], F32)
            b_col = c.sb(root, "b_col", [128, NF], F32)
            gb_sb = c.sb(gb_st, "gb_stage", [8, 256], F32)
            gb_ps = c.ps(gb_st, "gb_ps", [128, 16], F32)

            def _(sy):
                c.fence(sy)
                c.inc(sy.dma_start(out=gb_sb[0:8, 0:128], in_=A["ln_g"][:, :]), "wd", 16)
                c.inc(sy.dma_start(out=gb_sb[0:8, 128:256], in_=A["ln_b"][:, :]), "wd", 16)
            block.sync(_)
            wd_here = c.cnt["wd"]

            def _(te):
                te.wait_ge(c.S["wd"], wd_here)
                c.psum_guard(te)
                te.transpose(gb_ps[:, 0:8], gb_sb[0:8, 0:128], c.T["ident"][0:8, 0:8])
                c.inc(te.transpose(gb_ps[:, 8:16], gb_sb[0:8, 128:256],
                                   c.T["ident"][0:8, 0:8]), "mm", 1)
            block.tensor(_)
            mm_here = c.cnt["mm"]

            def _(ve):
                ve.wait_ge(c.S["mm"], mm_here)
                ve.tensor_copy(g_col[:, :], gb_ps[:, 0:8])
                c.inc(ve.tensor_copy(b_col[:, :], gb_ps[:, 8:16]), "dv", 1)
            block.vector(_)
            if "dbg_gb" in A:
                def _(sy):
                    sy.wait_ge(c.S["dv"], c.cnt["dv"])
                    c.inc(sy.dma_start(out=A["dbg_gb"][0:8, 16:48],
                                       in_=gb_sb[0:8, 0:32]), "wd", 16)
                block.sync(_)
            gb_st.close()

        _p_load(c)

        if "p1" in phases:
            if "dbg_gb" in A:
                def _(sy):
                    sy.wait_ge(c.S["dv"], c.cnt["dv"])
                    c.inc(sy.dma_start(out=A["dbg_gb"][:, 0:8], in_=g_col[:, :]), "wd", 16)
                    c.inc(sy.dma_start(out=A["dbg_gb"][:, 8:16], in_=b_col[:, :]), "wd", 16)
                block.sync(_)

            if "p2" in phases:
                p23_st = contextlib.ExitStack()
                c.T["Q_sb"] = c.sb(p23_st, "Q_sb", [128, 16 * 2048], BF16)
                c.T["bias_cols"] = c.sb(p23_st, "bias_cols", [128, 32], F32)
            xn_st = contextlib.ExitStack()
            c.T["xn_bf"] = c.sb(xn_st, "xn_bf", [128, 16384], BF16)
            p_layernorm(c, "xn_bf", eps=1e-5, fold_eps2=1e-6,
                        affine=(g_col, b_col), add_to_stream=True,
                        dbg_rows_ap=A.get("dbg_rows"), tag="_p1")

            if "dbg_xn" in A:
                st2 = contextlib.ExitStack()
                xf = c.sb(st2, "dbg_xf", [128, 2048], F32)
                for fc in range(NF):
                    def _(ve, fc=fc):
                        ve.wait_ge(c.S["wd"], c.cnt["wd"])
                        c.inc(ve.tensor_copy(xf[:, :],
                                             c.T["xn_bf"][:, fc * 2048:(fc + 1) * 2048]),
                              "dv", 1)
                    block.vector(_)
                    dv_here = c.cnt["dv"]

                    def _(sy, fc=fc, dv_here=dv_here):
                        sy.wait_ge(c.S["dv"], dv_here)
                        c.inc(sy.dma_start(out=A["dbg_xn"][:, fc * 2048:(fc + 1) * 2048],
                                           in_=xf[:, :]), "wd", 16)
                    block.sync(_)
                st2.close()

        if "p2" in phases:
            p2_qkv_l0(c)
            xn_st.close()
            if "p8" in phases:
                _p8_weight_cast(c)
            if "p3" in phases:
                p3_flash_l0(c)
            p23_st.close()

        if "p4" in phases:
            p4_ffn_l0(c)

        if "p5" in phases:
            p5to7_l1_attn(c)

        if "p8" in phases:
            p8_ffn_l1(c)

        _p_store(c)
    return nc



def _consts():
    return {"ident_in": np.eye(128, dtype=np.float32)}


def _dup_wq(wq):
    return np.ascontiguousarray(np.concatenate(
        [np.concatenate([wq[:, h * 64:(h + 1) * 64]] * 2, axis=1) for h in range(16)],
        axis=1))


def _sel8():
    s = np.zeros((8, 1024), dtype=BF)
    for e in range(8):
        s[e, e * 128:(e + 1) * 128] = 1
    return s


def _rope_tables():
    t = np.arange(8192, dtype=np.float32)
    invf = (1.0 / 10000.0 ** (np.arange(0, 128, 2, dtype=np.float32) / 128.0)).astype(np.float32)
    freqs = np.outer(t, invf).astype(np.float32)
    emb = np.concatenate([freqs, freqs], axis=-1)
    return (np.ascontiguousarray(np.cos(emb).T.astype(BF)),
            np.ascontiguousarray(np.sin(emb).T.astype(BF)))


def _pmat():
    p = np.zeros((128, 128), dtype=BF)
    for d in range(64):
        p[d + 64, d] = -1.0
        p[d, d + 64] = 1.0
    return p


def _masks():
    m = np.zeros((128, 8192), dtype=np.float32)
    for j in range(8):
        for kk in range(128):
            qq = np.arange(1024)
            m[kk, j * 1024:(j + 1) * 1024] = np.where(j * 128 + kk > qq, -1e30, 0.0)
    return m.astype(BF)


def _idx_o(cid):
    idx = np.zeros((128, 16), dtype=np.uint32)
    for r in range(8):
        for p in range(128):
            idx[p, r] = (r * 128 + p) * 8 + cid
    return idx


def _sel16():
    s = np.zeros((16, 2048), dtype=BF)
    for e in range(16):
        s[e, e * 128:(e + 1) * 128] = 8.0
    return s


def _launch_d_maps(inputs, x3s):
    consts = _consts()
    f32 = lambda k: np.ascontiguousarray(np.asarray(inputs[k], np.float32))
    maps = []
    for cid in range(NCORES):
        m = {"x_in": x3s[cid],
             "l1scale": f32("L1_ffn_scale").reshape(8, 128),
             "gw1": f32("L1_ffn_gate_w"), "gb1": f32("L1_ffn_gate_b").reshape(1, 16),
             "m1w": f32("L1_ffn_mlp1_w"), "m1b": f32("L1_ffn_mlp1_b"),
             "m2w": f32("L1_ffn_mlp2_w"), "m2b": f32("L1_ffn_mlp2_b"),
             "sel16": _sel16()}
        m.update(consts)
        maps.append(m)
    return maps


def _launch_c_maps(inputs, x2s):
    consts = _consts()
    cos_t, sin_t = _rope_tables()
    f32 = lambda k: np.ascontiguousarray(np.asarray(inputs[k], np.float32))
    Wq = f32("L1_attn_Wq")
    Wk = f32("L1_attn_Wk")
    Wv = f32("L1_attn_Wv")
    maps = []
    for cid in range(NCORES):
        m = {"x_in": x2s[cid],
             "wq1o": np.ascontiguousarray(Wq[:, 128 * cid:128 * (cid + 1)]),
             "wk1o": np.ascontiguousarray(Wk[:, 128 * (cid // 2):128 * (cid // 2 + 1)]),
             "wv1o": np.ascontiguousarray(Wv[:, 128 * (cid // 2):128 * (cid // 2 + 1)]),
             "wo1": f32("L1_attn_Wo"),
             "cos_in": cos_t, "sin_in": sin_t, "pmat_in": _pmat(),
             "masks_in": _masks(), "idx_o": _idx_o(cid)}
        m.update(consts)
        maps.append(m)
    return maps


def _launch_b_maps(inputs, x1s):
    consts = _consts()
    f32 = lambda k: np.ascontiguousarray(np.asarray(inputs[k], np.float32))
    maps = []
    for cid in range(NCORES):
        m = {"x_in": x1s[cid],
             "gate_W": f32("L0_ffn_gate_W"),
             "W1": f32("L0_ffn_W1"), "b1": f32("L0_ffn_b1"),
             "W2": f32("L0_ffn_W2"), "b2": f32("L0_ffn_b2"),
             "Ws1": f32("L0_ffn_Ws1"), "bs1": f32("L0_ffn_bs1").reshape(1, 512),
             "Ws2": f32("L0_ffn_Ws2"), "bs2": f32("L0_ffn_bs2").reshape(1, 1024),
             "rs": f32("L0_ffn_route_scale").reshape(1, 1),
             "sel8": _sel8()}
        m.update(consts)
        maps.append(m)
    return maps


def _shard_x(x):
    out = []
    for cid in range(NCORES):
        out.append(np.ascontiguousarray(
            np.concatenate([x[0, 1024 * cid:1024 * (cid + 1)],
                            x[1, 1024 * cid:1024 * (cid + 1)]], axis=0)))
    return out


def _unshard_y(shards):
    y = np.empty((2, 8192, 1024), dtype=np.float32)
    for cid in range(NCORES):
        y[0, 1024 * cid:1024 * (cid + 1)] = shards[cid][0:1024]
        y[1, 1024 * cid:1024 * (cid + 1)] = shards[cid][1024:2048]
    return y


def _launch_a_maps(inputs, xs):
    consts = _consts()
    f32 = lambda k: np.ascontiguousarray(np.asarray(inputs[k], np.float32))
    maps = []
    for cid in range(NCORES):
        m = {"x_in": xs[cid],
             "ln_g": f32("L0_attn_ln_scale").reshape(8, 128),
             "ln_b": f32("L0_attn_ln_bias").reshape(8, 128),
             "wq": f32("L0_attn_wq"), "wk": f32("L0_attn_wk"), "wv": f32("L0_attn_wv")}
        m.update(consts)
        maps.append(m)
    return maps


FUSED = True


def kernel(**inputs):
    x = np.asarray(inputs["L0_attn_x"], dtype=np.float32)
    xs = _shard_x(x)
    if FUSED:
        nc = build_launch({"p1", "p2", "p3", "p4", "p5", "p8"})
        maps = []
        for cid in range(NCORES):
            m = {}
            m.update(_launch_a_maps(inputs, xs)[cid])
            m.update(_launch_b_maps(inputs, xs)[cid])
            m.update(_launch_c_maps(inputs, xs)[cid])
            m.update(_launch_d_maps(inputs, xs)[cid])
            m["x_in"] = xs[cid]
            maps.append(m)
        res = run_bass_kernel_spmd(nc, maps, core_ids=list(range(NCORES)))
        return _unshard_y([r["y_out"] for r in res.results])
    nc_a = build_launch({"p1", "p2", "p3"})
    res = run_bass_kernel_spmd(nc_a, _launch_a_maps(inputs, xs), core_ids=list(range(NCORES)))
    x1s = [np.ascontiguousarray(r["y_out"]) for r in res.results]
    nc_b = build_launch({"p4"})
    res = run_bass_kernel_spmd(nc_b, _launch_b_maps(inputs, x1s), core_ids=list(range(NCORES)))
    x2s = [np.ascontiguousarray(r["y_out"]) for r in res.results]
    nc_c = build_launch({"p5"})
    res = run_bass_kernel_spmd(nc_c, _launch_c_maps(inputs, x2s), core_ids=list(range(NCORES)))
    x3s = [np.ascontiguousarray(r["y_out"]) for r in res.results]
    nc_d = build_launch({"p8"})
    res = run_bass_kernel_spmd(nc_d, _launch_d_maps(inputs, x3s), core_ids=list(range(NCORES)))
    return _unshard_y([r["y_out"] for r in res.results])


# version 15
# speedup vs baseline: 1.4843x; 1.0023x over previous
import contextlib
import numpy as np
import ml_dtypes
import concourse.bass as bass
import concourse.mybir as mybir
from concourse.bass_utils import run_bass_kernel_spmd

F32 = mybir.dt.float32
BF16 = mybir.dt.bfloat16
U32 = mybir.dt.uint32
AF = mybir.ActivationFunctionType
ALU = mybir.AluOpType
AX = mybir.AxisListType

NCORES = 8
D = 1024
NF = 8
TOWN = 2048
TB = 1024
SEQ = 8192
NB = 2

BF = ml_dtypes.bfloat16



class Ctx:

    def __init__(self, nc, block, stack):
        self.nc = nc
        self.block = block
        self.stack = stack
        self.S = {}
        self.cnt = {}
        self.T = {}
        self.A = {}
        for s in ("gd", "wd", "sd", "mm", "ac", "dv", "pl", "cc"):
            self.S[s] = stack.enter_context(nc.semaphore(s))
            self.cnt[s] = 0

    def inc(self, instr, s, n=1):
        instr.then_inc(self.S[s], n)
        self.cnt[s] += n
        return self.cnt[s]

    def sb(self, stack, name, shape, dtype):
        t = stack.enter_context(self.nc.sbuf_tensor(name, shape, dtype))
        return t

    def psum_guard(self, te):
        te.wait_ge(self.S["dv"], self.cnt["dv"])
        te.wait_ge(self.S["ac"], self.cnt["ac"])

    def fence(self, eng, sems=("mm", "dv", "ac", "pl", "wd", "gd", "sd")):
        for s in sems:
            if self.cnt[s]:
                eng.wait_ge(self.S[s], self.cnt[s])

    def ps(self, stack, name, shape, dtype=F32):
        return stack.enter_context(self.nc.psum_tensor(name, shape, dtype))



def _p_load(c):
    nc, block = c.nc, c.block
    x_in = c.A["x_in"]
    x_c = c.T["x_c"]
    ident = c.T["ident"]
    st = contextlib.ExitStack()
    xa = c.sb(st, "xa_stage", [128, 16 * 1024], F32)
    ps_tr = [c.ps(st, f"ps_tr{i}", [128, 512], F32) for i in range(2)]

    def _(g):
        c.fence(g)
        c.inc(g.dma_start(out=xa[:, :].rearrange("p (tt f) -> p tt f", tt=16),
                          in_=x_in[:, :].rearrange("(tt p) f -> p tt f", p=128)), "gd", 16)
    block.gpsimd(_)
    ld_done = c.cnt["gd"]

    evac_cnt = [0]
    groups = [(tt, fg) for tt in range(16) for fg in range(2)]
    for gi, (tt, fg) in enumerate(groups):
        def _(te, tt=tt, fg=fg, gi=gi):
            if gi == 0:
                te.wait_ge(c.S["gd"], ld_done)
                te.wait_ge(c.S["wd"], c.cnt["wd"])
            if gi >= 2:
                te.wait_ge(c.S["dv"], evac_done[gi - 2])
            for j in range(4):
                fc = fg * 4 + j
                ins = te.transpose(ps_tr[gi % 2][:, j * 128:(j + 1) * 128],
                                   xa[:, tt * 1024 + fc * 128: tt * 1024 + (fc + 1) * 128],
                                   ident[:, :])
                if j == 3:
                    c.inc(ins, "mm", 1)
        block.tensor(_)
        mm_here = c.cnt["mm"]

        def _(ve, tt=tt, fg=fg, gi=gi, mm_here=mm_here):
            ve.wait_ge(c.S["mm"], mm_here)
            dst = x_c[:, :].rearrange("p (f t) -> p f t", f=8)[
                :, fg * 4:(fg + 1) * 4, tt * 128:(tt + 1) * 128]
            c.inc(ve.tensor_copy(dst, ps_tr[gi % 2][:, :].rearrange("p (f t) -> p f t", f=4)),
                  "dv", 1)
        block.vector(_)
        if gi == 0:
            evac_done = {}
        evac_done[gi] = c.cnt["dv"]
    st.close()


def _p_store(c, dst_name="y_out"):
    nc, block = c.nc, c.block
    y = c.A[dst_name]
    x_c = c.T["x_c"]
    ident = c.T["ident"]
    st = contextlib.ExitStack()
    ya = c.sb(st, "ya_stage", [128, 1024], F32)
    ps_tr = [c.ps(st, f"ps_trS{i}", [128, 512], F32) for i in range(2)]
    store_done = {}
    for tt in range(16):
        for fg in range(2):
            gi = tt * 2 + fg

            def _(te, tt=tt, fg=fg, gi=gi):
                if gi == 0:
                    c.fence(te)
                if gi >= 2:
                    te.wait_ge(c.S["dv"], evac_done[gi - 2])
                if tt >= 1:
                    te.wait_ge(c.S["wd"], store_done.get(tt - 1, 0))
                for j in range(4):
                    fc = fg * 4 + j
                    ins = te.transpose(ps_tr[gi % 2][:, j * 128:(j + 1) * 128],
                                       x_c[:, fc * 2048 + tt * 128: fc * 2048 + (tt + 1) * 128],
                                       ident[:, :])
                    if j == 3:
                        c.inc(ins, "mm", 1)
            block.tensor(_)
            mm_here = c.cnt["mm"]

            def _(ve, fg=fg, gi=gi, mm_here=mm_here):
                if gi == 0:
                    c.fence(ve)
                ve.wait_ge(c.S["mm"], mm_here)
                c.inc(ve.tensor_copy(ya[:, fg * 512:(fg + 1) * 512], ps_tr[gi % 2][:, :]),
                      "dv", 1)
            block.vector(_)
            if gi == 0:
                evac_done = {}
            evac_done[gi] = c.cnt["dv"]

        dv_here = c.cnt["dv"]

        def _(sy, tt=tt, dv_here=dv_here):
            sy.wait_ge(c.S["dv"], dv_here)
            c.inc(sy.dma_start(out=y[tt * 128:(tt + 1) * 128, :], in_=ya[:, :]), "wd", 16)
        block.sync(_)
        store_done[tt] = c.cnt["wd"]

    def _(sy):
        sy.wait_ge(c.S["wd"], c.cnt["wd"])
    block.sync(_)
    st.close()



def _row_broadcast(c, st, rows, n_tok=TOWN, tag=""):
    out = {}
    ones_row = c.T["ones_row"]
    ps = {}
    for ap, name in rows:
        ps[name] = c.ps(st, f"bc_{name}{tag}", [128, n_tok], F32)

    def _(te):
        te.wait_ge(c.S["dv"], c.cnt["dv"])
        te.wait_ge(c.S["ac"], c.cnt["ac"])
        n = len(rows) * (n_tok // 512)
        i = 0
        for ap, name in rows:
            for q in range(n_tok // 512):
                ins = te.matmul(ps[name][:, q * 512:(q + 1) * 512], ones_row[:, :],
                                ap[0:1, q * 512:(q + 1) * 512], start=True, stop=True)
                i += 1
                if i == n:
                    c.inc(ins, "mm", 1)
    c.block.tensor(_)
    return ps, c.cnt["mm"]


def p_layernorm(c, out_name, eps=1e-5, fold_eps2=None, affine=None,
                add_to_stream=False, rms=False, src_bf=None, dbg_rows_ap=None,
                tag=""):
    nc, block = c.nc, c.block
    x_c = c.T["x_c"]
    ones_col = c.T["ones_col_bf"]
    out_bf = c.T[out_name]
    st = contextlib.ExitStack()
    xbf = [c.sb(st, f"ln_xbf{i}{tag}", [128, 2048], BF16) for i in range(1)] * 2
    sqbf = [c.sb(st, f"ln_sqbf{i}{tag}", [128, 2048], BF16) for i in range(1)] * 2
    rowbuf = c.sb(st, f"ln_rows{tag}", [1, 3 * 2048], F32)
    mu_row = rowbuf[0:1, 0:2048]
    tA_row = rowbuf[0:1, 2048:4096]
    tB_row = rowbuf[0:1, 4096:6144]
    t1_row = tA_row
    t2_row = tB_row
    c_row = tA_row

    stats_st = contextlib.ExitStack()
    sum_ps = None if rms else c.ps(stats_st, f"ln_sum{tag}", [1, 2048], F32)
    sq_ps = c.ps(stats_st, f"ln_sq{tag}", [1, 2048], F32)

    prep_done = {}
    mm_chunk = {}
    for fc in range(NF):
        def _(ve, fc=fc):
            if fc >= 1:
                ve.wait_ge(c.S["mm"], mm_chunk[fc - 1])
            if rms:
                src = src_bf[:, fc * 2048:(fc + 1) * 2048]
                ins = ve.tensor_tensor(sqbf[fc % 2][:, :], src, src, op=ALU.mult)
            else:
                ve.tensor_copy(xbf[fc % 2][:, :], x_c[:, fc * 2048:(fc + 1) * 2048])
                ins = ve.tensor_tensor(sqbf[fc % 2][:, :], xbf[fc % 2][:, :], xbf[fc % 2][:, :],
                                       op=ALU.mult)
            c.inc(ins, "dv", 1)
        block.vector(_)
        prep_done[fc] = c.cnt["dv"]

        def _(te, fc=fc):
            te.wait_ge(c.S["dv"], prep_done[fc])
            for q in range(4):
                if not rms:
                    te.matmul(sum_ps[0:1, q * 512:(q + 1) * 512], ones_col[:, :],
                              xbf[fc % 2][:, q * 512:(q + 1) * 512],
                              start=(fc == 0), stop=(fc == NF - 1))
                ins = te.matmul(sq_ps[0:1, q * 512:(q + 1) * 512], ones_col[:, :],
                                sqbf[fc % 2][:, q * 512:(q + 1) * 512],
                                start=(fc == 0), stop=(fc == NF - 1))
                if q == 3:
                    c.inc(ins, "mm", 1)
        block.tensor(_)
        mm_chunk[fc] = c.cnt["mm"]

    stats_mm = c.cnt["mm"]

    def _(ve):
        ve.wait_ge(c.S["mm"], stats_mm)
        if rms:
            ins = ve.tensor_scalar(t1_row, sq_ps[0:1, :], 1.0 / D, eps, op0=ALU.mult, op1=ALU.add)
        else:
            ve.tensor_scalar(mu_row, sum_ps[0:1, :], 1.0 / D, None, op0=ALU.mult)
            ve.tensor_scalar(t1_row, sq_ps[0:1, :], 1.0 / D, None, op0=ALU.mult)
            ve.tensor_tensor(t2_row, mu_row, mu_row, op=ALU.mult)
            ve.tensor_tensor(t1_row, t1_row, t2_row, op=ALU.subtract)
            if fold_eps2 is not None:
                e2 = fold_eps2
                ins = ve.tensor_scalar(t1_row, t1_row, 1.0 + e2, eps * e2, op0=ALU.mult, op1=ALU.add)
            else:
                ins = ve.tensor_scalar(t1_row, t1_row, eps, None, op0=ALU.add)
        c.inc(ins, "dv", 1)
    block.vector(_)
    row_dv = c.cnt["dv"]

    def _(se):
        se.wait_ge(c.S["dv"], row_dv)
        c.inc(se.activation(t2_row, t1_row, AF.Sqrt), "ac", 1)
    block.scalar(_)
    row_ac = c.cnt["ac"]

    def _(ve):
        ve.wait_ge(c.S["ac"], row_ac)
        c.inc(ve.reciprocal(c_row, t2_row), "dv", 1)
    block.vector(_)

    stats_st.close()

    rows = [(c_row, "c")] if rms else [(mu_row, "mu"), (c_row, "c")]
    bc, bc_mm = _row_broadcast(c, st, rows, tag=tag)

    dtmp = c.sb(st, f"ln_dtmp{tag}", [128, 2048], F32)
    etmp = sqbf[0]
    for fc in range(NF):
        def _(ve, fc=fc):
            if fc == 0:
                ve.wait_ge(c.S["mm"], bc_mm)
            dst = out_bf[:, fc * 2048:(fc + 1) * 2048]
            if rms:
                src = src_bf[:, fc * 2048:(fc + 1) * 2048]
                ve.tensor_tensor(dtmp[:, :], bc["c"][:, :], src, op=ALU.mult)
                g_col = affine[0]
                ins = ve.tensor_scalar(dst, dtmp[:, :], g_col[:, fc:fc + 1], None, op0=ALU.mult)
            else:
                src = x_c[:, fc * 2048:(fc + 1) * 2048]
                ve.tensor_tensor(dtmp[:, :], src, bc["mu"][:, :], op=ALU.subtract)
                if affine is not None:
                    ve.tensor_tensor(etmp[:, :], dtmp[:, :], bc["c"][:, :], op=ALU.mult)
                    g_col, b_col = affine
                    ins = ve.tensor_scalar(dst, etmp[:, :], g_col[:, fc:fc + 1],
                                           b_col[:, fc:fc + 1], op0=ALU.mult, op1=ALU.add)
                else:
                    ins = ve.tensor_tensor(dst, dtmp[:, :], bc["c"][:, :], op=ALU.mult)
            if add_to_stream:
                ins = ve.tensor_tensor(src, src, dst, op=ALU.add)
            c.inc(ins, "dv", 1)
        block.vector(_)
    final_dv = c.cnt["dv"]

    if dbg_rows_ap is not None:
        def _(sy):
            sy.wait_ge(c.S["dv"], final_dv)
            c.inc(sy.dma_start(out=dbg_rows_ap[:, :], in_=rowbuf[:, :]), "wd", 16)
            sy.wait_ge(c.S["wd"], c.cnt["wd"])
        block.sync(_)

    def _(te):
        te.wait_ge(c.S["dv"], final_dv)
    block.tensor(_)
    st.close()



def p2_qkv_l0(c):
    nc, block = c.nc, c.block
    A = c.A
    xn = c.T["xn_bf"]
    ones_row = c.T["ones_row"]
    ones_col = c.T["ones_col_bf"]
    st = contextlib.ExitStack()

    k_stage = [c.sb(st, f"k_stage{i}", [128, 2048], BF16) for i in range(2)]
    sq = [c.sb(st, f"p2_sq{i}", [128, 2048], BF16) for i in range(2)]
    nrows = c.sb(st, "p2_nrows", [1, 512], F32)
    qrow = nrows[0:1, 0:32]
    krow = nrows[0:1, 32:64]
    kall = nrows[0:1, 64:320]
    trow = nrows[0:1, 320:352]
    srow = nrows[0:1, 352:384]
    brow = nrows[0:1, 384:416]

    wst = contextlib.ExitStack()
    w_sb = c.sb(wst, "w_sb", [128, 8 * 1024], BF16)

    def _(g):
        c.fence(g)
        c.inc(g.dma_start(out=w_sb[:, :].rearrange("p (fc n) -> p fc n", fc=8),
                          in_=A["wk"][:, :].rearrange("(fc p) n -> p fc n", p=128)),
              "gd", 16)
    block.gpsimd(_)
    w_loaded = c.cnt["gd"]

    ps_st = contextlib.ExitStack()
    qk_ps = c.ps(ps_st, "qk_ps", [128, 2048], F32)
    nr_ps = c.ps(ps_st, "nr_ps", [1, 2048], F32)

    evac_done = {}
    red_done = {}
    pay_done = {}
    for jt in range(24):
        is_q = jt >= 8
        p = jt - 8 if is_q else jt

        if jt == 8:
            def _(g):
                g.wait_ge(c.S["dv"], red_done[(7, 1)][0])
                c.inc(g.dma_start(out=A["pay0m"][:, :], in_=krow), "gd", 16)
                g.wait_ge(c.S["gd"], c.cnt["gd"])
                g.wait_ge(c.S["wd"], c.cnt["wd"])
                c.inc(g.collective_compute(
                    "AllGather", ALU.bypass, replica_groups=[list(range(8))],
                    ins=[A["pay0m"][:, :].opt()], outs=[A["gath0m"][:, :].opt()]),
                    "cc", 1)
                c.p2_ccm = c.cnt["cc"]
                c.inc(g.collective_compute(
                    "AllGather", ALU.bypass, replica_groups=[list(range(8))],
                    ins=[A["pay0k0"][:, :].opt()], outs=[A["gath0k0"][:, :].opt()]),
                    "cc", 1)
            block.gpsimd(_)
            def _(g):
                g.wait_ge(c.S["mm"], c.cnt["mm"])
                c.inc(g.dma_start(out=w_sb[:, :].rearrange("p (fc n) -> p fc n", fc=8),
                                  in_=A["wq"][:, :].rearrange("(fc p) n -> p fc n", p=128)),
                      "gd", 16)
            block.gpsimd(_)
            wq_loaded = c.cnt["gd"]

        def _(te, jt=jt, p=p, is_q=is_q):
            if jt == 0:
                c.fence(te)
                te.wait_ge(c.S["gd"], w_loaded)
            if jt == 8:
                te.wait_ge(c.S["gd"], wq_loaded)
            if jt >= 1:
                te.wait_ge(c.S["ac"], evac_done[jt - 1])
            for tq in range(4):
                if is_q:
                    for hb in range(2):
                        for fc in range(NF):
                            ins = te.matmul(
                                qk_ps[hb * 64:hb * 64 + 64, tq * 512:(tq + 1) * 512],
                                w_sb[:, fc * 1024 + p * 64: fc * 1024 + (p + 1) * 64],
                                xn[:, fc * 2048 + tq * 512: fc * 2048 + (tq + 1) * 512],
                                start=(fc == 0), stop=(fc == NF - 1))
                else:
                    for fc in range(NF):
                        ins = te.matmul(
                            qk_ps[:, tq * 512:(tq + 1) * 512],
                            w_sb[:, fc * 1024 + p * 128: fc * 1024 + (p + 1) * 128],
                            xn[:, fc * 2048 + tq * 512: fc * 2048 + (tq + 1) * 512],
                            start=(fc == 0), stop=(fc == NF - 1))
            c.inc(ins, "mm", 1)
        block.tensor(_)
        mm_proj = c.cnt["mm"]

        def _(se, jt=jt, p=p, is_q=is_q, mm_proj=mm_proj):
            se.wait_ge(c.S["mm"], mm_proj)
            if is_q:
                dst = c.T["Q_sb"][:, p * 2048:(p + 1) * 2048]
            else:
                if jt >= 2:
                    se.wait_ge(c.S["wd"], pay_done[jt - 2])
                dst = k_stage[jt % 2][:, :]
            c.inc(se.copy(dst, qk_ps[:, :]), "ac", 1)
        block.scalar(_)
        evac_done[jt] = c.cnt["ac"]

        if not is_q:
            def _(sy, jt=jt, p=p):
                sy.wait_ge(c.S["ac"], evac_done[jt])
                c.inc(sy.dma_start(
                    out=A[f"pay0k{p // 4}"][(p % 4) * 128:(p % 4 + 1) * 128, :],
                    in_=k_stage[jt % 2][:, :]), "wd", 16)
            block.sync(_)
            pay_done[jt] = c.cnt["wd"]

        def _(ve, jt=jt, p=p, is_q=is_q):
            ve.wait_ge(c.S["ac"], evac_done[jt])
            if jt >= 2:
                ve.wait_ge(c.S["mm"], red_done[(jt - 2, 1)][1])
            src = (c.T["Q_sb"][:, p * 2048:(p + 1) * 2048] if is_q
                   else k_stage[jt % 2][:, :])
            c.inc(ve.tensor_tensor(sq[jt % 2][:, :], src, src, op=ALU.mult), "dv", 1)
        block.vector(_)
        sq_dv = c.cnt["dv"]

        hh_list = [0] if is_q else [0, 1]
        for hh in hh_list:
            def _(te, jt=jt, hh=hh, is_q=is_q, sq_dv=sq_dv):
                te.wait_ge(c.S["dv"], sq_dv)
                if (jt, hh) != (0, 0):
                    prev = (jt, hh - 1) if hh == 1 else (jt - 1, 1)
                    te.wait_ge(c.S["dv"], red_done[prev][0])
                if is_q:
                    base, kk = 0, 128
                else:
                    base, kk = hh * 64, 64
                for tq in range(4):
                    ins = te.matmul(nr_ps[0:1, tq * 512:(tq + 1) * 512],
                                    ones_col[base:base + kk, 0:1],
                                    sq[jt % 2][base:base + kk, tq * 512:(tq + 1) * 512],
                                    start=True, stop=True)
                c.inc(ins, "mm", 1)
            block.tensor(_)
            nm_mm = c.cnt["mm"]

            def _(ve, jt=jt, p=p, hh=hh, is_q=is_q, nm_mm=nm_mm):
                ve.wait_ge(c.S["mm"], nm_mm)
                row = qrow if is_q else krow
                h = p if is_q else 2 * p + hh
                for b in range(2):
                    ins = ve.reduce_max(row[0:1, b * 16 + h: b * 16 + h + 1],
                                        nr_ps[0:1, b * 1024:(b + 1) * 1024],
                                        axis=AX.X)
                c.inc(ins, "dv", 1)
            block.vector(_)
            red_done[(jt, hh)] = (c.cnt["dv"], c.cnt["mm"])
            if is_q:
                red_done[(jt, 1)] = red_done[(jt, 0)]

    wst.close()
    ps_st.close()

    wv_st = contextlib.ExitStack()
    wv_sb = c.sb(wv_st, "wv_sb", [128, 8 * 1024], BF16)
    v_stage = [c.sb(wv_st, f"v_stage{i}", [128, 1024], BF16) for i in range(2)]
    v_ps = c.ps(wv_st, "v_ps", [128, 1024], F32)

    def _(g):
        c.fence(g)
        c.inc(g.dma_start(out=wv_sb[:, :].rearrange("p (fc n) -> p fc n", fc=8),
                          in_=A["wv"][:, :].rearrange("(fc p) n -> p fc n", p=128)),
              "gd", 16)
    block.gpsimd(_)
    wv_loaded = c.cnt["gd"]

    vevac = {}
    vpay = {}
    for vj, (hg, ts_i) in enumerate([(hg, t) for hg in range(2) for t in range(16)]):
        def _(te, vj=vj, hg=hg, ts_i=ts_i):
            if vj == 0:
                c.fence(te)
                te.wait_ge(c.S["gd"], wv_loaded)
            if vj >= 2:
                te.wait_ge(c.S["ac"], vevac[vj - 2])
            for fc in range(NF):
                ins = te.matmul(v_ps[:, (vj % 2) * 512:(vj % 2 + 1) * 512],
                                xn[:, fc * 2048 + ts_i * 128: fc * 2048 + (ts_i + 1) * 128],
                                wv_sb[:, fc * 1024 + hg * 512: fc * 1024 + (hg + 1) * 512],
                                start=(fc == 0), stop=(fc == NF - 1))
            c.inc(ins, "mm", 1)
        block.tensor(_)
        vmm = c.cnt["mm"]

        def _(se, vj=vj, vmm=vmm):
            se.wait_ge(c.S["mm"], vmm)
            if vj >= 2:
                se.wait_ge(c.S["wd"], vpay[vj - 2])
            c.inc(se.copy(v_stage[vj % 2][:, 0:512], v_ps[:, (vj % 2) * 512:(vj % 2 + 1) * 512]),
                  "ac", 1)
        block.scalar(_)
        vevac[vj] = c.cnt["ac"]

        def _(sy, vj=vj, hg=hg, ts_i=ts_i):
            sy.wait_ge(c.S["ac"], vevac[vj])
            dst = A[f"pay0v{hg}"][ts_i * 128:(ts_i + 1) * 128, :]
            c.inc(sy.dma_start(out=dst, in_=v_stage[vj % 2][:, 0:512]), "wd", 16)
        block.sync(_)
        vpay[vj] = c.cnt["wd"]

        if ts_i == 15:
            def _(g, hg=hg):
                g.wait_ge(c.S["wd"], c.cnt["wd"])
                c.inc(g.collective_compute(
                    "AllGather", ALU.bypass, replica_groups=[list(range(8))],
                    ins=[A[f"pay0v{hg}"][:, :].opt()],
                    outs=[A[f"gath0v{hg}"][:, :].opt()]), "cc", 1)
                if hg == 0:
                    c.inc(g.collective_compute(
                        "AllGather", ALU.bypass, replica_groups=[list(range(8))],
                        ins=[A["pay0k1"][:, :].opt()],
                        outs=[A["gath0k1"][:, :].opt()]), "cc", 1)
            block.gpsimd(_)
            if hg == 0:
                c.p2_cc_g0 = c.cnt["cc"] - 1
            else:
                c.p2_cc_g1 = c.cnt["cc"]

    wv_st.close()

    def _(g):
        g.wait_ge(c.S["cc"], c.p2_ccm)
        src = A["gath0m"][:, :].rearrange("(o r) n -> o r n", o=1)
        c.inc(g.dma_start(out=kall.rearrange("o (r n) -> o r n", r=8), in_=src), "gd", 16)
    block.gpsimd(_)
    kall_ld = c.cnt["gd"]

    def _(ve):
        ve.wait_ge(c.S["gd"], kall_ld)
        ve.tensor_tensor(kall[0:1, 0:128], kall[0:1, 0:128], kall[0:1, 128:256], op=ALU.max)
        ve.tensor_tensor(kall[0:1, 0:64], kall[0:1, 0:64], kall[0:1, 64:128], op=ALU.max)
        ve.tensor_tensor(kall[0:1, 0:32], kall[0:1, 0:32], kall[0:1, 32:64], op=ALU.max)
        c.inc(ve.tensor_tensor(trow, qrow, kall[0:1, 0:32], op=ALU.mult), "dv", 1)
    block.vector(_)
    t_dv = c.cnt["dv"]

    def _(se):
        se.wait_ge(c.S["dv"], t_dv)
        c.inc(se.activation(srow, trow, AF.Sqrt), "ac", 1)
    block.scalar(_)
    s_ac = c.cnt["ac"]

    def _(ve):
        ve.wait_ge(c.S["ac"], s_ac)
        c.inc(ve.tensor_scalar(brow, srow, -1.0 / (32.0 * float(np.sqrt(2.0))), None, op0=ALU.mult), "dv", 1)
    block.vector(_)
    b_dv = c.cnt["dv"]

    bb_ps = c.ps(st, "bb_ps", [128, 32], F32)

    def _(te):
        te.wait_ge(c.S["dv"], b_dv)
        c.psum_guard(te)
        c.inc(te.matmul(bb_ps[:, :], ones_row[:, :], brow, start=True, stop=True), "mm", 1)
    block.tensor(_)
    bb_mm = c.cnt["mm"]

    def _(ve):
        ve.wait_ge(c.S["mm"], bb_mm)
        c.inc(ve.tensor_copy(c.T["bias_cols"][:, :], bb_ps[:, :]), "dv", 1)
    block.vector(_)

    def _(te):
        te.wait_ge(c.S["dv"], c.cnt["dv"])
    block.tensor(_)
    st.close()


def p3_flash_l0(c):
    nc, block = c.nc, c.block
    A = c.A
    Q_sb = c.T["Q_sb"]
    bias_cols = c.T["bias_cols"]
    st = contextlib.ExitStack()
    Kbuf = [c.sb(st, f"l0_K{i}", [128, 8192], BF16) for i in range(2)]
    Vbuf = [c.sb(st, f"l0_V{i}", [128, 65 * 64], BF16) for i in range(2)]
    NBUF = 3
    probs = [c.sb(st, f"l0_pr{i}", [128, 1024], BF16) for i in range(NBUF)]
    oacc_sb = c.sb(st, "l0_oacc", [128, 1024], F32)
    otmp = c.sb(st, "l0_otmp", [64, 1024], BF16)
    rc = c.sb(st, "l0_rc", [1, 1024], F32)
    S_ps = [c.ps(st, f"l0_S{i}", [128, 1024], F32) for i in range(NBUF)]
    oa_ps = c.ps(st, "l0_oa", [65, 1024], F32)

    def _(g):
        c.fence(g)
        c.inc(g.memset(Vbuf[0][:, :], 1.0), "pl", 1)
        c.inc(g.memset(Vbuf[1][:, :], 1.0), "pl", 1)
    block.gpsimd(_)
    ones_set = c.cnt["pl"]

    KB = 64
    it_list = [(b, h) for h in range(16) for b in range(2)]
    ld_done = {}
    it_done = {}

    for it, (b, h) in enumerate(it_list):
        def _(sy, it=it, b=b, h=h):
            if it == 0:
                c.fence(sy)
                sy.wait_ge(c.S["pl"], ones_set)
            if it >= 2:
                sy.wait_ge(c.S["mm"], it_done[it - 2])
            buf = it % 2
            sy.wait_ge(c.S["cc"], c.p2_cc_g0 if h < 8 else c.p2_cc_g1)
            gk = A[f"gath0k{h // 8}"]
            gv = A[f"gath0v{h // 8}"]
            hl = h % 8
            for r in range(8):
                for hb in range(2):
                    c.inc(sy.dma_start(
                        out=Kbuf[buf][hb * 64: hb * 64 + 64, r * 1024:(r + 1) * 1024],
                        in_=gk[r * 512 + 64 * hl: r * 512 + 64 * hl + 64,
                               b * 1024:(b + 1) * 1024]), "wd", 16)
            for r in range(8):
                src = gv[r * 2048 + b * 1024: r * 2048 + (b + 1) * 1024,
                         64 * hl: 64 * hl + 64]
                src = src.rearrange("(blk p) n -> p blk n", p=128)
                dst = Vbuf[buf][:, :].rearrange("p (blk n) -> p blk n",
                                                n=65)[:, r * 8:(r + 1) * 8, 0:64]
                c.inc(sy.dma_start(out=dst, in_=src), "wd", 16)
        block.sync(_)
        ld_done[it] = c.cnt["wd"]

        buf = it % 2
        qcol0 = h * 2048 + b * 1024
        prow0 = 64 * (h % 2)
        mm_base = c.cnt["mm"]
        ac_base = c.cnt["ac"]
        qk_sched = {kb: mm_base + (kb + 1 if kb < 2 else 2 * kb - 1) for kb in range(KB)}
        exp_sched = {kb: ac_base + kb + 1 for kb in range(KB)}

        def _(te, it=it, buf=buf, qcol0=qcol0):
            for kb in range(KB):
                if kb == 0:
                    te.wait_ge(c.S["wd"], ld_done[it])
                    if it >= 1:
                        te.wait_ge(c.S["ac"], exp_prev_last[0])
                        te.wait_ge(c.S["dv"], epi_prev[0])
                if kb >= NBUF:
                    te.wait_ge(c.S["ac"], exp_sched[kb - NBUF])
                for half in range(2):
                    ins = te.matmul(
                        S_ps[kb % NBUF][:, half * 512:(half + 1) * 512],
                        Kbuf[buf][:, kb * 128:(kb + 1) * 128],
                        Q_sb[:, qcol0 + half * 512: qcol0 + (half + 1) * 512],
                        start=True, stop=True)
                c.inc(ins, "mm", 1)
                if kb >= 2:
                    pkb = kb - 2
                    te.wait_ge(c.S["ac"], exp_sched[pkb])
                    for half in range(2):
                        ins = te.matmul(
                            oa_ps[0:65, half * 512:(half + 1) * 512],
                            Vbuf[buf][:, 65 * pkb: 65 * pkb + 65],
                            probs[pkb % NBUF][:, half * 512:(half + 1) * 512],
                            start=(pkb == 0), stop=False)
                    c.inc(ins, "mm", 1)
            for pkb in (KB - 2, KB - 1):
                te.wait_ge(c.S["ac"], exp_sched[pkb])
                for half in range(2):
                    ins = te.matmul(oa_ps[0:65, half * 512:(half + 1) * 512],
                                    Vbuf[buf][:, 65 * pkb: 65 * pkb + 65],
                                    probs[pkb % NBUF][:, half * 512:(half + 1) * 512],
                                    start=False, stop=(pkb == KB - 1))
                c.inc(ins, "mm", 1)
        block.tensor(_)
        assert c.cnt["mm"] == mm_base + 2 * KB

        def _(se, b=b, h=h):
            for kb in range(KB):
                se.wait_ge(c.S["mm"], qk_sched[kb])
                c.inc(se.activation(probs[kb % NBUF][:, :], S_ps[kb % NBUF][:, :], AF.Exp,
                                    bias=bias_cols[:, b * 16 + h: b * 16 + h + 1],
                                    scale=1.0 / 64.0), "ac", 1)
        block.scalar(_)
        assert c.cnt["ac"] == ac_base + KB
        it_done[it] = c.cnt["mm"]
        exp_prev_last = (ac_base + KB,)

        def _(ve, it=it):
            ve.wait_ge(c.S["mm"], it_done[it])
            c.inc(ve.tensor_copy(oacc_sb[0:65, :], oa_ps[0:65, :]), "dv", 1)
            c.inc(ve.reciprocal(rc[0:1, :], oacc_sb[64:65, :]), "dv", 1)
        block.vector(_)
        rc_dv = c.cnt["dv"]

        def _(te, it=it):
            te.wait_ge(c.S["dv"], rc_dv)
            for half in range(2):
                ins = te.matmul(S_ps[0][0:64, half * 512:(half + 1) * 512],
                                c.T["ones_row"][0:1, 0:64],
                                rc[0:1, half * 512:(half + 1) * 512],
                                start=True, stop=True)
            c.inc(ins, "mm", 1)
        block.tensor(_)
        rb_mm = c.cnt["mm"]

        def _(ve, it=it, rb_mm=rb_mm):
            ve.wait_ge(c.S["mm"], rb_mm)
            c.inc(ve.tensor_tensor(otmp[:, :], S_ps[0][0:64, :], oacc_sb[0:64, :],
                                   op=ALU.mult), "dv", 1)
        block.vector(_)
        mul_dv = c.cnt["dv"]

        xsl = c.T["x_c"][64 * (h % 2): 64 * (h % 2) + 64,
                         (h // 2) * 2048 + b * 1024: (h // 2) * 2048 + (b + 1) * 1024]
        if h % 2 == 0:
            def _(ve, xsl=xsl, mul_dv=mul_dv):
                c.inc(ve.tensor_tensor(xsl, xsl, otmp[:, :], op=ALU.add), "dv", 1)
            block.vector(_)
        else:
            def _(te, mul_dv=mul_dv):
                te.wait_ge(c.S["dv"], mul_dv)
                for half in range(2):
                    ins = te.matmul(S_ps[1][64:128, half * 512:(half + 1) * 512],
                                    c.T["ident_bf"][0:64, 0:64],
                                    otmp[:, half * 512:(half + 1) * 512],
                                    start=True, stop=True)
                c.inc(ins, "mm", 1)
            block.tensor(_)
            mv_mm = c.cnt["mm"]

            def _(ve, xsl=xsl, mv_mm=mv_mm):
                ve.wait_ge(c.S["mm"], mv_mm)
                c.inc(ve.tensor_tensor(xsl, xsl, S_ps[1][64:128, :], op=ALU.add), "dv", 1)
            block.vector(_)
        epi_prev = (c.cnt["dv"],)

    def _(te):
        te.wait_ge(c.S["dv"], c.cnt["dv"])
    block.tensor(_)
    st.close()



def p4_ffn_l0(c):
    nc, block = c.nc, c.block
    A = c.A
    st = contextlib.ExitStack()
    h1 = c.sb(st, "h1_bf", [128, 16384], BF16)
    c.T["h1_bf"] = h1
    p_layernorm(c, "h1_bf", eps=1e-5, tag="_p4")

    ones_row = c.T["ones_row"]

    cst = contextlib.ExitStack()
    gw_sb = c.sb(st, "gw_sb", [128, 64], BF16)
    bvec = c.sb(st, "bvec", [128, 36], F32)
    b2_sb = c.sb(st, "b2_sb", [8, 1024], BF16)
    bs2row = c.sb(st, "bs2row", [1, 1024], BF16)
    sel8 = c.sb(st, "sel8_sb", [8, 1024], BF16)
    ones_bfr = c.sb(st, "ones_bfr", [1, 512], BF16)
    rs_col = c.sb(st, "rs_col", [128, 1], F32)
    routeT = c.sb(st, "routeT", [8, 2048], BF16)
    gfb_sb = c.sb(st, "gfb_sb", [128, 2048], BF16)
    s_sb = c.sb(st, "s_sb", [128, 4 * 2048], BF16)
    ge_tmp = [c.sb(st, f"ge_tmp{i}", [128, 1024], BF16) for i in range(2)]

    b_stage = c.sb(cst, "b_stage", [8, 1024], F32)

    def _(g):
        c.fence(g)
        c.inc(g.dma_start(out=gw_sb[:, :].rearrange("p (fc e) -> p fc e", fc=8),
                          in_=A["gate_W"][:, :].rearrange("(fc p) e -> p fc e", p=128)),
              "gd", 16)
        c.inc(g.dma_start(out=b2_sb[:, :], in_=A["b2"][:, :]), "gd", 16)
        c.inc(g.dma_start(out=bs2row[:, :], in_=A["bs2"][:, :]), "gd", 16)
        c.inc(g.dma_start(out=sel8[:, :], in_=A["sel8"][:, :]), "gd", 16)
        c.inc(g.memset(ones_bfr[:, :], 1.0), "pl", 1)
        c.inc(g.dma_start(out=b_stage[0:8, 0:512], in_=A["b1"][:, :]), "gd", 16)
        c.inc(g.dma_start(out=b_stage[0:1, 512:1024], in_=A["bs1"][:, :]), "gd", 16)
    block.gpsimd(_)
    consts_gd = c.cnt["gd"]

    bps = c.ps(cst, "b_ps", [128, 40], F32)
    rs_row = c.sb(cst, "rs_row", [1, 1], F32)

    def _(g):
        c.inc(g.dma_start(out=rs_row[:, :], in_=A["rs"][:, :]), "gd", 16)
    block.gpsimd(_)
    rs_gd = c.cnt["gd"]

    def _(te):
        c.fence(te)
        te.wait_ge(c.S["gd"], consts_gd)
        te.wait_ge(c.S["gd"], rs_gd)
        for hc in range(4):
            te.transpose(bps[:, hc * 8:(hc + 1) * 8],
                         b_stage[0:8, hc * 128:(hc + 1) * 128], c.T["ident"][0:8, 0:8])
        for hc in range(4):
            te.transpose(bps[:, 32 + hc: 33 + hc],
                         b_stage[0:1, 512 + hc * 128: 512 + (hc + 1) * 128],
                         c.T["ident"][0:1, 0:1])
        c.inc(te.matmul(bps[:, 36:37], ones_row[:, :], rs_row[:, :],
                        start=True, stop=True), "mm", 1)
    block.tensor(_)
    bt_mm = c.cnt["mm"]

    def _(ve):
        ve.wait_ge(c.S["mm"], bt_mm)
        ve.tensor_copy(bvec[:, :], bps[:, 0:36])
        c.inc(ve.tensor_copy(rs_col[:, :], bps[:, 36:37]), "dv", 1)
    block.vector(_)
    bt_dv = c.cnt["dv"]

    def _(te):
        te.wait_ge(c.S["dv"], bt_dv)
    block.tensor(_)
    cst.close()

    gate_st = contextlib.ExitStack()
    lg_ps = c.ps(gate_st, "lg_ps", [128, 8], F32)
    rt_ps = c.ps(gate_st, "rt_ps", [8, 128], F32)
    gsc = c.sb(gate_st, "gate_sc", [128, 64], F32)
    sg = gsc[:, 0:8]
    s2 = gsc[:, 8:16]
    eq1 = gsc[:, 16:24]
    s3 = gsc[:, 24:32]
    eq2 = gsc[:, 32:40]
    route_t = gsc[:, 40:48]
    colv = gsc[:, 48:56]
    m1 = colv[:, 0:1]
    m2 = colv[:, 1:2]
    wsum = colv[:, 2:3]
    rw = colv[:, 3:4]
    w1c = colv[:, 4:5]
    w2c = colv[:, 5:6]

    lg_done = {}
    sg_done = {}
    gm_done = {}
    tr_done = {}
    rt_done = {}
    for tt in range(16):
        def _(te, tt=tt):
            if tt == 0:
                c.psum_guard(te)
            else:
                te.wait_ge(c.S["ac"], sg_done[tt - 1])
                te.wait_ge(c.S["ac"], rt_done.get(tt - 1, 0))
            for fc in range(NF):
                ins = te.matmul(lg_ps[:, :],
                                h1[:, fc * 2048 + tt * 128: fc * 2048 + (tt + 1) * 128],
                                gw_sb[:, fc * 8:(fc + 1) * 8],
                                start=(fc == 0), stop=(fc == NF - 1))
            c.inc(ins, "mm", 1)
        block.tensor(_)
        lg_done[tt] = c.cnt["mm"]

        def _(se, tt=tt):
            se.wait_ge(c.S["mm"], lg_done[tt])
            if tt >= 1:
                se.wait_ge(c.S["dv"], gm_done[tt - 1])
            c.inc(se.activation(sg, lg_ps[:, :], AF.Sigmoid), "ac", 1)
        block.scalar(_)
        sg_done[tt] = c.cnt["ac"]

        def _(ve, tt=tt):
            ve.wait_ge(c.S["ac"], sg_done[tt])
            if tt >= 1:
                ve.wait_ge(c.S["mm"], tr_done[tt - 1])
            ve.tensor_scalar(s2, sg, rs_col[:, 0:1], None, op0=ALU.mult)
            ve.drain()
            ve.reduce_max(m1, s2, axis=AX.X)
            ve.drain()
            ve.tensor_scalar(eq1, s2, m1, None, op0=ALU.is_equal)
            ve.drain()
            ve.tensor_scalar(s3, eq1, -1e30, None, op0=ALU.mult)
            ve.drain()
            ve.tensor_tensor(s3, s3, s2, op=ALU.add)
            ve.drain()
            ve.reduce_max(m2, s3, axis=AX.X)
            ve.drain()
            ve.tensor_scalar(eq2, s3, m2, None, op0=ALU.is_equal)
            ve.tensor_tensor(wsum, m1, m2, op=ALU.add)
            ve.drain()
            ve.reciprocal(rw, wsum)
            ve.drain()
            ve.tensor_tensor(w1c, m1, rw, op=ALU.mult)
            ve.tensor_tensor(w2c, m2, rw, op=ALU.mult)
            ve.drain()
            ve.tensor_scalar(route_t, eq1, w1c, None, op0=ALU.mult)
            ve.drain()
            c.inc(ve.scalar_tensor_tensor(route_t, eq2, w2c, route_t,
                                          op0=ALU.mult, op1=ALU.add), "dv", 1)
        block.vector(_)
        gm_done[tt] = c.cnt["dv"]

        def _(te, tt=tt):
            te.wait_ge(c.S["dv"], gm_done[tt])
            c.inc(te.transpose(rt_ps[:, :], route_t, c.T["ident"][:, :]), "mm", 1)
        block.tensor(_)
        tr_done[tt] = c.cnt["mm"]

        def _(se, tt=tt):
            se.wait_ge(c.S["mm"], tr_done[tt])
            c.inc(se.copy(routeT[0:8, tt * 128:(tt + 1) * 128], rt_ps[0:8, :]), "ac", 1)
        block.scalar(_)
        rt_done[tt] = c.cnt["ac"]

    if "dbg_gsc" in A:
        def _(g):
            g.wait_ge(c.S["ac"], rt_done[15])
            c.inc(g.dma_start(out=A["dbg_gsc"][:, :], in_=gsc[:, :]), "gd", 16)
            g.wait_ge(c.S["gd"], c.cnt["gd"])
        block.gpsimd(_)

    def _(te):
        te.wait_ge(c.S["ac"], rt_done[15])
        if "dbg_gsc" in A:
            te.wait_ge(c.S["gd"], c.cnt["gd"])
    block.tensor(_)
    gate_st.close()

    wbuf_st = contextlib.ExitStack()
    w1_sb = [c.sb(wbuf_st, f"w1_sb{i}", [128, 8 * 512], BF16) for i in range(2)]
    w2_sb = [c.sb(wbuf_st, f"w2_sb{i}", [128, 4 * 1024], BF16) for i in range(2)]
    h_ps = [c.ps(wbuf_st, f"h_ps{i}", [128, 1024], F32) for i in range(2)]
    y_ps = [c.ps(wbuf_st, f"y_ps{i}", [128, 1024], F32) for i in range(2)]

    wld = {}
    mlp1_done = {}
    mlp2_done = {}
    gelu_done = {}
    smul_done = {}
    yadd_done = {}
    gfb_done = {}

    for e in range(9):
        shared = (e == 8)

        def _(g, e=e, shared=shared):
            if e == 0:
                c.fence(g)
            if e >= 2:
                g.wait_ge(c.S["mm"], mlp1_done[e - 2])
                g.wait_ge(c.S["mm"], mlp2_done[e - 2])
            if shared:
                s1, s2_ = A["Ws1"][:, :], A["Ws2"][:, :]
            else:
                s1 = A["W1"][e, :, :]
                s2_ = A["W2"][e, :, :]
            c.inc(g.dma_start(out=w1_sb[e % 2][:, :].rearrange("p (fc h) -> p fc h", fc=8),
                              in_=s1.rearrange("(fc p) h -> p fc h", p=128)), "gd", 16)
            c.inc(g.dma_start(out=w2_sb[e % 2][:, :].rearrange("p (hc n) -> p hc n", hc=4),
                              in_=s2_.rearrange("(hc p) n -> p hc n", p=128)), "gd", 16)
        block.gpsimd(_)
        wld[e] = c.cnt["gd"]

        if not shared:
            def _(te, e=e):
                if e == 0:
                    te.wait_ge(c.S["ac"], c.cnt["ac"])
                else:
                    te.wait_ge(c.S["dv"], smul_done[(e - 1, 7)])
                for yb in range(2):
                    for q2 in range(2):
                        ins = te.matmul(
                            h_ps[yb][:, q2 * 512:(q2 + 1) * 512],
                            sel8[:, e * 128:(e + 1) * 128],
                            routeT[0:8, yb * 1024 + q2 * 512: yb * 1024 + (q2 + 1) * 512],
                            start=True, stop=True)
                c.inc(ins, "mm", 1)
            block.tensor(_)
            gfb_mm = c.cnt["mm"]

            def _(ve, e=e, gfb_mm=gfb_mm):
                ve.wait_ge(c.S["mm"], gfb_mm)
                ve.tensor_copy(gfb_sb[:, 0:1024], h_ps[0][:, :])
                c.inc(ve.tensor_copy(gfb_sb[:, 1024:2048], h_ps[1][:, :]), "dv", 1)
            block.vector(_)
            gfb_done[e] = c.cnt["dv"]

            if e == 0 and "dbg_gfb" in A:
                def _(g):
                    g.wait_ge(c.S["dv"], gfb_done[0])
                    c.inc(g.dma_start(out=A["dbg_gfb"][:, :], in_=gfb_sb[:, :]), "gd", 16)
                    c.inc(g.dma_start(out=A["dbg_route"][:, :], in_=routeT[:, :]), "gd", 16)
                block.gpsimd(_)

        ac_base4 = c.cnt["ac"]
        gelu_sched = {j: ac_base4 + j + 1 for j in range(8)}
        mlp1_mm = {}
        jlist = [(hc, th) for hc in range(4) for th in range(2)]
        for j, (hc, th) in enumerate(jlist):
            def _(te, e=e, j=j, hc=hc, th=th):
                if j == 0:
                    te.wait_ge(c.S["gd"], wld[e])
                    if not shared:
                        te.wait_ge(c.S["dv"], gfb_done[e])
                if j >= 2:
                    te.wait_ge(c.S["ac"], gelu_sched[j - 2])
                for q2 in range(2):
                    for fc in range(NF):
                        ins = te.matmul(
                            h_ps[j % 2][:, q2 * 512:(q2 + 1) * 512],
                            w1_sb[e % 2][:, fc * 512 + hc * 128: fc * 512 + (hc + 1) * 128],
                            h1[:, fc * 2048 + th * 1024 + q2 * 512:
                               fc * 2048 + th * 1024 + (q2 + 1) * 512],
                            start=(fc == 0), stop=(fc == NF - 1))
                c.inc(ins, "mm", 1)
            block.tensor(_)
            mlp1_done[e] = c.cnt["mm"]
            mlp1_mm[j] = c.cnt["mm"]

            if not shared:
                def _(ve, e=e, j=j, hc=hc, th=th):
                    ve.wait_ge(c.S["ac"], gelu_sched[j])
                    dst = s_sb[:, hc * 2048 + th * 1024: hc * 2048 + (th + 1) * 1024]
                    c.inc(ve.tensor_tensor(
                        dst, ge_tmp[j % 2][:, :],
                        gfb_sb[:, th * 1024:(th + 1) * 1024], op=ALU.mult), "dv", 1)
                block.vector(_)
                smul_done[(e, j)] = c.cnt["dv"]

        def _(se, e=e, shared=shared):
            for j, (hc, th) in enumerate(jlist):
                se.wait_ge(c.S["mm"], mlp1_mm[j])
                bias = (bvec[:, (32 + hc):(33 + hc)] if shared
                        else bvec[:, hc * 8 + e: hc * 8 + e + 1])
                if shared:
                    dst = s_sb[:, hc * 2048 + th * 1024: hc * 2048 + (th + 1) * 1024]
                    c.inc(se.activation(dst, h_ps[j % 2][:, :], AF.Gelu, bias=bias),
                          "ac", 1)
                else:
                    if j >= 2:
                        se.wait_ge(c.S["dv"], smul_done[(e, j - 2)])
                    c.inc(se.activation(ge_tmp[j % 2][:, :], h_ps[j % 2][:, :], AF.Gelu,
                                        bias=bias), "ac", 1)
        block.scalar(_)
        assert c.cnt["ac"] == ac_base4 + 8

        s_ready = c.cnt["dv"] if not shared else None
        s_ready_ac = c.cnt["ac"]

        for j, (fc, th) in enumerate([(fc, th) for fc in range(NF) for th in range(2)]):
            def _(te, e=e, j=j, fc=fc, th=th, shared=shared,
                  s_ready=s_ready, s_ready_ac=s_ready_ac):
                if j == 0:
                    if shared:
                        te.wait_ge(c.S["ac"], s_ready_ac)
                    else:
                        te.wait_ge(c.S["dv"], s_ready)
                if j >= 2:
                    te.wait_ge(c.S["dv"], yadd_done[(e, j - 2)])
                for q2 in range(2):
                    col0 = th * 1024 + q2 * 512
                    for hc in range(4):
                        ins = te.matmul(
                            y_ps[j % 2][:, q2 * 512:(q2 + 1) * 512],
                            w2_sb[e % 2][:, hc * 1024 + fc * 128: hc * 1024 + (fc + 1) * 128],
                            s_sb[:, hc * 2048 + col0: hc * 2048 + col0 + 512],
                            start=(hc == 0), stop=(hc == 3 and not shared))
                    if shared:
                        te.matmul(y_ps[j % 2][:, q2 * 512:(q2 + 1) * 512],
                                  b2_sb[0:8, fc * 128:(fc + 1) * 128],
                                  routeT[0:8, col0: col0 + 512],
                                  start=False, stop=False)
                        ins = te.matmul(y_ps[j % 2][:, q2 * 512:(q2 + 1) * 512],
                                        bs2row[0:1, fc * 128:(fc + 1) * 128],
                                        ones_bfr[0:1, 0:512],
                                        start=False, stop=True)
                c.inc(ins, "mm", 1)
            block.tensor(_)
            mlp2_done[e] = c.cnt["mm"]
            y_mm = c.cnt["mm"]

            def _(ve, e=e, j=j, fc=fc, th=th, y_mm=y_mm):
                ve.wait_ge(c.S["mm"], y_mm)
                xsl = c.T["x_c"][:, fc * 2048 + th * 1024: fc * 2048 + (th + 1) * 1024]
                c.inc(ve.tensor_tensor(xsl, xsl, y_ps[j % 2][:, :], op=ALU.add), "dv", 1)
            block.vector(_)
            yadd_done[(e, j)] = c.cnt["dv"]
        yadd_done[e] = c.cnt["dv"]

    def _(te):
        te.wait_ge(c.S["dv"], c.cnt["dv"])
    block.tensor(_)
    wbuf_st.close()
    st.close()



def p5_proj_l1(c, b, st):
    nc, block = c.nc, c.block
    A = c.A
    Q1, K1, V1 = c.T["Q1_sb"], c.T["K1_sb"], c.T["V1_sb"]
    wq1, wk1, wv1 = c.T["wq1_sb"], c.T["wk1_sb"], c.T["wv1_sb"]
    cosT, sinT, pmat = c.T["cosT"], c.T["sinT"], c.T["pmat"]
    ones_col = c.T["ones_col_bf"]
    ones_row = c.T["ones_row"]

    pst = contextlib.ExitStack()
    h2r = c.sb(pst, f"h2r_{b}", [128, 8 * 1024], BF16)
    proj_ps_st = contextlib.ExitStack()
    q_ps = c.ps(proj_ps_st, f"l1q_ps_{b}", [128, 1024], F32)
    k_ps = c.ps(proj_ps_st, f"l1k_ps_{b}", [128, 1024], F32)
    v_ps = c.ps(proj_ps_st, f"l1v_ps_{b}", [128, 1024], F32)

    ld_done = {}
    ev_done = {}
    for r in range(8):
        def _(g, r=r):
            if r == 0:
                c.fence(g)
                g.wait_ge(c.S["cc"], c.p5_cc)
            else:
                g.wait_ge(c.S["mm"], ld_done[r - 1][1])
            src = A["gath1"][r * 1024:(r + 1) * 1024, b * 1024:(b + 1) * 1024]
            c.inc(g.dma_start(out=h2r[:, :].rearrange("p (fc n) -> p fc n", fc=8),
                              in_=src.rearrange("(fc p) n -> p fc n", p=128)), "gd", 16)
        block.gpsimd(_)
        gd_here = c.cnt["gd"]

        def _(te, r=r, gd_here=gd_here):
            te.wait_ge(c.S["gd"], gd_here)
            if r == 0:
                c.psum_guard(te)
            else:
                te.wait_ge(c.S["ac"], ev_done[r - 1])
            for th in range(2):
                for fc in range(NF):
                    te.matmul(q_ps[:, th * 512:(th + 1) * 512],
                              wq1[:, fc * 128:(fc + 1) * 128],
                              h2r[:, fc * 1024 + th * 512: fc * 1024 + (th + 1) * 512],
                              start=(fc == 0), stop=(fc == NF - 1))
            for th in range(2):
                for fc in range(NF):
                    te.matmul(k_ps[:, th * 512:(th + 1) * 512],
                              wk1[:, fc * 128:(fc + 1) * 128],
                              h2r[:, fc * 1024 + th * 512: fc * 1024 + (th + 1) * 512],
                              start=(fc == 0), stop=(fc == NF - 1))
            for s in range(8):
                for fc in range(NF):
                    ins = te.matmul(v_ps[:, s * 128:(s + 1) * 128],
                                    h2r[:, fc * 1024 + s * 128: fc * 1024 + (s + 1) * 128],
                                    wv1[:, fc * 128:(fc + 1) * 128],
                                    start=(fc == 0), stop=(fc == NF - 1))
            c.inc(ins, "mm", 1)
        block.tensor(_)
        ld_done[r] = (gd_here, c.cnt["mm"])
        mm_here = c.cnt["mm"]

        def _(se, r=r, mm_here=mm_here):
            se.wait_ge(c.S["mm"], mm_here)
            se.copy(Q1[:, r * 1024:(r + 1) * 1024], q_ps[:, :])
            se.copy(K1[:, r * 1024:(r + 1) * 1024], k_ps[:, :])
            c.inc(se.copy(V1[:, r * 1024:(r + 1) * 1024], v_ps[:, :]), "ac", 1)
        block.scalar(_)
        ev_done[r] = c.cnt["ac"]

    proj_ps_st.close()

    rope_ps_st = contextlib.ExitStack()
    rot_ps = [c.ps(rope_ps_st, f"rot_ps{i}_{b}", [128, 512], F32) for i in range(2)]
    rsc = c.sb(pst, f"rope_sc_{b}", [128, 1024], F32)
    rope_mm = {}
    rope_dv = {}
    for ti, (tgt, tile) in enumerate([(t, i) for t in (Q1, K1) for i in range(16)]):
        def _(te, ti=ti, tgt=tgt, tile=tile):
            if ti == 0:
                te.wait_ge(c.S["ac"], ev_done[7])
                c.psum_guard(te)
            if ti >= 2:
                te.wait_ge(c.S["dv"], rope_dv[ti - 2])
            c.inc(te.matmul(rot_ps[ti % 2][:, :], pmat[:, :],
                            tgt[:, tile * 512:(tile + 1) * 512],
                            start=True, stop=True), "mm", 1)
        block.tensor(_)
        rope_mm[ti] = c.cnt["mm"]

        def _(ve, ti=ti, tgt=tgt, tile=tile):
            ve.wait_ge(c.S["mm"], rope_mm[ti])
            sl = slice(tile * 512, (tile + 1) * 512)
            ve.tensor_tensor(rsc[:, 0:512], rot_ps[ti % 2][:, :], sinT[:, sl], op=ALU.mult)
            ve.tensor_tensor(rsc[:, 512:1024], tgt[:, sl], cosT[:, sl], op=ALU.mult)
            ve.drain()
            c.inc(ve.tensor_tensor(tgt[:, sl], rsc[:, 0:512], rsc[:, 512:1024],
                                   op=ALU.add), "dv", 1)
        block.vector(_)
        rope_dv[ti] = c.cnt["dv"]

    rope_ps_st.close()

    bnd_ps_st = contextlib.ExitStack()
    nb_ps = c.ps(bnd_ps_st, f"nb_ps_{b}", [1, 2048], F32)
    bb_ps = c.ps(bnd_ps_st, f"l1bb_ps_{b}", [128, 2], F32)
    sqc = c.sb(pst, f"l1_sqc_{b}", [128, 2048], BF16)
    nrow = c.sb(pst, f"l1_nrow_{b}", [1, 16], F32)
    for si, src in enumerate((Q1, K1)):
        for tc in range(4):
            def _(ve, si=si, tc=tc, src=src):
                if (si, tc) == (0, 0):
                    ve.wait_ge(c.S["dv"], rope_dv[31])
                else:
                    ve.wait_ge(c.S["mm"], c.cnt["mm"])
                c.inc(ve.tensor_tensor(sqc[:, :], src[:, tc * 2048:(tc + 1) * 2048],
                                       src[:, tc * 2048:(tc + 1) * 2048], op=ALU.mult),
                      "dv", 1)
            block.vector(_)
            sq_dv = c.cnt["dv"]

            def _(te, si=si, tc=tc, sq_dv=sq_dv):
                te.wait_ge(c.S["dv"], sq_dv)
                if (si, tc) == (0, 0):
                    c.psum_guard(te)
                for q in range(4):
                    ins = te.matmul(nb_ps[0:1, q * 512:(q + 1) * 512], ones_col[:, 0:1],
                                    sqc[:, q * 512:(q + 1) * 512], start=True, stop=True)
                c.inc(ins, "mm", 1)
            block.tensor(_)
            nb_mm = c.cnt["mm"]

            def _(ve, si=si, tc=tc, nb_mm=nb_mm):
                ve.wait_ge(c.S["mm"], nb_mm)
                c.inc(ve.reduce_max(nrow[0:1, tc:tc + 1], nb_ps[0:1, :], axis=AX.X),
                      "dv", 1)
            block.vector(_)

        def _(ve, si=si):
            ve.drain()
            ve.tensor_tensor(nrow[0:1, 0:2], nrow[0:1, 0:2], nrow[0:1, 2:4], op=ALU.max)
            ve.drain()
            c.inc(ve.tensor_tensor(nrow[0:1, 4 + si:5 + si], nrow[0:1, 0:1],
                                   nrow[0:1, 1:2], op=ALU.max), "dv", 1)
        block.vector(_)

    def _(ve):
        ve.drain()
        c.inc(ve.tensor_tensor(nrow[0:1, 6:7], nrow[0:1, 4:5], nrow[0:1, 5:6],
                               op=ALU.mult), "dv", 1)
    block.vector(_)
    t_dv = c.cnt["dv"]

    def _(se):
        se.wait_ge(c.S["dv"], t_dv)
        c.inc(se.activation(nrow[0:1, 7:8], nrow[0:1, 6:7], AF.Sqrt), "ac", 1)
    block.scalar(_)
    s_ac = c.cnt["ac"]

    def _(ve):
        ve.wait_ge(c.S["ac"], s_ac)
        c.inc(ve.tensor_scalar(nrow[0:1, 8:9], nrow[0:1, 7:8],
                               -1.0 / float(np.sqrt(128.0)), None, op0=ALU.mult), "dv", 1)
    block.vector(_)
    br_dv = c.cnt["dv"]

    def _(te):
        te.wait_ge(c.S["dv"], br_dv)
        c.inc(te.matmul(bb_ps[:, b:b + 1], ones_row[:, :], nrow[0:1, 8:9],
                        start=True, stop=True), "mm", 1)
    block.tensor(_)
    bb_mm = c.cnt["mm"]

    def _(ve):
        ve.wait_ge(c.S["mm"], bb_mm)
        c.inc(ve.tensor_copy(c.T["bias1"][:, b:b + 1], bb_ps[:, b:b + 1]), "dv", 1)
    block.vector(_)

    def _(te):
        te.wait_ge(c.S["dv"], c.cnt["dv"])
    block.tensor(_)
    bnd_ps_st.close()
    pst.close()


def p6_flash_l1(c, b, st):
    nc, block = c.nc, c.block
    A = c.A
    Q1, K1, V1 = c.T["Q1_sb"], c.T["K1_sb"], c.T["V1_sb"]
    masks = c.T["masks_sb"]
    ones_col = c.T["ones_col_bf"]

    fst = contextlib.ExitStack()
    probs = [c.sb(fst, f"l1_pr{i}_{b}", [128, 1024], BF16) for i in range(2)]
    oacc = c.sb(fst, f"l1_oacc_{b}", [128, 1024], F32)
    obuf = c.sb(fst, f"l1_obuf_{b}", [128, 1024], BF16)
    rc = c.sb(fst, f"l1_rc_{b}", [1, 1024], F32)
    S_ps = [c.ps(fst, f"l1_S{i}_{b}", [128, 1024], F32) for i in range(2)]
    oa_ps = c.ps(fst, f"l1_oa_{b}", [128, 1024], F32)
    sum_ps = c.ps(fst, f"l1_sum_{b}", [1, 1024], F32)

    opay_done = 0
    for qh in range(8):
        nkb = 8 * qh + 8
        mm_base = c.cnt["mm"]
        ac_base = c.cnt["ac"]
        dv_base = c.cnt["dv"]
        qk_sched = {kb: mm_base + (1 if kb == 0 else 2 * kb) for kb in range(nkb)}
        exp_sched = {kb: ac_base + kb + 1 for kb in range(nkb)}
        mask_sched = {kb: dv_base + (kb - 8 * qh) + 1 for kb in range(8 * qh, nkb)}

        def _(te, qh=qh, nkb=nkb):
            for kb in range(nkb):
                if qh == 0 and kb == 0:
                    c.fence(te)
                if kb >= 2:
                    te.wait_ge(c.S["ac"], exp_sched[kb - 2])
                elif kb < 2 and qh > 0:
                    te.wait_ge(c.S["dv"], epi_dv)
                for half in range(2):
                    ins = te.matmul(
                        S_ps[kb % 2][:, half * 512:(half + 1) * 512],
                        K1[:, kb * 128:(kb + 1) * 128],
                        Q1[:, qh * 1024 + half * 512: qh * 1024 + (half + 1) * 512],
                        start=True, stop=True)
                c.inc(ins, "mm", 1)
                if kb >= 1:
                    te.wait_ge(c.S["ac"], exp_sched[kb - 1])
                    pkb = kb - 1
                    for half in range(2):
                        te.matmul(oa_ps[:, half * 512:(half + 1) * 512],
                                  V1[:, pkb * 128:(pkb + 1) * 128],
                                  probs[pkb % 2][:, half * 512:(half + 1) * 512],
                                  start=(pkb == 0), stop=False)
                        ins = te.matmul(sum_ps[0:1, half * 512:(half + 1) * 512],
                                        ones_col[:, 0:1],
                                        probs[pkb % 2][:, half * 512:(half + 1) * 512],
                                        start=(pkb == 0), stop=False)
                    c.inc(ins, "mm", 1)
            te.wait_ge(c.S["ac"], exp_sched[nkb - 1])
            pkb = nkb - 1
            for half in range(2):
                te.matmul(oa_ps[:, half * 512:(half + 1) * 512],
                          V1[:, pkb * 128:(pkb + 1) * 128],
                          probs[pkb % 2][:, half * 512:(half + 1) * 512],
                          start=(pkb == 0), stop=True)
                ins = te.matmul(sum_ps[0:1, half * 512:(half + 1) * 512],
                                ones_col[:, 0:1],
                                probs[pkb % 2][:, half * 512:(half + 1) * 512],
                                start=(pkb == 0), stop=True)
            c.inc(ins, "mm", 1)
        block.tensor(_)
        assert c.cnt["mm"] == mm_base + 2 * nkb

        def _(ve, qh=qh, nkb=nkb):
            for kb in range(8 * qh, nkb):
                ve.wait_ge(c.S["mm"], qk_sched[kb])
                j = kb - 8 * qh
                c.inc(ve.tensor_tensor(S_ps[kb % 2][:, :], S_ps[kb % 2][:, :],
                                       masks[:, j * 1024:(j + 1) * 1024],
                                       op=ALU.add), "dv", 1)
        block.vector(_)
        assert c.cnt["dv"] == dv_base + 8

        def _(se, qh=qh, nkb=nkb, b=b):
            for kb in range(nkb):
                diag = kb >= 8 * qh
                if diag:
                    se.wait_ge(c.S["dv"], mask_sched[kb])
                else:
                    se.wait_ge(c.S["mm"], qk_sched[kb])
                c.inc(se.activation(probs[kb % 2][:, :], S_ps[kb % 2][:, :], AF.Exp,
                                    bias=c.T["bias1"][:, b:b + 1],
                                    scale=float(1.0 / np.sqrt(128.0))), "ac", 1)
        block.scalar(_)
        assert c.cnt["ac"] == ac_base + nkb
        pv_mm = c.cnt["mm"]

        def _(ve, pv_mm=pv_mm):
            ve.wait_ge(c.S["mm"], pv_mm)
            ve.tensor_copy(oacc[:, :], oa_ps[:, :])
            c.inc(ve.reciprocal(rc[0:1, :], sum_ps[0:1, :]), "dv", 1)
        block.vector(_)
        rc_dv = c.cnt["dv"]

        def _(te, rc_dv=rc_dv):
            te.wait_ge(c.S["dv"], rc_dv)
            for half in range(2):
                ins = te.matmul(S_ps[0][:, half * 512:(half + 1) * 512],
                                c.T["ones_row"][:, :],
                                rc[0:1, half * 512:(half + 1) * 512],
                                start=True, stop=True)
            c.inc(ins, "mm", 1)
        block.tensor(_)
        rb_mm = c.cnt["mm"]

        def _(ve, rb_mm=rb_mm, opay_done=opay_done):
            ve.wait_ge(c.S["mm"], rb_mm)
            if opay_done:
                ve.wait_ge(c.S["wd"], opay_done)
            c.inc(ve.tensor_tensor(obuf[:, :], S_ps[0][:, :], oacc[:, :],
                                   op=ALU.mult), "dv", 1)
        block.vector(_)
        ob_dv = c.cnt["dv"]
        epi_dv = c.cnt["dv"]

        def _(sy, qh=qh, b=b, ob_dv=ob_dv):
            sy.wait_ge(c.S["dv"], ob_dv)
            c.inc(sy.dma_start(out=A[f"pay2b{b}"][:, qh * 1024:(qh + 1) * 1024],
                               in_=obuf[:, :]), "wd", 16)
        block.sync(_)
        opay_done = c.cnt["wd"]

    def _(te):
        te.wait_ge(c.S["wd"], c.cnt["wd"])
        te.wait_ge(c.S["dv"], c.cnt["dv"])
    block.tensor(_)
    fst.close()


def p7_oproj_b(c, b, o_own, wo_sb, idx_sb):
    nc, block = c.nc, c.block
    A = c.A
    ps_st = contextlib.ExitStack()
    y_ps = [c.ps(ps_st, f"l1y_ps{i}_{b}", [128, 1024], F32) for i in range(2)]

    def _(g, b=b):
        g.wait_ge(c.S["cc"], c.p6_ccb[b])
        g.wait_ge(c.S["mm"], c.cnt["mm"])
        gview = A[f"gath2b{b}"][:, :].rearrange("r (cb n) -> (r cb) n", cb=8)
        for r in range(8):
            c.inc(g.indirect_dma_start(
                out=o_own[:, r * 1024:(r + 1) * 1024],
                out_offset=None,
                in_=gview,
                in_offset=bass.IndirectOffsetOnAxis(
                    ap=idx_sb[:, r: r + 1], axis=0)), "gd", 16)
    block.gpsimd(_)
    ext_gd = c.cnt["gd"]

    yadds = {}
    for j, fc in enumerate(range(NF)):
        def _(te, j=j, b=b, fc=fc):
            if j == 0:
                te.wait_ge(c.S["gd"], ext_gd)
                c.psum_guard(te)
            if j >= 2:
                te.wait_ge(c.S["dv"], yadds[j - 2])
            for q2 in range(2):
                for r in range(8):
                    ins = te.matmul(
                        y_ps[j % 2][:, q2 * 512:(q2 + 1) * 512],
                        wo_sb[:, r * 1024 + fc * 128: r * 1024 + (fc + 1) * 128],
                        o_own[:, r * 1024 + q2 * 512: r * 1024 + (q2 + 1) * 512],
                        start=(r == 0), stop=(r == 7))
            c.inc(ins, "mm", 1)
        block.tensor(_)
        y_mm = c.cnt["mm"]

        def _(ve, j=j, b=b, fc=fc, y_mm=y_mm):
            ve.wait_ge(c.S["mm"], y_mm)
            xsl = c.T["x_c"][:, fc * 2048 + b * 1024: fc * 2048 + (b + 1) * 1024]
            c.inc(ve.tensor_tensor(xsl, xsl, y_ps[j % 2][:, :], op=ALU.add), "dv", 1)
        block.vector(_)
        yadds[j] = c.cnt["dv"]

    def _(te):
        te.wait_ge(c.S["dv"], c.cnt["dv"])
    block.tensor(_)
    ps_st.close()


def p5to7_l1_attn(c):
    nc, block = c.nc, c.block
    A = c.A
    st = contextlib.ExitStack()

    h2_st = contextlib.ExitStack()
    c.T["h2_bf"] = c.sb(h2_st, "h2_bf", [128, 16384], BF16)
    p_layernorm(c, "h2_bf", eps=1e-5, tag="_p5")
    hdv = c.cnt["dv"]

    def _(sy):
        sy.wait_ge(c.S["dv"], hdv)
        for fc in range(NF):
            c.inc(sy.dma_start(out=A["pay1"][fc * 128:(fc + 1) * 128, :],
                               in_=c.T["h2_bf"][:, fc * 2048:(fc + 1) * 2048]), "wd", 16)
    block.sync(_)
    pay_wd = c.cnt["wd"]

    def _(g):
        g.wait_ge(c.S["wd"], pay_wd)
        c.inc(g.collective_compute(
            "AllGather", ALU.bypass, replica_groups=[list(range(8))],
            ins=[A["pay1"][:, :].opt()], outs=[A["gath1"][:, :].opt()]), "cc", 1)
    block.gpsimd(_)
    c.p5_cc = c.cnt["cc"]
    h2_st.close()

    attn_st = contextlib.ExitStack()
    c.T["Q1_sb"] = c.sb(attn_st, "Q1_sb", [128, 8192], BF16)
    c.T["K1_sb"] = c.sb(attn_st, "K1_sb", [128, 8192], BF16)
    c.T["V1_sb"] = c.sb(attn_st, "V1_sb", [128, 8192], BF16)
    c.T["wq1_sb"] = c.sb(attn_st, "wq1_sb", [128, 8 * 128], BF16)
    c.T["wk1_sb"] = c.sb(attn_st, "wk1_sb", [128, 8 * 128], BF16)
    c.T["wv1_sb"] = c.sb(attn_st, "wv1_sb", [128, 8 * 128], BF16)
    c.T["masks_sb"] = c.sb(attn_st, "masks_sb", [128, 8 * 1024], BF16)
    c.T["bias1"] = c.sb(attn_st, "bias1", [128, 4], F32)
    rope_st = contextlib.ExitStack()
    c.T["cosT"] = c.sb(rope_st, "cosT", [128, 8192], BF16)
    c.T["sinT"] = c.sb(rope_st, "sinT", [128, 8192], BF16)
    c.T["pmat"] = c.sb(rope_st, "pmat", [128, 128], BF16)

    def _(g):
        c.fence(g)
        for name, dst in (("wq1o", "wq1_sb"), ("wk1o", "wk1_sb"), ("wv1o", "wv1_sb")):
            c.inc(g.dma_start(out=c.T[dst][:, :].rearrange("p (fc n) -> p fc n", fc=8),
                              in_=A[name][:, :].rearrange("(fc p) n -> p fc n", p=128)),
                  "gd", 16)
        c.inc(g.dma_start(out=c.T["cosT"][:, :], in_=A["cos_in"][:, :]), "gd", 16)
        c.inc(g.dma_start(out=c.T["sinT"][:, :], in_=A["sin_in"][:, :]), "gd", 16)
        c.inc(g.dma_start(out=c.T["pmat"][:, :], in_=A["pmat_in"][:, :]), "gd", 16)
        c.inc(g.dma_start(out=c.T["masks_sb"][:, :], in_=A["masks_in"][:, :]), "gd", 16)
    block.gpsimd(_)

    c.p6_ccb = {}
    p5_proj_l1(c, 0, st)
    p6_flash_l1(c, 0, st)

    def _(g):
        g.wait_ge(c.S["wd"], c.cnt["wd"])
        c.inc(g.collective_compute(
            "AllGather", ALU.bypass, replica_groups=[list(range(8))],
            ins=[A["pay2b0"][:, :].opt()], outs=[A["gath2b0"][:, :].opt()]), "cc", 1)
    block.gpsimd(_)
    c.p6_ccb[0] = c.cnt["cc"]

    p5_proj_l1(c, 1, st)
    rope_st.close()

    o_own = c.sb(attn_st, "o_own", [128, 8 * 1024], BF16)
    wo_sb = c.sb(attn_st, "wo_sb", [128, 8 * 1024], BF16)
    idx_sb = c.sb(attn_st, "idx_sb", [128, 16], U32)

    def _(g):
        c.fence(g)
        c.inc(g.dma_start(out=wo_sb[:, :].rearrange("p (ic n) -> p ic n", ic=8),
                          in_=A["wo1"][:, :].rearrange("(ic p) n -> p ic n", p=128)),
              "gd", 16)
        c.inc(g.dma_start(out=idx_sb[:, :], in_=A["idx_o"][:, :]), "gd", 16)
    block.gpsimd(_)

    p6_flash_l1(c, 1, st)

    def _(g):
        g.wait_ge(c.S["wd"], c.cnt["wd"])
        c.inc(g.collective_compute(
            "AllGather", ALU.bypass, replica_groups=[list(range(8))],
            ins=[A["pay2b1"][:, :].opt()], outs=[A["gath2b1"][:, :].opt()]), "cc", 1)
    block.gpsimd(_)
    c.p6_ccb[1] = c.cnt["cc"]

    p7_oproj_b(c, 0, o_own, wo_sb, idx_sb)
    p7_oproj_b(c, 1, o_own, wo_sb, idx_sb)
    attn_st.close()
    st.close()



def _p8_weight_cast(c):
    A, block = c.A, c.block

    def _(g):
        for e in range(16):
            c.inc(g.dma_start(out=A["m1e_bf"][e, :, :], in_=A["m1w"][e, 0:2048:2, :]),
                  "gd", 16)
            c.inc(g.dma_start(out=A["m2_bf"][e, :, :], in_=A["m2w"][e, :, :]), "gd", 16)
    block.gpsimd(_)
    c.p8_cast_gd = c.cnt["gd"]


def p8_ffn_l1(c):
    nc, block = c.nc, c.block
    A = c.A
    st = contextlib.ExitStack()

    cast_gd = getattr(c, "p8_cast_gd", None)
    if cast_gd is None:
        _p8_weight_cast(c)
        cast_gd = c.p8_cast_gd

    t_bf = c.sb(st, "t_bf", [128, 16384], BF16)
    c.T["t_bf"] = t_bf
    h3_st = contextlib.ExitStack()
    c.T["h3_bf"] = c.sb(h3_st, "h3_bf", [128, 16384], BF16)
    scol_st = contextlib.ExitStack()
    sc_col = c.sb(scol_st, "l1sc_col", [128, NF], F32)
    sc_stage = c.sb(scol_st, "l1sc_stage", [8, 128], F32)
    scps_st = contextlib.ExitStack()
    sc_ps = c.ps(scps_st, "l1sc_ps", [128, 8], F32)

    def _(g):
        c.fence(g)
        c.inc(g.dma_start(out=sc_stage[:, :], in_=A["l1scale"][:, :]), "gd", 16)
    block.gpsimd(_)
    sc_gd = c.cnt["gd"]

    def _(te):
        te.wait_ge(c.S["gd"], sc_gd)
        c.psum_guard(te)
        c.inc(te.transpose(sc_ps[:, :], sc_stage[:, :], c.T["ident"][0:8, 0:8]), "mm", 1)
    block.tensor(_)
    sc_mm = c.cnt["mm"]

    def _(ve):
        ve.wait_ge(c.S["mm"], sc_mm)
        c.inc(ve.tensor_copy(sc_col[:, :], sc_ps[:, :]), "dv", 1)
    block.vector(_)

    def _(te):
        te.wait_ge(c.S["dv"], c.cnt["dv"])
    block.tensor(_)
    scps_st.close()

    p_layernorm(c, "h3_bf", eps=1e-5, add_to_stream=True, tag="_p8a")
    p_layernorm(c, "t_bf", eps=1e-5, rms=True, src_bf=c.T["h3_bf"],
                affine=(sc_col,), tag="_p8b")
    scol_st.close()
    h3_st.close()

    cst = contextlib.ExitStack()
    gwT = c.sb(st, "gwT", [128, 8 * 16], BF16)
    gb_bf = c.sb(st, "gb_bf", [1, 16], BF16)
    onesr_bf = c.sb(st, "onesr_bf", [1, 128], BF16)
    b1e_t = c.sb(st, "b1e_t", [128, 8 * 16], F32)
    m2b_sb = c.sb(st, "m2b_sb", [16, 1024], BF16)
    routeT1 = c.sb(st, "routeT1", [16, 2048], BF16)
    sel16_sb = c.sb(st, "sel16_sb", [16, 16 * 128], BF16)
    gfb8 = c.sb(st, "gfb8_sb", [128, 2048], BF16)

    gw_stage = c.sb(cst, "gw_stage", [16, 2048], F32)
    b_stage = c.sb(cst, "l1b_stage", [16, 2048], F32)
    bev = c.sb(cst, "l1_bev", [16, 1024], F32)
    gt_ps = c.ps(cst, "l1gt_ps", [128, 16], F32)

    def _(g):
        c.fence(g)
        c.inc(g.dma_start(out=gw_stage[0:16, 0:1024], in_=A["gw1"][:, :]), "gd", 16)
        c.inc(g.dma_start(out=gb_bf[:, :], in_=A["gb1"][:, :]), "gd", 16)
        c.inc(g.dma_start(out=b_stage[:, :], in_=A["m1b"][:, :]), "gd", 16)
        c.inc(g.dma_start(out=m2b_sb[:, :], in_=A["m2b"][:, :]), "gd", 16)
        c.inc(g.dma_start(out=sel16_sb[:, :], in_=A["sel16"][:, :]), "gd", 16)
        c.inc(g.memset(onesr_bf[:, :], 1.0), "pl", 1)
    block.gpsimd(_)
    cst_gd = c.cnt["gd"]
    cst_pl = c.cnt["pl"]

    def _(ve):
        ve.wait_ge(c.S["gd"], cst_gd)
        c.inc(ve.tensor_copy(bev[:, :],
                             b_stage[:, :].rearrange("p (i two) -> p i two", two=2)[:, :, 0]),
              "dv", 1)
    block.vector(_)
    bev_dv = c.cnt["dv"]

    jobs = [("gw", fc) for fc in range(8)] + [("b1", ic) for ic in range(8)]
    tcopy_dv = {}
    for ji, (kind, idx) in enumerate(jobs):
        def _(te, ji=ji, kind=kind, idx=idx):
            if ji == 0:
                te.wait_ge(c.S["dv"], bev_dv)
                c.psum_guard(te)
            else:
                te.wait_ge(c.S["dv"], tcopy_dv[ji - 1])
            src = (gw_stage[0:16, idx * 128:(idx + 1) * 128] if kind == "gw"
                   else bev[0:16, idx * 128:(idx + 1) * 128])
            c.inc(te.transpose(gt_ps[:, :], src, c.T["ident"][0:16, 0:16]), "mm", 1)
        block.tensor(_)
        mm_h = c.cnt["mm"]

        def _(ve, ji=ji, kind=kind, idx=idx, mm_h=mm_h):
            ve.wait_ge(c.S["mm"], mm_h)
            dst = (gwT[:, idx * 16:(idx + 1) * 16] if kind == "gw"
                   else b1e_t[:, idx * 16:(idx + 1) * 16])
            c.inc(ve.tensor_copy(dst, gt_ps[:, :]), "dv", 1)
        block.vector(_)
        tcopy_dv[ji] = c.cnt["dv"]

    gate_st = contextlib.ExitStack()
    lg_ps = c.ps(gate_st, "lg1_ps", [128, 16], F32)
    rt_ps = c.ps(gate_st, "rt1_ps", [16, 128], F32)
    gsc = c.sb(gate_st, "l1gate_sc", [128, 160], F32)
    cur = gsc[:, 0:16]
    eqs = [gsc[:, 16 + i * 16: 32 + i * 16] for i in range(4)]
    route_t = gsc[:, 80:96]
    tmp16 = gsc[:, 96:112]
    mcol = [gsc[:, 112 + i: 113 + i] for i in range(4)]
    ecol = [gsc[:, 116 + i: 117 + i] for i in range(4)]
    nm1 = gsc[:, 120:121]
    ssum = gsc[:, 121:122]
    rw = gsc[:, 122:123]
    wcol = [gsc[:, 123 + i: 124 + i] for i in range(4)]

    lg_done = {}
    gm_done = {}
    tr_done = {}
    rt_done = {}
    for tt in range(16):
        def _(te, tt=tt):
            if tt == 0:
                c.psum_guard(te)
                te.wait_ge(c.S["pl"], cst_pl)
            else:
                te.wait_ge(c.S["dv"], gm_done[tt - 1])
                te.wait_ge(c.S["ac"], rt_done.get(tt - 1, 0))
            for fc in range(NF):
                te.matmul(lg_ps[:, :],
                          t_bf[:, fc * 2048 + tt * 128: fc * 2048 + (tt + 1) * 128],
                          gwT[:, fc * 16:(fc + 1) * 16],
                          start=(fc == 0), stop=False)
            c.inc(te.matmul(lg_ps[:, :], onesr_bf[:, :], gb_bf[:, :],
                            start=False, stop=True), "mm", 1)
        block.tensor(_)
        lg_done[tt] = c.cnt["mm"]

        def _(ve, tt=tt):
            ve.wait_ge(c.S["mm"], lg_done[tt])
            ve.tensor_copy(cur[:, :], lg_ps[:, :])
            ve.drain()
            for i in range(4):
                ve.reduce_max(mcol[i], cur[:, :], axis=AX.X)
                ve.drain()
                ve.tensor_scalar(eqs[i], cur[:, :], mcol[i], None, op0=ALU.is_equal)
                ve.drain()
                if i < 3:
                    ve.tensor_scalar(tmp16, eqs[i], -1e30, None, op0=ALU.mult)
                    ve.drain()
                    ve.tensor_tensor(cur[:, :], cur[:, :], tmp16, op=ALU.add)
                    ve.drain()
            c.inc(ve.tensor_scalar(nm1, mcol[0], -1.0, None, op0=ALU.mult), "dv", 1)
        block.vector(_)
        topk_dv = c.cnt["dv"]

        def _(se, tt=tt, topk_dv=topk_dv):
            se.wait_ge(c.S["dv"], topk_dv)
            for i in range(4):
                ins = se.activation(ecol[i], mcol[i], AF.Exp, bias=nm1)
            c.inc(ins, "ac", 1)
        block.scalar(_)
        exp_ac = c.cnt["ac"]

        def _(ve, tt=tt, exp_ac=exp_ac):
            ve.wait_ge(c.S["ac"], exp_ac)
            ve.tensor_tensor(ssum, ecol[0], ecol[1], op=ALU.add)
            ve.drain()
            ve.tensor_tensor(ssum, ssum, ecol[2], op=ALU.add)
            ve.drain()
            ve.tensor_tensor(ssum, ssum, ecol[3], op=ALU.add)
            ve.drain()
            ve.reciprocal(rw, ssum)
            ve.drain()
            for i in range(4):
                ve.tensor_tensor(wcol[i], ecol[i], rw, op=ALU.mult)
            ve.drain()
            ve.tensor_scalar(route_t, eqs[0], wcol[0], None, op0=ALU.mult)
            ve.drain()
            for i in range(1, 4):
                ve.scalar_tensor_tensor(route_t, eqs[i], wcol[i], route_t,
                                        op0=ALU.mult, op1=ALU.add)
                ve.drain()
            c.inc(ve.tensor_copy(tmp16, route_t), "dv", 1)
        block.vector(_)
        gm_done[tt] = c.cnt["dv"]

        def _(te, tt=tt):
            te.wait_ge(c.S["dv"], gm_done[tt])
            c.inc(te.transpose(rt_ps[:, :], route_t, c.T["ident"][:, :]), "mm", 1)
        block.tensor(_)
        tr_done[tt] = c.cnt["mm"]

        def _(se, tt=tt):
            se.wait_ge(c.S["mm"], tr_done[tt])
            c.inc(se.copy(routeT1[0:16, tt * 128:(tt + 1) * 128], rt_ps[0:16, :]), "ac", 1)
        block.scalar(_)
        rt_done[tt] = c.cnt["ac"]

    def _(te):
        te.wait_ge(c.S["ac"], rt_done[15])
    block.tensor(_)
    gate_st.close()
    cst.close()

    wbuf_st = contextlib.ExitStack()
    w1T = [c.sb(wbuf_st, f"w1T{i}", [128, 8 * 1024], BF16) for i in range(2)]
    w2T = c.sb(wbuf_st, "w2T", [128, 8 * 1024], BF16)
    s1_sb = c.sb(wbuf_st, "s1_sb", [128, 8 * 2048], BF16)
    lim_sc = [c.sb(wbuf_st, f"lim_sc{i}", [128, 1024], BF16) for i in range(2)]
    sig_sc = [c.sb(wbuf_st, f"sig_sc{i}", [128, 1024], BF16) for i in range(2)]
    h_ps = [c.ps(wbuf_st, f"h1_ps{i}", [128, 1024], F32) for i in range(2)]
    y_ps = [c.ps(wbuf_st, f"y1_ps{i}", [128, 1024], F32) for i in range(2)]

    w1ld = {}
    w2ld = {}
    mlp1_done = {}
    mlp2_done = {}
    gelu_done = {}
    smul_done = {}
    yadd_done = {}
    gfb_done = {}

    for e in range(16):
        def _(sy, e=e):
            if e == 0:
                c.fence(sy)
                sy.wait_ge(c.S["gd"], cast_gd)
            if e >= 2:
                sy.wait_ge(c.S["mm"], mlp1_done[e - 2])
            for fc in range(NF):
                c.inc(sy.dma_start(out=w1T[e % 2][:, fc * 1024:(fc + 1) * 1024],
                                   in_=A["m1e_bf"][e, :, fc * 128:(fc + 1) * 128],
                                   transpose=True), "wd", 16)
        block.sync(_)
        w1ld[e] = c.cnt["wd"]

        def _(te, e=e):
            if e == 0:
                te.wait_ge(c.S["ac"], rt_done[15])
                c.psum_guard(te)
            else:
                te.wait_ge(c.S["dv"], smul_done[(e - 1, 15)])
            for yb in range(2):
                for q2 in range(2):
                    ins = te.matmul(
                        h_ps[yb][:, q2 * 512:(q2 + 1) * 512],
                        sel16_sb[:, e * 128:(e + 1) * 128],
                        routeT1[0:16, yb * 1024 + q2 * 512: yb * 1024 + (q2 + 1) * 512],
                        start=True, stop=True)
            c.inc(ins, "mm", 1)
        block.tensor(_)
        gfb_mm = c.cnt["mm"]

        def _(ve, e=e, gfb_mm=gfb_mm):
            ve.wait_ge(c.S["mm"], gfb_mm)
            ve.tensor_copy(gfb8[:, 0:1024], h_ps[0][:, :])
            c.inc(ve.tensor_copy(gfb8[:, 1024:2048], h_ps[1][:, :]), "dv", 1)
        block.vector(_)
        gfb_done[e] = c.cnt["dv"]

        ac_base8 = c.cnt["ac"]
        sig_sched = {j: ac_base8 + j + 1 for j in range(16)}
        lim_dvs = {}
        h_mms = {}
        jlist8 = [(ic, th) for ic in range(8) for th in range(2)]
        for j, (ic, th) in enumerate(jlist8):
            def _(te, e=e, j=j, ic=ic, th=th):
                if j == 0:
                    te.wait_ge(c.S["wd"], w1ld[e])
                    te.wait_ge(c.S["dv"], gfb_done[e])
                if j >= 2:
                    te.wait_ge(c.S["dv"], smul_done[(e, j - 2)])
                for q2 in range(2):
                    for fc in range(NF):
                        ins = te.matmul(
                            h_ps[j % 2][:, q2 * 512:(q2 + 1) * 512],
                            w1T[e % 2][:, fc * 1024 + ic * 128: fc * 1024 + (ic + 1) * 128],
                            t_bf[:, fc * 2048 + th * 1024 + q2 * 512:
                                 fc * 2048 + th * 1024 + (q2 + 1) * 512],
                            start=(fc == 0), stop=(fc == NF - 1))
                c.inc(ins, "mm", 1)
            block.tensor(_)
            mlp1_done[e] = c.cnt["mm"]
            h_mms[j] = c.cnt["mm"]

            def _(ve, e=e, j=j, ic=ic, th=th):
                ve.wait_ge(c.S["mm"], h_mms[j])
                if j >= 2:
                    ve.wait_ge(c.S["ac"], sig_sched[j - 2])
                c.inc(ve.tensor_scalar(lim_sc[j % 2][:, :], h_ps[j % 2][:, :],
                                       b1e_t[:, ic * 16 + e: ic * 16 + e + 1], 7.0,
                                       op0=ALU.add, op1=ALU.min), "dv", 1)
                ve.wait_ge(c.S["ac"], sig_sched[j])
                dst = s1_sb[:, ic * 2048 + th * 1024: ic * 2048 + (th + 1) * 1024]
                ve.tensor_tensor(dst, lim_sc[j % 2][:, :],
                                 gfb8[:, th * 1024:(th + 1) * 1024], op=ALU.mult)
                ve.drain()
                c.inc(ve.tensor_tensor(dst, dst, sig_sc[j % 2][:, :], op=ALU.mult),
                      "dv", 1)
            block.vector(_)
            lim_dvs[j] = c.cnt["dv"] - 1
            smul_done[(e, j)] = c.cnt["dv"]

        def _(se, e=e):
            for j in range(16):
                se.wait_ge(c.S["dv"], lim_dvs[j])
                if j >= 2:
                    se.wait_ge(c.S["dv"], smul_done[(e, j - 2)])
                c.inc(se.activation(sig_sc[j % 2][:, :], lim_sc[j % 2][:, :],
                                    AF.Sigmoid, scale=1.702), "ac", 1)
        block.scalar(_)
        assert c.cnt["ac"] == ac_base8 + 16

        s_ready = c.cnt["dv"]

        def _(sy, e=e):
            if e >= 1:
                sy.wait_ge(c.S["mm"], mlp2_done[e - 1])
            for ic in range(NF):
                c.inc(sy.dma_start(out=w2T[:, ic * 1024:(ic + 1) * 1024],
                                   in_=A["m2_bf"][e, :, ic * 128:(ic + 1) * 128],
                                   transpose=True), "wd", 16)
        block.sync(_)
        w2ld[e] = c.cnt["wd"]

        for j, (fc, th) in enumerate([(fc, th) for fc in range(NF) for th in range(2)]):
            def _(te, e=e, j=j, fc=fc, th=th, s_ready=s_ready):
                if j == 0:
                    te.wait_ge(c.S["dv"], s_ready)
                    te.wait_ge(c.S["wd"], w2ld[e])
                if j >= 2:
                    te.wait_ge(c.S["dv"], yadd_done[(e, j - 2)])
                for q2 in range(2):
                    col0 = th * 1024 + q2 * 512
                    for ic in range(8):
                        ins = te.matmul(
                            y_ps[j % 2][:, q2 * 512:(q2 + 1) * 512],
                            w2T[:, ic * 1024 + fc * 128: ic * 1024 + (fc + 1) * 128],
                            s1_sb[:, ic * 2048 + col0: ic * 2048 + col0 + 512],
                            start=(ic == 0), stop=(ic == 7 and e != 0))
                    if e == 0:
                        ins = te.matmul(y_ps[j % 2][:, q2 * 512:(q2 + 1) * 512],
                                        m2b_sb[0:16, fc * 128:(fc + 1) * 128],
                                        routeT1[0:16, col0: col0 + 512],
                                        start=False, stop=True)
                c.inc(ins, "mm", 1)
            block.tensor(_)
            mlp2_done[e] = c.cnt["mm"]
            y_mm = c.cnt["mm"]

            def _(ve, e=e, j=j, fc=fc, th=th, y_mm=y_mm):
                ve.wait_ge(c.S["mm"], y_mm)
                xsl = c.T["x_c"][:, fc * 2048 + th * 1024: fc * 2048 + (th + 1) * 1024]
                c.inc(ve.tensor_tensor(xsl, xsl, y_ps[j % 2][:, :], op=ALU.add), "dv", 1)
            block.vector(_)
            yadd_done[(e, j)] = c.cnt["dv"]
        yadd_done[e] = c.cnt["dv"]

    def _(te):
        te.wait_ge(c.S["dv"], c.cnt["dv"])
    block.tensor(_)
    wbuf_st.close()
    st.close()



def build_launch(phases, dbg=()):
    nc = bass.Bass("TRN2", target_bir_lowering=False)
    root = contextlib.ExitStack()
    A = {}

    A["x_in"] = nc.dram_tensor("x_in", [TOWN, D], F32, kind="ExternalInput").ap()
    A["ident_in"] = nc.dram_tensor("ident_in", [128, 128], F32, kind="ExternalInput").ap()
    A["y_out"] = nc.dram_tensor("y_out", [TOWN, D], F32, kind="ExternalOutput").ap()
    if "p1" in phases:
        A["ln_g"] = nc.dram_tensor("ln_g", [8, 128], F32, kind="ExternalInput").ap()
        A["ln_b"] = nc.dram_tensor("ln_b", [8, 128], F32, kind="ExternalInput").ap()
    if "p2" in phases:
        A["wq"] = nc.dram_tensor("wq", [D, D], F32, kind="ExternalInput").ap()
        A["wk"] = nc.dram_tensor("wk", [D, D], F32, kind="ExternalInput").ap()
        A["wv"] = nc.dram_tensor("wv", [D, D], F32, kind="ExternalInput").ap()
        A["pay0m"] = nc.dram_tensor("pay0m", [1, 32], BF16, kind="Internal").ap()
        A["gath0m"] = nc.dram_tensor("gath0m", [8, 32], BF16, kind="Internal",
                                     addr_space="Shared").ap()
        for g in range(2):
            A[f"pay0k{g}"] = nc.dram_tensor(f"pay0k{g}", [512, 2048], BF16,
                                            kind="Internal").ap()
            A[f"gath0k{g}"] = nc.dram_tensor(f"gath0k{g}", [8 * 512, 2048], BF16,
                                             kind="Internal", addr_space="Shared").ap()
            A[f"pay0v{g}"] = nc.dram_tensor(f"pay0v{g}", [2048, 512], BF16,
                                            kind="Internal").ap()
            A[f"gath0v{g}"] = nc.dram_tensor(f"gath0v{g}", [8 * 2048, 512], BF16,
                                             kind="Internal", addr_space="Shared").ap()
    if "p4" in phases:
        A["gate_W"] = nc.dram_tensor("gate_W", [D, 8], F32, kind="ExternalInput").ap()
        A["W1"] = nc.dram_tensor("W1", [8, D, 512], F32, kind="ExternalInput").ap()
        A["b1"] = nc.dram_tensor("b1", [8, 512], F32, kind="ExternalInput").ap()
        A["W2"] = nc.dram_tensor("W2", [8, 512, D], F32, kind="ExternalInput").ap()
        A["b2"] = nc.dram_tensor("b2", [8, D], F32, kind="ExternalInput").ap()
        A["Ws1"] = nc.dram_tensor("Ws1", [D, 512], F32, kind="ExternalInput").ap()
        A["bs1"] = nc.dram_tensor("bs1", [1, 512], F32, kind="ExternalInput").ap()
        A["Ws2"] = nc.dram_tensor("Ws2", [512, D], F32, kind="ExternalInput").ap()
        A["bs2"] = nc.dram_tensor("bs2", [1, D], F32, kind="ExternalInput").ap()
        A["rs"] = nc.dram_tensor("rs", [1, 1], F32, kind="ExternalInput").ap()
        A["sel8"] = nc.dram_tensor("sel8", [8, 1024], BF16, kind="ExternalInput").ap()
    if "p5" in phases:
        A["wq1o"] = nc.dram_tensor("wq1o", [D, 128], F32, kind="ExternalInput").ap()
        A["wk1o"] = nc.dram_tensor("wk1o", [D, 128], F32, kind="ExternalInput").ap()
        A["wv1o"] = nc.dram_tensor("wv1o", [D, 128], F32, kind="ExternalInput").ap()
        A["wo1"] = nc.dram_tensor("wo1", [D, D], F32, kind="ExternalInput").ap()
        A["cos_in"] = nc.dram_tensor("cos_in", [128, 8192], BF16, kind="ExternalInput").ap()
        A["sin_in"] = nc.dram_tensor("sin_in", [128, 8192], BF16, kind="ExternalInput").ap()
        A["pmat_in"] = nc.dram_tensor("pmat_in", [128, 128], BF16, kind="ExternalInput").ap()
        A["masks_in"] = nc.dram_tensor("masks_in", [128, 8192], BF16, kind="ExternalInput").ap()
        A["idx_o"] = nc.dram_tensor("idx_o", [128, 16], U32, kind="ExternalInput").ap()
        A["pay1"] = nc.dram_tensor("pay1", [1024, 2048], BF16, kind="Internal").ap()
        A["gath1"] = nc.dram_tensor("gath1", [8192, 2048], BF16, kind="Internal",
                                    addr_space="Shared").ap()
        for bb in range(2):
            A[f"pay2b{bb}"] = nc.dram_tensor(f"pay2b{bb}", [128, 8192], BF16,
                                             kind="Internal").ap()
            A[f"gath2b{bb}"] = nc.dram_tensor(f"gath2b{bb}", [1024, 8192], BF16,
                                              kind="Internal", addr_space="Shared").ap()
    if "p8" in phases:
        A["l1scale"] = nc.dram_tensor("l1scale", [8, 128], F32, kind="ExternalInput").ap()
        A["gw1"] = nc.dram_tensor("gw1", [16, D], F32, kind="ExternalInput").ap()
        A["gb1"] = nc.dram_tensor("gb1", [1, 16], F32, kind="ExternalInput").ap()
        A["m1w"] = nc.dram_tensor("m1w", [16, 2048, D], F32, kind="ExternalInput").ap()
        A["m1b"] = nc.dram_tensor("m1b", [16, 2048], F32, kind="ExternalInput").ap()
        A["m2w"] = nc.dram_tensor("m2w", [16, D, D], F32, kind="ExternalInput").ap()
        A["m2b"] = nc.dram_tensor("m2b", [16, D], F32, kind="ExternalInput").ap()
        A["sel16"] = nc.dram_tensor("sel16", [16, 2048], BF16, kind="ExternalInput").ap()
        A["m1e_bf"] = nc.dram_tensor("m1e_bf", [16, 1024, D], BF16, kind="Internal").ap()
        A["m2_bf"] = nc.dram_tensor("m2_bf", [16, D, D], BF16, kind="Internal").ap()
    for d in dbg:
        if d == "xn":
            A["dbg_xn"] = nc.dram_tensor("dbg_xn", [128, 16384], F32, kind="ExternalOutput").ap()
        if d == "gate":
            A["dbg_route"] = nc.dram_tensor("dbg_route", [8, 2048], F32, kind="ExternalOutput").ap()
            A["dbg_gfb"] = nc.dram_tensor("dbg_gfb", [128, 2048], F32, kind="ExternalOutput").ap()
            A["dbg_gsc"] = nc.dram_tensor("dbg_gsc", [128, 64], F32, kind="ExternalOutput").ap()
        if d == "rows":
            A["dbg_rows"] = nc.dram_tensor("dbg_rows", [1, 8192], F32, kind="ExternalOutput").ap()
            A["dbg_gb"] = nc.dram_tensor("dbg_gb", [128, 48], F32, kind="ExternalOutput").ap()

    with root:
        block = root.enter_context(nc.Block())
        c = Ctx(nc, block, root)
        c.A = A

        c.T["x_c"] = c.sb(root, "x_c", [128, 16384], F32)
        c.T["ident"] = c.sb(root, "ident", [128, 128], F32)
        c.T["ident_bf"] = c.sb(root, "ident_bf", [128, 128], BF16)
        c.T["ones_row"] = c.sb(root, "ones_row", [1, 128], F32)
        c.T["ones_col_bf"] = c.sb(root, "ones_col_bf", [128, 1], BF16)

        def _(sy):
            c.inc(sy.dma_start(out=c.T["ident"][:, :], in_=A["ident_in"][:, :]), "wd", 16)
        block.sync(_)
        id_wd = c.cnt["wd"]

        def _(g):
            g.memset(c.T["ones_row"][:, :], 1.0)
            c.inc(g.memset(c.T["ones_col_bf"][:, :], 1.0), "pl", 1)
        block.gpsimd(_)

        def _(ve):
            ve.wait_ge(c.S["wd"], id_wd)
            c.inc(ve.tensor_copy(c.T["ident_bf"][:, :], c.T["ident"][:, :]), "dv", 1)
        block.vector(_)

        def _(te):
            te.wait_ge(c.S["pl"], c.cnt["pl"])
        block.tensor(_)

        if "p1" in phases:
            gb_st = contextlib.ExitStack()
            g_col = c.sb(root, "g_col", [128, NF], F32)
            b_col = c.sb(root, "b_col", [128, NF], F32)
            gb_sb = c.sb(gb_st, "gb_stage", [8, 256], F32)
            gb_ps = c.ps(gb_st, "gb_ps", [128, 16], F32)

            def _(sy):
                c.fence(sy)
                c.inc(sy.dma_start(out=gb_sb[0:8, 0:128], in_=A["ln_g"][:, :]), "wd", 16)
                c.inc(sy.dma_start(out=gb_sb[0:8, 128:256], in_=A["ln_b"][:, :]), "wd", 16)
            block.sync(_)
            wd_here = c.cnt["wd"]

            def _(te):
                te.wait_ge(c.S["wd"], wd_here)
                c.psum_guard(te)
                te.transpose(gb_ps[:, 0:8], gb_sb[0:8, 0:128], c.T["ident"][0:8, 0:8])
                c.inc(te.transpose(gb_ps[:, 8:16], gb_sb[0:8, 128:256],
                                   c.T["ident"][0:8, 0:8]), "mm", 1)
            block.tensor(_)
            mm_here = c.cnt["mm"]

            def _(ve):
                ve.wait_ge(c.S["mm"], mm_here)
                ve.tensor_copy(g_col[:, :], gb_ps[:, 0:8])
                c.inc(ve.tensor_copy(b_col[:, :], gb_ps[:, 8:16]), "dv", 1)
            block.vector(_)
            if "dbg_gb" in A:
                def _(sy):
                    sy.wait_ge(c.S["dv"], c.cnt["dv"])
                    c.inc(sy.dma_start(out=A["dbg_gb"][0:8, 16:48],
                                       in_=gb_sb[0:8, 0:32]), "wd", 16)
                block.sync(_)
            gb_st.close()

        _p_load(c)

        if "p1" in phases:
            if "dbg_gb" in A:
                def _(sy):
                    sy.wait_ge(c.S["dv"], c.cnt["dv"])
                    c.inc(sy.dma_start(out=A["dbg_gb"][:, 0:8], in_=g_col[:, :]), "wd", 16)
                    c.inc(sy.dma_start(out=A["dbg_gb"][:, 8:16], in_=b_col[:, :]), "wd", 16)
                block.sync(_)

            if "p2" in phases:
                p23_st = contextlib.ExitStack()
                c.T["Q_sb"] = c.sb(p23_st, "Q_sb", [128, 16 * 2048], BF16)
                c.T["bias_cols"] = c.sb(p23_st, "bias_cols", [128, 32], F32)
            xn_st = contextlib.ExitStack()
            c.T["xn_bf"] = c.sb(xn_st, "xn_bf", [128, 16384], BF16)
            p_layernorm(c, "xn_bf", eps=1e-5, fold_eps2=1e-6,
                        affine=(g_col, b_col), add_to_stream=True,
                        dbg_rows_ap=A.get("dbg_rows"), tag="_p1")

            if "dbg_xn" in A:
                st2 = contextlib.ExitStack()
                xf = c.sb(st2, "dbg_xf", [128, 2048], F32)
                for fc in range(NF):
                    def _(ve, fc=fc):
                        ve.wait_ge(c.S["wd"], c.cnt["wd"])
                        c.inc(ve.tensor_copy(xf[:, :],
                                             c.T["xn_bf"][:, fc * 2048:(fc + 1) * 2048]),
                              "dv", 1)
                    block.vector(_)
                    dv_here = c.cnt["dv"]

                    def _(sy, fc=fc, dv_here=dv_here):
                        sy.wait_ge(c.S["dv"], dv_here)
                        c.inc(sy.dma_start(out=A["dbg_xn"][:, fc * 2048:(fc + 1) * 2048],
                                           in_=xf[:, :]), "wd", 16)
                    block.sync(_)
                st2.close()

        if "p2" in phases:
            p2_qkv_l0(c)
            xn_st.close()
            if "p8" in phases:
                _p8_weight_cast(c)
            if "p3" in phases:
                p3_flash_l0(c)
            p23_st.close()

        if "p4" in phases:
            p4_ffn_l0(c)

        if "p5" in phases:
            p5to7_l1_attn(c)

        if "p8" in phases:
            p8_ffn_l1(c)

        _p_store(c)
    return nc



def _consts():
    return {"ident_in": np.eye(128, dtype=np.float32)}


def _dup_wq(wq):
    return np.ascontiguousarray(np.concatenate(
        [np.concatenate([wq[:, h * 64:(h + 1) * 64]] * 2, axis=1) for h in range(16)],
        axis=1))


def _sel8():
    s = np.zeros((8, 1024), dtype=BF)
    for e in range(8):
        s[e, e * 128:(e + 1) * 128] = 1
    return s


def _rope_tables():
    t = np.arange(8192, dtype=np.float32)
    invf = (1.0 / 10000.0 ** (np.arange(0, 128, 2, dtype=np.float32) / 128.0)).astype(np.float32)
    freqs = np.outer(t, invf).astype(np.float32)
    emb = np.concatenate([freqs, freqs], axis=-1)
    return (np.ascontiguousarray(np.cos(emb).T.astype(BF)),
            np.ascontiguousarray(np.sin(emb).T.astype(BF)))


def _pmat():
    p = np.zeros((128, 128), dtype=BF)
    for d in range(64):
        p[d + 64, d] = -1.0
        p[d, d + 64] = 1.0
    return p


def _masks():
    m = np.zeros((128, 8192), dtype=np.float32)
    for j in range(8):
        for kk in range(128):
            qq = np.arange(1024)
            m[kk, j * 1024:(j + 1) * 1024] = np.where(j * 128 + kk > qq, -1e30, 0.0)
    return m.astype(BF)


def _idx_o(cid):
    idx = np.zeros((128, 16), dtype=np.uint32)
    for r in range(8):
        for p in range(128):
            idx[p, r] = (r * 128 + p) * 8 + cid
    return idx


def _sel16():
    s = np.zeros((16, 2048), dtype=BF)
    for e in range(16):
        s[e, e * 128:(e + 1) * 128] = 8.0
    return s


def _launch_d_maps(inputs, x3s):
    consts = _consts()
    f32 = lambda k: np.ascontiguousarray(np.asarray(inputs[k], np.float32))
    maps = []
    for cid in range(NCORES):
        m = {"x_in": x3s[cid],
             "l1scale": f32("L1_ffn_scale").reshape(8, 128),
             "gw1": f32("L1_ffn_gate_w"), "gb1": f32("L1_ffn_gate_b").reshape(1, 16),
             "m1w": f32("L1_ffn_mlp1_w"), "m1b": f32("L1_ffn_mlp1_b"),
             "m2w": f32("L1_ffn_mlp2_w"), "m2b": f32("L1_ffn_mlp2_b"),
             "sel16": _sel16()}
        m.update(consts)
        maps.append(m)
    return maps


def _launch_c_maps(inputs, x2s):
    consts = _consts()
    cos_t, sin_t = _rope_tables()
    f32 = lambda k: np.ascontiguousarray(np.asarray(inputs[k], np.float32))
    Wq = f32("L1_attn_Wq")
    Wk = f32("L1_attn_Wk")
    Wv = f32("L1_attn_Wv")
    maps = []
    for cid in range(NCORES):
        m = {"x_in": x2s[cid],
             "wq1o": np.ascontiguousarray(Wq[:, 128 * cid:128 * (cid + 1)]),
             "wk1o": np.ascontiguousarray(Wk[:, 128 * (cid // 2):128 * (cid // 2 + 1)]),
             "wv1o": np.ascontiguousarray(Wv[:, 128 * (cid // 2):128 * (cid // 2 + 1)]),
             "wo1": f32("L1_attn_Wo"),
             "cos_in": cos_t, "sin_in": sin_t, "pmat_in": _pmat(),
             "masks_in": _masks(), "idx_o": _idx_o(cid)}
        m.update(consts)
        maps.append(m)
    return maps


def _launch_b_maps(inputs, x1s):
    consts = _consts()
    f32 = lambda k: np.ascontiguousarray(np.asarray(inputs[k], np.float32))
    maps = []
    for cid in range(NCORES):
        m = {"x_in": x1s[cid],
             "gate_W": f32("L0_ffn_gate_W"),
             "W1": f32("L0_ffn_W1"), "b1": f32("L0_ffn_b1"),
             "W2": f32("L0_ffn_W2"), "b2": f32("L0_ffn_b2"),
             "Ws1": f32("L0_ffn_Ws1"), "bs1": f32("L0_ffn_bs1").reshape(1, 512),
             "Ws2": f32("L0_ffn_Ws2"), "bs2": f32("L0_ffn_bs2").reshape(1, 1024),
             "rs": f32("L0_ffn_route_scale").reshape(1, 1),
             "sel8": _sel8()}
        m.update(consts)
        maps.append(m)
    return maps


def _shard_x(x):
    out = []
    for cid in range(NCORES):
        out.append(np.ascontiguousarray(
            np.concatenate([x[0, 1024 * cid:1024 * (cid + 1)],
                            x[1, 1024 * cid:1024 * (cid + 1)]], axis=0)))
    return out


def _unshard_y(shards):
    y = np.empty((2, 8192, 1024), dtype=np.float32)
    for cid in range(NCORES):
        y[0, 1024 * cid:1024 * (cid + 1)] = shards[cid][0:1024]
        y[1, 1024 * cid:1024 * (cid + 1)] = shards[cid][1024:2048]
    return y


def _launch_a_maps(inputs, xs):
    consts = _consts()
    f32 = lambda k: np.ascontiguousarray(np.asarray(inputs[k], np.float32))
    maps = []
    for cid in range(NCORES):
        m = {"x_in": xs[cid],
             "ln_g": f32("L0_attn_ln_scale").reshape(8, 128),
             "ln_b": f32("L0_attn_ln_bias").reshape(8, 128),
             "wq": f32("L0_attn_wq"), "wk": f32("L0_attn_wk"), "wv": f32("L0_attn_wv")}
        m.update(consts)
        maps.append(m)
    return maps


FUSED = True


def kernel(**inputs):
    x = np.asarray(inputs["L0_attn_x"], dtype=np.float32)
    xs = _shard_x(x)
    if FUSED:
        nc = build_launch({"p1", "p2", "p3", "p4", "p5", "p8"})
        maps = []
        for cid in range(NCORES):
            m = {}
            m.update(_launch_a_maps(inputs, xs)[cid])
            m.update(_launch_b_maps(inputs, xs)[cid])
            m.update(_launch_c_maps(inputs, xs)[cid])
            m.update(_launch_d_maps(inputs, xs)[cid])
            m["x_in"] = xs[cid]
            maps.append(m)
        res = run_bass_kernel_spmd(nc, maps, core_ids=list(range(NCORES)))
        return _unshard_y([r["y_out"] for r in res.results])
    nc_a = build_launch({"p1", "p2", "p3"})
    res = run_bass_kernel_spmd(nc_a, _launch_a_maps(inputs, xs), core_ids=list(range(NCORES)))
    x1s = [np.ascontiguousarray(r["y_out"]) for r in res.results]
    nc_b = build_launch({"p4"})
    res = run_bass_kernel_spmd(nc_b, _launch_b_maps(inputs, x1s), core_ids=list(range(NCORES)))
    x2s = [np.ascontiguousarray(r["y_out"]) for r in res.results]
    nc_c = build_launch({"p5"})
    res = run_bass_kernel_spmd(nc_c, _launch_c_maps(inputs, x2s), core_ids=list(range(NCORES)))
    x3s = [np.ascontiguousarray(r["y_out"]) for r in res.results]
    nc_d = build_launch({"p8"})
    res = run_bass_kernel_spmd(nc_d, _launch_d_maps(inputs, x3s), core_ids=list(range(NCORES)))
    return _unshard_y([r["y_out"] for r in res.results])


# version 17
# speedup vs baseline: 1.4927x; 1.0057x over previous
import contextlib
import numpy as np
import ml_dtypes
import concourse.bass as bass
import concourse.mybir as mybir
from concourse.bass_utils import run_bass_kernel_spmd

F32 = mybir.dt.float32
BF16 = mybir.dt.bfloat16
U32 = mybir.dt.uint32
AF = mybir.ActivationFunctionType
ALU = mybir.AluOpType
AX = mybir.AxisListType

NCORES = 8
D = 1024
NF = 8
TOWN = 2048
TB = 1024
SEQ = 8192
NB = 2

BF = ml_dtypes.bfloat16



class Ctx:

    def __init__(self, nc, block, stack):
        self.nc = nc
        self.block = block
        self.stack = stack
        self.S = {}
        self.cnt = {}
        self.T = {}
        self.A = {}
        for s in ("gd", "wd", "sd", "mm", "ac", "dv", "pl", "cc"):
            self.S[s] = stack.enter_context(nc.semaphore(s))
            self.cnt[s] = 0

    def inc(self, instr, s, n=1):
        instr.then_inc(self.S[s], n)
        self.cnt[s] += n
        return self.cnt[s]

    def sb(self, stack, name, shape, dtype):
        t = stack.enter_context(self.nc.sbuf_tensor(name, shape, dtype))
        return t

    def psum_guard(self, te):
        te.wait_ge(self.S["dv"], self.cnt["dv"])
        te.wait_ge(self.S["ac"], self.cnt["ac"])

    def fence(self, eng, sems=("mm", "dv", "ac", "pl", "wd", "gd", "sd")):
        for s in sems:
            if self.cnt[s]:
                eng.wait_ge(self.S[s], self.cnt[s])

    def ps(self, stack, name, shape, dtype=F32):
        return stack.enter_context(self.nc.psum_tensor(name, shape, dtype))



def _p_load(c):
    nc, block = c.nc, c.block
    x_in = c.A["x_in"]
    x_c = c.T["x_c"]
    ident = c.T["ident"]
    st = contextlib.ExitStack()
    xa = c.sb(st, "xa_stage", [128, 16 * 1024], F32)
    ps_tr = [c.ps(st, f"ps_tr{i}", [128, 512], F32) for i in range(2)]

    def _(g):
        c.fence(g)
        c.inc(g.dma_start(out=xa[:, :].rearrange("p (tt f) -> p tt f", tt=16),
                          in_=x_in[:, :].rearrange("(tt p) f -> p tt f", p=128)), "gd", 16)
    block.gpsimd(_)
    ld_done = c.cnt["gd"]

    evac_cnt = [0]
    groups = [(tt, fg) for tt in range(16) for fg in range(2)]
    for gi, (tt, fg) in enumerate(groups):
        def _(te, tt=tt, fg=fg, gi=gi):
            if gi == 0:
                te.wait_ge(c.S["gd"], ld_done)
                te.wait_ge(c.S["wd"], c.cnt["wd"])
            if gi >= 2:
                te.wait_ge(c.S["dv"], evac_done[gi - 2])
            for j in range(4):
                fc = fg * 4 + j
                ins = te.transpose(ps_tr[gi % 2][:, j * 128:(j + 1) * 128],
                                   xa[:, tt * 1024 + fc * 128: tt * 1024 + (fc + 1) * 128],
                                   ident[:, :])
                if j == 3:
                    c.inc(ins, "mm", 1)
        block.tensor(_)
        mm_here = c.cnt["mm"]

        def _(ve, tt=tt, fg=fg, gi=gi, mm_here=mm_here):
            ve.wait_ge(c.S["mm"], mm_here)
            dst = x_c[:, :].rearrange("p (f t) -> p f t", f=8)[
                :, fg * 4:(fg + 1) * 4, tt * 128:(tt + 1) * 128]
            c.inc(ve.tensor_copy(dst, ps_tr[gi % 2][:, :].rearrange("p (f t) -> p f t", f=4)),
                  "dv", 1)
        block.vector(_)
        if gi == 0:
            evac_done = {}
        evac_done[gi] = c.cnt["dv"]
    st.close()


def _p_store(c, dst_name="y_out"):
    nc, block = c.nc, c.block
    y = c.A[dst_name]
    x_c = c.T["x_c"]
    ident = c.T["ident"]
    st = contextlib.ExitStack()
    ya = c.sb(st, "ya_stage", [128, 1024], F32)
    ps_tr = [c.ps(st, f"ps_trS{i}", [128, 512], F32) for i in range(2)]
    store_done = {}
    for tt in range(16):
        for fg in range(2):
            gi = tt * 2 + fg

            def _(te, tt=tt, fg=fg, gi=gi):
                if gi == 0:
                    c.fence(te)
                if gi >= 2:
                    te.wait_ge(c.S["dv"], evac_done[gi - 2])
                if tt >= 1:
                    te.wait_ge(c.S["wd"], store_done.get(tt - 1, 0))
                for j in range(4):
                    fc = fg * 4 + j
                    ins = te.transpose(ps_tr[gi % 2][:, j * 128:(j + 1) * 128],
                                       x_c[:, fc * 2048 + tt * 128: fc * 2048 + (tt + 1) * 128],
                                       ident[:, :])
                    if j == 3:
                        c.inc(ins, "mm", 1)
            block.tensor(_)
            mm_here = c.cnt["mm"]

            def _(ve, fg=fg, gi=gi, mm_here=mm_here):
                if gi == 0:
                    c.fence(ve)
                ve.wait_ge(c.S["mm"], mm_here)
                c.inc(ve.tensor_copy(ya[:, fg * 512:(fg + 1) * 512], ps_tr[gi % 2][:, :]),
                      "dv", 1)
            block.vector(_)
            if gi == 0:
                evac_done = {}
            evac_done[gi] = c.cnt["dv"]

        dv_here = c.cnt["dv"]

        def _(sy, tt=tt, dv_here=dv_here):
            sy.wait_ge(c.S["dv"], dv_here)
            c.inc(sy.dma_start(out=y[tt * 128:(tt + 1) * 128, :], in_=ya[:, :]), "wd", 16)
        block.sync(_)
        store_done[tt] = c.cnt["wd"]

    def _(sy):
        sy.wait_ge(c.S["wd"], c.cnt["wd"])
    block.sync(_)
    st.close()



def _row_broadcast(c, st, rows, n_tok=TOWN, tag=""):
    out = {}
    ones_row = c.T["ones_row"]
    ps = {}
    for ap, name in rows:
        ps[name] = c.ps(st, f"bc_{name}{tag}", [128, n_tok], F32)

    def _(te):
        te.wait_ge(c.S["dv"], c.cnt["dv"])
        te.wait_ge(c.S["ac"], c.cnt["ac"])
        n = len(rows) * (n_tok // 512)
        i = 0
        for ap, name in rows:
            for q in range(n_tok // 512):
                ins = te.matmul(ps[name][:, q * 512:(q + 1) * 512], ones_row[:, :],
                                ap[0:1, q * 512:(q + 1) * 512], start=True, stop=True)
                i += 1
                if i == n:
                    c.inc(ins, "mm", 1)
    c.block.tensor(_)
    return ps, c.cnt["mm"]


def p_layernorm(c, out_name, eps=1e-5, fold_eps2=None, affine=None,
                add_to_stream=False, rms=False, src_bf=None, dbg_rows_ap=None,
                tag=""):
    nc, block = c.nc, c.block
    x_c = c.T["x_c"]
    ones_col = c.T["ones_col_bf"]
    out_bf = c.T[out_name]
    st = contextlib.ExitStack()
    xbf = [c.sb(st, f"ln_xbf{i}{tag}", [128, 2048], BF16) for i in range(1)] * 2
    sqbf = [c.sb(st, f"ln_sqbf{i}{tag}", [128, 2048], BF16) for i in range(1)] * 2
    rowbuf = c.sb(st, f"ln_rows{tag}", [1, 3 * 2048], F32)
    mu_row = rowbuf[0:1, 0:2048]
    tA_row = rowbuf[0:1, 2048:4096]
    tB_row = rowbuf[0:1, 4096:6144]
    t1_row = tA_row
    t2_row = tB_row
    c_row = tA_row

    stats_st = contextlib.ExitStack()
    sum_ps = None if rms else c.ps(stats_st, f"ln_sum{tag}", [1, 2048], F32)
    sq_ps = c.ps(stats_st, f"ln_sq{tag}", [1, 2048], F32)

    prep_done = {}
    mm_chunk = {}
    for fc in range(NF):
        def _(ve, fc=fc):
            if fc >= 1:
                ve.wait_ge(c.S["mm"], mm_chunk[fc - 1])
            if rms:
                src = src_bf[:, fc * 2048:(fc + 1) * 2048]
                ins = ve.tensor_tensor(sqbf[fc % 2][:, :], src, src, op=ALU.mult)
            else:
                ve.tensor_copy(xbf[fc % 2][:, :], x_c[:, fc * 2048:(fc + 1) * 2048])
                ins = ve.tensor_tensor(sqbf[fc % 2][:, :], xbf[fc % 2][:, :], xbf[fc % 2][:, :],
                                       op=ALU.mult)
            c.inc(ins, "dv", 1)
        block.vector(_)
        prep_done[fc] = c.cnt["dv"]

        def _(te, fc=fc):
            te.wait_ge(c.S["dv"], prep_done[fc])
            for q in range(4):
                if not rms:
                    te.matmul(sum_ps[0:1, q * 512:(q + 1) * 512], ones_col[:, :],
                              xbf[fc % 2][:, q * 512:(q + 1) * 512],
                              start=(fc == 0), stop=(fc == NF - 1))
                ins = te.matmul(sq_ps[0:1, q * 512:(q + 1) * 512], ones_col[:, :],
                                sqbf[fc % 2][:, q * 512:(q + 1) * 512],
                                start=(fc == 0), stop=(fc == NF - 1))
                if q == 3:
                    c.inc(ins, "mm", 1)
        block.tensor(_)
        mm_chunk[fc] = c.cnt["mm"]

    stats_mm = c.cnt["mm"]

    def _(ve):
        ve.wait_ge(c.S["mm"], stats_mm)
        if rms:
            ins = ve.tensor_scalar(t1_row, sq_ps[0:1, :], 1.0 / D, eps, op0=ALU.mult, op1=ALU.add)
        else:
            ve.tensor_scalar(mu_row, sum_ps[0:1, :], 1.0 / D, None, op0=ALU.mult)
            ve.tensor_scalar(t1_row, sq_ps[0:1, :], 1.0 / D, None, op0=ALU.mult)
            ve.tensor_tensor(t2_row, mu_row, mu_row, op=ALU.mult)
            ve.tensor_tensor(t1_row, t1_row, t2_row, op=ALU.subtract)
            if fold_eps2 is not None:
                e2 = fold_eps2
                ins = ve.tensor_scalar(t1_row, t1_row, 1.0 + e2, eps * e2, op0=ALU.mult, op1=ALU.add)
            else:
                ins = ve.tensor_scalar(t1_row, t1_row, eps, None, op0=ALU.add)
        c.inc(ins, "dv", 1)
    block.vector(_)
    row_dv = c.cnt["dv"]

    def _(se):
        se.wait_ge(c.S["dv"], row_dv)
        c.inc(se.activation(t2_row, t1_row, AF.Sqrt), "ac", 1)
    block.scalar(_)
    row_ac = c.cnt["ac"]

    def _(ve):
        ve.wait_ge(c.S["ac"], row_ac)
        c.inc(ve.reciprocal(c_row, t2_row), "dv", 1)
    block.vector(_)

    stats_st.close()

    rows = [(c_row, "c")] if rms else [(mu_row, "mu"), (c_row, "c")]
    bc, bc_mm = _row_broadcast(c, st, rows, tag=tag)

    dtmp = c.sb(st, f"ln_dtmp{tag}", [128, 2048], F32)
    etmp = sqbf[0]
    for fc in range(NF):
        def _(ve, fc=fc):
            if fc == 0:
                ve.wait_ge(c.S["mm"], bc_mm)
            dst = out_bf[:, fc * 2048:(fc + 1) * 2048]
            if rms:
                src = src_bf[:, fc * 2048:(fc + 1) * 2048]
                ve.tensor_tensor(dtmp[:, :], bc["c"][:, :], src, op=ALU.mult)
                g_col = affine[0]
                ins = ve.tensor_scalar(dst, dtmp[:, :], g_col[:, fc:fc + 1], None, op0=ALU.mult)
            else:
                src = x_c[:, fc * 2048:(fc + 1) * 2048]
                ve.tensor_tensor(dtmp[:, :], src, bc["mu"][:, :], op=ALU.subtract)
                if affine is not None:
                    ve.tensor_tensor(etmp[:, :], dtmp[:, :], bc["c"][:, :], op=ALU.mult)
                    g_col, b_col = affine
                    ins = ve.tensor_scalar(dst, etmp[:, :], g_col[:, fc:fc + 1],
                                           b_col[:, fc:fc + 1], op0=ALU.mult, op1=ALU.add)
                else:
                    ins = ve.tensor_tensor(dst, dtmp[:, :], bc["c"][:, :], op=ALU.mult)
            if add_to_stream:
                ins = ve.tensor_tensor(src, src, dst, op=ALU.add)
            c.inc(ins, "dv", 1)
        block.vector(_)
    final_dv = c.cnt["dv"]

    if dbg_rows_ap is not None:
        def _(sy):
            sy.wait_ge(c.S["dv"], final_dv)
            c.inc(sy.dma_start(out=dbg_rows_ap[:, :], in_=rowbuf[:, :]), "wd", 16)
            sy.wait_ge(c.S["wd"], c.cnt["wd"])
        block.sync(_)

    def _(te):
        te.wait_ge(c.S["dv"], final_dv)
    block.tensor(_)
    st.close()



def p2_qkv_l0(c):
    nc, block = c.nc, c.block
    A = c.A
    xn = c.T["xn_bf"]
    ones_row = c.T["ones_row"]
    ones_col = c.T["ones_col_bf"]
    st = contextlib.ExitStack()

    k_stage = [c.sb(st, f"k_stage{i}", [128, 2048], BF16) for i in range(2)]
    sq = [c.sb(st, f"p2_sq{i}", [128, 2048], BF16) for i in range(2)]
    nrows = c.sb(st, "p2_nrows", [1, 512], F32)
    qrow = nrows[0:1, 0:32]
    krow = nrows[0:1, 32:64]
    kall = nrows[0:1, 64:320]
    trow = nrows[0:1, 320:352]
    srow = nrows[0:1, 352:384]
    brow = nrows[0:1, 384:416]

    wst = contextlib.ExitStack()
    w_sb = c.sb(wst, "w_sb", [128, 8 * 1024], BF16)

    def _(g):
        c.fence(g)
        c.inc(g.dma_start(out=w_sb[:, :].rearrange("p (fc n) -> p fc n", fc=8),
                          in_=A["wk"][:, :].rearrange("(fc p) n -> p fc n", p=128)),
              "gd", 16)
    block.gpsimd(_)
    w_loaded = c.cnt["gd"]

    ps_st = contextlib.ExitStack()
    qk_ps = c.ps(ps_st, "qk_ps", [128, 2048], F32)
    nr_ps = c.ps(ps_st, "nr_ps", [1, 2048], F32)

    evac_done = {}
    red_done = {}
    pay_done = {}
    for jt in range(24):
        is_q = jt >= 8
        p = jt - 8 if is_q else jt

        if jt == 8:
            def _(g):
                g.wait_ge(c.S["dv"], red_done[(7, 1)][0])
                c.inc(g.dma_start(out=A["pay0m"][:, :], in_=krow), "gd", 16)
                g.wait_ge(c.S["gd"], c.cnt["gd"])
                g.wait_ge(c.S["wd"], c.cnt["wd"])
                c.inc(g.collective_compute(
                    "AllGather", ALU.bypass, replica_groups=[list(range(8))],
                    ins=[A["pay0m"][:, :].opt()], outs=[A["gath0m"][:, :].opt()]),
                    "cc", 1)
                c.p2_ccm = c.cnt["cc"]
                c.inc(g.collective_compute(
                    "AllGather", ALU.bypass, replica_groups=[list(range(8))],
                    ins=[A["pay0k0"][:, :].opt()], outs=[A["gath0k0"][:, :].opt()]),
                    "cc", 1)
            block.gpsimd(_)
            def _(g):
                g.wait_ge(c.S["mm"], c.cnt["mm"])
                c.inc(g.dma_start(out=w_sb[:, :].rearrange("p (fc n) -> p fc n", fc=8),
                                  in_=A["wq"][:, :].rearrange("(fc p) n -> p fc n", p=128)),
                      "gd", 16)
            block.gpsimd(_)
            wq_loaded = c.cnt["gd"]

        def _(te, jt=jt, p=p, is_q=is_q):
            if jt == 0:
                c.fence(te)
                te.wait_ge(c.S["gd"], w_loaded)
            if jt == 8:
                te.wait_ge(c.S["gd"], wq_loaded)
            if jt >= 1:
                te.wait_ge(c.S["ac"], evac_done[jt - 1])
            for tq in range(4):
                if is_q:
                    for hb in range(2):
                        for fc in range(NF):
                            ins = te.matmul(
                                qk_ps[hb * 64:hb * 64 + 64, tq * 512:(tq + 1) * 512],
                                w_sb[:, fc * 1024 + p * 64: fc * 1024 + (p + 1) * 64],
                                xn[:, fc * 2048 + tq * 512: fc * 2048 + (tq + 1) * 512],
                                start=(fc == 0), stop=(fc == NF - 1))
                else:
                    for fc in range(NF):
                        ins = te.matmul(
                            qk_ps[:, tq * 512:(tq + 1) * 512],
                            w_sb[:, fc * 1024 + p * 128: fc * 1024 + (p + 1) * 128],
                            xn[:, fc * 2048 + tq * 512: fc * 2048 + (tq + 1) * 512],
                            start=(fc == 0), stop=(fc == NF - 1))
            c.inc(ins, "mm", 1)
        block.tensor(_)
        mm_proj = c.cnt["mm"]

        def _(se, jt=jt, p=p, is_q=is_q, mm_proj=mm_proj):
            se.wait_ge(c.S["mm"], mm_proj)
            if is_q:
                dst = c.T["Q_sb"][:, p * 2048:(p + 1) * 2048]
            else:
                if jt >= 2:
                    se.wait_ge(c.S["wd"], pay_done[jt - 2])
                dst = k_stage[jt % 2][:, :]
            c.inc(se.copy(dst, qk_ps[:, :]), "ac", 1)
        block.scalar(_)
        evac_done[jt] = c.cnt["ac"]

        if not is_q:
            def _(sy, jt=jt, p=p):
                sy.wait_ge(c.S["ac"], evac_done[jt])
                c.inc(sy.dma_start(
                    out=A[f"pay0k{p // 4}"][(p % 4) * 128:(p % 4 + 1) * 128, :],
                    in_=k_stage[jt % 2][:, :]), "wd", 16)
            block.sync(_)
            pay_done[jt] = c.cnt["wd"]

        def _(ve, jt=jt, p=p, is_q=is_q):
            ve.wait_ge(c.S["ac"], evac_done[jt])
            if jt >= 2:
                ve.wait_ge(c.S["mm"], red_done[(jt - 2, 1)][1])
            src = (c.T["Q_sb"][:, p * 2048:(p + 1) * 2048] if is_q
                   else k_stage[jt % 2][:, :])
            c.inc(ve.tensor_tensor(sq[jt % 2][:, :], src, src, op=ALU.mult), "dv", 1)
        block.vector(_)
        sq_dv = c.cnt["dv"]

        hh_list = [0] if is_q else [0, 1]
        for hh in hh_list:
            def _(te, jt=jt, hh=hh, is_q=is_q, sq_dv=sq_dv):
                te.wait_ge(c.S["dv"], sq_dv)
                if (jt, hh) != (0, 0):
                    prev = (jt, hh - 1) if hh == 1 else (jt - 1, 1)
                    te.wait_ge(c.S["dv"], red_done[prev][0])
                if is_q:
                    base, kk = 0, 128
                else:
                    base, kk = hh * 64, 64
                for tq in range(4):
                    ins = te.matmul(nr_ps[0:1, tq * 512:(tq + 1) * 512],
                                    ones_col[base:base + kk, 0:1],
                                    sq[jt % 2][base:base + kk, tq * 512:(tq + 1) * 512],
                                    start=True, stop=True)
                c.inc(ins, "mm", 1)
            block.tensor(_)
            nm_mm = c.cnt["mm"]

            def _(ve, jt=jt, p=p, hh=hh, is_q=is_q, nm_mm=nm_mm):
                ve.wait_ge(c.S["mm"], nm_mm)
                row = qrow if is_q else krow
                h = p if is_q else 2 * p + hh
                for b in range(2):
                    ins = ve.reduce_max(row[0:1, b * 16 + h: b * 16 + h + 1],
                                        nr_ps[0:1, b * 1024:(b + 1) * 1024],
                                        axis=AX.X)
                c.inc(ins, "dv", 1)
            block.vector(_)
            red_done[(jt, hh)] = (c.cnt["dv"], c.cnt["mm"])
            if is_q:
                red_done[(jt, 1)] = red_done[(jt, 0)]

    wst.close()
    ps_st.close()

    wv_st = contextlib.ExitStack()
    wv_sb = c.sb(wv_st, "wv_sb", [128, 8 * 1024], BF16)
    v_stage = [c.sb(wv_st, f"v_stage{i}", [128, 1024], BF16) for i in range(2)]
    v_ps = c.ps(wv_st, "v_ps", [128, 1024], F32)

    def _(g):
        c.fence(g)
        c.inc(g.dma_start(out=wv_sb[:, :].rearrange("p (fc n) -> p fc n", fc=8),
                          in_=A["wv"][:, :].rearrange("(fc p) n -> p fc n", p=128)),
              "gd", 16)
    block.gpsimd(_)
    wv_loaded = c.cnt["gd"]

    vevac = {}
    vpay = {}
    for vj, (hg, ts_i) in enumerate([(hg, t) for hg in range(2) for t in range(16)]):
        def _(te, vj=vj, hg=hg, ts_i=ts_i):
            if vj == 0:
                c.fence(te)
                te.wait_ge(c.S["gd"], wv_loaded)
            if vj >= 2:
                te.wait_ge(c.S["ac"], vevac[vj - 2])
            for fc in range(NF):
                ins = te.matmul(v_ps[:, (vj % 2) * 512:(vj % 2 + 1) * 512],
                                xn[:, fc * 2048 + ts_i * 128: fc * 2048 + (ts_i + 1) * 128],
                                wv_sb[:, fc * 1024 + hg * 512: fc * 1024 + (hg + 1) * 512],
                                start=(fc == 0), stop=(fc == NF - 1))
            c.inc(ins, "mm", 1)
        block.tensor(_)
        vmm = c.cnt["mm"]

        def _(se, vj=vj, vmm=vmm):
            se.wait_ge(c.S["mm"], vmm)
            if vj >= 2:
                se.wait_ge(c.S["wd"], vpay[vj - 2])
            c.inc(se.copy(v_stage[vj % 2][:, 0:512], v_ps[:, (vj % 2) * 512:(vj % 2 + 1) * 512]),
                  "ac", 1)
        block.scalar(_)
        vevac[vj] = c.cnt["ac"]

        def _(sy, vj=vj, hg=hg, ts_i=ts_i):
            sy.wait_ge(c.S["ac"], vevac[vj])
            dst = A[f"pay0v{hg}"][ts_i * 128:(ts_i + 1) * 128, :]
            c.inc(sy.dma_start(out=dst, in_=v_stage[vj % 2][:, 0:512]), "wd", 16)
        block.sync(_)
        vpay[vj] = c.cnt["wd"]

        if ts_i == 15:
            def _(g, hg=hg):
                g.wait_ge(c.S["wd"], c.cnt["wd"])
                c.inc(g.collective_compute(
                    "AllGather", ALU.bypass, replica_groups=[list(range(8))],
                    ins=[A[f"pay0v{hg}"][:, :].opt()],
                    outs=[A[f"gath0v{hg}"][:, :].opt()]), "cc", 1)
                if hg == 0:
                    c.inc(g.collective_compute(
                        "AllGather", ALU.bypass, replica_groups=[list(range(8))],
                        ins=[A["pay0k1"][:, :].opt()],
                        outs=[A["gath0k1"][:, :].opt()]), "cc", 1)
            block.gpsimd(_)
            if hg == 0:
                c.p2_cc_g0 = c.cnt["cc"] - 1
            else:
                c.p2_cc_g1 = c.cnt["cc"]

    wv_st.close()

    def _(g):
        g.wait_ge(c.S["cc"], c.p2_ccm)
        src = A["gath0m"][:, :].rearrange("(o r) n -> o r n", o=1)
        c.inc(g.dma_start(out=kall.rearrange("o (r n) -> o r n", r=8), in_=src), "gd", 16)
    block.gpsimd(_)
    kall_ld = c.cnt["gd"]

    def _(ve):
        ve.wait_ge(c.S["gd"], kall_ld)
        ve.tensor_tensor(kall[0:1, 0:128], kall[0:1, 0:128], kall[0:1, 128:256], op=ALU.max)
        ve.tensor_tensor(kall[0:1, 0:64], kall[0:1, 0:64], kall[0:1, 64:128], op=ALU.max)
        ve.tensor_tensor(kall[0:1, 0:32], kall[0:1, 0:32], kall[0:1, 32:64], op=ALU.max)
        c.inc(ve.tensor_tensor(trow, qrow, kall[0:1, 0:32], op=ALU.mult), "dv", 1)
    block.vector(_)
    t_dv = c.cnt["dv"]

    def _(se):
        se.wait_ge(c.S["dv"], t_dv)
        c.inc(se.activation(srow, trow, AF.Sqrt), "ac", 1)
    block.scalar(_)
    s_ac = c.cnt["ac"]

    def _(ve):
        ve.wait_ge(c.S["ac"], s_ac)
        c.inc(ve.tensor_scalar(brow, srow, -1.0 / (32.0 * float(np.sqrt(2.0))), None, op0=ALU.mult), "dv", 1)
    block.vector(_)
    b_dv = c.cnt["dv"]

    bb_ps = c.ps(st, "bb_ps", [128, 32], F32)

    def _(te):
        te.wait_ge(c.S["dv"], b_dv)
        c.psum_guard(te)
        c.inc(te.matmul(bb_ps[:, :], ones_row[:, :], brow, start=True, stop=True), "mm", 1)
    block.tensor(_)
    bb_mm = c.cnt["mm"]

    def _(ve):
        ve.wait_ge(c.S["mm"], bb_mm)
        c.inc(ve.tensor_copy(c.T["bias_cols"][:, :], bb_ps[:, :]), "dv", 1)
    block.vector(_)

    def _(te):
        te.wait_ge(c.S["dv"], c.cnt["dv"])
    block.tensor(_)
    st.close()


def p3_flash_l0(c):
    nc, block = c.nc, c.block
    A = c.A
    Q_sb = c.T["Q_sb"]
    bias_cols = c.T["bias_cols"]
    st = contextlib.ExitStack()
    Kbuf = [c.sb(st, f"l0_K{i}", [128, 8192], BF16) for i in range(2)]
    Vbuf = [c.sb(st, f"l0_V{i}", [128, 65 * 64], BF16) for i in range(2)]
    NBUF = 3
    probs = [c.sb(st, f"l0_pr{i}", [128, 1024], BF16) for i in range(NBUF)]
    oacc_sb = c.sb(st, "l0_oacc", [128, 1024], F32)
    otmp = c.sb(st, "l0_otmp", [64, 1024], BF16)
    rc = c.sb(st, "l0_rc", [1, 1024], F32)
    S_ps = [c.ps(st, f"l0_S{i}", [128, 1024], F32) for i in range(NBUF)]
    oa_ps = c.ps(st, "l0_oa", [65, 1024], F32)

    def _(g):
        c.fence(g)
        c.inc(g.memset(Vbuf[0][:, :], 1.0), "pl", 1)
        c.inc(g.memset(Vbuf[1][:, :], 1.0), "pl", 1)
    block.gpsimd(_)
    ones_set = c.cnt["pl"]

    KB = 64
    it_list = [(b, h) for h in range(16) for b in range(2)]
    ld_done = {}
    it_done = {}

    for it, (b, h) in enumerate(it_list):
        def _(sy, it=it, b=b, h=h):
            if it == 0:
                c.fence(sy)
                sy.wait_ge(c.S["pl"], ones_set)
            if it >= 2:
                sy.wait_ge(c.S["mm"], it_done[it - 2])
            buf = it % 2
            sy.wait_ge(c.S["cc"], c.p2_cc_g0 if h < 8 else c.p2_cc_g1)
            gk = A[f"gath0k{h // 8}"]
            gv = A[f"gath0v{h // 8}"]
            hl = h % 8
            for r in range(8):
                for hb in range(2):
                    c.inc(sy.dma_start(
                        out=Kbuf[buf][hb * 64: hb * 64 + 64, r * 1024:(r + 1) * 1024],
                        in_=gk[r * 512 + 64 * hl: r * 512 + 64 * hl + 64,
                               b * 1024:(b + 1) * 1024]), "wd", 16)
            for r in range(8):
                src = gv[r * 2048 + b * 1024: r * 2048 + (b + 1) * 1024,
                         64 * hl: 64 * hl + 64]
                src = src.rearrange("(blk p) n -> p blk n", p=128)
                dst = Vbuf[buf][:, :].rearrange("p (blk n) -> p blk n",
                                                n=65)[:, r * 8:(r + 1) * 8, 0:64]
                c.inc(sy.dma_start(out=dst, in_=src), "wd", 16)
        block.sync(_)
        ld_done[it] = c.cnt["wd"]

        buf = it % 2
        qcol0 = h * 2048 + b * 1024
        prow0 = 64 * (h % 2)
        mm_base = c.cnt["mm"]
        ac_base = c.cnt["ac"]
        qk_sched = {kb: mm_base + (kb + 1 if kb < 2 else 2 * kb - 1) for kb in range(KB)}
        exp_sched = {kb: ac_base + kb + 1 for kb in range(KB)}

        def _(te, it=it, buf=buf, qcol0=qcol0):
            for kb in range(KB):
                if kb == 0:
                    te.wait_ge(c.S["wd"], ld_done[it])
                    if it >= 1:
                        te.wait_ge(c.S["ac"], exp_prev_last[0])
                        te.wait_ge(c.S["dv"], epi_prev[0])
                if kb >= NBUF:
                    te.wait_ge(c.S["ac"], exp_sched[kb - NBUF])
                for half in range(2):
                    ins = te.matmul(
                        S_ps[kb % NBUF][:, half * 512:(half + 1) * 512],
                        Kbuf[buf][:, kb * 128:(kb + 1) * 128],
                        Q_sb[:, qcol0 + half * 512: qcol0 + (half + 1) * 512],
                        start=True, stop=True)
                c.inc(ins, "mm", 1)
                if kb >= 2:
                    pkb = kb - 2
                    te.wait_ge(c.S["ac"], exp_sched[pkb])
                    for half in range(2):
                        ins = te.matmul(
                            oa_ps[0:65, half * 512:(half + 1) * 512],
                            Vbuf[buf][:, 65 * pkb: 65 * pkb + 65],
                            probs[pkb % NBUF][:, half * 512:(half + 1) * 512],
                            start=(pkb == 0), stop=False)
                    c.inc(ins, "mm", 1)
            for pkb in (KB - 2, KB - 1):
                te.wait_ge(c.S["ac"], exp_sched[pkb])
                for half in range(2):
                    ins = te.matmul(oa_ps[0:65, half * 512:(half + 1) * 512],
                                    Vbuf[buf][:, 65 * pkb: 65 * pkb + 65],
                                    probs[pkb % NBUF][:, half * 512:(half + 1) * 512],
                                    start=False, stop=(pkb == KB - 1))
                c.inc(ins, "mm", 1)
        block.tensor(_)
        assert c.cnt["mm"] == mm_base + 2 * KB

        def _(se, b=b, h=h):
            for kb in range(KB):
                se.wait_ge(c.S["mm"], qk_sched[kb])
                c.inc(se.activation(probs[kb % NBUF][:, :], S_ps[kb % NBUF][:, :], AF.Exp,
                                    bias=bias_cols[:, b * 16 + h: b * 16 + h + 1],
                                    scale=1.0 / 64.0), "ac", 1)
        block.scalar(_)
        assert c.cnt["ac"] == ac_base + KB
        it_done[it] = c.cnt["mm"]
        exp_prev_last = (ac_base + KB,)

        def _(ve, it=it):
            ve.wait_ge(c.S["mm"], it_done[it])
            c.inc(ve.tensor_copy(oacc_sb[0:65, :], oa_ps[0:65, :]), "dv", 1)
            c.inc(ve.reciprocal(rc[0:1, :], oacc_sb[64:65, :]), "dv", 1)
        block.vector(_)
        rc_dv = c.cnt["dv"]

        def _(te, it=it):
            te.wait_ge(c.S["dv"], rc_dv)
            for half in range(2):
                ins = te.matmul(S_ps[0][0:64, half * 512:(half + 1) * 512],
                                c.T["ones_row"][0:1, 0:64],
                                rc[0:1, half * 512:(half + 1) * 512],
                                start=True, stop=True)
            c.inc(ins, "mm", 1)
        block.tensor(_)
        rb_mm = c.cnt["mm"]

        def _(ve, it=it, rb_mm=rb_mm):
            ve.wait_ge(c.S["mm"], rb_mm)
            c.inc(ve.tensor_tensor(otmp[:, :], S_ps[0][0:64, :], oacc_sb[0:64, :],
                                   op=ALU.mult), "dv", 1)
        block.vector(_)
        mul_dv = c.cnt["dv"]

        xsl = c.T["x_c"][64 * (h % 2): 64 * (h % 2) + 64,
                         (h // 2) * 2048 + b * 1024: (h // 2) * 2048 + (b + 1) * 1024]
        if h % 2 == 0:
            def _(ve, xsl=xsl, mul_dv=mul_dv):
                c.inc(ve.tensor_tensor(xsl, xsl, otmp[:, :], op=ALU.add), "dv", 1)
            block.vector(_)
        else:
            def _(te, mul_dv=mul_dv):
                te.wait_ge(c.S["dv"], mul_dv)
                for half in range(2):
                    ins = te.matmul(S_ps[1][64:128, half * 512:(half + 1) * 512],
                                    c.T["ident_bf"][0:64, 0:64],
                                    otmp[:, half * 512:(half + 1) * 512],
                                    start=True, stop=True)
                c.inc(ins, "mm", 1)
            block.tensor(_)
            mv_mm = c.cnt["mm"]

            def _(ve, xsl=xsl, mv_mm=mv_mm):
                ve.wait_ge(c.S["mm"], mv_mm)
                c.inc(ve.tensor_tensor(xsl, xsl, S_ps[1][64:128, :], op=ALU.add), "dv", 1)
            block.vector(_)
        epi_prev = (c.cnt["dv"],)

    def _(te):
        te.wait_ge(c.S["dv"], c.cnt["dv"])
    block.tensor(_)
    st.close()



def p4_ffn_l0(c):
    nc, block = c.nc, c.block
    A = c.A
    st = contextlib.ExitStack()
    h1 = c.sb(st, "h1_bf", [128, 16384], BF16)
    c.T["h1_bf"] = h1
    p_layernorm(c, "h1_bf", eps=1e-5, tag="_p4")

    ones_row = c.T["ones_row"]

    cst = contextlib.ExitStack()
    gw_sb = c.sb(st, "gw_sb", [128, 64], BF16)
    bvec = c.sb(st, "bvec", [128, 36], F32)
    b2_sb = c.sb(st, "b2_sb", [8, 1024], BF16)
    bs2row = c.sb(st, "bs2row", [1, 1024], BF16)
    sel8 = c.sb(st, "sel8_sb", [8, 1024], BF16)
    ones_bfr = c.sb(st, "ones_bfr", [1, 512], BF16)
    rs_col = c.sb(st, "rs_col", [128, 1], F32)
    routeT = c.sb(st, "routeT", [8, 2048], BF16)
    gfb_sb = c.sb(st, "gfb_sb", [128, 2048], BF16)
    s_sb = c.sb(st, "s_sb", [128, 4 * 2048], BF16)
    ge_tmp = [c.sb(st, f"ge_tmp{i}", [128, 1024], BF16) for i in range(2)]

    b_stage = c.sb(cst, "b_stage", [8, 1024], F32)

    def _(g):
        c.fence(g)
        c.inc(g.dma_start(out=gw_sb[:, :].rearrange("p (fc e) -> p fc e", fc=8),
                          in_=A["gate_W"][:, :].rearrange("(fc p) e -> p fc e", p=128)),
              "gd", 16)
        c.inc(g.dma_start(out=b2_sb[:, :], in_=A["b2"][:, :]), "gd", 16)
        c.inc(g.dma_start(out=bs2row[:, :], in_=A["bs2"][:, :]), "gd", 16)
        c.inc(g.dma_start(out=sel8[:, :], in_=A["sel8"][:, :]), "gd", 16)
        c.inc(g.memset(ones_bfr[:, :], 1.0), "pl", 1)
        c.inc(g.dma_start(out=b_stage[0:8, 0:512], in_=A["b1"][:, :]), "gd", 16)
        c.inc(g.dma_start(out=b_stage[0:1, 512:1024], in_=A["bs1"][:, :]), "gd", 16)
    block.gpsimd(_)
    consts_gd = c.cnt["gd"]

    bps = c.ps(cst, "b_ps", [128, 40], F32)
    rs_row = c.sb(cst, "rs_row", [1, 1], F32)

    def _(g):
        c.inc(g.dma_start(out=rs_row[:, :], in_=A["rs"][:, :]), "gd", 16)
    block.gpsimd(_)
    rs_gd = c.cnt["gd"]

    def _(te):
        c.fence(te)
        te.wait_ge(c.S["gd"], consts_gd)
        te.wait_ge(c.S["gd"], rs_gd)
        for hc in range(4):
            te.transpose(bps[:, hc * 8:(hc + 1) * 8],
                         b_stage[0:8, hc * 128:(hc + 1) * 128], c.T["ident"][0:8, 0:8])
        for hc in range(4):
            te.transpose(bps[:, 32 + hc: 33 + hc],
                         b_stage[0:1, 512 + hc * 128: 512 + (hc + 1) * 128],
                         c.T["ident"][0:1, 0:1])
        c.inc(te.matmul(bps[:, 36:37], ones_row[:, :], rs_row[:, :],
                        start=True, stop=True), "mm", 1)
    block.tensor(_)
    bt_mm = c.cnt["mm"]

    def _(ve):
        ve.wait_ge(c.S["mm"], bt_mm)
        ve.tensor_copy(bvec[:, :], bps[:, 0:36])
        c.inc(ve.tensor_copy(rs_col[:, :], bps[:, 36:37]), "dv", 1)
    block.vector(_)
    bt_dv = c.cnt["dv"]

    def _(te):
        te.wait_ge(c.S["dv"], bt_dv)
    block.tensor(_)
    cst.close()

    gate_st = contextlib.ExitStack()
    lg_ps = c.ps(gate_st, "lg_ps", [128, 8], F32)
    rt_ps = c.ps(gate_st, "rt_ps", [8, 128], F32)
    gsc = c.sb(gate_st, "gate_sc", [128, 64], F32)
    sg = gsc[:, 0:8]
    s2 = gsc[:, 8:16]
    eq1 = gsc[:, 16:24]
    s3 = gsc[:, 24:32]
    eq2 = gsc[:, 32:40]
    route_t = gsc[:, 40:48]
    colv = gsc[:, 48:56]
    m1 = colv[:, 0:1]
    m2 = colv[:, 1:2]
    wsum = colv[:, 2:3]
    rw = colv[:, 3:4]
    w1c = colv[:, 4:5]
    w2c = colv[:, 5:6]

    lg_done = {}
    sg_done = {}
    gm_done = {}
    tr_done = {}
    rt_done = {}
    for tt in range(16):
        def _(te, tt=tt):
            if tt == 0:
                c.psum_guard(te)
            else:
                te.wait_ge(c.S["ac"], sg_done[tt - 1])
                te.wait_ge(c.S["ac"], rt_done.get(tt - 1, 0))
            for fc in range(NF):
                ins = te.matmul(lg_ps[:, :],
                                h1[:, fc * 2048 + tt * 128: fc * 2048 + (tt + 1) * 128],
                                gw_sb[:, fc * 8:(fc + 1) * 8],
                                start=(fc == 0), stop=(fc == NF - 1))
            c.inc(ins, "mm", 1)
        block.tensor(_)
        lg_done[tt] = c.cnt["mm"]

        def _(se, tt=tt):
            se.wait_ge(c.S["mm"], lg_done[tt])
            if tt >= 1:
                se.wait_ge(c.S["dv"], gm_done[tt - 1])
            c.inc(se.activation(sg, lg_ps[:, :], AF.Sigmoid), "ac", 1)
        block.scalar(_)
        sg_done[tt] = c.cnt["ac"]

        def _(ve, tt=tt):
            ve.wait_ge(c.S["ac"], sg_done[tt])
            if tt >= 1:
                ve.wait_ge(c.S["mm"], tr_done[tt - 1])
            ve.tensor_scalar(s2, sg, rs_col[:, 0:1], None, op0=ALU.mult)
            ve.drain()
            ve.reduce_max(m1, s2, axis=AX.X)
            ve.drain()
            ve.tensor_scalar(eq1, s2, m1, None, op0=ALU.is_equal)
            ve.drain()
            ve.tensor_scalar(s3, eq1, -1e30, None, op0=ALU.mult)
            ve.drain()
            ve.tensor_tensor(s3, s3, s2, op=ALU.add)
            ve.drain()
            ve.reduce_max(m2, s3, axis=AX.X)
            ve.drain()
            ve.tensor_scalar(eq2, s3, m2, None, op0=ALU.is_equal)
            ve.tensor_tensor(wsum, m1, m2, op=ALU.add)
            ve.drain()
            ve.reciprocal(rw, wsum)
            ve.drain()
            ve.tensor_tensor(w1c, m1, rw, op=ALU.mult)
            ve.tensor_tensor(w2c, m2, rw, op=ALU.mult)
            ve.drain()
            ve.tensor_scalar(route_t, eq1, w1c, None, op0=ALU.mult)
            ve.drain()
            c.inc(ve.scalar_tensor_tensor(route_t, eq2, w2c, route_t,
                                          op0=ALU.mult, op1=ALU.add), "dv", 1)
        block.vector(_)
        gm_done[tt] = c.cnt["dv"]

        def _(te, tt=tt):
            te.wait_ge(c.S["dv"], gm_done[tt])
            c.inc(te.transpose(rt_ps[:, :], route_t, c.T["ident"][:, :]), "mm", 1)
        block.tensor(_)
        tr_done[tt] = c.cnt["mm"]

        def _(se, tt=tt):
            se.wait_ge(c.S["mm"], tr_done[tt])
            c.inc(se.copy(routeT[0:8, tt * 128:(tt + 1) * 128], rt_ps[0:8, :]), "ac", 1)
        block.scalar(_)
        rt_done[tt] = c.cnt["ac"]

    if "dbg_gsc" in A:
        def _(g):
            g.wait_ge(c.S["ac"], rt_done[15])
            c.inc(g.dma_start(out=A["dbg_gsc"][:, :], in_=gsc[:, :]), "gd", 16)
            g.wait_ge(c.S["gd"], c.cnt["gd"])
        block.gpsimd(_)

    def _(te):
        te.wait_ge(c.S["ac"], rt_done[15])
        if "dbg_gsc" in A:
            te.wait_ge(c.S["gd"], c.cnt["gd"])
    block.tensor(_)
    gate_st.close()

    wbuf_st = contextlib.ExitStack()
    w1_sb = [c.sb(wbuf_st, f"w1_sb{i}", [128, 8 * 512], BF16) for i in range(2)]
    w2_sb = [c.sb(wbuf_st, f"w2_sb{i}", [128, 4 * 1024], BF16) for i in range(2)]
    h_ps = [c.ps(wbuf_st, f"h_ps{i}", [128, 1024], F32) for i in range(2)]
    y_ps = [c.ps(wbuf_st, f"y_ps{i}", [128, 1024], F32) for i in range(2)]

    wld = {}
    mlp1_done = {}
    mlp2_done = {}
    gelu_done = {}
    smul_done = {}
    yadd_done = {}
    gfb_done = {}

    for e in range(9):
        shared = (e == 8)

        def _(g, e=e, shared=shared):
            if e == 0:
                c.fence(g)
            if e >= 2:
                g.wait_ge(c.S["mm"], mlp1_done[e - 2])
                g.wait_ge(c.S["mm"], mlp2_done[e - 2])
            if shared:
                s1, s2_ = A["Ws1"][:, :], A["Ws2"][:, :]
            else:
                s1 = A["W1"][e, :, :]
                s2_ = A["W2"][e, :, :]
            c.inc(g.dma_start(out=w1_sb[e % 2][:, :].rearrange("p (fc h) -> p fc h", fc=8),
                              in_=s1.rearrange("(fc p) h -> p fc h", p=128)), "gd", 16)
            c.inc(g.dma_start(out=w2_sb[e % 2][:, :].rearrange("p (hc n) -> p hc n", hc=4),
                              in_=s2_.rearrange("(hc p) n -> p hc n", p=128)), "gd", 16)
        block.gpsimd(_)
        wld[e] = c.cnt["gd"]

        if not shared:
            def _(te, e=e):
                te.wait_ge(c.S["ac"], c.cnt["ac"])
                if e >= 1:
                    te.wait_ge(c.S["dv"], yadd_done[e - 1])
                for yb in range(2):
                    for q2 in range(2):
                        ins = te.matmul(
                            y_ps[yb][:, q2 * 512:(q2 + 1) * 512],
                            sel8[:, e * 128:(e + 1) * 128],
                            routeT[0:8, yb * 1024 + q2 * 512: yb * 1024 + (q2 + 1) * 512],
                            start=True, stop=True)
                c.inc(ins, "mm", 1)
            block.tensor(_)
            gfb_mm = c.cnt["mm"]

            def _(ve, e=e, gfb_mm=gfb_mm):
                ve.wait_ge(c.S["mm"], gfb_mm)
                ve.tensor_copy(gfb_sb[:, 0:1024], y_ps[0][:, :])
                c.inc(ve.tensor_copy(gfb_sb[:, 1024:2048], y_ps[1][:, :]), "dv", 1)
            block.vector(_)
            gfb_done[e] = c.cnt["dv"]

            if e == 0 and "dbg_gfb" in A:
                def _(g):
                    g.wait_ge(c.S["dv"], gfb_done[0])
                    c.inc(g.dma_start(out=A["dbg_gfb"][:, :], in_=gfb_sb[:, :]), "gd", 16)
                    c.inc(g.dma_start(out=A["dbg_route"][:, :], in_=routeT[:, :]), "gd", 16)
                block.gpsimd(_)

        ac_base4 = c.cnt["ac"]
        gelu_sched = {j: ac_base4 + j + 1 for j in range(8)}
        mlp1_mm = {}
        jlist = [(hc, th) for hc in range(4) for th in range(2)]
        for j, (hc, th) in enumerate(jlist):
            def _(te, e=e, j=j, hc=hc, th=th):
                if j == 0:
                    te.wait_ge(c.S["gd"], wld[e])
                    if not shared:
                        te.wait_ge(c.S["dv"], gfb_done[e])
                if j >= 2:
                    te.wait_ge(c.S["ac"], gelu_sched[j - 2])
                for q2 in range(2):
                    for fc in range(NF):
                        ins = te.matmul(
                            h_ps[j % 2][:, q2 * 512:(q2 + 1) * 512],
                            w1_sb[e % 2][:, fc * 512 + hc * 128: fc * 512 + (hc + 1) * 128],
                            h1[:, fc * 2048 + th * 1024 + q2 * 512:
                               fc * 2048 + th * 1024 + (q2 + 1) * 512],
                            start=(fc == 0), stop=(fc == NF - 1))
                c.inc(ins, "mm", 1)
            block.tensor(_)
            mlp1_done[e] = c.cnt["mm"]
            mlp1_mm[j] = c.cnt["mm"]

            if not shared:
                def _(ve, e=e, j=j, hc=hc, th=th):
                    ve.wait_ge(c.S["ac"], gelu_sched[j])
                    dst = s_sb[:, hc * 2048 + th * 1024: hc * 2048 + (th + 1) * 1024]
                    c.inc(ve.tensor_tensor(
                        dst, ge_tmp[j % 2][:, :],
                        gfb_sb[:, th * 1024:(th + 1) * 1024], op=ALU.mult), "dv", 1)
                block.vector(_)
                smul_done[(e, j)] = c.cnt["dv"]

        def _(se, e=e, shared=shared):
            for j, (hc, th) in enumerate(jlist):
                se.wait_ge(c.S["mm"], mlp1_mm[j])
                bias = (bvec[:, (32 + hc):(33 + hc)] if shared
                        else bvec[:, hc * 8 + e: hc * 8 + e + 1])
                if shared:
                    dst = s_sb[:, hc * 2048 + th * 1024: hc * 2048 + (th + 1) * 1024]
                    c.inc(se.activation(dst, h_ps[j % 2][:, :], AF.Gelu, bias=bias),
                          "ac", 1)
                else:
                    if j >= 2:
                        se.wait_ge(c.S["dv"], smul_done[(e, j - 2)])
                    c.inc(se.activation(ge_tmp[j % 2][:, :], h_ps[j % 2][:, :], AF.Gelu,
                                        bias=bias), "ac", 1)
        block.scalar(_)
        assert c.cnt["ac"] == ac_base4 + 8

        s_ready = c.cnt["dv"] if not shared else None
        s_ready_ac = c.cnt["ac"]

        for j, (fc, th) in enumerate([(fc, th) for fc in range(NF) for th in range(2)]):
            def _(te, e=e, j=j, fc=fc, th=th, shared=shared,
                  s_ready=s_ready, s_ready_ac=s_ready_ac):
                if j == 0:
                    if shared:
                        te.wait_ge(c.S["ac"], s_ready_ac)
                    else:
                        te.wait_ge(c.S["dv"], s_ready)
                if j >= 2:
                    te.wait_ge(c.S["dv"], yadd_done[(e, j - 2)])
                for q2 in range(2):
                    col0 = th * 1024 + q2 * 512
                    for hc in range(4):
                        ins = te.matmul(
                            y_ps[j % 2][:, q2 * 512:(q2 + 1) * 512],
                            w2_sb[e % 2][:, hc * 1024 + fc * 128: hc * 1024 + (fc + 1) * 128],
                            s_sb[:, hc * 2048 + col0: hc * 2048 + col0 + 512],
                            start=(hc == 0), stop=(hc == 3 and not shared))
                    if shared:
                        te.matmul(y_ps[j % 2][:, q2 * 512:(q2 + 1) * 512],
                                  b2_sb[0:8, fc * 128:(fc + 1) * 128],
                                  routeT[0:8, col0: col0 + 512],
                                  start=False, stop=False)
                        ins = te.matmul(y_ps[j % 2][:, q2 * 512:(q2 + 1) * 512],
                                        bs2row[0:1, fc * 128:(fc + 1) * 128],
                                        ones_bfr[0:1, 0:512],
                                        start=False, stop=True)
                c.inc(ins, "mm", 1)
            block.tensor(_)
            mlp2_done[e] = c.cnt["mm"]
            y_mm = c.cnt["mm"]

            def _(ve, e=e, j=j, fc=fc, th=th, y_mm=y_mm):
                ve.wait_ge(c.S["mm"], y_mm)
                xsl = c.T["x_c"][:, fc * 2048 + th * 1024: fc * 2048 + (th + 1) * 1024]
                c.inc(ve.tensor_tensor(xsl, xsl, y_ps[j % 2][:, :], op=ALU.add), "dv", 1)
            block.vector(_)
            yadd_done[(e, j)] = c.cnt["dv"]
        yadd_done[e] = c.cnt["dv"]

    def _(te):
        te.wait_ge(c.S["dv"], c.cnt["dv"])
    block.tensor(_)
    wbuf_st.close()
    st.close()



def p5_proj_l1(c, b, st):
    nc, block = c.nc, c.block
    A = c.A
    Q1, K1, V1 = c.T["Q1_sb"], c.T["K1_sb"], c.T["V1_sb"]
    wq1, wk1, wv1 = c.T["wq1_sb"], c.T["wk1_sb"], c.T["wv1_sb"]
    cosT, sinT, pmat = c.T["cosT"], c.T["sinT"], c.T["pmat"]
    ones_col = c.T["ones_col_bf"]
    ones_row = c.T["ones_row"]

    pst = contextlib.ExitStack()
    h2r = c.sb(pst, f"h2r_{b}", [128, 8 * 1024], BF16)
    proj_ps_st = contextlib.ExitStack()
    q_ps = c.ps(proj_ps_st, f"l1q_ps_{b}", [128, 1024], F32)
    k_ps = c.ps(proj_ps_st, f"l1k_ps_{b}", [128, 1024], F32)
    v_ps = c.ps(proj_ps_st, f"l1v_ps_{b}", [128, 1024], F32)

    ld_done = {}
    ev_done = {}
    for r in range(8):
        def _(g, r=r):
            if r == 0:
                c.fence(g)
                g.wait_ge(c.S["cc"], c.p5_cc_a)
            else:
                g.wait_ge(c.S["mm"], ld_done[r - 1][1])
            src = A["gath1a"][r * 512:(r + 1) * 512, b * 1024:(b + 1) * 1024]
            c.inc(g.dma_start(out=h2r[:, 0:4096].rearrange("p (fc n) -> p fc n", fc=4),
                              in_=src.rearrange("(fc p) n -> p fc n", p=128)), "gd", 16)
            if r == 0:
                g.wait_ge(c.S["cc"], c.p5_cc)
            src = A["gath1b"][r * 512:(r + 1) * 512, b * 1024:(b + 1) * 1024]
            c.inc(g.dma_start(out=h2r[:, 4096:8192].rearrange("p (fc n) -> p fc n", fc=4),
                              in_=src.rearrange("(fc p) n -> p fc n", p=128)), "gd", 16)
        block.gpsimd(_)
        gd_b = c.cnt["gd"]
        gd_a = gd_b - 16

        def _(te, r=r, gd_a=gd_a, gd_b=gd_b):
            te.wait_ge(c.S["gd"], gd_b)
            if r == 0:
                c.psum_guard(te)
            else:
                te.wait_ge(c.S["ac"], ev_done[r - 1])
            for th in range(2):
                for fc in range(NF):
                    te.matmul(q_ps[:, th * 512:(th + 1) * 512],
                              wq1[:, fc * 128:(fc + 1) * 128],
                              h2r[:, fc * 1024 + th * 512: fc * 1024 + (th + 1) * 512],
                              start=(fc == 0), stop=(fc == NF - 1))
            for th in range(2):
                for fc in range(NF):
                    te.matmul(k_ps[:, th * 512:(th + 1) * 512],
                              wk1[:, fc * 128:(fc + 1) * 128],
                              h2r[:, fc * 1024 + th * 512: fc * 1024 + (th + 1) * 512],
                              start=(fc == 0), stop=(fc == NF - 1))
            for s in range(8):
                for fc in range(NF):
                    ins = te.matmul(v_ps[:, s * 128:(s + 1) * 128],
                                    h2r[:, fc * 1024 + s * 128: fc * 1024 + (s + 1) * 128],
                                    wv1[:, fc * 128:(fc + 1) * 128],
                                    start=(fc == 0), stop=(fc == NF - 1))
            c.inc(ins, "mm", 1)
        block.tensor(_)
        ld_done[r] = (gd_b, c.cnt["mm"])
        mm_here = c.cnt["mm"]

        def _(se, r=r, mm_here=mm_here):
            se.wait_ge(c.S["mm"], mm_here)
            se.copy(Q1[:, r * 1024:(r + 1) * 1024], q_ps[:, :])
            se.copy(K1[:, r * 1024:(r + 1) * 1024], k_ps[:, :])
            c.inc(se.copy(V1[:, r * 1024:(r + 1) * 1024], v_ps[:, :]), "ac", 1)
        block.scalar(_)
        ev_done[r] = c.cnt["ac"]

    proj_ps_st.close()

    rope_ps_st = contextlib.ExitStack()
    rot_ps = [c.ps(rope_ps_st, f"rot_ps{i}_{b}", [128, 512], F32) for i in range(2)]
    rsc = c.sb(pst, f"rope_sc_{b}", [128, 1024], F32)
    rope_mm = {}
    rope_dv = {}
    for ti, (tgt, tile) in enumerate([(t, i) for t in (Q1, K1) for i in range(16)]):
        def _(te, ti=ti, tgt=tgt, tile=tile):
            if ti == 0:
                te.wait_ge(c.S["ac"], ev_done[7])
                c.psum_guard(te)
            if ti >= 2:
                te.wait_ge(c.S["dv"], rope_dv[ti - 2])
            c.inc(te.matmul(rot_ps[ti % 2][:, :], pmat[:, :],
                            tgt[:, tile * 512:(tile + 1) * 512],
                            start=True, stop=True), "mm", 1)
        block.tensor(_)
        rope_mm[ti] = c.cnt["mm"]

        def _(ve, ti=ti, tgt=tgt, tile=tile):
            ve.wait_ge(c.S["mm"], rope_mm[ti])
            sl = slice(tile * 512, (tile + 1) * 512)
            ve.tensor_tensor(rsc[:, 0:512], rot_ps[ti % 2][:, :], sinT[:, sl], op=ALU.mult)
            ve.tensor_tensor(rsc[:, 512:1024], tgt[:, sl], cosT[:, sl], op=ALU.mult)
            ve.drain()
            c.inc(ve.tensor_tensor(tgt[:, sl], rsc[:, 0:512], rsc[:, 512:1024],
                                   op=ALU.add), "dv", 1)
        block.vector(_)
        rope_dv[ti] = c.cnt["dv"]

    rope_ps_st.close()

    bnd_ps_st = contextlib.ExitStack()
    nb_ps = c.ps(bnd_ps_st, f"nb_ps_{b}", [1, 2048], F32)
    bb_ps = c.ps(bnd_ps_st, f"l1bb_ps_{b}", [128, 2], F32)
    sqc = c.sb(pst, f"l1_sqc_{b}", [128, 2048], BF16)
    nrow = c.sb(pst, f"l1_nrow_{b}", [1, 16], F32)
    for si, src in enumerate((Q1, K1)):
        for tc in range(4):
            def _(ve, si=si, tc=tc, src=src):
                if (si, tc) == (0, 0):
                    ve.wait_ge(c.S["dv"], rope_dv[31])
                else:
                    ve.wait_ge(c.S["mm"], c.cnt["mm"])
                c.inc(ve.tensor_tensor(sqc[:, :], src[:, tc * 2048:(tc + 1) * 2048],
                                       src[:, tc * 2048:(tc + 1) * 2048], op=ALU.mult),
                      "dv", 1)
            block.vector(_)
            sq_dv = c.cnt["dv"]

            def _(te, si=si, tc=tc, sq_dv=sq_dv):
                te.wait_ge(c.S["dv"], sq_dv)
                if (si, tc) == (0, 0):
                    c.psum_guard(te)
                for q in range(4):
                    ins = te.matmul(nb_ps[0:1, q * 512:(q + 1) * 512], ones_col[:, 0:1],
                                    sqc[:, q * 512:(q + 1) * 512], start=True, stop=True)
                c.inc(ins, "mm", 1)
            block.tensor(_)
            nb_mm = c.cnt["mm"]

            def _(ve, si=si, tc=tc, nb_mm=nb_mm):
                ve.wait_ge(c.S["mm"], nb_mm)
                c.inc(ve.reduce_max(nrow[0:1, tc:tc + 1], nb_ps[0:1, :], axis=AX.X),
                      "dv", 1)
            block.vector(_)

        def _(ve, si=si):
            ve.drain()
            ve.tensor_tensor(nrow[0:1, 0:2], nrow[0:1, 0:2], nrow[0:1, 2:4], op=ALU.max)
            ve.drain()
            c.inc(ve.tensor_tensor(nrow[0:1, 4 + si:5 + si], nrow[0:1, 0:1],
                                   nrow[0:1, 1:2], op=ALU.max), "dv", 1)
        block.vector(_)

    def _(ve):
        ve.drain()
        c.inc(ve.tensor_tensor(nrow[0:1, 6:7], nrow[0:1, 4:5], nrow[0:1, 5:6],
                               op=ALU.mult), "dv", 1)
    block.vector(_)
    t_dv = c.cnt["dv"]

    def _(se):
        se.wait_ge(c.S["dv"], t_dv)
        c.inc(se.activation(nrow[0:1, 7:8], nrow[0:1, 6:7], AF.Sqrt), "ac", 1)
    block.scalar(_)
    s_ac = c.cnt["ac"]

    def _(ve):
        ve.wait_ge(c.S["ac"], s_ac)
        c.inc(ve.tensor_scalar(nrow[0:1, 8:9], nrow[0:1, 7:8],
                               -1.0 / float(np.sqrt(128.0)), None, op0=ALU.mult), "dv", 1)
    block.vector(_)
    br_dv = c.cnt["dv"]

    def _(te):
        te.wait_ge(c.S["dv"], br_dv)
        c.inc(te.matmul(bb_ps[:, b:b + 1], ones_row[:, :], nrow[0:1, 8:9],
                        start=True, stop=True), "mm", 1)
    block.tensor(_)
    bb_mm = c.cnt["mm"]

    def _(ve):
        ve.wait_ge(c.S["mm"], bb_mm)
        c.inc(ve.tensor_copy(c.T["bias1"][:, b:b + 1], bb_ps[:, b:b + 1]), "dv", 1)
    block.vector(_)

    def _(te):
        te.wait_ge(c.S["dv"], c.cnt["dv"])
    block.tensor(_)
    bnd_ps_st.close()
    pst.close()


def p6_flash_l1(c, b, st):
    nc, block = c.nc, c.block
    A = c.A
    Q1, K1, V1 = c.T["Q1_sb"], c.T["K1_sb"], c.T["V1_sb"]
    masks = c.T["masks_sb"]
    ones_col = c.T["ones_col_bf"]

    fst = contextlib.ExitStack()
    probs = [c.sb(fst, f"l1_pr{i}_{b}", [128, 1024], BF16) for i in range(2)]
    oacc = c.sb(fst, f"l1_oacc_{b}", [128, 1024], F32)
    obuf = c.sb(fst, f"l1_obuf_{b}", [128, 1024], BF16)
    rc = c.sb(fst, f"l1_rc_{b}", [1, 1024], F32)
    S_ps = [c.ps(fst, f"l1_S{i}_{b}", [128, 1024], F32) for i in range(2)]
    oa_ps = c.ps(fst, f"l1_oa_{b}", [128, 1024], F32)
    sum_ps = c.ps(fst, f"l1_sum_{b}", [1, 1024], F32)

    opay_done = 0
    for qh in range(8):
        nkb = 8 * qh + 8
        mm_base = c.cnt["mm"]
        ac_base = c.cnt["ac"]
        dv_base = c.cnt["dv"]
        qk_sched = {kb: mm_base + (1 if kb == 0 else 2 * kb) for kb in range(nkb)}
        exp_sched = {kb: ac_base + kb + 1 for kb in range(nkb)}
        mask_sched = {kb: dv_base + (kb - 8 * qh) + 1 for kb in range(8 * qh, nkb)}

        def _(te, qh=qh, nkb=nkb):
            for kb in range(nkb):
                if qh == 0 and kb == 0:
                    c.fence(te)
                if kb >= 2:
                    te.wait_ge(c.S["ac"], exp_sched[kb - 2])
                elif kb < 2 and qh > 0:
                    te.wait_ge(c.S["dv"], epi_dv)
                for half in range(2):
                    ins = te.matmul(
                        S_ps[kb % 2][:, half * 512:(half + 1) * 512],
                        K1[:, kb * 128:(kb + 1) * 128],
                        Q1[:, qh * 1024 + half * 512: qh * 1024 + (half + 1) * 512],
                        start=True, stop=True)
                c.inc(ins, "mm", 1)
                if kb >= 1:
                    te.wait_ge(c.S["ac"], exp_sched[kb - 1])
                    pkb = kb - 1
                    for half in range(2):
                        te.matmul(oa_ps[:, half * 512:(half + 1) * 512],
                                  V1[:, pkb * 128:(pkb + 1) * 128],
                                  probs[pkb % 2][:, half * 512:(half + 1) * 512],
                                  start=(pkb == 0), stop=False)
                        ins = te.matmul(sum_ps[0:1, half * 512:(half + 1) * 512],
                                        ones_col[:, 0:1],
                                        probs[pkb % 2][:, half * 512:(half + 1) * 512],
                                        start=(pkb == 0), stop=False)
                    c.inc(ins, "mm", 1)
            te.wait_ge(c.S["ac"], exp_sched[nkb - 1])
            pkb = nkb - 1
            for half in range(2):
                te.matmul(oa_ps[:, half * 512:(half + 1) * 512],
                          V1[:, pkb * 128:(pkb + 1) * 128],
                          probs[pkb % 2][:, half * 512:(half + 1) * 512],
                          start=(pkb == 0), stop=True)
                ins = te.matmul(sum_ps[0:1, half * 512:(half + 1) * 512],
                                ones_col[:, 0:1],
                                probs[pkb % 2][:, half * 512:(half + 1) * 512],
                                start=(pkb == 0), stop=True)
            c.inc(ins, "mm", 1)
        block.tensor(_)
        assert c.cnt["mm"] == mm_base + 2 * nkb

        def _(ve, qh=qh, nkb=nkb):
            for kb in range(8 * qh, nkb):
                ve.wait_ge(c.S["mm"], qk_sched[kb])
                j = kb - 8 * qh
                c.inc(ve.tensor_tensor(S_ps[kb % 2][:, :], S_ps[kb % 2][:, :],
                                       masks[:, j * 1024:(j + 1) * 1024],
                                       op=ALU.add), "dv", 1)
        block.vector(_)
        assert c.cnt["dv"] == dv_base + 8

        def _(se, qh=qh, nkb=nkb, b=b):
            for kb in range(nkb):
                diag = kb >= 8 * qh
                if diag:
                    se.wait_ge(c.S["dv"], mask_sched[kb])
                else:
                    se.wait_ge(c.S["mm"], qk_sched[kb])
                c.inc(se.activation(probs[kb % 2][:, :], S_ps[kb % 2][:, :], AF.Exp,
                                    bias=c.T["bias1"][:, b:b + 1],
                                    scale=float(1.0 / np.sqrt(128.0))), "ac", 1)
        block.scalar(_)
        assert c.cnt["ac"] == ac_base + nkb
        pv_mm = c.cnt["mm"]

        def _(ve, pv_mm=pv_mm):
            ve.wait_ge(c.S["mm"], pv_mm)
            ve.tensor_copy(oacc[:, :], oa_ps[:, :])
            c.inc(ve.reciprocal(rc[0:1, :], sum_ps[0:1, :]), "dv", 1)
        block.vector(_)
        rc_dv = c.cnt["dv"]

        def _(te, rc_dv=rc_dv):
            te.wait_ge(c.S["dv"], rc_dv)
            for half in range(2):
                ins = te.matmul(S_ps[0][:, half * 512:(half + 1) * 512],
                                c.T["ones_row"][:, :],
                                rc[0:1, half * 512:(half + 1) * 512],
                                start=True, stop=True)
            c.inc(ins, "mm", 1)
        block.tensor(_)
        rb_mm = c.cnt["mm"]

        def _(ve, rb_mm=rb_mm, opay_done=opay_done):
            ve.wait_ge(c.S["mm"], rb_mm)
            if opay_done:
                ve.wait_ge(c.S["wd"], opay_done)
            c.inc(ve.tensor_tensor(obuf[:, :], S_ps[0][:, :], oacc[:, :],
                                   op=ALU.mult), "dv", 1)
        block.vector(_)
        ob_dv = c.cnt["dv"]
        epi_dv = c.cnt["dv"]

        def _(sy, qh=qh, b=b, ob_dv=ob_dv):
            sy.wait_ge(c.S["dv"], ob_dv)
            c.inc(sy.dma_start(out=A[f"pay2b{b}"][:, qh * 1024:(qh + 1) * 1024],
                               in_=obuf[:, :]), "wd", 16)
        block.sync(_)
        opay_done = c.cnt["wd"]

    def _(te):
        te.wait_ge(c.S["wd"], c.cnt["wd"])
        te.wait_ge(c.S["dv"], c.cnt["dv"])
    block.tensor(_)
    fst.close()


def p7_oproj_b(c, b, o_own, wo_sb, idx_sb):
    nc, block = c.nc, c.block
    A = c.A
    ps_st = contextlib.ExitStack()
    y_ps = [c.ps(ps_st, f"l1y_ps{i}_{b}", [128, 1024], F32) for i in range(2)]

    def _(g, b=b):
        g.wait_ge(c.S["cc"], c.p6_ccb[b])
        g.wait_ge(c.S["mm"], c.cnt["mm"])
        gview = A[f"gath2b{b}"][:, :].rearrange("r (cb n) -> (r cb) n", cb=8)
        for r in range(8):
            c.inc(g.indirect_dma_start(
                out=o_own[:, r * 1024:(r + 1) * 1024],
                out_offset=None,
                in_=gview,
                in_offset=bass.IndirectOffsetOnAxis(
                    ap=idx_sb[:, r: r + 1], axis=0)), "gd", 16)
    block.gpsimd(_)
    ext_gd = c.cnt["gd"]

    yadds = {}
    for j, fc in enumerate(range(NF)):
        def _(te, j=j, b=b, fc=fc):
            if j == 0:
                te.wait_ge(c.S["gd"], ext_gd)
                c.psum_guard(te)
            if j >= 2:
                te.wait_ge(c.S["dv"], yadds[j - 2])
            for q2 in range(2):
                for r in range(8):
                    ins = te.matmul(
                        y_ps[j % 2][:, q2 * 512:(q2 + 1) * 512],
                        wo_sb[:, r * 1024 + fc * 128: r * 1024 + (fc + 1) * 128],
                        o_own[:, r * 1024 + q2 * 512: r * 1024 + (q2 + 1) * 512],
                        start=(r == 0), stop=(r == 7))
            c.inc(ins, "mm", 1)
        block.tensor(_)
        y_mm = c.cnt["mm"]

        def _(ve, j=j, b=b, fc=fc, y_mm=y_mm):
            ve.wait_ge(c.S["mm"], y_mm)
            xsl = c.T["x_c"][:, fc * 2048 + b * 1024: fc * 2048 + (b + 1) * 1024]
            c.inc(ve.tensor_tensor(xsl, xsl, y_ps[j % 2][:, :], op=ALU.add), "dv", 1)
        block.vector(_)
        yadds[j] = c.cnt["dv"]

    def _(te):
        te.wait_ge(c.S["dv"], c.cnt["dv"])
    block.tensor(_)
    ps_st.close()


def p5to7_l1_attn(c):
    nc, block = c.nc, c.block
    A = c.A
    st = contextlib.ExitStack()

    h2_st = contextlib.ExitStack()
    c.T["h2_bf"] = c.sb(h2_st, "h2_bf", [128, 16384], BF16)
    p_layernorm(c, "h2_bf", eps=1e-5, tag="_p5")
    hdv = c.cnt["dv"]

    def _(sy):
        for fc in range(4):
            sy.wait_ge(c.S["dv"], hdv - (NF - 1 - fc))
            c.inc(sy.dma_start(out=A["pay1a"][fc * 128:(fc + 1) * 128, :],
                               in_=c.T["h2_bf"][:, fc * 2048:(fc + 1) * 2048]), "wd", 16)
    block.sync(_)
    pay_wd_a = c.cnt["wd"]

    def _(g):
        g.wait_ge(c.S["wd"], pay_wd_a)
        c.inc(g.collective_compute(
            "AllGather", ALU.bypass, replica_groups=[list(range(8))],
            ins=[A["pay1a"][:, :].opt()], outs=[A["gath1a"][:, :].opt()]), "cc", 1)
    block.gpsimd(_)
    c.p5_cc_a = c.cnt["cc"]

    def _(sy):
        for fc in range(4, NF):
            sy.wait_ge(c.S["dv"], hdv - (NF - 1 - fc))
            c.inc(sy.dma_start(out=A["pay1b"][(fc - 4) * 128:(fc - 3) * 128, :],
                               in_=c.T["h2_bf"][:, fc * 2048:(fc + 1) * 2048]), "wd", 16)
    block.sync(_)
    pay_wd = c.cnt["wd"]

    def _(g):
        g.wait_ge(c.S["wd"], pay_wd)
        c.inc(g.collective_compute(
            "AllGather", ALU.bypass, replica_groups=[list(range(8))],
            ins=[A["pay1b"][:, :].opt()], outs=[A["gath1b"][:, :].opt()]), "cc", 1)
    block.gpsimd(_)
    c.p5_cc = c.cnt["cc"]
    h2_st.close()

    attn_st = contextlib.ExitStack()
    c.T["Q1_sb"] = c.sb(attn_st, "Q1_sb", [128, 8192], BF16)
    c.T["K1_sb"] = c.sb(attn_st, "K1_sb", [128, 8192], BF16)
    c.T["V1_sb"] = c.sb(attn_st, "V1_sb", [128, 8192], BF16)
    c.T["wq1_sb"] = c.sb(attn_st, "wq1_sb", [128, 8 * 128], BF16)
    c.T["wk1_sb"] = c.sb(attn_st, "wk1_sb", [128, 8 * 128], BF16)
    c.T["wv1_sb"] = c.sb(attn_st, "wv1_sb", [128, 8 * 128], BF16)
    c.T["masks_sb"] = c.sb(attn_st, "masks_sb", [128, 8 * 1024], BF16)
    c.T["bias1"] = c.sb(attn_st, "bias1", [128, 4], F32)
    rope_st = contextlib.ExitStack()
    c.T["cosT"] = c.sb(rope_st, "cosT", [128, 8192], BF16)
    c.T["sinT"] = c.sb(rope_st, "sinT", [128, 8192], BF16)
    c.T["pmat"] = c.sb(rope_st, "pmat", [128, 128], BF16)

    def _(g):
        c.fence(g)
        for name, dst in (("wq1o", "wq1_sb"), ("wk1o", "wk1_sb"), ("wv1o", "wv1_sb")):
            c.inc(g.dma_start(out=c.T[dst][:, :].rearrange("p (fc n) -> p fc n", fc=8),
                              in_=A[name][:, :].rearrange("(fc p) n -> p fc n", p=128)),
                  "gd", 16)
        c.inc(g.dma_start(out=c.T["cosT"][:, :], in_=A["cos_in"][:, :]), "gd", 16)
        c.inc(g.dma_start(out=c.T["sinT"][:, :], in_=A["sin_in"][:, :]), "gd", 16)
        c.inc(g.dma_start(out=c.T["pmat"][:, :], in_=A["pmat_in"][:, :]), "gd", 16)
        c.inc(g.dma_start(out=c.T["masks_sb"][:, :], in_=A["masks_in"][:, :]), "gd", 16)
    block.gpsimd(_)

    c.p6_ccb = {}
    p5_proj_l1(c, 0, st)
    p6_flash_l1(c, 0, st)

    def _(g):
        g.wait_ge(c.S["wd"], c.cnt["wd"])
        c.inc(g.collective_compute(
            "AllGather", ALU.bypass, replica_groups=[list(range(8))],
            ins=[A["pay2b0"][:, :].opt()], outs=[A["gath2b0"][:, :].opt()]), "cc", 1)
    block.gpsimd(_)
    c.p6_ccb[0] = c.cnt["cc"]

    p5_proj_l1(c, 1, st)
    rope_st.close()

    o_own = c.sb(attn_st, "o_own", [128, 8 * 1024], BF16)
    wo_sb = c.sb(attn_st, "wo_sb", [128, 8 * 1024], BF16)
    idx_sb = c.sb(attn_st, "idx_sb", [128, 16], U32)

    def _(g):
        c.fence(g)
        c.inc(g.dma_start(out=wo_sb[:, :].rearrange("p (ic n) -> p ic n", ic=8),
                          in_=A["wo1"][:, :].rearrange("(ic p) n -> p ic n", p=128)),
              "gd", 16)
        c.inc(g.dma_start(out=idx_sb[:, :], in_=A["idx_o"][:, :]), "gd", 16)
    block.gpsimd(_)

    p6_flash_l1(c, 1, st)

    def _(g):
        g.wait_ge(c.S["wd"], c.cnt["wd"])
        c.inc(g.collective_compute(
            "AllGather", ALU.bypass, replica_groups=[list(range(8))],
            ins=[A["pay2b1"][:, :].opt()], outs=[A["gath2b1"][:, :].opt()]), "cc", 1)
    block.gpsimd(_)
    c.p6_ccb[1] = c.cnt["cc"]

    p7_oproj_b(c, 0, o_own, wo_sb, idx_sb)
    p7_oproj_b(c, 1, o_own, wo_sb, idx_sb)
    attn_st.close()
    st.close()



def _p8_weight_cast(c):
    A, block = c.A, c.block

    def _(g):
        for e in range(16):
            c.inc(g.dma_start(out=A["m1e_bf"][e, :, :], in_=A["m1w"][e, 0:2048:2, :]),
                  "gd", 16)
            c.inc(g.dma_start(out=A["m2_bf"][e, :, :], in_=A["m2w"][e, :, :]), "gd", 16)
    block.gpsimd(_)
    c.p8_cast_gd = c.cnt["gd"]


def p8_ffn_l1(c):
    nc, block = c.nc, c.block
    A = c.A
    st = contextlib.ExitStack()

    cast_gd = getattr(c, "p8_cast_gd", None)
    if cast_gd is None:
        _p8_weight_cast(c)
        cast_gd = c.p8_cast_gd

    t_bf = c.sb(st, "t_bf", [128, 16384], BF16)
    c.T["t_bf"] = t_bf
    h3_st = contextlib.ExitStack()
    c.T["h3_bf"] = c.sb(h3_st, "h3_bf", [128, 16384], BF16)
    scol_st = contextlib.ExitStack()
    sc_col = c.sb(scol_st, "l1sc_col", [128, NF], F32)
    sc_stage = c.sb(scol_st, "l1sc_stage", [8, 128], F32)
    scps_st = contextlib.ExitStack()
    sc_ps = c.ps(scps_st, "l1sc_ps", [128, 8], F32)

    def _(g):
        c.fence(g)
        c.inc(g.dma_start(out=sc_stage[:, :], in_=A["l1scale"][:, :]), "gd", 16)
    block.gpsimd(_)
    sc_gd = c.cnt["gd"]

    def _(te):
        te.wait_ge(c.S["gd"], sc_gd)
        c.psum_guard(te)
        c.inc(te.transpose(sc_ps[:, :], sc_stage[:, :], c.T["ident"][0:8, 0:8]), "mm", 1)
    block.tensor(_)
    sc_mm = c.cnt["mm"]

    def _(ve):
        ve.wait_ge(c.S["mm"], sc_mm)
        c.inc(ve.tensor_copy(sc_col[:, :], sc_ps[:, :]), "dv", 1)
    block.vector(_)

    def _(te):
        te.wait_ge(c.S["dv"], c.cnt["dv"])
    block.tensor(_)
    scps_st.close()

    p_layernorm(c, "h3_bf", eps=1e-5, add_to_stream=True, tag="_p8a")
    p_layernorm(c, "t_bf", eps=1e-5, rms=True, src_bf=c.T["h3_bf"],
                affine=(sc_col,), tag="_p8b")
    scol_st.close()
    h3_st.close()

    cst = contextlib.ExitStack()
    gwT = c.sb(st, "gwT", [128, 8 * 16], BF16)
    gb_bf = c.sb(st, "gb_bf", [1, 16], BF16)
    onesr_bf = c.sb(st, "onesr_bf", [1, 128], BF16)
    b1e_t = c.sb(st, "b1e_t", [128, 8 * 16], F32)
    m2b_sb = c.sb(st, "m2b_sb", [16, 1024], BF16)
    routeT1 = c.sb(st, "routeT1", [16, 2048], BF16)
    sel16_sb = c.sb(st, "sel16_sb", [16, 16 * 128], BF16)
    gfb8 = c.sb(st, "gfb8_sb", [128, 2048], BF16)

    gw_stage = c.sb(cst, "gw_stage", [16, 2048], F32)
    b_stage = c.sb(cst, "l1b_stage", [16, 2048], F32)
    bev = c.sb(cst, "l1_bev", [16, 1024], F32)
    gt_ps = c.ps(cst, "l1gt_ps", [128, 16], F32)

    def _(g):
        c.fence(g)
        c.inc(g.dma_start(out=gw_stage[0:16, 0:1024], in_=A["gw1"][:, :]), "gd", 16)
        c.inc(g.dma_start(out=gb_bf[:, :], in_=A["gb1"][:, :]), "gd", 16)
        c.inc(g.dma_start(out=b_stage[:, :], in_=A["m1b"][:, :]), "gd", 16)
        c.inc(g.dma_start(out=m2b_sb[:, :], in_=A["m2b"][:, :]), "gd", 16)
        c.inc(g.dma_start(out=sel16_sb[:, :], in_=A["sel16"][:, :]), "gd", 16)
        c.inc(g.memset(onesr_bf[:, :], 1.0), "pl", 1)
    block.gpsimd(_)
    cst_gd = c.cnt["gd"]
    cst_pl = c.cnt["pl"]

    def _(ve):
        ve.wait_ge(c.S["gd"], cst_gd)
        c.inc(ve.tensor_copy(bev[:, :],
                             b_stage[:, :].rearrange("p (i two) -> p i two", two=2)[:, :, 0]),
              "dv", 1)
    block.vector(_)
    bev_dv = c.cnt["dv"]

    jobs = [("gw", fc) for fc in range(8)] + [("b1", ic) for ic in range(8)]
    tcopy_dv = {}
    for ji, (kind, idx) in enumerate(jobs):
        def _(te, ji=ji, kind=kind, idx=idx):
            if ji == 0:
                te.wait_ge(c.S["dv"], bev_dv)
                c.psum_guard(te)
            else:
                te.wait_ge(c.S["dv"], tcopy_dv[ji - 1])
            src = (gw_stage[0:16, idx * 128:(idx + 1) * 128] if kind == "gw"
                   else bev[0:16, idx * 128:(idx + 1) * 128])
            c.inc(te.transpose(gt_ps[:, :], src, c.T["ident"][0:16, 0:16]), "mm", 1)
        block.tensor(_)
        mm_h = c.cnt["mm"]

        def _(ve, ji=ji, kind=kind, idx=idx, mm_h=mm_h):
            ve.wait_ge(c.S["mm"], mm_h)
            dst = (gwT[:, idx * 16:(idx + 1) * 16] if kind == "gw"
                   else b1e_t[:, idx * 16:(idx + 1) * 16])
            c.inc(ve.tensor_copy(dst, gt_ps[:, :]), "dv", 1)
        block.vector(_)
        tcopy_dv[ji] = c.cnt["dv"]

    gate_st = contextlib.ExitStack()
    lg_ps = c.ps(gate_st, "lg1_ps", [128, 16], F32)
    rt_ps = c.ps(gate_st, "rt1_ps", [16, 128], F32)
    gsc = c.sb(gate_st, "l1gate_sc", [128, 160], F32)
    cur = gsc[:, 0:16]
    eqs = [gsc[:, 16 + i * 16: 32 + i * 16] for i in range(4)]
    route_t = gsc[:, 80:96]
    tmp16 = gsc[:, 96:112]
    mcol = [gsc[:, 112 + i: 113 + i] for i in range(4)]
    ecol = [gsc[:, 116 + i: 117 + i] for i in range(4)]
    nm1 = gsc[:, 120:121]
    ssum = gsc[:, 121:122]
    rw = gsc[:, 122:123]
    wcol = [gsc[:, 123 + i: 124 + i] for i in range(4)]

    lg_done = {}
    gm_done = {}
    tr_done = {}
    rt_done = {}
    for tt in range(16):
        def _(te, tt=tt):
            if tt == 0:
                c.psum_guard(te)
                te.wait_ge(c.S["pl"], cst_pl)
            else:
                te.wait_ge(c.S["dv"], gm_done[tt - 1])
                te.wait_ge(c.S["ac"], rt_done.get(tt - 1, 0))
            for fc in range(NF):
                te.matmul(lg_ps[:, :],
                          t_bf[:, fc * 2048 + tt * 128: fc * 2048 + (tt + 1) * 128],
                          gwT[:, fc * 16:(fc + 1) * 16],
                          start=(fc == 0), stop=False)
            c.inc(te.matmul(lg_ps[:, :], onesr_bf[:, :], gb_bf[:, :],
                            start=False, stop=True), "mm", 1)
        block.tensor(_)
        lg_done[tt] = c.cnt["mm"]

        def _(ve, tt=tt):
            ve.wait_ge(c.S["mm"], lg_done[tt])
            ve.tensor_copy(cur[:, :], lg_ps[:, :])
            ve.drain()
            for i in range(4):
                ve.reduce_max(mcol[i], cur[:, :], axis=AX.X)
                ve.drain()
                ve.tensor_scalar(eqs[i], cur[:, :], mcol[i], None, op0=ALU.is_equal)
                ve.drain()
                if i < 3:
                    ve.tensor_scalar(tmp16, eqs[i], -1e30, None, op0=ALU.mult)
                    ve.drain()
                    ve.tensor_tensor(cur[:, :], cur[:, :], tmp16, op=ALU.add)
                    ve.drain()
            c.inc(ve.tensor_scalar(nm1, mcol[0], -1.0, None, op0=ALU.mult), "dv", 1)
        block.vector(_)
        topk_dv = c.cnt["dv"]

        def _(se, tt=tt, topk_dv=topk_dv):
            se.wait_ge(c.S["dv"], topk_dv)
            for i in range(4):
                ins = se.activation(ecol[i], mcol[i], AF.Exp, bias=nm1)
            c.inc(ins, "ac", 1)
        block.scalar(_)
        exp_ac = c.cnt["ac"]

        def _(ve, tt=tt, exp_ac=exp_ac):
            ve.wait_ge(c.S["ac"], exp_ac)
            ve.tensor_tensor(ssum, ecol[0], ecol[1], op=ALU.add)
            ve.drain()
            ve.tensor_tensor(ssum, ssum, ecol[2], op=ALU.add)
            ve.drain()
            ve.tensor_tensor(ssum, ssum, ecol[3], op=ALU.add)
            ve.drain()
            ve.reciprocal(rw, ssum)
            ve.drain()
            for i in range(4):
                ve.tensor_tensor(wcol[i], ecol[i], rw, op=ALU.mult)
            ve.drain()
            ve.tensor_scalar(route_t, eqs[0], wcol[0], None, op0=ALU.mult)
            ve.drain()
            for i in range(1, 4):
                ve.scalar_tensor_tensor(route_t, eqs[i], wcol[i], route_t,
                                        op0=ALU.mult, op1=ALU.add)
                ve.drain()
            c.inc(ve.tensor_copy(tmp16, route_t), "dv", 1)
        block.vector(_)
        gm_done[tt] = c.cnt["dv"]

        def _(te, tt=tt):
            te.wait_ge(c.S["dv"], gm_done[tt])
            c.inc(te.transpose(rt_ps[:, :], route_t, c.T["ident"][:, :]), "mm", 1)
        block.tensor(_)
        tr_done[tt] = c.cnt["mm"]

        def _(se, tt=tt):
            se.wait_ge(c.S["mm"], tr_done[tt])
            c.inc(se.copy(routeT1[0:16, tt * 128:(tt + 1) * 128], rt_ps[0:16, :]), "ac", 1)
        block.scalar(_)
        rt_done[tt] = c.cnt["ac"]

    def _(te):
        te.wait_ge(c.S["ac"], rt_done[15])
    block.tensor(_)
    gate_st.close()
    cst.close()

    wbuf_st = contextlib.ExitStack()
    w1T = [c.sb(wbuf_st, f"w1T{i}", [128, 8 * 1024], BF16) for i in range(2)]
    w2T = c.sb(wbuf_st, "w2T", [128, 8 * 1024], BF16)
    s1_sb = c.sb(wbuf_st, "s1_sb", [128, 8 * 2048], BF16)
    lim_sc = [c.sb(wbuf_st, f"lim_sc{i}", [128, 1024], BF16) for i in range(2)]
    sig_sc = [c.sb(wbuf_st, f"sig_sc{i}", [128, 1024], BF16) for i in range(2)]
    h_ps = [c.ps(wbuf_st, f"h1_ps{i}", [128, 1024], F32) for i in range(2)]
    y_ps = [c.ps(wbuf_st, f"y1_ps{i}", [128, 1024], F32) for i in range(2)]

    w1ld = {}
    w2ld = {}
    mlp1_done = {}
    mlp2_done = {}
    gelu_done = {}
    smul_done = {}
    yadd_done = {}
    gfb_done = {}

    for e in range(16):
        def _(sy, e=e):
            if e == 0:
                c.fence(sy)
                sy.wait_ge(c.S["gd"], cast_gd)
            if e >= 2:
                sy.wait_ge(c.S["mm"], mlp1_done[e - 2])
            for fc in range(NF):
                c.inc(sy.dma_start(out=w1T[e % 2][:, fc * 1024:(fc + 1) * 1024],
                                   in_=A["m1e_bf"][e, :, fc * 128:(fc + 1) * 128],
                                   transpose=True), "wd", 16)
        block.sync(_)
        w1ld[e] = c.cnt["wd"]

        def _(te, e=e):
            if e == 0:
                te.wait_ge(c.S["ac"], rt_done[15])
                c.psum_guard(te)
            else:
                te.wait_ge(c.S["dv"], yadd_done[e - 1])
            for yb in range(2):
                for q2 in range(2):
                    ins = te.matmul(
                        y_ps[yb][:, q2 * 512:(q2 + 1) * 512],
                        sel16_sb[:, e * 128:(e + 1) * 128],
                        routeT1[0:16, yb * 1024 + q2 * 512: yb * 1024 + (q2 + 1) * 512],
                        start=True, stop=True)
            c.inc(ins, "mm", 1)
        block.tensor(_)
        gfb_mm = c.cnt["mm"]

        def _(ve, e=e, gfb_mm=gfb_mm):
            ve.wait_ge(c.S["mm"], gfb_mm)
            ve.tensor_copy(gfb8[:, 0:1024], y_ps[0][:, :])
            c.inc(ve.tensor_copy(gfb8[:, 1024:2048], y_ps[1][:, :]), "dv", 1)
        block.vector(_)
        gfb_done[e] = c.cnt["dv"]

        ac_base8 = c.cnt["ac"]
        sig_sched = {j: ac_base8 + j + 1 for j in range(16)}
        lim_dvs = {}
        h_mms = {}
        jlist8 = [(ic, th) for ic in range(8) for th in range(2)]
        for j, (ic, th) in enumerate(jlist8):
            def _(te, e=e, j=j, ic=ic, th=th):
                if j == 0:
                    te.wait_ge(c.S["wd"], w1ld[e])
                    te.wait_ge(c.S["dv"], gfb_done[e])
                if j >= 2:
                    te.wait_ge(c.S["dv"], smul_done[(e, j - 2)])
                for q2 in range(2):
                    for fc in range(NF):
                        ins = te.matmul(
                            h_ps[j % 2][:, q2 * 512:(q2 + 1) * 512],
                            w1T[e % 2][:, fc * 1024 + ic * 128: fc * 1024 + (ic + 1) * 128],
                            t_bf[:, fc * 2048 + th * 1024 + q2 * 512:
                                 fc * 2048 + th * 1024 + (q2 + 1) * 512],
                            start=(fc == 0), stop=(fc == NF - 1))
                c.inc(ins, "mm", 1)
            block.tensor(_)
            mlp1_done[e] = c.cnt["mm"]
            h_mms[j] = c.cnt["mm"]

            def _(ve, e=e, j=j, ic=ic, th=th):
                ve.wait_ge(c.S["mm"], h_mms[j])
                if j >= 2:
                    ve.wait_ge(c.S["ac"], sig_sched[j - 2])
                c.inc(ve.tensor_scalar(lim_sc[j % 2][:, :], h_ps[j % 2][:, :],
                                       b1e_t[:, ic * 16 + e: ic * 16 + e + 1], 7.0,
                                       op0=ALU.add, op1=ALU.min), "dv", 1)
                ve.wait_ge(c.S["ac"], sig_sched[j])
                dst = s1_sb[:, ic * 2048 + th * 1024: ic * 2048 + (th + 1) * 1024]
                ve.tensor_tensor(dst, lim_sc[j % 2][:, :],
                                 gfb8[:, th * 1024:(th + 1) * 1024], op=ALU.mult)
                ve.drain()
                c.inc(ve.tensor_tensor(dst, dst, sig_sc[j % 2][:, :], op=ALU.mult),
                      "dv", 1)
            block.vector(_)
            lim_dvs[j] = c.cnt["dv"] - 1
            smul_done[(e, j)] = c.cnt["dv"]

        def _(se, e=e):
            for j in range(16):
                se.wait_ge(c.S["dv"], lim_dvs[j])
                if j >= 2:
                    se.wait_ge(c.S["dv"], smul_done[(e, j - 2)])
                c.inc(se.activation(sig_sc[j % 2][:, :], lim_sc[j % 2][:, :],
                                    AF.Sigmoid, scale=1.702), "ac", 1)
        block.scalar(_)
        assert c.cnt["ac"] == ac_base8 + 16

        s_ready = c.cnt["dv"]

        def _(sy, e=e):
            if e >= 1:
                sy.wait_ge(c.S["mm"], mlp2_done[e - 1])
            for ic in range(NF):
                c.inc(sy.dma_start(out=w2T[:, ic * 1024:(ic + 1) * 1024],
                                   in_=A["m2_bf"][e, :, ic * 128:(ic + 1) * 128],
                                   transpose=True), "wd", 16)
        block.sync(_)
        w2ld[e] = c.cnt["wd"]

        for j, (fc, th) in enumerate([(fc, th) for fc in range(NF) for th in range(2)]):
            def _(te, e=e, j=j, fc=fc, th=th, s_ready=s_ready):
                if j == 0:
                    te.wait_ge(c.S["dv"], s_ready)
                    te.wait_ge(c.S["wd"], w2ld[e])
                if j >= 2:
                    te.wait_ge(c.S["dv"], yadd_done[(e, j - 2)])
                for q2 in range(2):
                    col0 = th * 1024 + q2 * 512
                    for ic in range(8):
                        ins = te.matmul(
                            y_ps[j % 2][:, q2 * 512:(q2 + 1) * 512],
                            w2T[:, ic * 1024 + fc * 128: ic * 1024 + (fc + 1) * 128],
                            s1_sb[:, ic * 2048 + col0: ic * 2048 + col0 + 512],
                            start=(ic == 0), stop=(ic == 7 and e != 0))
                    if e == 0:
                        ins = te.matmul(y_ps[j % 2][:, q2 * 512:(q2 + 1) * 512],
                                        m2b_sb[0:16, fc * 128:(fc + 1) * 128],
                                        routeT1[0:16, col0: col0 + 512],
                                        start=False, stop=True)
                c.inc(ins, "mm", 1)
            block.tensor(_)
            mlp2_done[e] = c.cnt["mm"]
            y_mm = c.cnt["mm"]

            def _(ve, e=e, j=j, fc=fc, th=th, y_mm=y_mm):
                ve.wait_ge(c.S["mm"], y_mm)
                xsl = c.T["x_c"][:, fc * 2048 + th * 1024: fc * 2048 + (th + 1) * 1024]
                c.inc(ve.tensor_tensor(xsl, xsl, y_ps[j % 2][:, :], op=ALU.add), "dv", 1)
            block.vector(_)
            yadd_done[(e, j)] = c.cnt["dv"]
        yadd_done[e] = c.cnt["dv"]

    def _(te):
        te.wait_ge(c.S["dv"], c.cnt["dv"])
    block.tensor(_)
    wbuf_st.close()
    st.close()



def build_launch(phases, dbg=()):
    nc = bass.Bass("TRN2", target_bir_lowering=False)
    root = contextlib.ExitStack()
    A = {}

    A["x_in"] = nc.dram_tensor("x_in", [TOWN, D], F32, kind="ExternalInput").ap()
    A["ident_in"] = nc.dram_tensor("ident_in", [128, 128], F32, kind="ExternalInput").ap()
    A["y_out"] = nc.dram_tensor("y_out", [TOWN, D], F32, kind="ExternalOutput").ap()
    if "p1" in phases:
        A["ln_g"] = nc.dram_tensor("ln_g", [8, 128], F32, kind="ExternalInput").ap()
        A["ln_b"] = nc.dram_tensor("ln_b", [8, 128], F32, kind="ExternalInput").ap()
    if "p2" in phases:
        A["wq"] = nc.dram_tensor("wq", [D, D], F32, kind="ExternalInput").ap()
        A["wk"] = nc.dram_tensor("wk", [D, D], F32, kind="ExternalInput").ap()
        A["wv"] = nc.dram_tensor("wv", [D, D], F32, kind="ExternalInput").ap()
        A["pay0m"] = nc.dram_tensor("pay0m", [1, 32], BF16, kind="Internal").ap()
        A["gath0m"] = nc.dram_tensor("gath0m", [8, 32], BF16, kind="Internal",
                                     addr_space="Shared").ap()
        for g in range(2):
            A[f"pay0k{g}"] = nc.dram_tensor(f"pay0k{g}", [512, 2048], BF16,
                                            kind="Internal").ap()
            A[f"gath0k{g}"] = nc.dram_tensor(f"gath0k{g}", [8 * 512, 2048], BF16,
                                             kind="Internal", addr_space="Shared").ap()
            A[f"pay0v{g}"] = nc.dram_tensor(f"pay0v{g}", [2048, 512], BF16,
                                            kind="Internal").ap()
            A[f"gath0v{g}"] = nc.dram_tensor(f"gath0v{g}", [8 * 2048, 512], BF16,
                                             kind="Internal", addr_space="Shared").ap()
    if "p4" in phases:
        A["gate_W"] = nc.dram_tensor("gate_W", [D, 8], F32, kind="ExternalInput").ap()
        A["W1"] = nc.dram_tensor("W1", [8, D, 512], F32, kind="ExternalInput").ap()
        A["b1"] = nc.dram_tensor("b1", [8, 512], F32, kind="ExternalInput").ap()
        A["W2"] = nc.dram_tensor("W2", [8, 512, D], F32, kind="ExternalInput").ap()
        A["b2"] = nc.dram_tensor("b2", [8, D], F32, kind="ExternalInput").ap()
        A["Ws1"] = nc.dram_tensor("Ws1", [D, 512], F32, kind="ExternalInput").ap()
        A["bs1"] = nc.dram_tensor("bs1", [1, 512], F32, kind="ExternalInput").ap()
        A["Ws2"] = nc.dram_tensor("Ws2", [512, D], F32, kind="ExternalInput").ap()
        A["bs2"] = nc.dram_tensor("bs2", [1, D], F32, kind="ExternalInput").ap()
        A["rs"] = nc.dram_tensor("rs", [1, 1], F32, kind="ExternalInput").ap()
        A["sel8"] = nc.dram_tensor("sel8", [8, 1024], BF16, kind="ExternalInput").ap()
    if "p5" in phases:
        A["wq1o"] = nc.dram_tensor("wq1o", [D, 128], F32, kind="ExternalInput").ap()
        A["wk1o"] = nc.dram_tensor("wk1o", [D, 128], F32, kind="ExternalInput").ap()
        A["wv1o"] = nc.dram_tensor("wv1o", [D, 128], F32, kind="ExternalInput").ap()
        A["wo1"] = nc.dram_tensor("wo1", [D, D], F32, kind="ExternalInput").ap()
        A["cos_in"] = nc.dram_tensor("cos_in", [128, 8192], BF16, kind="ExternalInput").ap()
        A["sin_in"] = nc.dram_tensor("sin_in", [128, 8192], BF16, kind="ExternalInput").ap()
        A["pmat_in"] = nc.dram_tensor("pmat_in", [128, 128], BF16, kind="ExternalInput").ap()
        A["masks_in"] = nc.dram_tensor("masks_in", [128, 8192], BF16, kind="ExternalInput").ap()
        A["idx_o"] = nc.dram_tensor("idx_o", [128, 16], U32, kind="ExternalInput").ap()
        A["pay1a"] = nc.dram_tensor("pay1a", [512, 2048], BF16, kind="Internal").ap()
        A["pay1b"] = nc.dram_tensor("pay1b", [512, 2048], BF16, kind="Internal").ap()
        A["gath1a"] = nc.dram_tensor("gath1a", [4096, 2048], BF16, kind="Internal",
                                     addr_space="Shared").ap()
        A["gath1b"] = nc.dram_tensor("gath1b", [4096, 2048], BF16, kind="Internal",
                                     addr_space="Shared").ap()
        for bb in range(2):
            A[f"pay2b{bb}"] = nc.dram_tensor(f"pay2b{bb}", [128, 8192], BF16,
                                             kind="Internal").ap()
            A[f"gath2b{bb}"] = nc.dram_tensor(f"gath2b{bb}", [1024, 8192], BF16,
                                              kind="Internal", addr_space="Shared").ap()
    if "p8" in phases:
        A["l1scale"] = nc.dram_tensor("l1scale", [8, 128], F32, kind="ExternalInput").ap()
        A["gw1"] = nc.dram_tensor("gw1", [16, D], F32, kind="ExternalInput").ap()
        A["gb1"] = nc.dram_tensor("gb1", [1, 16], F32, kind="ExternalInput").ap()
        A["m1w"] = nc.dram_tensor("m1w", [16, 2048, D], F32, kind="ExternalInput").ap()
        A["m1b"] = nc.dram_tensor("m1b", [16, 2048], F32, kind="ExternalInput").ap()
        A["m2w"] = nc.dram_tensor("m2w", [16, D, D], F32, kind="ExternalInput").ap()
        A["m2b"] = nc.dram_tensor("m2b", [16, D], F32, kind="ExternalInput").ap()
        A["sel16"] = nc.dram_tensor("sel16", [16, 2048], BF16, kind="ExternalInput").ap()
        A["m1e_bf"] = nc.dram_tensor("m1e_bf", [16, 1024, D], BF16, kind="Internal").ap()
        A["m2_bf"] = nc.dram_tensor("m2_bf", [16, D, D], BF16, kind="Internal").ap()
    for d in dbg:
        if d == "xn":
            A["dbg_xn"] = nc.dram_tensor("dbg_xn", [128, 16384], F32, kind="ExternalOutput").ap()
        if d == "gate":
            A["dbg_route"] = nc.dram_tensor("dbg_route", [8, 2048], F32, kind="ExternalOutput").ap()
            A["dbg_gfb"] = nc.dram_tensor("dbg_gfb", [128, 2048], F32, kind="ExternalOutput").ap()
            A["dbg_gsc"] = nc.dram_tensor("dbg_gsc", [128, 64], F32, kind="ExternalOutput").ap()
        if d == "rows":
            A["dbg_rows"] = nc.dram_tensor("dbg_rows", [1, 8192], F32, kind="ExternalOutput").ap()
            A["dbg_gb"] = nc.dram_tensor("dbg_gb", [128, 48], F32, kind="ExternalOutput").ap()

    with root:
        block = root.enter_context(nc.Block())
        c = Ctx(nc, block, root)
        c.A = A

        c.T["x_c"] = c.sb(root, "x_c", [128, 16384], F32)
        c.T["ident"] = c.sb(root, "ident", [128, 128], F32)
        c.T["ident_bf"] = c.sb(root, "ident_bf", [128, 128], BF16)
        c.T["ones_row"] = c.sb(root, "ones_row", [1, 128], F32)
        c.T["ones_col_bf"] = c.sb(root, "ones_col_bf", [128, 1], BF16)

        def _(sy):
            c.inc(sy.dma_start(out=c.T["ident"][:, :], in_=A["ident_in"][:, :]), "wd", 16)
        block.sync(_)
        id_wd = c.cnt["wd"]

        def _(g):
            g.memset(c.T["ones_row"][:, :], 1.0)
            c.inc(g.memset(c.T["ones_col_bf"][:, :], 1.0), "pl", 1)
        block.gpsimd(_)

        def _(ve):
            ve.wait_ge(c.S["wd"], id_wd)
            c.inc(ve.tensor_copy(c.T["ident_bf"][:, :], c.T["ident"][:, :]), "dv", 1)
        block.vector(_)

        def _(te):
            te.wait_ge(c.S["pl"], c.cnt["pl"])
        block.tensor(_)

        if "p1" in phases:
            gb_st = contextlib.ExitStack()
            g_col = c.sb(root, "g_col", [128, NF], F32)
            b_col = c.sb(root, "b_col", [128, NF], F32)
            gb_sb = c.sb(gb_st, "gb_stage", [8, 256], F32)
            gb_ps = c.ps(gb_st, "gb_ps", [128, 16], F32)

            def _(sy):
                c.fence(sy)
                c.inc(sy.dma_start(out=gb_sb[0:8, 0:128], in_=A["ln_g"][:, :]), "wd", 16)
                c.inc(sy.dma_start(out=gb_sb[0:8, 128:256], in_=A["ln_b"][:, :]), "wd", 16)
            block.sync(_)
            wd_here = c.cnt["wd"]

            def _(te):
                te.wait_ge(c.S["wd"], wd_here)
                c.psum_guard(te)
                te.transpose(gb_ps[:, 0:8], gb_sb[0:8, 0:128], c.T["ident"][0:8, 0:8])
                c.inc(te.transpose(gb_ps[:, 8:16], gb_sb[0:8, 128:256],
                                   c.T["ident"][0:8, 0:8]), "mm", 1)
            block.tensor(_)
            mm_here = c.cnt["mm"]

            def _(ve):
                ve.wait_ge(c.S["mm"], mm_here)
                ve.tensor_copy(g_col[:, :], gb_ps[:, 0:8])
                c.inc(ve.tensor_copy(b_col[:, :], gb_ps[:, 8:16]), "dv", 1)
            block.vector(_)
            if "dbg_gb" in A:
                def _(sy):
                    sy.wait_ge(c.S["dv"], c.cnt["dv"])
                    c.inc(sy.dma_start(out=A["dbg_gb"][0:8, 16:48],
                                       in_=gb_sb[0:8, 0:32]), "wd", 16)
                block.sync(_)
            gb_st.close()

        _p_load(c)

        if "p1" in phases:
            if "dbg_gb" in A:
                def _(sy):
                    sy.wait_ge(c.S["dv"], c.cnt["dv"])
                    c.inc(sy.dma_start(out=A["dbg_gb"][:, 0:8], in_=g_col[:, :]), "wd", 16)
                    c.inc(sy.dma_start(out=A["dbg_gb"][:, 8:16], in_=b_col[:, :]), "wd", 16)
                block.sync(_)

            if "p2" in phases:
                p23_st = contextlib.ExitStack()
                c.T["Q_sb"] = c.sb(p23_st, "Q_sb", [128, 16 * 2048], BF16)
                c.T["bias_cols"] = c.sb(p23_st, "bias_cols", [128, 32], F32)
            xn_st = contextlib.ExitStack()
            c.T["xn_bf"] = c.sb(xn_st, "xn_bf", [128, 16384], BF16)
            p_layernorm(c, "xn_bf", eps=1e-5, fold_eps2=1e-6,
                        affine=(g_col, b_col), add_to_stream=True,
                        dbg_rows_ap=A.get("dbg_rows"), tag="_p1")

            if "dbg_xn" in A:
                st2 = contextlib.ExitStack()
                xf = c.sb(st2, "dbg_xf", [128, 2048], F32)
                for fc in range(NF):
                    def _(ve, fc=fc):
                        ve.wait_ge(c.S["wd"], c.cnt["wd"])
                        c.inc(ve.tensor_copy(xf[:, :],
                                             c.T["xn_bf"][:, fc * 2048:(fc + 1) * 2048]),
                              "dv", 1)
                    block.vector(_)
                    dv_here = c.cnt["dv"]

                    def _(sy, fc=fc, dv_here=dv_here):
                        sy.wait_ge(c.S["dv"], dv_here)
                        c.inc(sy.dma_start(out=A["dbg_xn"][:, fc * 2048:(fc + 1) * 2048],
                                           in_=xf[:, :]), "wd", 16)
                    block.sync(_)
                st2.close()

        if "p2" in phases:
            p2_qkv_l0(c)
            xn_st.close()
            if "p8" in phases:
                _p8_weight_cast(c)
            if "p3" in phases:
                p3_flash_l0(c)
            p23_st.close()

        if "p4" in phases:
            p4_ffn_l0(c)

        if "p5" in phases:
            p5to7_l1_attn(c)

        if "p8" in phases:
            p8_ffn_l1(c)

        _p_store(c)
    return nc



def _consts():
    return {"ident_in": np.eye(128, dtype=np.float32)}


def _dup_wq(wq):
    return np.ascontiguousarray(np.concatenate(
        [np.concatenate([wq[:, h * 64:(h + 1) * 64]] * 2, axis=1) for h in range(16)],
        axis=1))


def _sel8():
    s = np.zeros((8, 1024), dtype=BF)
    for e in range(8):
        s[e, e * 128:(e + 1) * 128] = 1
    return s


def _rope_tables():
    t = np.arange(8192, dtype=np.float32)
    invf = (1.0 / 10000.0 ** (np.arange(0, 128, 2, dtype=np.float32) / 128.0)).astype(np.float32)
    freqs = np.outer(t, invf).astype(np.float32)
    emb = np.concatenate([freqs, freqs], axis=-1)
    return (np.ascontiguousarray(np.cos(emb).T.astype(BF)),
            np.ascontiguousarray(np.sin(emb).T.astype(BF)))


def _pmat():
    p = np.zeros((128, 128), dtype=BF)
    for d in range(64):
        p[d + 64, d] = -1.0
        p[d, d + 64] = 1.0
    return p


def _masks():
    m = np.zeros((128, 8192), dtype=np.float32)
    for j in range(8):
        for kk in range(128):
            qq = np.arange(1024)
            m[kk, j * 1024:(j + 1) * 1024] = np.where(j * 128 + kk > qq, -1e30, 0.0)
    return m.astype(BF)


def _idx_o(cid):
    idx = np.zeros((128, 16), dtype=np.uint32)
    for r in range(8):
        for p in range(128):
            idx[p, r] = (r * 128 + p) * 8 + cid
    return idx


def _sel16():
    s = np.zeros((16, 2048), dtype=BF)
    for e in range(16):
        s[e, e * 128:(e + 1) * 128] = 8.0
    return s


def _launch_d_maps(inputs, x3s):
    consts = _consts()
    f32 = lambda k: np.ascontiguousarray(np.asarray(inputs[k], np.float32))
    maps = []
    for cid in range(NCORES):
        m = {"x_in": x3s[cid],
             "l1scale": f32("L1_ffn_scale").reshape(8, 128),
             "gw1": f32("L1_ffn_gate_w"), "gb1": f32("L1_ffn_gate_b").reshape(1, 16),
             "m1w": f32("L1_ffn_mlp1_w"), "m1b": f32("L1_ffn_mlp1_b"),
             "m2w": f32("L1_ffn_mlp2_w"), "m2b": f32("L1_ffn_mlp2_b"),
             "sel16": _sel16()}
        m.update(consts)
        maps.append(m)
    return maps


def _launch_c_maps(inputs, x2s):
    consts = _consts()
    cos_t, sin_t = _rope_tables()
    f32 = lambda k: np.ascontiguousarray(np.asarray(inputs[k], np.float32))
    Wq = f32("L1_attn_Wq")
    Wk = f32("L1_attn_Wk")
    Wv = f32("L1_attn_Wv")
    maps = []
    for cid in range(NCORES):
        m = {"x_in": x2s[cid],
             "wq1o": np.ascontiguousarray(Wq[:, 128 * cid:128 * (cid + 1)]),
             "wk1o": np.ascontiguousarray(Wk[:, 128 * (cid // 2):128 * (cid // 2 + 1)]),
             "wv1o": np.ascontiguousarray(Wv[:, 128 * (cid // 2):128 * (cid // 2 + 1)]),
             "wo1": f32("L1_attn_Wo"),
             "cos_in": cos_t, "sin_in": sin_t, "pmat_in": _pmat(),
             "masks_in": _masks(), "idx_o": _idx_o(cid)}
        m.update(consts)
        maps.append(m)
    return maps


def _launch_b_maps(inputs, x1s):
    consts = _consts()
    f32 = lambda k: np.ascontiguousarray(np.asarray(inputs[k], np.float32))
    maps = []
    for cid in range(NCORES):
        m = {"x_in": x1s[cid],
             "gate_W": f32("L0_ffn_gate_W"),
             "W1": f32("L0_ffn_W1"), "b1": f32("L0_ffn_b1"),
             "W2": f32("L0_ffn_W2"), "b2": f32("L0_ffn_b2"),
             "Ws1": f32("L0_ffn_Ws1"), "bs1": f32("L0_ffn_bs1").reshape(1, 512),
             "Ws2": f32("L0_ffn_Ws2"), "bs2": f32("L0_ffn_bs2").reshape(1, 1024),
             "rs": f32("L0_ffn_route_scale").reshape(1, 1),
             "sel8": _sel8()}
        m.update(consts)
        maps.append(m)
    return maps


def _shard_x(x):
    out = []
    for cid in range(NCORES):
        out.append(np.ascontiguousarray(
            np.concatenate([x[0, 1024 * cid:1024 * (cid + 1)],
                            x[1, 1024 * cid:1024 * (cid + 1)]], axis=0)))
    return out


def _unshard_y(shards):
    y = np.empty((2, 8192, 1024), dtype=np.float32)
    for cid in range(NCORES):
        y[0, 1024 * cid:1024 * (cid + 1)] = shards[cid][0:1024]
        y[1, 1024 * cid:1024 * (cid + 1)] = shards[cid][1024:2048]
    return y


def _launch_a_maps(inputs, xs):
    consts = _consts()
    f32 = lambda k: np.ascontiguousarray(np.asarray(inputs[k], np.float32))
    maps = []
    for cid in range(NCORES):
        m = {"x_in": xs[cid],
             "ln_g": f32("L0_attn_ln_scale").reshape(8, 128),
             "ln_b": f32("L0_attn_ln_bias").reshape(8, 128),
             "wq": f32("L0_attn_wq"), "wk": f32("L0_attn_wk"), "wv": f32("L0_attn_wv")}
        m.update(consts)
        maps.append(m)
    return maps


FUSED = True


def kernel(**inputs):
    x = np.asarray(inputs["L0_attn_x"], dtype=np.float32)
    xs = _shard_x(x)
    if FUSED:
        nc = build_launch({"p1", "p2", "p3", "p4", "p5", "p8"})
        maps = []
        for cid in range(NCORES):
            m = {}
            m.update(_launch_a_maps(inputs, xs)[cid])
            m.update(_launch_b_maps(inputs, xs)[cid])
            m.update(_launch_c_maps(inputs, xs)[cid])
            m.update(_launch_d_maps(inputs, xs)[cid])
            m["x_in"] = xs[cid]
            maps.append(m)
        res = run_bass_kernel_spmd(nc, maps, core_ids=list(range(NCORES)))
        return _unshard_y([r["y_out"] for r in res.results])
    nc_a = build_launch({"p1", "p2", "p3"})
    res = run_bass_kernel_spmd(nc_a, _launch_a_maps(inputs, xs), core_ids=list(range(NCORES)))
    x1s = [np.ascontiguousarray(r["y_out"]) for r in res.results]
    nc_b = build_launch({"p4"})
    res = run_bass_kernel_spmd(nc_b, _launch_b_maps(inputs, x1s), core_ids=list(range(NCORES)))
    x2s = [np.ascontiguousarray(r["y_out"]) for r in res.results]
    nc_c = build_launch({"p5"})
    res = run_bass_kernel_spmd(nc_c, _launch_c_maps(inputs, x2s), core_ids=list(range(NCORES)))
    x3s = [np.ascontiguousarray(r["y_out"]) for r in res.results]
    nc_d = build_launch({"p8"})
    res = run_bass_kernel_spmd(nc_d, _launch_d_maps(inputs, x3s), core_ids=list(range(NCORES)))
    return _unshard_y([r["y_out"] for r in res.results])
